# version 01
# baseline (speedup 1.0000x reference)
import math
import numpy as np
import ml_dtypes
import concourse.bass as bass
import concourse.mybir as mybir
from concourse.bass_utils import run_bass_kernel_spmd
from concourse.bass_isa_rust import ReduceOp

F32 = mybir.dt.float32
BF16 = mybir.dt.bfloat16
I32 = mybir.dt.int32
ALU = mybir.AluOpType
AF = mybir.ActivationFunctionType
AX = mybir.AxisListType

B_PER_CORE = 2
S = 2048
E = 1024
H = 16
HD = 64
TC = B_PER_CORE * S
NCHUNK = TC // 128
CPB = S // 128
LN_EPS = 1e-5
GLA_EPS = 1e-5
LOG8 = math.log(8.0)

DEV_PHASES = 4
DEV_NCHUNK = None
DEV_TAP = None
DEV_STAGE = 99
DEV_NO_SMM = False
DEV_SOLO = False
DEV_NO_OINTER = False
DEV_FAKE_S = False
ARENA_COLS = 46300


def _nchunks():
    return DEV_NCHUNK if DEV_NCHUNK is not None else NCHUNK


class Arena:
    def __init__(self, nc, ctx, f32_cols):
        self.t = ctx.enter_context(nc.sbuf_tensor("arena", [128, f32_cols], F32))
        self.cols = f32_cols
        self.off = 0

    def alloc(self, name, shape, dt=F32):
        rows, cols = shape
        exact = cols if dt in (F32, I32) else (cols + 1) // 2
        need = exact + (exact % 2)
        if True:
            import os
            if os.environ.get("ARENA_TRACE"):
                print("ARENA", name, shape, dt, "off", self.off, "need", need)
        assert self.off + need <= self.cols, f"arena overflow at {name}: {self.off}+{need}"
        ap = self.t[0:rows, self.off:self.off + exact]
        self.off += need
        if dt == BF16:
            ap = ap.bitcast(BF16)
        elif dt == I32:
            ap = ap.bitcast(I32)
        if dt == BF16:
            ap = ap[:, 0:cols * 1]
        return ap

    def reset(self):
        self.off = 0

    def mark(self):
        return self.off

    def rollback(self, m):
        self.off = m


class Sem:
    def __init__(self, ctx, nc, name):
        self.h = ctx.enter_context(nc.semaphore(name))
        self.n = 0

    def dma(self, eng, out, in_):
        eng.wait_ge(self.h, self.n)
        r = eng.dma_start(out=out, in_=in_).then_inc(self.h, 16)
        self.n += 16
        return r


def host_constants():
    cb = {}
    cb["ident"] = np.eye(128, dtype=np.float32)
    tri = np.zeros((128, 128), np.float32)
    for blk in range(2):
        for tau in range(64):
            tri[blk * 64 + tau, blk * 64 + tau: blk * 64 + 64] = 1.0
    cb["tric"] = tri
    am = np.zeros((128, 128), np.float32)
    for p in range(128):
        for c in range(128):
            if (p % 64) <= (c % 64):
                am[p, c] = 1.0
    cb["amask"] = am
    ones = np.ones((128, 4), np.float32)
    ones[64:128, 1] = 0.0
    ones[0:64, 2] = 0.0
    cb["ones"] = ones
    wm = np.zeros((256, 512), np.float32)
    for i in range(256):
        wm[i, i + 1: i + 257] = 1.0
    cb["wmask0"] = wm[:128]
    cb["wmask1"] = wm[128:]
    m0 = np.tril(np.ones((256, 256), np.float32))
    cb["bmask0"] = m0[:128]
    cb["bmask1"] = m0[128:]
    lm = np.zeros((128, 16 * 64), np.float32)
    for qc in range(16):
        for p in range(128):
            lm[p, qc * 64: qc * 64 + (qc * 128 + p) // 32 + 1] = 1.0
    cb["lmask"] = lm
    return cb


CONST_ORDER = ["ident", "tric", "amask", "ones", "wmask0", "wmask1",
               "bmask0", "bmask1", "lmask"]

CONSTF_VALS = [-LOG8, 0.5, 32768.5, 256.5, 8.5, 128.5]
CF_ONES = len(CONSTF_VALS)
CF_IDENT = CF_ONES + 128
CF_W = CF_IDENT + 128


def const_blob_f32():
    m = np.zeros((128, CF_W), np.float32)
    m[:, :len(CONSTF_VALS)] = np.asarray(CONSTF_VALS, np.float32)[None, :]
    m[:, CF_ONES:CF_IDENT] = 1.0
    m[:, CF_IDENT:CF_W] = np.eye(128, dtype=np.float32)
    return m


def const_blob_bf16():
    cb = host_constants()
    cols, mats, off = {}, [], 0
    for k in CONST_ORDER:
        m = cb[k]
        cols[k] = (off, m.shape[1])
        mats.append(m.astype(ml_dtypes.bfloat16))
        off += m.shape[1]
    return np.concatenate(mats, axis=1), cols


def build_program():
    import contextlib
    nc = bass.Bass("TRN2", target_bir_lowering=False, debug=False)
    ctx = contextlib.ExitStack()

    x_in = nc.dram_tensor("x_in", [TC, E], F32, kind="ExternalInput").ap()
    y_out = nc.dram_tensor("y_out", [TC, E], F32, kind="ExternalOutput").ap()
    W = {}
    for nm, shp in [
        ("l0_wq", [E, E]), ("l0_wk", [E, E]), ("l0_wv", [E, E]), ("l0_wo", [E, E]),
        ("l0_wf1", [E, 64]), ("l0_wf2", [64, E]), ("l0_wg1", [E, 64]), ("l0_wg2", [64, E]),
        ("l0_nw_fm", [128, 8]),
        ("l0f_wgate", [3072, E]), ("l0f_wup", [3072, E]), ("l0f_wdown", [E, 3072]),
        ("l1_wq", [E, E]), ("l1_wk", [E, E]), ("l1_wv", [E, E]),
        ("l1_wgq", [E, 512]), ("l1_wgk", [E, 512]), ("l1_wgv", [E, 512]),
        ("l1_rout", [512, E]), ("l1_glw_fm", [128, 32]),
        ("l1_wout", [E, E]),
        ("l1f_w1", [4096, E]), ("l1f_b1_fm", [128, 32]), ("l1f_w2", [E, 4096]),
        ("l1f_b2_fm", [128, 8]),
    ]:
        W[nm] = nc.dram_tensor(nm, shp, F32, kind="ExternalInput").ap()
    cbv, ccols = const_blob_bf16()
    cb_h = nc.dram_tensor("constb", [128, cbv.shape[1]], BF16, kind="ExternalInput").ap()
    cf_h = nc.dram_tensor("constf", [128, CF_W], F32, kind="ExternalInput").ap()

    xres = nc.dram_tensor("xres", [TC, E], F32, kind="Internal").ap()
    ln1_sp = nc.dram_tensor("ln1_sp", [TC, E], BF16, kind="Internal").ap()
    up_sp = nc.dram_tensor("up_sp", [TC, 3072], BF16, kind="Internal").ap()
    gate_sp = nc.dram_tensor("gate_sp", [TC, 3072], BF16, kind="Internal").ap()
    t_sp = nc.dram_tensor("t_sp", [TC, 3072], BF16, kind="Internal").ap()
    zf_sp = nc.dram_tensor("zf_sp", [NCHUNK * 128, 3072], BF16, kind="Internal").ap()
    h_sp = nc.dram_tensor("h_sp", [NCHUNK * 128, 4096], BF16, kind="Internal").ap()
    cc_in = [nc.dram_tensor(f"cc_in{i}", [1, 4], F32, kind="Internal").ap() for i in range(8)]
    cc_stage = nc.dram_tensor("cc_stage", [128, 4], F32, kind="Internal").ap()
    bb_sp = nc.dram_tensor("bb_sp", [32, 128], F32, kind="Internal").ap()
    cc_out = [nc.dram_tensor(f"cc_out{i}", [1, 4], F32, kind="Internal").ap() for i in range(8)]

    sb = lambda name, shape, dt=F32: ctx.enter_context(nc.sbuf_tensor(name, shape, dt))
    ps = lambda name, shape, dt=F32: ctx.enter_context(nc.psum_tensor(name, shape, dt))

    consts = sb("consts", [128, cbv.shape[1]], BF16)
    C = {k: consts[:, off:off + w] for k, (off, w) in ccols.items()}
    constf = sb("constf_s", [128, CF_W], F32)
    for i, val in enumerate(CONSTF_VALS):
        nc.const_aps.register(F32, float(val), constf[:, i:i + 1])
    C_ONESROW = constf[0:1, CF_ONES:CF_IDENT]
    C_ONESCOL = constf[:, CF_ONES:CF_ONES + 4]
    C_IDF = constf[:, CF_IDENT:CF_W]

    arena = Arena(nc, ctx, ARENA_COLS)
    pb_bf = ctx.enter_context(nc.psum_tensor("pb_bf", [128, 1024], BF16))
    pb = [ctx.enter_context(nc.psum_tensor(f"pb{i}", [128, 512], F32)) for i in range(7)]

    block = ctx.enter_context(nc.Block())
    sems = {k: Sem(ctx, nc, "sem_" + k) for k in
            ("ld", "gd", "t", "v", "s", "g", "st", "cc")}

    @block.sync
    def _(sy):
        sems["ld"].dma(sy, consts[:], cb_h[:])
        sems["ld"].dma(sy, constf[:], cf_h[:])

    st = dict(nc=nc, block=block, sems=sems, C=C, W=W, sb=arena.alloc, ps=ps, ctx=ctx,
              arena=arena, sb_persist=sb,
              x_in=x_in, y_out=y_out, xres=xres, ln1_sp=ln1_sp, up_sp=up_sp,
              gate_sp=gate_sp, t_sp=t_sp, zf_sp=zf_sp, h_sp=h_sp, bb_sp=bb_sp,
              cc_in=cc_in, cc_out=cc_out, cc_stage=cc_stage, consts_ready=sems["ld"].n,
              C_ONESROW=C_ONESROW, C_ONESCOL=C_ONESCOL, C_IDF=C_IDF,
              pb_bf=pb_bf[:], pb=[p[:] for p in pb])

    def phase_barrier():
        marks = [(sems[k].h, sems[k].n) for k in ("ld", "gd", "t", "v", "s", "g", "st", "cc")]

        for engname in ("sync", "tensor", "vector", "scalar", "gpsimd"):
            @getattr(block, engname)
            def _(e):
                for h, n in marks:
                    e.wait_ge(h, n)
    st["phase_barrier"] = phase_barrier

    st["taps"] = {}

    def tap(name, ap, sem, val):
        if DEV_TAP is None:
            return
        if name in st["taps"]:
            return
        dt_ = ap.dtype
        t_h = nc.dram_tensor("tap_" + name, list(ap.shape), dt_, kind="ExternalOutput").ap()
        st["taps"][name] = t_h

        @block.gpsimd
        def _(g):
            g.wait_ge(sem.h, val)
            sems["gd"].dma(g, t_h[:], ap)
        done = sems["gd"].n

        @block.vector
        def _(v):
            v.wait_ge(sems["gd"].h, done)
            v.drain().then_inc(sems["v"].h, 1)
        sems["v"].n += 1

        @block.scalar
        def _(sc):
            sc.wait_ge(sems["gd"].h, done)
            sc.drain().then_inc(sems["s"].h, 1)
        sems["s"].n += 1
    st["tap"] = tap

    st["scal"] = sb("scal", [128, 16], F32)
    st["ln1_stats"] = sb("ln1_stats", [128, 2 * NCHUNK], F32)
    st["mxacc"] = sb("mxacc", [128, 12], F32)
    st["scrow"] = sb("scrow", [1, 16], F32)

    emit_l0a(st)
    if DEV_PHASES >= 2:
        emit_l0f(st)
    if DEV_PHASES >= 3:
        emit_l1a(st)
    if DEV_PHASES >= 4:
        emit_l1f(st)
    emit_finish(st)

    ctx.close()
    return nc


def emit_l0a(st):
    nc, block, C, W, sb, ps = (st[k] for k in ("nc", "block", "C", "W", "sb", "ps"))
    s_ld, s_gd, s_t, s_v, s_s, s_g, s_st = (st["sems"][k] for k in
                                            ("ld", "gd", "t", "v", "s", "g", "st"))
    ln1_stats, mxacc = st["ln1_stats"], st["mxacc"]

    wq = sb("l0_wq_s", [128, 8 * E], BF16)
    wk = sb("l0_wk_s", [128, 8 * E], BF16)
    wv = sb("l0_wv_s", [128, 8 * E], BF16)
    wo = sb("l0_wo_s", [128, 8 * E], BF16)
    wf1 = sb("l0_wf1_s", [128, 8 * 64], BF16)
    wg1 = sb("l0_wg1_s", [128, 8 * 64], BF16)
    wf2 = sb("l0_wf2_s", [64, E], BF16)
    wg2 = sb("l0_wg2_s", [64, E], BF16)
    nwf = sb("l0_nwf", [128, 8], F32)

    @block.gpsimd
    def _(g):
        for wsb, hbm in ((wq, W["l0_wq"]), (wk, W["l0_wk"]), (wv, W["l0_wv"]),
                         (wo, W["l0_wo"])):
            s_gd.dma(g, wsb[:].rearrange("p (c e) -> p c e", c=8),
                     hbm.rearrange("(c p) e -> p c e", p=128))
        for wsb, hbm in ((wf1, W["l0_wf1"]), (wg1, W["l0_wg1"])):
            s_gd.dma(g, wsb[:].rearrange("p (c e) -> p c e", c=8),
                     hbm.rearrange("(c p) e -> p c e", p=128))
        for wsb, hbm in ((wf2, W["l0_wf2"]), (wg2, W["l0_wg2"])):
            s_gd.dma(g, wsb[:], hbm[:])
    w_loaded = s_gd.n

    @block.sync
    def _(sy):
        s_ld.dma(sy, nwf[:], W["l0_nw_fm"][:])
    nw_loaded = s_ld.n

    @block.vector
    def _(v):
        v.wait_ge(s_gd.h, w_loaded)
        v.wait_ge(s_ld.h, nw_loaded)
        for c in range(8):
            v.tensor_scalar(out=wo[:, c * E:(c + 1) * E], in0=wo[:, c * E:(c + 1) * E],
                            scalar1=nwf[:, c:c + 1], scalar2=None, op0=ALU.mult)
        v.memset(st["mxacc"][:], 0.0)
        v.drain().then_inc(s_v.h, 1)
    s_v.n += 1
    wo_ready = s_v.n

    p_tr = st["pb_bf"]
    p_qk = p_tr
    p_pr = st["pb"][0:4]
    p_nl = [p_pr[0], p_pr[1]]
    p_o = [p_pr[2], p_pr[3]]
    p_sml = st["pb"][4]
    p_sm = p_sml[:, 0:64]
    p_nc = p_sml[:, 64:80]
    p_at = p_sml[:, 128:256]
    p_S = st["pb"][5]
    p_S2 = st["pb"][6]

    xin = [sb(f"l0a_xin{i}", [128, E], F32) for i in range(2)]
    x1t = [sb(f"l0a_x1t{i}", [128, E], F32) for i in range(2)]
    ln_tm = sb("l0a_lntm", [128, E], BF16)
    ln_fm = sb("l0a_lnfm", [128, E], BF16)
    q_tm = sb("l0a_qtm", [128, E], BF16)
    k_tm = sb("l0a_ktm", [128, E], BF16)
    v_tm = sb("l0a_vtm", [128, E], BF16)
    g_tm = sb("l0a_gtm", [128, E], BF16)
    f1_tm = sb("l0a_f1tm", [128, 64], BF16)
    f1_fm = sb("l0a_f1fm", [64, 128], BF16)
    sp32 = sb("l0a_sp32", [128, E], F32)
    fv32 = sb("l0a_fv32", [128, E], F32)
    sq32 = sb("l0a_sq32", [128, E], F32)
    sp_hi = sb("l0a_sphi", [128, E], BF16)
    sp_lo = sb("l0a_splo", [128, E], BF16)
    e1t = sb("l0a_e1", [128, E], BF16)
    e2t = sb("l0a_e2", [128, E], BF16)
    qt_t = sb("l0a_qt", [128, E], BF16)
    kt_t = sb("l0a_kt", [128, E], BF16)
    qf_pad = [sb(f"l0a_qfp{i}", [128, 1024], BF16) for i in range(2)]
    kf_full = sb("l0a_kff", [128, 1024], BF16)
    kt_p = [sb(f"l0a_ktp{i}", [128, 1024], BF16) for i in range(2)]
    am_g = [sb(f"l0a_am{i}", [128, 8 * 128], BF16) for i in range(2)]
    aC = sb("l0a_aC", [128, 16], F32)
    S_all = sb("l0a_S", [128, 2 * 512], F32)
    S_bf = sb("l0a_Sbf", [128, 2 * 512], BF16)
    S_tmp = sb("l0a_Stmp", [128, 512], F32)
    og = sb("l0a_og", [128, E], F32)
    ms_t = sb("l0a_ms", [128, 16], F32)
    rstd_h = sb("l0a_rstdh", [128, 16], F32)
    ogn = sb("l0a_ogn", [128, E], BF16)
    og_fm = sb("l0a_ogfm", [128, E], BF16)
    lnst = sb("l0a_lnst", [128, 8], F32)
    ln1t = sb("l0a_ln1t", [128, E], BF16)

    @block.vector
    def _(v):
        v.memset(S_all[:], 0.0)
        v.memset(S_bf[:], 0.0)
        v.memset(am_g[0][:], 0.0)
        v.memset(am_g[1][:], 0.0)
        v.memset(qf_pad[0][:], 0.0)
        v.memset(qf_pad[1][:], 0.0)
        v.memset(kt_p[0][:], 0.0)
        v.memset(kt_p[1][:], 0.0)
        v.drain().then_inc(s_v.h, 1)
    s_v.n += 1
    sinit = s_v.n

    chunk_v_done = {}
    chunk_st_done = {}
    prev_supd = 0
    prev_qkevac = 0
    prev_amdone = 0

    def ln_block(get_in, out_bf16, store_stats_ci=None, s_wait_v=None):
        xi = get_in

        @block.vector
        def _(v):
            v.reduce_sum(out=lnst[:, 0:1], in_=xi[:], axis=AX.X)
            v.drain().then_inc(s_v.h, 1)
        s_v.n += 1

        @block.scalar
        def _(s, s_wait_v=s_wait_v):
            if s_wait_v is not None:
                s.wait_ge(s_v.h, s_wait_v)
            s.activation(out=sq32[:], in_=xi[:], func=AF.Square)
            s.drain().then_inc(s_s.h, 1)
        s_s.n += 1
        sqd = s_s.n

        @block.vector
        def _(v, sqd=sqd):
            v.wait_ge(s_s.h, sqd)
            v.reduce_sum(out=lnst[:, 1:2], in_=sq32[:], axis=AX.X)
            v.drain()
            v.tensor_scalar(out=lnst[:, 2:3], in0=lnst[:, 0:1], scalar1=1.0 / E,
                            scalar2=None, op0=ALU.mult)
            v.drain()
            v.tensor_scalar(out=lnst[:, 3:4], in0=lnst[:, 2:3], scalar1=lnst[:, 2:3],
                            scalar2=None, op0=ALU.mult)
            v.drain()
            v.scalar_tensor_tensor(out=lnst[:, 4:5], in0=lnst[:, 1:2], scalar=1.0 / E,
                                   in1=lnst[:, 3:4], op0=ALU.mult, op1=ALU.subtract)
            v.drain()
            v.tensor_scalar(out=lnst[:, 4:5], in0=lnst[:, 4:5], scalar1=LN_EPS,
                            scalar2=None, op0=ALU.add)
            v.drain().then_inc(s_v.h, 1)
        s_v.n += 1
        vard = s_v.n

        @block.scalar
        def _(s, vard=vard):
            s.wait_ge(s_v.h, vard)
            s.activation(out=lnst[:, 5:6], in_=lnst[:, 4:5], func=AF.Ln)
            s.drain()
            s.activation(out=lnst[:, 6:7], in_=lnst[:, 5:6], func=AF.Exp, scale=-0.5)
            s.drain().then_inc(s_s.h, 1)
        s_s.n += 1
        sqrtd = s_s.n

        @block.vector
        def _(v, sqrtd=sqrtd):
            v.wait_ge(s_s.h, sqrtd)
            if store_stats_ci is not None:
                ci2 = store_stats_ci
                v.tensor_copy(out=ln1_stats[:, 2 * ci2:2 * ci2 + 1], in_=lnst[:, 2:3])
                v.tensor_copy(out=ln1_stats[:, 2 * ci2 + 1:2 * ci2 + 2], in_=lnst[:, 6:7])
                v.drain()
            v.tensor_scalar(out=out_bf16[:], in0=xi[:], scalar1=lnst[:, 2:3],
                            scalar2=lnst[:, 6:7], op0=ALU.subtract, op1=ALU.mult)
            v.drain().then_inc(s_v.h, 1)
        s_v.n += 1
        return s_v.n

    for ci in range(_nchunks()):
        b = ci // CPB
        xi = xin[ci % 2]
        x1 = x1t[ci % 2]
        r0 = ci * 128

        @block.sync
        def _(sy, ci=ci, xi=xi, r0=r0):
            if ci >= 2:
                sy.wait_ge(s_v.h, chunk_v_done[ci - 2])
            s_ld.dma(sy, xi[:], st["x_in"][r0:r0 + 128, :])
        ld_x = s_ld.n

        @block.vector
        def _(v, ci=ci, ld_x=ld_x):
            v.wait_ge(s_ld.h, ld_x)
            if ci == 0:
                v.wait_ge(s_v.h, sinit)
                v.wait_ge(s_v.h, wo_ready)
                v.wait_ge(s_ld.h, st["consts_ready"])
            v.engine_nop().then_inc(s_v.h, 1)
        s_v.n += 1

        @block.scalar
        def _(s, ld_x=ld_x):
            s.wait_ge(s_ld.h, ld_x)
            s.nop().then_inc(s_s.h, 1)
        s_s.n += 1

        ln_done = ln_block(xi, ln_tm)

        if DEV_STAGE >= 2:
            @block.tensor
            def _(t, ln_done=ln_done, ci=ci):
                t.wait_ge(s_v.h, ln_done)
                for c in range(8):
                    li = t.transpose(p_tr[:, c * 128:(c + 1) * 128],
                                     ln_tm[:, c * 128:(c + 1) * 128], C["ident"])
                li.then_inc(s_t.h, 1)
            s_t.n += 1
            lnT = s_t.n

            @block.vector
            def _(v, lnT=lnT):
                v.wait_ge(s_t.h, lnT)
                v.tensor_copy(out=ln_fm[:], in_=p_tr[:])
                v.drain().then_inc(s_v.h, 1)
            s_v.n += 1
            lnfm_done = s_v.n

        if DEV_STAGE >= 3:
            def proj_mm(t, wsb, pl, ph, src=None):
                src = ln_fm if src is None else src
                for c in range(8):
                    t.matmul(pl[:], src[:, c * 128:(c + 1) * 128],
                             wsb[:, c * E: c * E + 512], start=(c == 0), stop=(c == 7))
                for c in range(8):
                    li = t.matmul(ph[:], src[:, c * 128:(c + 1) * 128],
                                  wsb[:, c * E + 512: (c + 1) * E], start=(c == 0), stop=(c == 7))
                return li

            @block.tensor
            def _(t, lnfm_done=lnfm_done):
                t.wait_ge(s_v.h, lnfm_done)
                proj_mm(t, wq, p_pr[0], p_pr[1]).then_inc(s_t.h, 1)
            s_t.n += 1
            qmm = s_t.n

            @block.scalar
            def _(s, qmm=qmm):
                s.wait_ge(s_t.h, qmm)
                s.activation(out=q_tm[:, 0:512], in_=p_pr[0][:], func=AF.Silu)
                s.activation(out=q_tm[:, 512:1024], in_=p_pr[1][:], func=AF.Silu)
                s.drain().then_inc(s_s.h, 1)
            s_s.n += 1
            q_done = s_s.n

            @block.tensor
            def _(t, q_done=q_done):
                t.wait_ge(s_s.h, q_done)
                proj_mm(t, wk, p_pr[0], p_pr[1]).then_inc(s_t.h, 1)
            s_t.n += 1
            kmm = s_t.n

            @block.scalar
            def _(s, kmm=kmm):
                s.wait_ge(s_t.h, kmm)
                s.activation(out=k_tm[:, 0:512], in_=p_pr[0][:], func=AF.Silu)
                s.activation(out=k_tm[:, 512:1024], in_=p_pr[1][:], func=AF.Silu)
                s.drain().then_inc(s_s.h, 1)
            s_s.n += 1
            k_done = s_s.n

            @block.tensor
            def _(t, k_done=k_done):
                t.wait_ge(s_s.h, k_done)
                proj_mm(t, wv, p_pr[2], p_pr[3]).then_inc(s_t.h, 1)
            s_t.n += 1
            vmm = s_t.n

            @block.vector
            def _(v, vmm=vmm):
                v.wait_ge(s_t.h, vmm)
                v.tensor_copy(out=v_tm[:, 0:512], in_=p_pr[2][:])
                v.tensor_copy(out=v_tm[:, 512:1024], in_=p_pr[3][:])
                v.drain().then_inc(s_v.h, 1)
            s_v.n += 1
            vevac = s_v.n

        if DEV_STAGE >= 4:
            @block.tensor
            def _(t, vevac=vevac):
                for c in range(8):
                    li = t.matmul(p_sm, ln_fm[:, c * 128:(c + 1) * 128],
                                  wf1[:, c * 64:(c + 1) * 64], start=(c == 0), stop=(c == 7))
                li.then_inc(s_t.h, 1)
            s_t.n += 1
            f1mm = s_t.n

            @block.vector
            def _(v, f1mm=f1mm):
                v.wait_ge(s_t.h, f1mm)
                v.tensor_copy(out=f1_tm[:], in_=p_sm)
                v.drain().then_inc(s_v.h, 1)
            s_v.n += 1
            f1e = s_v.n

            @block.tensor
            def _(t, f1e=f1e):
                t.wait_ge(s_v.h, f1e)
                t.transpose(p_tr[0:64, 0:128], f1_tm[:], C["ident"]).then_inc(s_t.h, 1)
            s_t.n += 1
            f1T = s_t.n

            @block.vector
            def _(v, f1T=f1T):
                v.wait_ge(s_t.h, f1T)
                v.tensor_copy(out=f1_fm[:], in_=p_tr[0:64, 0:128])
                v.drain().then_inc(s_v.h, 1)
            s_v.n += 1
            f1fm = s_v.n

            @block.tensor
            def _(t, f1fm=f1fm, vevac=vevac):
                t.wait_ge(s_v.h, f1fm)
                t.wait_ge(s_v.h, vevac)
                t.matmul(p_pr[0][:], f1_fm[:], wf2[:, 0:512], start=True, stop=True)
                t.matmul(p_pr[1][:], f1_fm[:], wf2[:, 512:1024], start=True,
                         stop=True).then_inc(s_t.h, 1)
            s_t.n += 1
            zmm = s_t.n

            @block.scalar
            def _(s, zmm=zmm):
                s.wait_ge(s_t.h, zmm)
                s.activation(out=fv32[:, 0:512], in_=p_pr[0][:], func=AF.Sigmoid)
                s.activation(out=fv32[:, 512:1024], in_=p_pr[1][:], func=AF.Sigmoid)
                s.drain()
                s.activation(out=sp32[:], in_=fv32[:], func=AF.Ln)
                s.drain().then_inc(s_s.h, 1)
            s_s.n += 1
            spd = s_s.n

            @block.vector
            def _(v, spd=spd):
                v.wait_ge(s_s.h, spd)
                v.tensor_copy(out=sp_hi[:], in_=sp32[:])
                v.drain()
                v.scalar_tensor_tensor(out=sp_lo[:], in0=sp_hi[:], scalar=-1.0,
                                       in1=sp32[:], op0=ALU.mult, op1=ALU.add)
                v.drain().then_inc(s_v.h, 1)
            s_v.n += 1
            sphl = s_v.n

            @block.tensor
            def _(t, sphl=sphl):
                t.wait_ge(s_v.h, sphl)
                for c in range(8):
                    li = t.matmul(p_sm, ln_fm[:, c * 128:(c + 1) * 128],
                                  wg1[:, c * 64:(c + 1) * 64], start=(c == 0), stop=(c == 7))
                li.then_inc(s_t.h, 1)
            s_t.n += 1
            g1mm = s_t.n

            @block.vector
            def _(v, g1mm=g1mm):
                v.wait_ge(s_t.h, g1mm)
                v.tensor_copy(out=f1_tm[:], in_=p_sm)
                v.drain().then_inc(s_v.h, 1)
            s_v.n += 1
            g1e = s_v.n

            @block.tensor
            def _(t, g1e=g1e):
                t.wait_ge(s_v.h, g1e)
                t.transpose(p_tr[0:64, 0:128], f1_tm[:], C["ident"]).then_inc(s_t.h, 1)
            s_t.n += 1
            g1T = s_t.n

            @block.vector
            def _(v, g1T=g1T):
                v.wait_ge(s_t.h, g1T)
                v.tensor_copy(out=f1_fm[:], in_=p_tr[0:64, 0:128])
                v.drain().then_inc(s_v.h, 1)
            s_v.n += 1
            g1fm = s_v.n

            @block.tensor
            def _(t, g1fm=g1fm):
                t.wait_ge(s_v.h, g1fm)
                t.matmul(p_pr[2][:], f1_fm[:], wg2[:, 0:512], start=True, stop=True)
                t.matmul(p_pr[3][:], f1_fm[:], wg2[:, 512:1024], start=True,
                         stop=True).then_inc(s_t.h, 1)
            s_t.n += 1
            g2mm = s_t.n

            @block.scalar
            def _(s, g2mm=g2mm):
                s.wait_ge(s_t.h, g2mm)
                s.activation(out=g_tm[:, 0:512], in_=p_pr[2][:], func=AF.Sigmoid)
                s.activation(out=g_tm[:, 512:1024], in_=p_pr[3][:], func=AF.Sigmoid)
                s.drain().then_inc(s_s.h, 1)
            s_s.n += 1
            g_done = s_s.n

        if DEV_STAGE >= 5:
            @block.tensor
            def _(t, g_done=g_done):
                t.wait_ge(s_s.h, g_done)
                for half in range(2):
                    t.matmul(p_nl[half][:], C["tric"], sp_hi[:, half * 512:(half + 1) * 512],
                             start=True, stop=False)
                    t.matmul(p_nl[half][:], C["tric"], sp_lo[:, half * 512:(half + 1) * 512],
                             start=False, stop=True)
                for gc in range(2):
                    for c in range(8):
                        cc = gc * 8 + c
                        t.matmul(p_nc[:, cc:cc + 1],
                                 sp_hi[:, c * 128:(c + 1) * 128],
                                 C["ones"][:, 1 + gc:2 + gc], start=True, stop=False)
                        li = t.matmul(p_nc[:, cc:cc + 1],
                                      sp_lo[:, c * 128:(c + 1) * 128],
                                      C["ones"][:, 1 + gc:2 + gc],
                                      start=False, stop=True)
                li.then_inc(s_t.h, 1)
            s_t.n += 1
            nld = s_t.n

            @block.scalar
            def _(s, nld=nld):
                s.wait_ge(s_t.h, nld)
                s.activation(out=e1t[:, 0:512], in_=p_nl[0][:], func=AF.Exp, scale=1.0, bias=-LOG8)
                s.activation(out=e1t[:, 512:1024], in_=p_nl[1][:], func=AF.Exp, scale=1.0, bias=-LOG8)
                s.activation(out=e2t[:, 0:512], in_=p_nl[0][:], func=AF.Exp, scale=-1.0)
                s.activation(out=e2t[:, 512:1024], in_=p_nl[1][:], func=AF.Exp, scale=-1.0)
                s.activation(out=aC[:], in_=p_nc, func=AF.Exp, scale=1.0)
                s.drain().then_inc(s_s.h, 1)
            s_s.n += 1
            e_done = s_s.n

            @block.gpsimd
            def _(g, e_done=e_done):
                g.wait_ge(s_s.h, e_done)
                g.tensor_tensor(out=qt_t[:], in0=q_tm[:], in1=e1t[:], op=ALU.mult)
                g.drain()
                g.tensor_tensor(out=kt_t[:], in0=k_tm[:], in1=e2t[:], op=ALU.mult)
                g.drain().then_inc(s_g.h, 1)
            s_g.n += 1
            qkd = s_g.n

            @block.vector
            def _(v, qkd=qkd):
                v.wait_ge(s_g.h, qkd)
                v.tensor_copy(out=kt_p[0][0:64, :], in_=kt_t[0:64, :])
                v.tensor_copy(out=kt_p[1][64:128, :], in_=kt_t[64:128, :])
                v.drain().then_inc(s_v.h, 1)
            s_v.n += 1
            ktp_done = s_v.n

        if DEV_STAGE >= 7:
            @block.tensor
            def _(t, qkd=qkd, pq=prev_qkevac):
                t.wait_ge(s_g.h, qkd)
                t.wait_ge(s_v.h, pq)
                for c in range(8):
                    li = t.transpose(p_qk[:, c * 128:(c + 1) * 128],
                                     qt_t[:, c * 128:(c + 1) * 128], C["ident"])
                li.then_inc(s_t.h, 1)
            s_t.n += 1
            qT = s_t.n

            @block.vector
            def _(v, qT=qT):
                v.wait_ge(s_t.h, qT)
                v.tensor_copy(out=qf_pad[0][0:64, :], in_=p_qk[0:64, :])
                v.tensor_copy(out=qf_pad[1][64:128, :], in_=p_qk[64:128, :])
                v.drain().then_inc(s_v.h, 1)
            s_v.n += 1
            qev = s_v.n

            @block.tensor
            def _(t, qev=qev):
                t.wait_ge(s_v.h, qev)
                for c in range(8):
                    li = t.transpose(p_tr[:, c * 128:(c + 1) * 128],
                                     kt_t[:, c * 128:(c + 1) * 128], C["ident"])
                li.then_inc(s_t.h, 1)
            s_t.n += 1
            kT = s_t.n

            @block.vector
            def _(v, kT=kT):
                v.wait_ge(s_t.h, kT)
                v.tensor_copy(out=kf_full[:], in_=p_tr[:])
                v.drain().then_inc(s_v.h, 1)
            s_v.n += 1
            prev_qkevac = s_v.n
            qkf = s_v.n

            amds = []
            for hp in (range(8) if (DEV_STAGE >= 72 or DEV_STAGE == 99) else []):
                @block.tensor
                def _(t, hp=hp, qkf=qkf, pa=prev_amdone):
                    t.wait_ge(s_v.h, qkf)
                    t.wait_ge(s_v.h, pa)
                    for hh in range(2):
                        for gc in range(2):
                            rr = gc * 64
                            co = hp * 128 + gc * 64
                            li = t.matmul(
                                p_at[rr:rr + 64, hh * 64:(hh + 1) * 64],
                                kf_full[:, co:co + 64],
                                qf_pad[hh][:, co:co + 64],
                                start=True, stop=True)
                    li.then_inc(s_t.h, 1)
                s_t.n += 1
                atm = s_t.n

                @block.vector
                def _(v, atm=atm, hp=hp):
                    v.wait_ge(s_t.h, atm)
                    v.tensor_tensor(out=am_g[0][0:64, hp * 128:(hp + 1) * 128],
                                    in0=p_at[0:64, :],
                                    in1=C["amask"][0:64, :], op=ALU.mult)
                    v.tensor_tensor(out=am_g[1][64:128, hp * 128:(hp + 1) * 128],
                                    in0=p_at[64:128, :],
                                    in1=C["amask"][64:128, :], op=ALU.mult)
                    v.drain().then_inc(s_v.h, 1)
                s_v.n += 1
                amd = s_v.n
                prev_amdone = amd
                amds.append(amd)

                if DEV_STAGE < 73 and DEV_STAGE != 99:
                    continue

                @block.tensor
                def _(t, hp=hp, b=b, amd=amd, psu=prev_supd):
                    t.wait_ge(s_v.h, amd)
                    if hp == 0:
                        t.wait_ge(s_v.h, psu)
                    for hh in range(2):
                        h = 2 * hp + hh
                        col = h * 64
                        po = p_o[col // 512]
                        pc = col % 512
                        t.matmul(po[0:64, pc:pc + 64],
                                 am_g[0][:, hp * 128 + hh * 64: hp * 128 + (hh + 1) * 64],
                                 v_tm[:, col:col + 64], start=True, stop=False)
                        li = t.matmul(po[0:64, pc:pc + 64],
                                      qf_pad[hh][:, hp * 128: hp * 128 + 64],
                                      S_bf[:, b * 512 + hp * 64: b * 512 + hp * 64 + 64],
                                      start=False, stop=True)
                    li.then_inc(s_t.h, 1)
                s_t.n += 1

            for gc in (range(2) if (DEV_STAGE >= 74 or DEV_STAGE == 99) else []):
                if gc == 1 and DEV_STAGE < 75 and DEV_STAGE != 99:
                    break
                rr = gc * 64
                if gc == 1:
                    if DEV_TAP == ci:
                        dbg_sbf = sb(f"dbg_sbf{ci}", [128, 512], BF16)

                        @block.vector
                        def _(v, psu=prev_supd, b=b):
                            v.wait_ge(s_v.h, psu)
                            v.tensor_copy(out=dbg_sbf[:], in_=S_bf[:, b * 512:(b + 1) * 512])
                            v.drain().then_inc(s_v.h, 1)
                        s_v.n += 1
                        st["tap"]("sbf_preB", dbg_sbf[:], s_v, s_v.n)
                    for hp in (range(8) if (DEV_STAGE >= 75 or DEV_STAGE == 99) else []):
                        @block.tensor
                        def _(t, hp=hp, b=b, psu=prev_supd, amd=amds[hp]):
                            t.wait_ge(s_v.h, psu)
                            t.wait_ge(s_v.h, amd)
                            for hh in range(2):
                                h = 2 * hp + hh
                                col = h * 64
                                po = p_o[col // 512]
                                pc = col % 512
                                if DEV_NO_OINTER:
                                    li = t.matmul(po[64:128, pc:pc + 64],
                                                  am_g[1][:, hp * 128 + hh * 64: hp * 128 + (hh + 1) * 64],
                                                  v_tm[:, col:col + 64], start=True, stop=True)
                                    continue
                                t.matmul(po[64:128, pc:pc + 64],
                                         am_g[1][:, hp * 128 + hh * 64: hp * 128 + (hh + 1) * 64],
                                         v_tm[:, col:col + 64], start=True, stop=False)
                                rhs_s = (qf_pad[0][:, 0:64] if DEV_FAKE_S else
                                         S_bf[:, b * 512 + hp * 64: b * 512 + hp * 64 + 64])
                                li = t.matmul(po[64:128, pc:pc + 64],
                                              qf_pad[hh][:, hp * 128 + 64: hp * 128 + 128],
                                              rhs_s,
                                              start=False, stop=True)
                            li.then_inc(s_t.h, 1)
                        s_t.n += 1

                if gc == 1 and DEV_STAGE < 76 and DEV_STAGE != 99:
                    break

                @block.tensor
                def _(t, gc=gc, psu=prev_supd, ktp=ktp_done):
                    if gc == 0:
                        t.wait_ge(s_v.h, psu)
                    t.wait_ge(s_v.h, ktp)
                    for hp in (range(8) if not DEV_NO_SMM else []):
                        for hh in range(2):
                            col = (2 * hp + hh) * 64
                            pdst = p_S if hh == 0 else p_S2
                            li = t.matmul(pdst[:, hp * 64:(hp + 1) * 64],
                                          kt_p[gc][:, (2 * hp) * 64:(2 * hp + 2) * 64],
                                          v_tm[:, col:col + 64],
                                          start=True, stop=True)
                    if DEV_NO_SMM:
                        li = t.nop()
                    li.then_inc(s_t.h, 1)
                s_t.n += 1
                omm = s_t.n

                @block.vector
                def _(v, gc=gc, b=b, omm=omm):
                    v.wait_ge(s_t.h, omm)
                    v.tensor_tensor(out=S_tmp[0:64, :], in0=S_all[0:64, b * 512:(b + 1) * 512],
                                    in1=p_S[0:64, :], op=ALU.add)
                    v.tensor_tensor(out=S_tmp[64:128, :], in0=S_all[64:128, b * 512:(b + 1) * 512],
                                    in1=p_S2[64:128, :], op=ALU.add)
                    v.drain()
                    for hp8 in range(8):
                        v.tensor_scalar(out=S_all[:, b * 512 + hp8 * 64: b * 512 + (hp8 + 1) * 64],
                                        in0=S_tmp[:, hp8 * 64:(hp8 + 1) * 64],
                                        scalar1=aC[:, gc * 8 + hp8: gc * 8 + hp8 + 1],
                                        scalar2=None, op0=ALU.mult)
                    v.drain()
                    v.tensor_copy(out=S_bf[:, b * 512:(b + 1) * 512],
                                  in_=S_all[:, b * 512:(b + 1) * 512])
                    v.drain().then_inc(s_v.h, 1)
                s_v.n += 1
                prev_supd = s_v.n

        if DEV_TAP == ci:
            st["tap"]("ln_tm", ln_tm[:], s_v, lnfm_done)
            st["tap"]("q_tm", q_tm[:], s_s, k_done)
            st["tap"]("k_tm", k_tm[:], s_s, k_done)
            st["tap"]("v_tm", v_tm[:], s_v, vevac)
            st["tap"]("fv32", fv32[:], s_s, spd)
            st["tap"]("sp32", sp32[:], s_s, spd)
            st["tap"]("g_tm", g_tm[:], s_s, g_done)
            st["tap"]("e1t", e1t[:], s_s, e_done)
            st["tap"]("e2t", e2t[:], s_s, e_done)
            st["tap"]("aC", aC[:], s_s, e_done)
            st["tap"]("qt_t", qt_t[:], s_g, qkd)
            st["tap"]("kt_t", kt_t[:], s_g, qkd)
            st["tap"]("S_all", S_all[:], s_v, prev_supd)
            st["tap"]("am_g0", am_g[0][:], s_v, prev_supd)
            st["tap"]("am_g1", am_g[1][:], s_v, prev_supd)
            st["tap"]("qf_p1", qf_pad[1][:], s_v, qkf)
            dbg_po = sb(f"dbg_po{ci}", [128, 1024], F32)

            @block.vector
            def _(v, allmm_d=s_t.n):
                v.wait_ge(s_t.h, allmm_d)
                v.tensor_copy(out=dbg_po[:, 0:512], in_=p_o[0][:])
                v.tensor_copy(out=dbg_po[:, 512:1024], in_=p_o[1][:])
                v.drain().then_inc(s_v.h, 1)
            s_v.n += 1
            st["tap"]("p_o_dump", dbg_po[:], s_v, s_v.n)
            st["tap"]("kt_p0", kt_p[0][:], s_g, qkd)
            st["tap"]("kt_p1", kt_p[1][:], s_g, qkd)
            st["tap"]("S_tmp", S_tmp[:], s_v, prev_supd)
            st["tap"]("qf_p0", qf_pad[0][:], s_v, qkf)
            st["tap"]("kf_f", kf_full[:], s_v, qkf)
            st["tap"]("S_bf", S_bf[:], s_v, prev_supd)

        if DEV_STAGE >= 99:
            allmm = s_t.n

            @block.vector
            def _(v, allmm=allmm):
                v.wait_ge(s_t.h, allmm)
                v.tensor_tensor(out=og[:, 0:512], in0=p_o[0][:], in1=g_tm[:, 0:512], op=ALU.mult)
                v.tensor_tensor(out=og[:, 512:1024], in0=p_o[1][:], in1=g_tm[:, 512:1024], op=ALU.mult)
                v.drain().then_inc(s_v.h, 1)
            s_v.n += 1
            ogd = s_v.n

            @block.scalar
            def _(s, ogd=ogd):
                s.wait_ge(s_v.h, ogd)
                s.activation(out=sq32[:], in_=og[:], func=AF.Square)
                s.drain().then_inc(s_s.h, 1)
            s_s.n += 1
            sqd2 = s_s.n

            @block.vector
            def _(v, sqd2=sqd2):
                v.wait_ge(s_s.h, sqd2)
                v.tensor_reduce(out=ms_t[:], in_=sq32[:].rearrange("p (h d) -> p h d", d=64),
                                axis=AX.X, op=ALU.add)
                v.drain()
                v.tensor_scalar(out=ms_t[:], in0=ms_t[:], scalar1=1.0 / 64, scalar2=GLA_EPS,
                                op0=ALU.mult, op1=ALU.add)
                v.drain().then_inc(s_v.h, 1)
            s_v.n += 1
            msd = s_v.n

            @block.scalar
            def _(s, msd=msd):
                s.wait_ge(s_v.h, msd)
                s.activation(out=rstd_h[:], in_=ms_t[:], func=AF.Ln)
                s.drain()
                s.activation(out=rstd_h[:], in_=rstd_h[:], func=AF.Exp, scale=-0.5)
                s.drain().then_inc(s_s.h, 1)
            s_s.n += 1
            msq = s_s.n

            @block.vector
            def _(v, msq=msq):
                v.wait_ge(s_s.h, msq)
                for h16 in range(16):
                    v.tensor_scalar(out=ogn[:, h16 * 64:(h16 + 1) * 64],
                                    in0=og[:, h16 * 64:(h16 + 1) * 64],
                                    scalar1=rstd_h[:, h16:h16 + 1],
                                    scalar2=None, op0=ALU.mult)
                v.drain().then_inc(s_v.h, 1)
            s_v.n += 1
            ognd = s_v.n

            @block.tensor
            def _(t, ognd=ognd):
                t.wait_ge(s_v.h, ognd)
                for c in range(8):
                    li = t.transpose(p_tr[:, c * 128:(c + 1) * 128],
                                     ogn[:, c * 128:(c + 1) * 128], C["ident"])
                li.then_inc(s_t.h, 1)
            s_t.n += 1
            ognT = s_t.n

            @block.vector
            def _(v, ognT=ognT):
                v.wait_ge(s_t.h, ognT)
                v.tensor_copy(out=og_fm[:], in_=p_tr[:])
                v.drain().then_inc(s_v.h, 1)
            s_v.n += 1
            ogfm = s_v.n

            @block.tensor
            def _(t, ogfm=ogfm):
                t.wait_ge(s_v.h, ogfm)
                proj_mm(t, wo, p_pr[0], p_pr[1], src=og_fm).then_inc(s_t.h, 1)
            s_t.n += 1
            womm = s_t.n

        if DEV_STAGE >= 99:
            @block.vector
            def _(v, ci=ci, xi=xi, x1=x1, womm=womm):
                v.wait_ge(s_t.h, womm)
                if ci >= 2:
                    v.wait_ge(s_st.h, chunk_st_done[ci - 2])
                v.tensor_tensor(out=x1[:, 0:512], in0=xi[:, 0:512], in1=p_pr[0][:], op=ALU.add)
                v.tensor_tensor(out=x1[:, 512:1024], in0=xi[:, 512:1024], in1=p_pr[1][:], op=ALU.add)
                v.drain().then_inc(s_v.h, 1)
            s_v.n += 1
            x1d = s_v.n
        else:
            @block.vector
            def _(v, ci=ci, xi=xi, x1=x1):
                v.wait_ge(s_t.h, s_t.n)
                v.wait_ge(s_s.h, s_s.n)
                v.wait_ge(s_g.h, s_g.n)
                if ci >= 2:
                    v.wait_ge(s_st.h, chunk_st_done[ci - 2])
                v.tensor_copy(out=x1[:], in_=xi[:])
                v.drain().then_inc(s_v.h, 1)
            s_v.n += 1
            x1d = s_v.n

        if DEV_TAP == ci:
            st["tap"]("og", og[:], s_v, ogd)
            st["tap"]("ogn", ogn[:], s_v, ognd)
            st["tap"]("x1", x1[:], s_v, x1d)

        @block.sync
        def _(sy, x1=x1, r0=r0, x1d=x1d):
            sy.wait_ge(s_v.h, x1d)
            s_st.dma(sy, st["xres"][r0:r0 + 128, :], x1[:])

        ln1_done = ln_block(x1, ln1t, store_stats_ci=ci, s_wait_v=x1d)

        @block.vector
        def _(v, ln1_done=ln1_done):
            v.reduce_max(out=lnst[:, 7:8], in_=ln1t[:], axis=AX.X, apply_absolute_value=True)
            v.drain()
            v.tensor_tensor(out=mxacc[:, 0:1], in0=mxacc[:, 0:1], in1=lnst[:, 7:8], op=ALU.max)
            v.drain().then_inc(s_v.h, 1)
        s_v.n += 1
        chunk_v_done[ci] = s_v.n

        @block.sync
        def _(sy, r0=r0):
            sy.wait_ge(s_v.h, chunk_v_done[ci])
            s_st.dma(sy, st["ln1_sp"][r0:r0 + 128, :], ln1t[:])
        chunk_st_done[ci] = s_st.n

        @block.vector
        def _(v, ci=ci):
            v.wait_ge(s_st.h, chunk_st_done[ci])
            v.engine_nop().then_inc(s_v.h, 1)
        s_v.n += 1
        chunk_v_done[ci] = s_v.n



def emit_allreduce(st, idx, cols, compute_row, out_cols):
    nc, block = st["nc"], st["block"]
    s_cc, s_t, s_v = st["sems"]["cc"], st["sems"]["t"], st["sems"]["v"]
    mxacc, scal, scrow = st["mxacc"], st["scal"], st["scrow"]
    ps, C_IDF, C_ONESROW = st["ps"], st["C_IDF"], st["C_ONESROW"]
    c0 = cols[0]
    k = 4
    pbank = st["pb"][0]
    scal_base = st.setdefault("scal_next", 0)
    st["scal_next"] = scal_base + out_cols

    s_g = st["sems"]["g"]
    mxpr = st["mxpr"]

    @block.gpsimd
    def _(g, pre=s_v.n):
        g.wait_ge(s_v.h, pre)
        g.wait_ge(s_cc.h, s_cc.n)
        g.dma_start(out=st["cc_stage"][:], in_=mxacc[:, c0:c0 + k]).then_inc(s_cc.h, 16)
        s_cc.n += 16
        g.wait_ge(s_cc.h, s_cc.n)
        g.dma_start(out=mxpr[0:1, 0:512],
                    in_=st["cc_stage"].rearrange("p c -> (p c)").unsqueeze(0)
                    ).then_inc(s_cc.h, 16)
        s_cc.n += 16

    @block.vector
    def _(v, cv=s_cc.n):
        v.wait_ge(s_cc.h, cv)
        for c in range(k):
            v.reduce_max(out=mxpr[0:1, 512 + c:512 + c + 1],
                         in_=mxpr[0:1, c:512:4].unsqueeze(1) if False else
                         mxpr[0:1, 0:512].rearrange("a (p c) -> a p c", c=4)[:, :, c],
                         axis=AX.X)
        v.drain().then_inc(s_v.h, 1)
    s_v.n += 1

    @block.gpsimd
    def _(g, vv=s_v.n):
        g.wait_ge(s_v.h, vv)
        g.wait_ge(s_cc.h, s_cc.n)
        g.dma_start(out=st["cc_in"][idx][:],
                    in_=mxpr[0:1, 512:512 + k]).then_inc(s_cc.h, 16)
        s_cc.n += 16
        g.wait_ge(s_cc.h, s_cc.n)
        g.collective_compute(
            "AllReduce", ALU.max,
            replica_groups=[[0]] if DEV_SOLO else [[0, 1, 2, 3, 4, 5, 6, 7]],
            ins=[st["cc_in"][idx][:]], outs=[st["cc_out"][idx][:]],
        ).then_inc(s_g.h, 1)
        s_g.n += 1
        g.wait_ge(s_g.h, s_g.n)
        g.dma_start(out=scrow[0:1, 0:4], in_=st["cc_out"][idx][:]).then_inc(s_cc.h, 16)
        s_cc.n += 16

    @block.vector
    def _(v, cv=s_cc.n):
        v.wait_ge(s_cc.h, cv)
        compute_row(v, scrow[0:1, 0:4], scrow[0:1, 8:16])
        v.drain().then_inc(s_v.h, 1)
    s_v.n += 1

    s_s_ = st["sems"]["s"]

    @block.tensor
    def _(t, vv=s_v.n, ss=s_s_.n):
        t.wait_ge(s_v.h, vv)
        t.wait_ge(s_s_.h, ss)
        t.matmul(pbank[0:128, 0:out_cols], C_ONESROW[:],
                 scrow[0:1, 8:8 + out_cols], start=True, stop=True).then_inc(s_t.h, 1)
    s_t.n += 1

    @block.vector
    def _(v, tv=s_t.n):
        v.wait_ge(s_t.h, tv)
        v.tensor_copy(out=scal[:, scal_base:scal_base + out_cols],
                      in_=pbank[0:128, 0:out_cols])
        v.drain().then_inc(s_v.h, 1)
    s_v.n += 1
    return scal_base, s_v.n


def emit_wquant(st, w_hbm, n_rows, kdim, wT, blk=32, qmax=7.0):
    nc, block, sb, C = st["nc"], st["block"], st["sb"], st["C"]
    s_gd, s_t, s_v, s_s = (st["sems"][k] for k in ("gd", "t", "v", "s"))
    p_bf = st["pb_bf"]
    nblk = kdim // blk
    strip_rows = 384 if n_rows % 384 == 0 else 128
    rc_per = strip_rows // 128
    nstrip = n_rows // strip_rows
    if "wq_scr" not in st:
        st["wq_scr"] = dict(
            wstrip=sb("wq_strip", [128, 3072], F32),
            sblk=sb("wq_sblk", [128, 96], F32),
            sinv=sb("wq_sinv", [128, 96], F32),
            tscr=sb("wq_tscr", [128, 3072], F32),
            iscr=sb("wq_iscr", [128, 3072], I32),
        )
        st["wq_scr"]["qscr"] = st["wq_scr"]["iscr"][:, 0:1536].bitcast(BF16)
    sc_ = st["wq_scr"]
    wstrip = [sc_["wstrip"], sc_["wstrip"]]
    sblk = sc_["sblk"][:, 0:nblk]
    sinv = sc_["sinv"][:, 0:nblk]
    tscr = sc_["tscr"][:, 0:kdim]
    iscr = sc_["iscr"][:, 0:kdim]
    qscr = sc_["qscr"][:, 0:kdim]
    strip_done = {}

    for sidx in range(nstrip):
        buf = wstrip[sidx % 2]

        @block.gpsimd
        def _(g, sidx=sidx, buf=buf):
            if sidx >= 1:
                g.wait_ge(s_v.h, strip_done[(sidx - 1) * rc_per + rc_per - 1])
            else:
                g.wait_ge(s_v.h, st.get("wq_last_v", 0))
            s_gd.dma(g, buf[:].rearrange("p (c e) -> p c e", c=rc_per),
                     w_hbm[sidx * strip_rows:(sidx + 1) * strip_rows, :]
                     .rearrange("(c p) e -> p c e", p=128))
        ld = s_gd.n

        for rc in range(rc_per):
            f_row = sidx * rc_per + rc
            wt = buf[:, rc * kdim:(rc + 1) * kdim]

            @block.vector
            def _(v, wt=wt, ld=ld):
                v.wait_ge(s_gd.h, ld)
                v.tensor_reduce(out=sblk[:], in_=wt.rearrange("p (b i) -> p b i", i=blk),
                                axis=AX.X, op=ALU.max, apply_absolute_value=True)
                v.drain()
                v.tensor_scalar(out=sblk[:], in0=sblk[:], scalar1=1.0 / qmax,
                                scalar2=1e-12, op0=ALU.mult, op1=ALU.max)
                v.drain()
                v.reciprocal(out=sinv[:], in_=sblk[:])
                v.drain()
                sb_b = sinv[:].unsqueeze(2).broadcast_to([128, nblk, blk])
                v.tensor_tensor(out=tscr[:].rearrange("p (b i) -> p b i", i=blk),
                                in0=wt.rearrange("p (b i) -> p b i", i=blk),
                                in1=sb_b, op=ALU.mult)
                v.drain()
                v.tensor_copy(out=iscr[:], in_=tscr[:])
                v.drain()
                sb_b2 = sblk[:].unsqueeze(2).broadcast_to([128, nblk, blk])
                v.tensor_tensor(out=qscr[:].rearrange("p (b i) -> p b i", i=blk),
                                in0=iscr[:].rearrange("p (b i) -> p b i", i=blk),
                                in1=sb_b2, op=ALU.mult)
                v.drain().then_inc(s_v.h, 1)
            s_v.n += 1
            qd = s_v.n

            for t8 in range(kdim // 1024):
                @block.tensor
                def _(t, qd=qd, t8=t8):
                    t.wait_ge(s_v.h, qd)
                    if t8 > 0:
                        t.wait_ge(s_v.h, s_v.n)
                    for c in range(8):
                        cc = t8 * 8 + c
                        li = t.transpose(p_bf[:, c * 128:(c + 1) * 128],
                                         qscr[:, cc * 128:(cc + 1) * 128], C["ident"])
                    li.then_inc(s_t.h, 1)
                s_t.n += 1
                td = s_t.n

                @block.vector
                def _(v, td=td, f_row=f_row, t8=t8):
                    v.wait_ge(s_t.h, td)
                    v.tensor_copy(
                        out=wT[:].rearrange("p (c r) -> p c r", r=n_rows)[
                            :, t8 * 8:(t8 + 1) * 8, f_row * 128:(f_row + 1) * 128],
                        in_=p_bf[:].rearrange("p (c r) -> p c r", r=128))
                    v.drain().then_inc(s_v.h, 1)
                s_v.n += 1
            strip_done[f_row] = s_v.n
    st["wq_last_v"] = s_v.n
    return s_v.n


def emit_l0f(st):
    nc, block, C, W, sb, ps = (st[k] for k in ("nc", "block", "C", "W", "sb", "ps"))
    s_ld, s_gd, s_t, s_v, s_s, s_g, s_st = (st["sems"][k] for k in
                                            ("ld", "gd", "t", "v", "s", "g", "st"))
    scal, mxacc = st["scal"], st["mxacc"]
    st["phase_barrier"]()
    st["arena"].reset()
    st["mxpr"] = st["mxpr"] if "mxpr" in st else st["sb_persist"]("mxpr", [1, 520], F32)
    pb = st["pb"]
    p_bf = st["pb_bf"]

    IN = 3072

    def row1(v, rrow, wrow):
        v.tensor_scalar(out=wrow[0:1, 0:1], in0=rrow[0:1, 0:1], scalar1=1.0 / 32767.0,
                        scalar2=1e-12, op0=ALU.mult, op1=ALU.max)
        v.drain()
        v.reciprocal(out=wrow[0:1, 1:2], in_=wrow[0:1, 0:1])
        v.drain()
    c_s1, ar1_done = emit_allreduce(st, 0, [0], row1, 2)
    S1, IS1 = c_s1, c_s1 + 1

    colm = sb("l0f_colm", [128, 1], F32)
    mark_a = st["arena"].mark()
    xqf = sb("l0f_xqf", [128, 8 * TC], BF16)
    wT = sb("l0f_wT", [128, 8 * IN], BF16)
    wup_done = emit_wquant(st, W["l0f_wup"], IN, E, wT)

    lnin = [sb(f"l0f_lnin{i}", [128, E], BF16) for i in range(2)]
    i32t = sb("l0f_i32", [128, E], I32)
    xq_tm = sb("l0f_xqtm", [128, E], BF16)
    upo = [sb(f"l0f_upo{i}", [128, IN], BF16) for i in range(2)]

    @block.vector
    def _(v):
        v.memset(mxacc[:, 1:3], 0.0)
        v.drain().then_inc(s_v.h, 1)
    s_v.n += 1

    def ffn_mm_pass(src_sp, dst_sp, wT_ap, mx_col, first_pass, wprep_done, scale_col):
        ch_done = {}
        st_done = {}
        ev_done = dict(st.get("ffn_ev_tail", {}))
        rd_done = dict(st.get("ffn_rd_tail", {}))
        for ci in range(_nchunks()):
            r0 = ci * 128
            lb = lnin[ci % 2]
            ub = upo[ci % 2]

            if first_pass:
                @block.sync
                def _(sy, ci=ci, r0=r0, lb=lb):
                    if ci >= 2:
                        sy.wait_ge(s_v.h, ch_done[ci - 2])
                    s_ld.dma(sy, lb[:], st["ln1_sp"][r0:r0 + 128, :])
                ld = s_ld.n

                @block.vector
                def _(v, ci=ci, lb=lb, ld=ld):
                    v.wait_ge(s_ld.h, ld)
                    if ci == 0:
                        v.wait_ge(s_v.h, ar1_done)
                    v.tensor_scalar(out=i32t[:], in0=lb[:], scalar1=scal[:, IS1:IS1 + 1],
                                    scalar2=None, op0=ALU.mult)
                    v.drain()
                    v.tensor_copy(out=xq_tm[:], in_=i32t[:])
                    v.drain().then_inc(s_v.h, 1)
                s_v.n += 1
                xqd = s_v.n

                @block.tensor
                def _(t, xqd=xqd, ci=ci, wprep_done=wprep_done):
                    t.wait_ge(s_v.h, xqd)
                    if ci == 0:
                        t.wait_ge(s_v.h, wprep_done)
                    for c in range(8):
                        li = t.transpose(p_bf[:, c * 128:(c + 1) * 128],
                                         xq_tm[:, c * 128:(c + 1) * 128], C["ident"])
                    li.then_inc(s_t.h, 1)
                s_t.n += 1
                xqT = s_t.n

                @block.vector
                def _(v, xqT=xqT, ci=ci):
                    v.wait_ge(s_t.h, xqT)
                    v.tensor_copy(
                        out=xqf[:].rearrange("p (c r) -> p c r", r=TC)[:, :, ci * 128:(ci + 1) * 128],
                        in_=p_bf[:].rearrange("p (c r) -> p c r", r=128))
                    v.drain().then_inc(s_v.h, 1)
                s_v.n += 1
                ch_done[ci] = s_v.n
            else:
                @block.vector
                def _(v, ci=ci):
                    if ci == 0:
                        v.wait_ge(s_v.h, wprep_done)
                    v.drain().then_inc(s_v.h, 1)
                s_v.n += 1
                ch_done[ci] = s_v.n

            for fs in range(6):
                pm = pb[fs % 2]
                prev_ev = ev_done.get((ci, fs - 2), ev_done.get((ci - 1, 4 + fs), 0))
                prev_rd = rd_done.get((ci, fs - 2), rd_done.get((ci - 1, 4 + fs), 0))

                @block.tensor
                def _(t, ci=ci, fs=fs, pm=pm, dep=ch_done[ci], prev_ev=prev_ev, prev_rd=prev_rd):
                    t.wait_ge(s_v.h, dep)
                    if prev_ev:
                        t.wait_ge(s_s.h, prev_ev)
                    if prev_rd:
                        t.wait_ge(s_v.h, prev_rd)
                    xc = xqf[:].rearrange("p (c r) -> p c r", r=TC)
                    for c in range(8):
                        li = t.matmul(pm[:], xc[:, c, ci * 128:(ci + 1) * 128],
                                      wT_ap[:].rearrange("p (c r) -> p c r", r=IN)[
                                          :, c, fs * 512:(fs + 1) * 512],
                                      start=(c == 0), stop=(c == 7))
                    li.then_inc(s_t.h, 1)
                s_t.n += 1
                mmd = s_t.n

                @block.scalar
                def _(sc, fs=fs, pm=pm, mmd=mmd, ub=ub, ci=ci):
                    sc.wait_ge(s_t.h, mmd)
                    if fs == 0:
                        sc.wait_ge(s_st.h, st_done.get(ci - 2, st.get("ffn_st_tail", 0)))
                    sc.activation(out=ub[:, fs * 512:(fs + 1) * 512], in_=pm[:],
                                  func=AF.Identity, scale=scal[:, scale_col:scale_col + 1])
                    sc.drain().then_inc(s_s.h, 1)
                s_s.n += 1
                ev_done[(ci, fs)] = s_s.n

                @block.vector
                def _(v, fs=fs, pm=pm, mmd=mmd, mx_col=mx_col):
                    v.wait_ge(s_t.h, mmd)
                    v.reduce_max(out=colm[:], in_=pm[:], axis=AX.X, apply_absolute_value=True)
                    v.drain()
                    v.tensor_tensor(out=mxacc[:, mx_col:mx_col + 1],
                                    in0=mxacc[:, mx_col:mx_col + 1], in1=colm[:], op=ALU.max)
                    v.drain().then_inc(s_v.h, 1)
                s_v.n += 1
                rd_done[(ci, fs)] = s_v.n

            @block.sync
            def _(sy, ci=ci, r0=r0, ub=ub, ev=s_s.n):
                sy.wait_ge(s_s.h, ev)
                if ci >= 2:
                    sy.wait_ge(s_st.h, st_done[ci - 2])
                s_st.dma(sy, dst_sp[r0:r0 + 128, :], ub[:])
            st_done[ci] = s_st.n
        last = _nchunks() - 1
        st["ffn_ev_tail"] = {(-1, 4): ev_done[(last, 4)], (-1, 5): ev_done[(last, 5)]}
        st["ffn_rd_tail"] = {(-1, 4): rd_done[(last, 4)], (-1, 5): rd_done[(last, 5)]}
        st["ffn_st_tail"] = st_done[last]
        return s_v.n

    if DEV_TAP is not None:
        st["tap"]("scal_ar1", st["scal"][:, 0:8], st["sems"]["v"], ar1_done)
    up_done = ffn_mm_pass(None, st["up_sp"], wT, 1, True, wup_done, S1)
    if DEV_TAP is not None:
        st["tap"]("wT_up", wT[:], st["sems"]["v"], wup_done)
        st["tap"]("xqf", xqf[:], st["sems"]["v"], up_done)

        @block.gpsimd
        def _(g, dep=up_done):
            g.wait_ge(st["sems"]["v"].h, dep)
            st["sems"]["gd"].dma(g, st["taps"].setdefault(
                "up_c0", nc.dram_tensor("tap_up_c0", [128, 3072], BF16,
                                        kind="ExternalOutput").ap())[:],
                st["up_sp"][0:128, :])

    wgate_done = emit_wquant(st, W["l0f_wgate"], IN, E, wT)
    gate_done = ffn_mm_pass(None, st["gate_sp"], wT, 2, False, wgate_done, S1)

    def row2(v, rrow, wrow):
        v.tensor_scalar(out=wrow[0:1, 0:2], in0=rrow[0:1, 0:2],
                        scalar1=st["scal"][0:1, S1:S1 + 1], scalar2=1.0 / 32767.0,
                        op0=ALU.mult, op1=ALU.mult)
        v.drain()
        v.tensor_scalar(out=wrow[0:1, 0:2], in0=wrow[0:1, 0:2], scalar1=1e-12,
                        scalar2=None, op0=ALU.max)
        v.drain()
        v.reciprocal(out=wrow[0:1, 2:3], in_=wrow[0:1, 0:1])
        v.drain()
        v.reciprocal(out=wrow[0:1, 3:4], in_=wrow[0:1, 1:2])
        v.drain()
    c2, ar2_done = emit_allreduce(st, 1, [1, 2], row2, 4)
    S2, S3, IS2, IS3 = c2, c2 + 1, c2 + 2, c2 + 3

    if DEV_TAP is not None:
        st["tap"]("scal_ar2", st["scal"][:, 0:8], st["sems"]["v"], ar2_done)
        st["tap"]("mxacc_l0f", st["mxacc"][:], st["sems"]["v"], ar2_done)

    st["phase_barrier"]()
    st["arena"].rollback(mark_a)
    st.pop("wq_scr")
    tvals = [sb(f"l0f_tv{i}", [128, IN], BF16) for i in range(2)]
    gin = [sb(f"l0f_gin{i}", [128, IN], BF16) for i in range(2)]
    i32b = sb("l0f_i32b", [128, IN], I32)
    gint = sb("l0f_gint", [128, IN], F32)
    upq = sb("l0f_upq", [128, IN], F32)
    sig32 = upq
    sig_i = i32b

    @block.vector
    def _(v):
        v.memset(mxacc[:, 3:4], 0.0)
        v.drain().then_inc(s_v.h, 1)
    s_v.n += 1

    p3_done = {}
    p3_st = {}
    for ci in range(_nchunks()):
        r0 = ci * 128
        gb = gin[ci % 2]
        tb = tvals[ci % 2]

        @block.sync
        def _(sy, ci=ci, r0=r0, gb=gb):
            if ci >= 2:
                sy.wait_ge(s_v.h, p3_done[ci - 2])
            s_ld.dma(sy, gb[:], st["gate_sp"][r0:r0 + 128, :])
        ld = s_ld.n

        @block.scalar
        def _(sc, ci=ci, gb=gb, ld=ld, ar2_done=ar2_done):
            sc.wait_ge(s_ld.h, ld)
            if ci == 0:
                sc.wait_ge(s_v.h, ar2_done)
            sc.activation(out=i32b[:], in_=gb[:], func=AF.Identity,
                          scale=scal[:, IS3:IS3 + 1])
            sc.drain().then_inc(s_s.h, 1)
        s_s.n += 1
        gq = s_s.n

        @block.vector
        def _(v, gq=gq):
            v.wait_ge(s_s.h, gq)
            v.tensor_copy(out=gint[:], in_=i32b[:])
            v.drain().then_inc(s_v.h, 1)
        s_v.n += 1
        gi = s_v.n

        @block.scalar
        def _(sc, gi=gi):
            sc.wait_ge(s_v.h, gi)
            sc.activation(out=sig32[:], in_=gint[:], func=AF.Sigmoid,
                          scale=scal[:, S3:S3 + 1])
            sc.drain()
            sc.activation(out=sig_i[:], in_=sig32[:], func=AF.Identity,
                          scale=65536.0)
            sc.drain().then_inc(s_s.h, 1)
        s_s.n += 1
        sgd = s_s.n

        @block.vector
        def _(v, sgd=sgd, tb=tb, ci=ci):
            v.wait_ge(s_s.h, sgd)
            v.scalar_tensor_tensor(out=tb[:], in0=sig_i[:], scalar=65535.0,
                                   in1=gint[:], op0=ALU.min, op1=ALU.mult)
            v.drain()
            v.reduce_max(out=colm[:], in_=tb[:], axis=AX.X, apply_absolute_value=True)
            v.drain()
            v.tensor_tensor(out=mxacc[:, 3:4], in0=mxacc[:, 3:4], in1=colm[:], op=ALU.max)
            v.drain().then_inc(s_v.h, 1)
        s_v.n += 1
        p3_done[ci] = s_v.n

        @block.sync
        def _(sy, ci=ci, r0=r0, tb=tb, dep=p3_done[ci]):
            sy.wait_ge(s_v.h, dep)
            if ci >= 2:
                sy.wait_ge(s_st.h, p3_st[ci - 2])
            s_st.dma(sy, st["t_sp"][r0:r0 + 128, :], tb[:])
        p3_st[ci] = s_st.n

    def row3(v, rrow, wrow):
        v.tensor_scalar(out=wrow[0:1, 0:1], in0=rrow[0:1, 0:1],
                        scalar1=st["scal"][0:1, S3:S3 + 1], scalar2=1.0 / 65536.0,
                        op0=ALU.mult, op1=ALU.mult)
        v.drain()
        v.tensor_scalar(out=wrow[0:1, 0:1], in0=wrow[0:1, 0:1], scalar1=1.0 / 32767.0,
                        scalar2=1e-12, op0=ALU.mult, op1=ALU.max)
        v.drain()
        v.reciprocal(out=wrow[0:1, 1:2], in_=wrow[0:1, 0:1])
        v.drain()
        v.tensor_scalar(out=wrow[0:1, 2:3], in0=st["scal"][0:1, S3:S3 + 1],
                        scalar1=1.0 / 65536.0, scalar2=wrow[0:1, 1:2],
                        op0=ALU.mult, op1=ALU.mult)
        v.drain()
    c3, ar3_done = emit_allreduce(st, 2, [3], row3, 3)
    S4, IS4, Q34 = c3, c3 + 1, c3 + 2

    upin = [sb(f"l0f_upin{i}", [128, IN], BF16) for i in range(2)]
    zt = sb("l0f_zt", [128, IN], BF16)
    zf = sb("l0f_zf", [128, IN], BF16)

    @block.vector
    def _(v):
        v.memset(mxacc[:, 4:5], 0.0)
        v.drain().then_inc(s_v.h, 1)
    s_v.n += 1

    p4_done = {}
    p4_st = {}
    for ci in range(_nchunks()):
        r0 = ci * 128
        tb = tvals[ci % 2]
        ub = upin[ci % 2]

        @block.sync
        def _(sy, ci=ci, r0=r0, tb=tb, ub=ub):
            if ci >= 2:
                sy.wait_ge(s_v.h, p4_done[ci - 2])
            s_ld.dma(sy, tb[:], st["t_sp"][r0:r0 + 128, :])
            s_ld.dma(sy, ub[:], st["up_sp"][r0:r0 + 128, :])
        ld = s_ld.n

        @block.scalar
        def _(sc, ci=ci, tb=tb, ld=ld, ar3_done=ar3_done):
            sc.wait_ge(s_ld.h, ld)
            if ci == 0:
                sc.wait_ge(s_v.h, ar3_done)
            sc.activation(out=i32b[:], in_=tb[:], func=AF.Identity,
                          scale=scal[:, Q34:Q34 + 1])
            sc.drain().then_inc(s_s.h, 1)
        s_s.n += 1
        qd1 = s_s.n

        @block.vector
        def _(v, qd1=qd1):
            v.wait_ge(s_s.h, qd1)
            v.tensor_copy(out=gint[:], in_=i32b[:])
            v.drain().then_inc(s_v.h, 1)
        s_v.n += 1
        gd1 = s_v.n

        @block.scalar
        def _(sc, ub=ub, gd1=gd1):
            sc.wait_ge(s_v.h, gd1)
            sc.activation(out=i32b[:], in_=ub[:], func=AF.Identity,
                          scale=scal[:, IS2:IS2 + 1])
            sc.drain().then_inc(s_s.h, 1)
        s_s.n += 1
        qd = s_s.n

        @block.vector
        def _(v, qd=qd, ci=ci):
            v.wait_ge(s_s.h, qd)
            v.tensor_copy(out=upq[:], in_=i32b[:])
            v.drain()
            v.tensor_tensor(out=zt[:], in0=gint[:], in1=upq[:], op=ALU.mult)
            v.drain()
            v.reduce_max(out=colm[:], in_=zt[:], axis=AX.X, apply_absolute_value=True)
            v.drain()
            v.tensor_tensor(out=mxacc[:, 4:5], in0=mxacc[:, 4:5], in1=colm[:], op=ALU.max)
            v.drain().then_inc(s_v.h, 1)
        s_v.n += 1
        zd = s_v.n

        for tb8 in range(3):
            @block.tensor
            def _(t, tb8=tb8, zd=zd, ci=ci, pe=s_s.n):
                t.wait_ge(s_v.h, zd)
                if tb8 > 0:
                    t.wait_ge(s_s.h, pe)
                for c in range(8):
                    cc = tb8 * 8 + c
                    li = t.transpose(p_bf[:, c * 128:(c + 1) * 128],
                                     zt[:, cc * 128:(cc + 1) * 128], C["ident"])
                li.then_inc(s_t.h, 1)
            s_t.n += 1
            td = s_t.n

            @block.scalar
            def _(sc, tb8=tb8, td=td, ci=ci):
                sc.wait_ge(s_t.h, td)
                if tb8 == 0 and ci >= 1:
                    sc.wait_ge(s_st.h, p4_st[ci - 1])
                sc.activation(out=zf[:, tb8 * 1024:(tb8 + 1) * 1024], in_=p_bf[:],
                              func=AF.Identity)
                sc.drain().then_inc(s_s.h, 1)
            s_s.n += 1
        zfd = s_s.n

        @block.vector
        def _(v, zfd=zfd, ci=ci):
            v.wait_ge(s_s.h, zfd)
            v.drain().then_inc(s_v.h, 1)
        s_v.n += 1
        p4_done[ci] = s_v.n

        @block.sync
        def _(sy, ci=ci, r0=r0, dep=zfd):
            sy.wait_ge(s_s.h, dep)
            if ci >= 2:
                sy.wait_ge(s_st.h, p4_st[ci - 2])
            s_st.dma(sy, st["zf_sp"][r0:r0 + 128, :], zf[:])
        p4_st[ci] = s_st.n

    def row4(v, rrow, wrow):
        v.tensor_scalar(out=wrow[0:1, 3:4], in0=st["scal"][0:1, S4:S4 + 1],
                        scalar1=st["scal"][0:1, S2:S2 + 1], scalar2=None, op0=ALU.mult)
        v.drain()
        v.tensor_scalar(out=wrow[0:1, 0:1], in0=rrow[0:1, 0:1],
                        scalar1=wrow[0:1, 3:4], scalar2=1.0 / 32767.0,
                        op0=ALU.mult, op1=ALU.mult)
        v.drain()
        v.tensor_scalar(out=wrow[0:1, 0:1], in0=wrow[0:1, 0:1], scalar1=1e-12,
                        scalar2=None, op0=ALU.max)
        v.drain()
        v.reciprocal(out=wrow[0:1, 1:2], in_=wrow[0:1, 0:1])
        v.drain()
        v.tensor_scalar(out=wrow[0:1, 2:3], in0=wrow[0:1, 3:4],
                        scalar1=wrow[0:1, 1:2], scalar2=None, op0=ALU.mult)
        v.drain()
    c4, ar4_done = emit_allreduce(st, 3, [4], row4, 3)
    S5, IS5, Q45 = c4, c4 + 1, c4 + 2

    if DEV_TAP is not None:
        st["tap"]("scal_ar4", st["scal"][:, 0:16], st["sems"]["v"], ar4_done)

    wdT = sb("l0f_wdT", [128, 24 * E], BF16)
    wdown_done = emit_wquant(st, W["l0f_wdown"], E, IN, wdT)

    x1in = [gin[0][:, 0:2048].bitcast(F32), gin[1][:, 0:2048].bitcast(F32)]
    x2t = [upq[:, 0:1024], upq[:, 1024:2048]]
    obf = zt[:, 0:IN]

    p5_done = {}
    p5_st = {}
    for ci in range(_nchunks()):
        r0 = ci * 128
        zb = upin[ci % 2]
        xb = x1in[ci % 2]
        x2 = x2t[ci % 2]

        @block.sync
        def _(sy, ci=ci, r0=r0, zb=zb, xb=xb):
            if ci >= 2:
                sy.wait_ge(s_v.h, p5_done[ci - 2])
            s_ld.dma(sy, zb[:], st["zf_sp"][r0:r0 + 128, :])
            s_ld.dma(sy, xb[:], st["xres"][r0:r0 + 128, :])
        ld = s_ld.n

        @block.scalar
        def _(sc, ci=ci, zb=zb, ld=ld, ar4_done=ar4_done, wdown_done=wdown_done):
            sc.wait_ge(s_ld.h, ld)
            if ci == 0:
                sc.wait_ge(s_v.h, ar4_done)
                sc.wait_ge(s_v.h, wdown_done)
            sc.activation(out=i32b[:], in_=zb[:], func=AF.Identity,
                          scale=scal[:, Q45:Q45 + 1])
            sc.drain().then_inc(s_s.h, 1)
        s_s.n += 1
        od = s_s.n

        @block.vector
        def _(v, od=od):
            v.wait_ge(s_s.h, od)
            v.tensor_copy(out=obf[:], in_=i32b[:])
            v.drain().then_inc(s_v.h, 1)
        s_v.n += 1
        ob = s_v.n

        for half in range(2):
            pm = pb[half]

            @block.tensor
            def _(t, half=half, pm=pm, ob=ob, ci=ci):
                t.wait_ge(s_v.h, ob)
                if half == 0 and ci > 0:
                    t.wait_ge(s_v.h, p5_done[ci - 1])
                for kc in range(24):
                    li = t.matmul(pm[:], obf[:, kc * 128:(kc + 1) * 128],
                                  wdT[:].rearrange("p (c r) -> p c r", r=E)[
                                      :, kc, half * 512:(half + 1) * 512],
                                  start=(kc == 0), stop=(kc == 23))
                li.then_inc(s_t.h, 1)
            s_t.n += 1
        dmm = s_t.n

        @block.vector
        def _(v, dmm=dmm, xb=xb, x2=x2):
            v.wait_ge(s_t.h, dmm)
            for half in range(2):
                v.scalar_tensor_tensor(out=x2[:, half * 512:(half + 1) * 512],
                                       in0=pb[half][:], scalar=scal[:, S5:S5 + 1],
                                       in1=xb[:, half * 512:(half + 1) * 512],
                                       op0=ALU.mult, op1=ALU.add)
            v.drain().then_inc(s_v.h, 1)
        s_v.n += 1
        p5_done[ci] = s_v.n

        @block.sync
        def _(sy, ci=ci, r0=r0, x2=x2, dep=p5_done[ci]):
            sy.wait_ge(s_v.h, dep)
            if ci >= 2:
                sy.wait_ge(s_st.h, p5_st[ci - 2])
            s_st.dma(sy, st["xres"][r0:r0 + 128, :], x2[:])
        p5_st[ci] = s_st.n


def emit_l1a(st):
    nc, block, C, W, sb, ps = (st[k] for k in ("nc", "block", "C", "W", "sb", "ps"))
    s_ld, s_gd, s_t, s_v, s_s, s_g, s_st = (st["sems"][k] for k in
                                            ("ld", "gd", "t", "v", "s", "g", "st"))
    mxacc = st["mxacc"]
    st["phase_barrier"]()
    st["arena"].reset()
    pb = st["pb"]
    p_bf = st["pb_bf"]

    wq = sb("l1_wq_s", [128, 8 * E], BF16)
    wk = sb("l1_wk_s", [128, 8 * E], BF16)
    wv = sb("l1_wv_s", [128, 8 * E], BF16)
    wout = sb("l1_wout_s", [128, 8 * E], BF16)
    wgq = sb("l1_wgq_s", [128, 8 * 512], BF16)
    wgk = sb("l1_wgk_s", [128, 8 * 512], BF16)
    wgv = sb("l1_wgv_s", [128, 8 * 512], BF16)
    rout = sb("l1_rout_s", [128, 4 * E], BF16)
    glw_sb = sb("l1_glw_s", [128, 32], BF16)

    @block.gpsimd
    def _(g):
        for wsb, hbm in ((wq, W["l1_wq"]), (wk, W["l1_wk"]), (wv, W["l1_wv"]),
                         (wout, W["l1_wout"])):
            s_gd.dma(g, wsb[:].rearrange("p (c e) -> p c e", c=8),
                     hbm.rearrange("(c p) e -> p c e", p=128))
        for wsb, hbm in ((wgq, W["l1_wgq"]), (wgk, W["l1_wgk"]), (wgv, W["l1_wgv"])):
            s_gd.dma(g, wsb[:].rearrange("p (c e) -> p c e", c=8),
                     hbm.rearrange("(c p) e -> p c e", p=128))
        s_gd.dma(g, rout[:].rearrange("p (c e) -> p c e", c=4),
                 W["l1_rout"].rearrange("(c p) e -> p c e", p=128))
        s_gd.dma(g, glw_sb[:], W["l1_glw_fm"][:])
    w_loaded = s_gd.n

    x2in = [sb(f"l1a_x2in{i}", [128, E], F32) for i in range(2)]
    x3t = [sb(f"l1a_x3t{i}", [128, E], F32) for i in range(2)]
    lnst = sb("l1a_lnst", [128, 8], F32)
    sq32 = sb("l1a_sq32", [128, E], F32)
    ln_tm = sb("l1a_lntm", [128, E], BF16)
    ln_fm = sb("l1a_lnfm", [128, E], BF16)
    q_tm = sb("l1a_qtm", [128, E], BF16)
    k_tm = sb("l1a_ktm", [128, E], BF16)
    gq_tm = sb("l1a_gqtm", [128, 512], BF16)
    gkx_tm = sb("l1a_gkxtm", [128, 512], BF16)
    gvx_tm = sb("l1a_gvxtm", [128, 512], BF16)
    q_pad = [sb(f"l1a_qpad{i}", [128, 8 * 256], BF16) for i in range(2)]
    k_ring = sb("l1a_kring", [128, 8 * 512], BF16)
    v_ring = [sb(f"l1a_vring{i}", [128, E], BF16) for i in range(4)]
    gq_pad = [sb(f"l1a_gqpad{i}", [128, 4 * 128], BF16) for i in range(2)]
    gkx_fm = sb("l1a_gkxfm", [128, 4 * 128], BF16)
    gk_fm = sb("l1a_gkfm", [128, 4 * 64], BF16)
    gv_fm = sb("l1a_gvfm", [128, 4 * 64], BF16)
    gv_tm = sb("l1a_gvtm", [128, 4 * 128], BF16)
    plt_pad = sb("l1a_plt", [128, 8 * 128], BF16)
    pl_sb = sb("l1a_plsb", [128, 512], BF16)
    plm_sb = sb("l1a_plmsb", [128, 512], BF16)
    mrow = sb("l1a_mrow", [128, 16], F32)
    rsl = sb("l1a_rsl", [128, 8], F32)
    rsw = sb("l1a_rsw", [128, 16], F32)
    rem512 = sb("l1a_rem512", [128, 512], BF16)
    rem_fm = sb("l1a_remfm", [128, 512], BF16)
    rem2_tm = [sb(f"l1a_rem2{i}", [128, E], BF16) for i in range(2)]
    glr_t = [sb(f"l1a_glr{i}", [128, 32], F32) for i in range(2)]
    p_sb = sb("l1a_psb", [128, 512], BF16)
    pm_sb = sb("l1a_pmsb", [128, 512], BF16)
    pt_sb = sb("l1a_ptsb", [128, 512], BF16)
    locg = [sb(f"l1a_locg{i}", [128, E], BF16) for i in range(2)]
    out_tm = sb("l1a_outtm", [128, E], BF16)
    out_fm = sb("l1a_outfm", [128, E], BF16)
    ln3t = sb("l1a_ln3t", [128, E], BF16)

    @block.vector
    def _(v, wl=w_loaded):
        v.wait_ge(s_gd.h, wl)
        for tl in (gk_fm, gv_fm, gv_tm, plt_pad, q_pad[0], q_pad[1],
                   gq_pad[0], gq_pad[1]):
            v.memset(tl[:], 0.0)
        v.memset(mxacc[:, 5:12], -1e30)
        v.drain().then_inc(s_v.h, 1)
    s_v.n += 1
    init_done = s_v.n

    chunk_done = {}
    ln3_st = {}
    pbf_free = [0]
    pr_free = [0]

    def ln_block1(xi, out_bf16, s_wait_v=None, st_wait=0):
        @block.vector
        def _(v):
            v.reduce_sum(out=lnst[:, 0:1], in_=xi[:], axis=AX.X)
            v.drain().then_inc(s_v.h, 1)
        s_v.n += 1

        @block.scalar
        def _(s2, s_wait_v=s_wait_v):
            if s_wait_v is not None:
                s2.wait_ge(s_v.h, s_wait_v)
            s2.activation(out=sq32[:], in_=xi[:], func=AF.Square)
            s2.drain().then_inc(s_s.h, 1)
        s_s.n += 1
        sqd = s_s.n

        @block.vector
        def _(v, sqd=sqd):
            v.wait_ge(s_s.h, sqd)
            v.reduce_sum(out=lnst[:, 1:2], in_=sq32[:], axis=AX.X)
            v.drain()
            v.tensor_scalar(out=lnst[:, 2:3], in0=lnst[:, 0:1], scalar1=1.0 / E,
                            scalar2=None, op0=ALU.mult)
            v.drain()
            v.tensor_scalar(out=lnst[:, 3:4], in0=lnst[:, 2:3], scalar1=lnst[:, 2:3],
                            scalar2=None, op0=ALU.mult)
            v.drain()
            v.scalar_tensor_tensor(out=lnst[:, 4:5], in0=lnst[:, 1:2], scalar=1.0 / E,
                                   in1=lnst[:, 3:4], op0=ALU.mult, op1=ALU.subtract)
            v.drain()
            v.tensor_scalar(out=lnst[:, 4:5], in0=lnst[:, 4:5], scalar1=LN_EPS,
                            scalar2=None, op0=ALU.add)
            v.drain().then_inc(s_v.h, 1)
        s_v.n += 1
        vard = s_v.n

        @block.scalar
        def _(s2, vard=vard):
            s2.wait_ge(s_v.h, vard)
            s2.activation(out=lnst[:, 5:6], in_=lnst[:, 4:5], func=AF.Ln)
            s2.drain()
            s2.activation(out=lnst[:, 6:7], in_=lnst[:, 5:6], func=AF.Exp, scale=-0.5)
            s2.drain().then_inc(s_s.h, 1)
        s_s.n += 1
        sqrtd = s_s.n

        @block.vector
        def _(v, sqrtd=sqrtd, st_wait=st_wait):
            v.wait_ge(s_s.h, sqrtd)
            if st_wait:
                v.wait_ge(s_st.h, st_wait)
            v.tensor_scalar(out=out_bf16[:], in0=xi[:], scalar1=lnst[:, 2:3],
                            scalar2=lnst[:, 6:7], op0=ALU.subtract, op1=ALU.mult)
            v.drain().then_inc(s_v.h, 1)
        s_v.n += 1
        return s_v.n

    def transpose_n(src_bf16, n, dst_writes, dep_v, s_wait=None):
        @block.tensor
        def _(t, dep=dep_v, pf=pbf_free[0], sw=s_wait):
            t.wait_ge(s_v.h, dep)
            t.wait_ge(s_v.h, pf)
            if sw is not None:
                t.wait_ge(s_s.h, sw)
            for c in range(n):
                li = t.transpose(p_bf[:, c * 128:(c + 1) * 128],
                                 src_bf16[:, c * 128:(c + 1) * 128], C["ident"])
            li.then_inc(s_t.h, 1)
        s_t.n += 1
        td = s_t.n

        @block.vector
        def _(v, td=td):
            v.wait_ge(s_t.h, td)
            dst_writes(v)
            v.drain().then_inc(s_v.h, 1)
        s_v.n += 1
        pbf_free[0] = s_v.n
        return s_v.n

    for b in range(B_PER_CORE):
        for blk in range(8):
            if b * CPB + blk * 2 + 2 > _nchunks():
                continue
            chunk_vals = {}
            for a in range(2):
                qc = blk * 2 + a
                ci = b * CPB + qc
                r0 = ci * 128
                xi = x2in[a]

                @block.sync
                def _(sy, ci=ci, r0=r0, xi=xi):
                    if ci >= 2:
                        sy.wait_ge(s_v.h, chunk_done[ci - 2])
                    s_ld.dma(sy, xi[:], st["xres"][r0:r0 + 128, :])
                ldx = s_ld.n

                @block.vector
                def _(v, ci=ci, ldx=ldx):
                    v.wait_ge(s_ld.h, ldx)
                    if ci == 0:
                        v.wait_ge(s_v.h, init_done)
                    v.drain().then_inc(s_v.h, 1)
                s_v.n += 1

                @block.scalar
                def _(s2, ldx=ldx):
                    s2.wait_ge(s_ld.h, ldx)
                    s2.drain().then_inc(s_s.h, 1)
                s_s.n += 1

                lnd = ln_block1(xi, ln_tm)
                lnfm = transpose_n(ln_tm, 8,
                                   lambda v: v.tensor_copy(out=ln_fm[:], in_=p_bf[:]),
                                   lnd)

                def proj(wsb, width, dst, dep):
                    @block.tensor
                    def _(t, dep=dep, f0=pr_free[0], wsb=wsb, width=width):
                        t.wait_ge(s_v.h, dep)
                        t.wait_ge(s_v.h, f0)
                        half = width // 2
                        for c in range(8):
                            t.matmul(pb[0][:, 0:half], ln_fm[:, c * 128:(c + 1) * 128],
                                     wsb[:, c * width: c * width + half],
                                     start=(c == 0), stop=(c == 7))
                        for c in range(8):
                            li = t.matmul(pb[1][:, 0:half], ln_fm[:, c * 128:(c + 1) * 128],
                                          wsb[:, c * width + half: (c + 1) * width],
                                          start=(c == 0), stop=(c == 7))
                        li.then_inc(s_t.h, 1)
                    s_t.n += 1
                    mmd = s_t.n

                    @block.vector
                    def _(v, mmd=mmd, dst=dst, width=width):
                        v.wait_ge(s_t.h, mmd)
                        half = width // 2
                        v.tensor_copy(out=dst[:, 0:half], in_=pb[0][:, 0:half])
                        v.tensor_copy(out=dst[:, half:width], in_=pb[1][:, 0:half])
                        v.drain().then_inc(s_v.h, 1)
                    s_v.n += 1
                    pr_free[0] = s_v.n
                    return s_v.n

                qd = proj(wq, E, q_tm, lnfm)
                kd = proj(wk, E, k_tm, qd)
                vd = proj(wv, E, v_ring[qc % 4], kd)
                gqd = proj(wgq, 512, gq_tm, vd)
                gkd = proj(wgk, 512, gkx_tm, gqd)
                gvd = proj(wgv, 512, gvx_tm, gkd)

                def wr_qpad(v, a=a):
                    v.tensor_copy(
                        out=q_pad[0][0:64, :].rearrange("p (c r) -> p c r", r=256)[
                            :, :, a * 128:(a + 1) * 128],
                        in_=p_bf[0:64, :].rearrange("p (c r) -> p c r", r=128))
                    v.tensor_copy(
                        out=q_pad[1][64:128, :].rearrange("p (c r) -> p c r", r=256)[
                            :, :, a * 128:(a + 1) * 128],
                        in_=p_bf[64:128, :].rearrange("p (c r) -> p c r", r=128))
                transpose_n(q_tm, 8, wr_qpad, qd)

                def wr_kring(v, blk=blk, a=a):
                    o0 = (blk % 2) * 256 + a * 128
                    v.tensor_copy(
                        out=k_ring[:].rearrange("p (c r) -> p c r", r=512)[
                            :, :, o0:o0 + 128],
                        in_=p_bf[:].rearrange("p (c r) -> p c r", r=128))
                transpose_n(k_tm, 8, wr_kring, kd)

                def wr_gqpad(v):
                    v.tensor_copy(out=gq_pad[0][0:64, :], in_=p_bf[0:64, 0:512])
                    v.tensor_copy(out=gq_pad[1][64:128, :], in_=p_bf[64:128, 0:512])
                gqpd = transpose_n(gq_tm, 4, wr_gqpad, gqd)

                gkfd = transpose_n(gkx_tm, 4,
                                   lambda v: v.tensor_copy(out=gkx_fm[:], in_=p_bf[:, 0:512]),
                                   gkd)

                @block.vector
                def _(v, qc=qc, dep=gkfd):
                    v.wait_ge(s_v.h, dep)
                    v.tensor_reduce(out=mrow[:, 0:16],
                                    in_=gkx_fm[:].rearrange("p (l i) -> p l i", i=32),
                                    axis=AX.X, op=ALU.add)
                    v.drain()
                    v.tensor_scalar(
                        out=gk_fm[:].rearrange("p (c n) -> p c n", n=64)[
                            :, :, qc * 4:(qc + 1) * 4],
                        in0=mrow[:, 0:16].rearrange("p (c g) -> p c g", g=4),
                        scalar1=1.0 / 32, scalar2=None, op0=ALU.mult)
                    v.drain().then_inc(s_v.h, 1)
                s_v.n += 1
                gk_done = s_v.n

                gvfd = transpose_n(gvx_tm, 4,
                                   lambda v: v.tensor_copy(out=gkx_fm[:], in_=p_bf[:, 0:512]),
                                   gvd)

                @block.vector
                def _(v, qc=qc, dep=gvfd):
                    v.wait_ge(s_v.h, dep)
                    v.tensor_reduce(out=mrow[:, 0:16],
                                    in_=gkx_fm[:].rearrange("p (l i) -> p l i", i=32),
                                    axis=AX.X, op=ALU.add)
                    v.drain()
                    v.tensor_scalar(
                        out=gv_fm[:].rearrange("p (c n) -> p c n", n=64)[
                            :, :, qc * 4:(qc + 1) * 4],
                        in0=mrow[:, 0:16].rearrange("p (c g) -> p c g", g=4),
                        scalar1=1.0 / 32, scalar2=None, op0=ALU.mult)
                    v.drain().then_inc(s_v.h, 1)
                s_v.n += 1
                gv_done = s_v.n

                @block.tensor
                def _(t, dep=gv_done, pf=pbf_free[0]):
                    t.wait_ge(s_v.h, dep)
                    t.wait_ge(s_v.h, pf)
                    for c in range(4):
                        li = t.transpose(p_bf[0:64, c * 128:(c + 1) * 128],
                                         gv_fm[:, c * 64:(c + 1) * 64], C["ident"])
                    li.then_inc(s_t.h, 1)
                s_t.n += 1
                gvtd_t = s_t.n

                @block.vector
                def _(v, td=gvtd_t):
                    v.wait_ge(s_t.h, td)
                    v.tensor_copy(out=gv_tm[0:64, :], in_=p_bf[0:64, 0:512])
                    v.drain().then_inc(s_v.h, 1)
                s_v.n += 1
                pbf_free[0] = s_v.n
                gvt_done = s_v.n

                @block.tensor
                def _(t, dep=gk_done, dep2=gqpd, f=pr_free[0]):
                    t.wait_ge(s_v.h, dep)
                    t.wait_ge(s_v.h, dep2)
                    for lh in range(8):
                        li = t.matmul(pb[2][:, lh * 64:(lh + 1) * 64],
                                      gq_pad[lh % 2][:, (lh // 2) * 128:(lh // 2 + 1) * 128],
                                      gk_fm[:, (lh // 2) * 64:(lh // 2) * 64 + 64],
                                      start=True, stop=True)
                    li.then_inc(s_t.h, 1)
                s_t.n += 1
                lscd = s_t.n

                @block.vector
                def _(v, d=lscd):
                    v.wait_ge(s_t.h, d)
                    v.reduce_max(out=mrow[:, 0:1], in_=pb[2][:], axis=AX.X)
                    v.drain()
                    v.tensor_scalar(out=mrow[:, 1:2], in0=mrow[:, 0:1], scalar1=-0.125,
                                    scalar2=None, op0=ALU.mult)
                    v.drain().then_inc(s_v.h, 1)
                s_v.n += 1
                lmx = s_v.n

                @block.scalar
                def _(s2, d=lmx):
                    s2.wait_ge(s_v.h, d)
                    s2.activation(out=pl_sb[:], in_=pb[2][:], func=AF.Exp,
                                  scale=0.125, bias=mrow[:, 1:2])
                    s2.drain().then_inc(s_s.h, 1)
                s_s.n += 1
                lexp = s_s.n

                @block.vector
                def _(v, d=lexp, qc=qc):
                    v.wait_ge(s_s.h, d)
                    lm = C["lmask"][:, qc * 64:(qc + 1) * 64]
                    v.tensor_tensor(out=plm_sb[:].rearrange("p (l n) -> p l n", n=64),
                                    in0=pl_sb[:].rearrange("p (l n) -> p l n", n=64),
                                    in1=lm.unsqueeze(1).broadcast_to([128, 8, 64]),
                                    op=ALU.mult)
                    v.drain()
                    v.tensor_reduce(out=rsl[:], in_=plm_sb[:].rearrange(
                        "p (l n) -> p l n", n=64), axis=AX.X, op=ALU.add)
                    v.drain()
                    v.reciprocal(out=rsl[:], in_=rsl[:])
                    v.drain().then_inc(s_v.h, 1)
                s_v.n += 1
                lsum = s_v.n

                @block.tensor
                def _(t, d=lsum, pf=pbf_free[0]):
                    t.wait_ge(s_v.h, d)
                    t.wait_ge(s_v.h, pf)
                    for lh in range(8):
                        li = t.transpose(p_bf[0:64, lh * 128:(lh + 1) * 128],
                                         plm_sb[:, lh * 64:(lh + 1) * 64], C["ident"])
                    li.then_inc(s_t.h, 1)
                s_t.n += 1
                pltd = s_t.n

                @block.vector
                def _(v, d=pltd):
                    v.wait_ge(s_t.h, d)
                    v.tensor_copy(out=plt_pad[0:64, :], in_=p_bf[0:64, :])
                    v.drain().then_inc(s_v.h, 1)
                s_v.n += 1
                pbf_free[0] = s_v.n
                plte = s_v.n

                @block.tensor
                def _(t, d=plte, d2=gvt_done):
                    t.wait_ge(s_v.h, d)
                    t.wait_ge(s_v.h, d2)
                    for lh in range(8):
                        li = t.matmul(pb[3][:, lh * 64:(lh + 1) * 64],
                                      plt_pad[:, lh * 128:(lh + 1) * 128],
                                      gv_tm[:, (lh // 2) * 128 + (lh % 2) * 64:
                                            (lh // 2) * 128 + (lh % 2) * 64 + 64],
                                      start=True, stop=True)
                    li.then_inc(s_t.h, 1)
                s_t.n += 1
                remd = s_t.n

                @block.vector
                def _(v, d=remd):
                    v.wait_ge(s_t.h, d)
                    for lh in range(8):
                        v.tensor_scalar(out=rem512[:, lh * 64:(lh + 1) * 64],
                                        in0=pb[3][:, lh * 64:(lh + 1) * 64],
                                        scalar1=rsl[:, lh:lh + 1], scalar2=None,
                                        op0=ALU.mult)
                    v.drain().then_inc(s_v.h, 1)
                s_v.n += 1
                rem512d = s_v.n

                remfmd = transpose_n(rem512, 4,
                                     lambda v: v.tensor_copy(out=rem_fm[:],
                                                             in_=p_bf[:, 0:512]),
                                     rem512d)

                @block.tensor
                def _(t, d=remfmd, f=pr_free[0]):
                    t.wait_ge(s_v.h, d)
                    t.wait_ge(s_v.h, f)
                    for half in range(2):
                        for c in range(4):
                            li = t.matmul(pb[half][:],
                                          rem_fm[:, c * 128:(c + 1) * 128],
                                          rout[:, c * E + half * 512: c * E + (half + 1) * 512],
                                          start=(c == 0), stop=(c == 3))
                    li.then_inc(s_t.h, 1)
                s_t.n += 1
                rem2d = s_t.n

                @block.vector
                def _(v, d=rem2d, a=a):
                    v.wait_ge(s_t.h, d)
                    v.tensor_copy(out=rem2_tm[a][:, 0:512], in_=pb[0][:])
                    v.tensor_copy(out=rem2_tm[a][:, 512:1024], in_=pb[1][:])
                    v.drain().then_inc(s_v.h, 1)
                s_v.n += 1
                pr_free[0] = s_v.n

                @block.tensor
                def _(t, d=pr_free[0], dl=lexp, a=a):
                    t.wait_ge(s_s.h, dl)
                    t.wait_ge(s_v.h, lsum)
                    for h in range(16):
                        t.matmul(pb[2][:, h:h + 1],
                                 q_pad[h % 2][:, (h // 2) * 256 + a * 128:
                                              (h // 2) * 256 + (a + 1) * 128],
                                 glw_sb[:, h:h + 1], start=True, stop=True)
                    for h in range(16):
                        li = t.matmul(pb[2][:, 16 + h:17 + h],
                                      q_pad[h % 2][:, (h // 2) * 256 + a * 128:
                                                   (h // 2) * 256 + (a + 1) * 128],
                                      glw_sb[:, 16 + h:17 + h], start=True, stop=True)
                    li.then_inc(s_t.h, 1)
                s_t.n += 1
                gld = s_t.n

                @block.scalar
                def _(s2, d=gld, a=a):
                    s2.wait_ge(s_t.h, d)
                    s2.activation(out=glr_t[a][:], in_=pb[2][:, 0:32], func=AF.Sigmoid)
                    s2.drain().then_inc(s_s.h, 1)
                s_s.n += 1
                chunk_vals[a] = dict(glr=s_s.n)

            nkc = 2 if blk == 0 else 4
            kw = nkc * 128
            for a in range(2):
                loc_started = {0: False, 1: False}
                for h in range(16):
                    @block.tensor
                    def _(t, h=h, a=a, blk=blk, sexp=s_s.n, svx=s_v.n):
                        if h == 0:
                            t.wait_ge(s_s.h, sexp)
                        else:
                            t.wait_ge(s_s.h, wexp_d[h - 1])
                            t.wait_ge(s_v.h, wmax_d[h - 1])
                        if blk == 0:
                            li = t.matmul(pb[4][:, 0:256],
                                          q_pad[h % 2][:, (h // 2) * 256 + a * 128:
                                                       (h // 2) * 256 + (a + 1) * 128],
                                          k_ring[:].rearrange("p (c r) -> p c r", r=512)[
                                              :, h // 2, 0:256],
                                          start=True, stop=True)
                        else:
                            po = ((blk - 1) % 2) * 256
                            co = (blk % 2) * 256
                            t.matmul(pb[4][:, 0:256],
                                     q_pad[h % 2][:, (h // 2) * 256 + a * 128:
                                                  (h // 2) * 256 + (a + 1) * 128],
                                     k_ring[:].rearrange("p (c r) -> p c r", r=512)[
                                         :, h // 2, po:po + 256],
                                     start=True, stop=True)
                            li = t.matmul(pb[4][:, 256:512],
                                          q_pad[h % 2][:, (h // 2) * 256 + a * 128:
                                                       (h // 2) * 256 + (a + 1) * 128],
                                          k_ring[:].rearrange("p (c r) -> p c r", r=512)[
                                              :, h // 2, co:co + 256],
                                          start=True, stop=True)
                        li.then_inc(s_t.h, 1)
                    s_t.n += 1
                    wsc = s_t.n

                    @block.vector
                    def _(v, d=wsc, kw=kw):
                        v.wait_ge(s_t.h, d)
                        v.reduce_max(out=mrow[:, 2:3], in_=pb[4][:, 0:kw], axis=AX.X)
                        v.drain()
                        v.tensor_scalar(out=mrow[:, 3:4], in0=mrow[:, 2:3], scalar1=-0.125,
                                        scalar2=None, op0=ALU.mult)
                        v.drain().then_inc(s_v.h, 1)
                    s_v.n += 1
                    if h == 0:
                        wmax_d = {}
                        wexp_d = {}
                    wmax_d[h] = s_v.n

                    @block.scalar
                    def _(s2, d=wmax_d[h], kw=kw):
                        s2.wait_ge(s_v.h, d)
                        s2.activation(out=p_sb[:, 0:kw], in_=pb[4][:, 0:kw], func=AF.Exp,
                                      scale=0.125, bias=mrow[:, 3:4])
                        s2.drain().then_inc(s_s.h, 1)
                    s_s.n += 1
                    wexp_d[h] = s_s.n

                    @block.vector
                    def _(v, d=wexp_d[h], a=a, h=h, blk=blk, kw=kw):
                        v.wait_ge(s_s.h, d)
                        msk = (C["bmask0"] if a == 0 else C["bmask1"]) if blk == 0 else \
                              (C["wmask0"] if a == 0 else C["wmask1"])
                        v.tensor_tensor(out=pm_sb[:, 0:kw], in0=p_sb[:, 0:kw],
                                        in1=msk[:, 0:kw], op=ALU.mult)
                        v.drain()
                        v.reduce_sum(out=mrow[:, 4:5], in_=pm_sb[:, 0:kw], axis=AX.X)
                        v.drain()
                        v.reciprocal(out=rsw[:, h:h + 1], in_=mrow[:, 4:5])
                        v.drain().then_inc(s_v.h, 1)
                    s_v.n += 1
                    wmsk = s_v.n

                    @block.tensor
                    def _(t, d=wmsk, pf=pbf_free[0], nkc=nkc):
                        t.wait_ge(s_v.h, d)
                        t.wait_ge(s_v.h, pf)
                        for c in range(nkc):
                            li = t.transpose(p_bf[:, c * 128:(c + 1) * 128],
                                             pm_sb[:, c * 128:(c + 1) * 128], C["ident"])
                        li.then_inc(s_t.h, 1)
                    s_t.n += 1
                    ptd = s_t.n

                    @block.vector
                    def _(v, d=ptd, nkc=nkc):
                        v.wait_ge(s_t.h, d)
                        v.tensor_copy(out=pt_sb[:, 0:nkc * 128], in_=p_bf[:, 0:nkc * 128])
                        v.drain().then_inc(s_v.h, 1)
                    s_v.n += 1
                    pbf_free[0] = s_v.n
                    pte = s_v.n

                    @block.tensor
                    def _(t, d=pte, h=h, blk=blk, nkc=nkc):
                        t.wait_ge(s_v.h, d)
                        bank = 5 + h // 8
                        col = (h % 8) * 64
                        first = not loc_started[h // 8]
                        loc_started[h // 8] = True
                        for c in range(nkc):
                            if blk == 0:
                                vr = v_ring[(2 * blk + c) % 4]
                            else:
                                vr = v_ring[(2 * (blk - 1) + c) % 4]
                            li = t.matmul(pb[bank][:, col:col + 64],
                                          pt_sb[:, c * 128:(c + 1) * 128],
                                          vr[:, h * 64:(h + 1) * 64],
                                          start=(c == 0), stop=(c == nkc - 1))
                        li.then_inc(s_t.h, 1)
                    s_t.n += 1
                pv_done = s_t.n

                @block.vector
                def _(v, d=pv_done, a=a, gd=chunk_vals[a]["glr"]):
                    v.wait_ge(s_t.h, d)
                    v.wait_ge(s_s.h, gd)
                    for h in range(16):
                        bank = 5 + h // 8
                        col = (h % 8) * 64
                        v.tensor_scalar(out=locg[a][:, h * 64:(h + 1) * 64],
                                        in0=pb[bank][:, col:col + 64],
                                        scalar1=rsw[:, h:h + 1],
                                        scalar2=glr_t[a][:, h:h + 1],
                                        op0=ALU.mult, op1=ALU.mult)
                    v.drain().then_inc(s_v.h, 1)
                s_v.n += 1
                locg_d = s_v.n

                @block.vector
                def _(v, d=locg_d, a=a):
                    for h in range(16):
                        v.scalar_tensor_tensor(out=out_tm[:, h * 64:(h + 1) * 64],
                                               in0=rem2_tm[a][:, h * 64:(h + 1) * 64],
                                               scalar=glr_t[a][:, 16 + h:17 + h],
                                               in1=locg[a][:, h * 64:(h + 1) * 64],
                                               op0=ALU.mult, op1=ALU.add)
                    v.drain().then_inc(s_v.h, 1)
                s_v.n += 1
                outd = s_v.n

                outfmd = transpose_n(out_tm, 8,
                                     lambda v: v.tensor_copy(out=out_fm[:], in_=p_bf[:]),
                                     outd)

                @block.tensor
                def _(t, d=outfmd, f=pr_free[0]):
                    t.wait_ge(s_v.h, d)
                    t.wait_ge(s_v.h, f)
                    for c in range(8):
                        t.matmul(pb[0][:], out_fm[:, c * 128:(c + 1) * 128],
                                 wout[:, c * E: c * E + 512],
                                 start=(c == 0), stop=(c == 7))
                    for c in range(8):
                        li = t.matmul(pb[1][:], out_fm[:, c * 128:(c + 1) * 128],
                                      wout[:, c * E + 512: (c + 1) * E],
                                      start=(c == 0), stop=(c == 7))
                    li.then_inc(s_t.h, 1)
                s_t.n += 1
                woutd = s_t.n

                ci = b * CPB + blk * 2 + a
                r0 = ci * 128
                x3 = x3t[a]

                @block.vector
                def _(v, d=woutd, a=a, x3=x3, ci=ci):
                    v.wait_ge(s_t.h, d)
                    if ci >= 2:
                        v.wait_ge(s_st.h, ln3_st.get(ci - 2, 0))
                    v.tensor_tensor(out=x3[:, 0:512], in0=x2in[a][:, 0:512],
                                    in1=pb[0][:], op=ALU.add)
                    v.tensor_tensor(out=x3[:, 512:1024], in0=x2in[a][:, 512:1024],
                                    in1=pb[1][:], op=ALU.add)
                    v.drain().then_inc(s_v.h, 1)
                s_v.n += 1
                pr_free[0] = s_v.n
                x3d = s_v.n

                @block.sync
                def _(sy, x3=x3, r0=r0, d=x3d):
                    sy.wait_ge(s_v.h, d)
                    s_st.dma(sy, st["xres"][r0:r0 + 128, :], x3[:])

                ln3d = ln_block1(x3, ln3t, s_wait_v=x3d,
                                 st_wait=ln3_st.get(ci - 1, 0) if ci >= 1 else 0)

                @block.vector
                def _(v, d=ln3d, ci=ci):
                    v.reduce_max(out=lnst[:, 7:8], in_=ln3t[:], axis=AX.X)
                    v.drain()
                    v.tensor_tensor(out=mxacc[:, 5:6], in0=mxacc[:, 5:6],
                                    in1=lnst[:, 7:8], op=ALU.max)
                    v.drain().then_inc(s_v.h, 1)
                s_v.n += 1
                chunk_done[ci] = s_v.n

                @block.sync
                def _(sy, r0=r0, d=chunk_done[ci]):
                    sy.wait_ge(s_v.h, d)
                    s_st.dma(sy, st["ln1_sp"][r0:r0 + 128, :], ln3t[:])
                ln3_st[ci] = s_st.n



def emit_l1f(st):
    nc, block, C, W, sb, ps = (st[k] for k in ("nc", "block", "C", "W", "sb", "ps"))
    s_ld, s_gd, s_t, s_v, s_s, s_g, s_st = (st["sems"][k] for k in
                                            ("ld", "gd", "t", "v", "s", "g", "st"))
    mxacc, scal = st["mxacc"], st["scal"]
    st["phase_barrier"]()
    st["arena"].reset()
    pb = st["pb"]
    p_bf = st["pb_bf"]
    nch = _nchunks()
    C_ONESROW = st["C_ONESROW"]

    def row5(v, rrow, wrow):
        v.tensor_scalar(out=wrow[0:1, 0:1], in0=rrow[0:1, 0:1], scalar1=1e-8,
                        scalar2=1.0 / 255.0, op0=ALU.max, op1=ALU.mult)
        v.drain()
        v.reciprocal(out=wrow[0:1, 1:2], in_=wrow[0:1, 0:1])
        v.drain()
    c5, ar5_done = emit_allreduce(st, 4, [5], row5, 2)
    S1f, IS1f = c5, c5 + 1

    mark = st["arena"].mark()
    wT1 = sb("l1f_wT1", [128, 8 * 4096], BF16)
    sw1 = sb("l1f_sw1", [128, 32], F32)
    isw1 = sb("l1f_isw1", [128, 32], F32)
    s1sw = sb("l1f_s1sw", [128, 32], F32)
    br1 = sb("l1f_br1", [1, 4096], F32)
    mark_wq = st["arena"].mark()
    wld = [sb(f"l1f_wld{i}", [128, 1024], F32) for i in range(2)]
    wi32 = sb("l1f_wi32", [128, 1024], I32)
    wbf = sb("l1f_wbf", [128, 1024], BF16)
    mcol = sb("l1f_mcol", [128, 1], F32)

    ld_done = {}
    ev_done = {}
    for t in range(32):
        @block.gpsimd
        def _(g, t=t, prev=ev_done.get(t - 2, 0)):
            g.wait_ge(s_v.h, prev)
            s_gd.dma(g, wld[t % 2][:], W["l1f_w1"][t * 128:(t + 1) * 128, :])
        ld_done[t] = s_gd.n

        @block.vector
        def _(v, t=t, ld=ld_done[t]):
            v.wait_ge(s_gd.h, ld)
            v.tensor_reduce(out=mcol[:], in_=wld[t % 2][:], axis=AX.X, op=ALU.max,
                            apply_absolute_value=True)
            v.drain()
            v.tensor_scalar(out=sw1[:, t:t + 1], in0=mcol[:], scalar1=1e-8,
                            scalar2=1.0 / 127.0, op0=ALU.max, op1=ALU.mult)
            v.drain()
            v.reciprocal(out=isw1[:, t:t + 1], in_=sw1[:, t:t + 1])
            v.drain()
            v.tensor_scalar(out=wi32[:], in0=wld[t % 2][:], scalar1=isw1[:, t:t + 1],
                            scalar2=None, op0=ALU.mult)
            v.drain()
            v.tensor_copy(out=wbf[:], in_=wi32[:])
            v.drain().then_inc(s_v.h, 1)
        s_v.n += 1
        qd = s_v.n

        @block.tensor
        def _(t2, qd=qd, pf=st.get("l1f_pbf", 0)):
            t2.wait_ge(s_v.h, qd)
            t2.wait_ge(s_v.h, pf)
            for c in range(8):
                li = t2.transpose(p_bf[:, c * 128:(c + 1) * 128],
                                  wbf[:, c * 128:(c + 1) * 128], C["ident"])
            li.then_inc(s_t.h, 1)
        s_t.n += 1
        td = s_t.n

        @block.vector
        def _(v, t=t, td=td):
            v.wait_ge(s_t.h, td)
            v.tensor_copy(
                out=wT1[:].rearrange("p (c f) -> p c f", f=4096)[:, :, t * 128:(t + 1) * 128],
                in_=p_bf[:].rearrange("p (c f) -> p c f", f=128))
            v.drain().then_inc(s_v.h, 1)
        s_v.n += 1
        st["l1f_pbf"] = s_v.n
        ev_done[t] = s_v.n
    w1_done = s_v.n

    b1f = sb("l1f_b1f", [128, 32], F32)
    bq1i = sb("l1f_bq1i", [128, 32], I32)
    bq1f = sb("l1f_bq1f", [128, 32], F32)

    @block.gpsimd
    def _(g):
        s_gd.dma(g, b1f[:], W["l1f_b1_fm"][:])
    b1_ld = s_gd.n

    bqb1 = sb("l1f_bqb1", [128, 32], BF16)
    brt1 = sb("l1f_brt1", [32, 128], BF16)

    @block.vector
    def _(v, ld=b1_ld, w1d=w1_done, ar=ar5_done):
        v.wait_ge(s_gd.h, ld)
        v.wait_ge(s_v.h, max(w1d, ar))
        v.tensor_tensor(out=bq1f[:], in0=b1f[:], in1=isw1[:], op=ALU.mult)
        v.drain()
        v.tensor_scalar(out=bq1i[:], in0=bq1f[:], scalar1=scal[:, IS1f:IS1f + 1],
                        scalar2=None, op0=ALU.mult)
        v.drain()
        v.tensor_copy(out=bqb1[:], in_=bq1i[:])
        v.drain()
        v.tensor_scalar(out=s1sw[:], in0=sw1[:], scalar1=scal[:, S1f:S1f + 1],
                        scalar2=None, op0=ALU.mult)
        v.drain().then_inc(s_v.h, 1)
    s_v.n += 1
    bq1_done = s_v.n

    @block.tensor
    def _(t2, d=bq1_done, pf=st["l1f_pbf"]):
        t2.wait_ge(s_v.h, d)
        t2.wait_ge(s_v.h, pf)
        t2.transpose(p_bf[0:32, 0:128], bqb1[:], C["ident"]).then_inc(s_t.h, 1)
    s_t.n += 1
    bq1_t = s_t.n

    @block.vector
    def _(v, d=bq1_t):
        v.wait_ge(s_t.h, d)
        v.tensor_copy(out=brt1[:], in_=p_bf[0:32, 0:128])
        v.drain().then_inc(s_v.h, 1)
    s_v.n += 1
    st["l1f_pbf"] = s_v.n

    @block.gpsimd
    def _(g, d=s_v.n):
        g.wait_ge(s_v.h, d)
        s_gd.dma(g, st["bb_sp"][:], brt1[:])

    @block.sync
    def _(sy, d=s_gd.n):
        sy.wait_ge(s_gd.h, d)
        s_ld.dma(sy, br1[0:1, :],
                 st["bb_sp"].rearrange("t p -> (t p)").unsqueeze(0))
    br1_ld = s_ld.n

    @block.vector
    def _(v, ld=br1_ld):
        v.wait_ge(s_ld.h, ld)
        v.drain().then_inc(s_v.h, 1)
    s_v.n += 1
    br1_done = s_v.n

    st["arena"].rollback(mark_wq)
    lb = [sb(f"l1f_lb{i}", [128, E], BF16) for i in range(2)]
    xqi = sb("l1f_xqi", [128, E], I32)
    xqt = sb("l1f_xqt", [128, E], BF16)
    xqf = sb("l1f_xqf", [128, E], BF16)
    hraw = sb("l1f_hraw", [128, 2048], BF16)
    hT = sb("l1f_hT", [128, 4096], BF16)
    hf = [sb(f"l1f_hf{i}", [128, 4096], BF16) for i in range(2)]
    mc2 = sb("l1f_mc2", [128, 1], F32)

    lds = {}
    hst = {}
    mm_done = 0
    ev_tail = 0
    for ci in range(nch):
        @block.sync
        def _(sy, ci=ci, prev=lds.get(ci - 2, 0)):
            if prev:
                sy.wait_ge(s_v.h, prev)
            r0 = ci * 128
            s_ld.dma(sy, lb[ci % 2][:], st["ln1_sp"][r0:r0 + 128, :])
        lb_ld = s_ld.n

        @block.vector
        def _(v, ci=ci, ld=lb_ld, br=br1_done):
            v.wait_ge(s_ld.h, ld)
            if ci == 0:
                v.wait_ge(s_v.h, br)
            v.tensor_scalar(out=xqi[:], in0=lb[ci % 2][:],
                            scalar1=scal[:, IS1f:IS1f + 1], scalar2=0.0,
                            op0=ALU.mult, op1=ALU.max)
            v.drain()
            v.tensor_copy(out=xqt[:], in_=xqi[:])
            v.drain().then_inc(s_v.h, 1)
        s_v.n += 1
        lds[ci] = s_v.n
        xq_done = s_v.n

        @block.tensor
        def _(t2, d=xq_done, pf=st["l1f_pbf"]):
            t2.wait_ge(s_v.h, d)
            t2.wait_ge(s_v.h, pf)
            for c in range(8):
                li = t2.transpose(p_bf[:, c * 128:(c + 1) * 128],
                                  xqt[:, c * 128:(c + 1) * 128], C["ident"])
            li.then_inc(s_t.h, 1)
        s_t.n += 1
        tq = s_t.n

        @block.vector
        def _(v, d=tq):
            v.wait_ge(s_t.h, d)
            v.tensor_copy(out=xqf[:], in_=p_bf[:])
            v.drain().then_inc(s_v.h, 1)
        s_v.n += 1
        st["l1f_pbf"] = s_v.n
        xqf_done = s_v.n

        for half in range(2):
            @block.tensor
            def _(t2, half=half, d=xqf_done, evt=ev_tail):
                t2.wait_ge(s_v.h, d)
                t2.wait_ge(s_v.h, evt)
                for b in range(4):
                    fb = (half * 4 + b) * 512
                    for c in range(8):
                        t2.matmul(pb[b][:], xqf[:, c * 128:(c + 1) * 128],
                                  wT1[:, c * 4096 + fb: c * 4096 + fb + 512],
                                  start=(c == 0), stop=False)
                for b in range(4):
                    fb = (half * 4 + b) * 512
                    li = t2.matmul(pb[b][:], C_ONESROW[:], br1[0:1, fb:fb + 512],
                                   start=False, stop=True)
                li.then_inc(s_t.h, 1)
            s_t.n += 1
            hm = s_t.n

            @block.vector
            def _(v, d=hm):
                v.wait_ge(s_t.h, d)
                for b in range(4):
                    v.tensor_copy(out=hraw[:, b * 512:(b + 1) * 512], in_=pb[b][:])
                v.drain().then_inc(s_v.h, 1)
            s_v.n += 1
            ev_tail = s_v.n
            hr_done = s_v.n

            for r in range(2):
                @block.tensor
                def _(t2, r=r, d=hr_done, pf=st["l1f_pbf"]):
                    t2.wait_ge(s_v.h, d)
                    t2.wait_ge(s_v.h, pf)
                    for c in range(8):
                        li = t2.transpose(p_bf[:, c * 128:(c + 1) * 128],
                                          hraw[:, r * 1024 + c * 128: r * 1024 + (c + 1) * 128],
                                          C["ident"])
                    li.then_inc(s_t.h, 1)
                s_t.n += 1
                tr = s_t.n

                @block.vector
                def _(v, half=half, r=r, d=tr):
                    v.wait_ge(s_t.h, d)
                    u0 = (half * 16 + r * 8) * 128
                    v.tensor_copy(out=hT[:, u0:u0 + 1024], in_=p_bf[:])
                    v.drain().then_inc(s_v.h, 1)
                s_v.n += 1
                st["l1f_pbf"] = s_v.n

        @block.vector
        def _(v, ci=ci, hs=hst.get(ci - 2, 0)):
            if hs:
                v.wait_ge(s_st.h, hs)
            v.tensor_tensor(out=hf[ci % 2][:].rearrange("p (m f) -> p m f", f=128),
                            in0=hT[:].rearrange("p (m f) -> p m f", f=128),
                            in1=s1sw[:].unsqueeze(2).broadcast_to([128, 32, 128]),
                            op=ALU.mult)
            v.drain()
            v.reduce_max(out=mc2[:], in_=hf[ci % 2][:], axis=AX.X)
            v.drain()
            v.tensor_tensor(out=mxacc[:, 6:7], in0=mxacc[:, 6:7], in1=mc2[:],
                            op=ALU.max)
            v.drain().then_inc(s_v.h, 1)
        s_v.n += 1
        hf_done = s_v.n

        @block.sync
        def _(sy, ci=ci, d=hf_done):
            sy.wait_ge(s_v.h, d)
            s_st.dma(sy, st["h_sp"][ci * 128:(ci + 1) * 128, :], hf[ci % 2][:])
        hst[ci] = s_st.n

    def row6(v, rrow, wrow):
        v.tensor_scalar(out=wrow[0:1, 0:1], in0=rrow[0:1, 0:1], scalar1=1e-8,
                        scalar2=1.0 / 255.0, op0=ALU.max, op1=ALU.mult)
        v.drain()
        v.reciprocal(out=wrow[0:1, 1:2], in_=wrow[0:1, 0:1])
        v.drain()
    c6, ar6_done = emit_allreduce(st, 5, [6], row6, 2)
    S2f, IS2f = c6, c6 + 1

    st["phase_barrier"]()
    st["arena"].rollback(mark)
    wT2 = sb("l1f_wT2", [128, 32 * 1024], BF16)
    sw2 = sb("l1f_sw2", [128, 8], F32)
    isw2 = sb("l1f_isw2", [128, 8], F32)
    br2 = sb("l1f_br2", [1, 1024], F32)
    mark_w2 = st["arena"].mark()
    w2ld = [sb(f"l1f_w2ld{i}", [128, 4096], F32) for i in range(2)]
    w2i = sb("l1f_w2i", [128, 4096], I32)
    w2bf = sb("l1f_w2bf", [128, 4096], BF16)
    mc3 = sb("l1f_mc3", [128, 1], F32)

    ev2 = {}
    for t in range(8):
        @block.gpsimd
        def _(g, t=t, prev=ev2.get(t - 2, 0)):
            g.wait_ge(s_v.h, prev)
            s_gd.dma(g, w2ld[t % 2][:], W["l1f_w2"][t * 128:(t + 1) * 128, :])
        w2_ld = s_gd.n

        @block.vector
        def _(v, t=t, ld=w2_ld):
            v.wait_ge(s_gd.h, ld)
            v.tensor_reduce(out=mc3[:], in_=w2ld[t % 2][:], axis=AX.X, op=ALU.max,
                            apply_absolute_value=True)
            v.drain()
            v.tensor_scalar(out=sw2[:, t:t + 1], in0=mc3[:], scalar1=1e-8,
                            scalar2=1.0 / 127.0, op0=ALU.max, op1=ALU.mult)
            v.drain()
            v.reciprocal(out=isw2[:, t:t + 1], in_=sw2[:, t:t + 1])
            v.drain()
            v.tensor_scalar(out=w2i[:], in0=w2ld[t % 2][:], scalar1=isw2[:, t:t + 1],
                            scalar2=None, op0=ALU.mult)
            v.drain()
            v.tensor_scalar(out=w2bf[:], in0=w2i[:], scalar1=sw2[:, t:t + 1],
                            scalar2=None, op0=ALU.mult)
            v.drain().then_inc(s_v.h, 1)
        s_v.n += 1
        q2 = s_v.n

        for r in range(4):
            @block.tensor
            def _(t2, r=r, d=q2, pf=st["l1f_pbf"]):
                t2.wait_ge(s_v.h, d)
                t2.wait_ge(s_v.h, pf)
                for c in range(8):
                    li = t2.transpose(p_bf[:, c * 128:(c + 1) * 128],
                                      w2bf[:, r * 1024 + c * 128: r * 1024 + (c + 1) * 128],
                                      C["ident"])
                li.then_inc(s_t.h, 1)
            s_t.n += 1
            t2d = s_t.n

            @block.vector
            def _(v, t=t, r=r, d=t2d):
                v.wait_ge(s_t.h, d)
                v.tensor_copy(
                    out=wT2[:].rearrange("p (c f) -> p c f", f=1024)[
                        :, r * 8:(r + 1) * 8, t * 128:(t + 1) * 128],
                    in_=p_bf[:].rearrange("p (c f) -> p c f", f=128))
                v.drain().then_inc(s_v.h, 1)
            s_v.n += 1
            st["l1f_pbf"] = s_v.n
        ev2[t] = s_v.n
    w2_done = s_v.n

    b2f = sb("l1f_b2f", [128, 8], F32)
    bq2i = sb("l1f_bq2i", [128, 8], I32)
    bq2f = sb("l1f_bq2f", [128, 8], F32)

    @block.gpsimd
    def _(g):
        s_gd.dma(g, b2f[:], W["l1f_b2_fm"][:])
    b2_ld = s_gd.n

    bqb2 = sb("l1f_bqb2", [128, 8], BF16)
    brt2 = sb("l1f_brt2", [8, 128], BF16)

    @block.vector
    def _(v, ld=b2_ld, w2d=w2_done, ar=ar6_done):
        v.wait_ge(s_gd.h, ld)
        v.wait_ge(s_v.h, max(w2d, ar))
        v.tensor_tensor(out=bq2f[:], in0=b2f[:], in1=isw2[:], op=ALU.mult)
        v.drain()
        v.tensor_scalar(out=bq2i[:], in0=bq2f[:], scalar1=scal[:, IS2f:IS2f + 1],
                        scalar2=None, op0=ALU.mult)
        v.drain()
        v.tensor_tensor(out=bq2f[:], in0=bq2i[:], in1=sw2[:], op=ALU.mult)
        v.drain()
        v.tensor_copy(out=bqb2[:], in_=bq2f[:])
        v.drain().then_inc(s_v.h, 1)
    s_v.n += 1
    bq2_done = s_v.n

    @block.tensor
    def _(t2, d=bq2_done, pf=st["l1f_pbf"]):
        t2.wait_ge(s_v.h, d)
        t2.wait_ge(s_v.h, pf)
        t2.transpose(p_bf[0:8, 0:128], bqb2[:], C["ident"]).then_inc(s_t.h, 1)
    s_t.n += 1
    bq2_t = s_t.n

    @block.vector
    def _(v, d=bq2_t):
        v.wait_ge(s_t.h, d)
        v.tensor_copy(out=brt2[:], in_=p_bf[0:8, 0:128])
        v.drain().then_inc(s_v.h, 1)
    s_v.n += 1
    st["l1f_pbf"] = s_v.n

    @block.gpsimd
    def _(g, d=s_v.n):
        g.wait_ge(s_v.h, d)
        s_gd.dma(g, st["bb_sp"][0:8, :], brt2[:])

    @block.sync
    def _(sy, d=s_gd.n):
        sy.wait_ge(s_gd.h, d)
        s_ld.dma(sy, br2[0:1, :],
                 st["bb_sp"][0:8, :].rearrange("t p -> (t p)").unsqueeze(0))
    br2_ld = s_ld.n

    @block.vector
    def _(v, ld=br2_ld):
        v.wait_ge(s_ld.h, ld)
        v.drain().then_inc(s_v.h, 1)
    s_v.n += 1
    br2_done = s_v.n

    st["arena"].rollback(mark_w2)
    hb = [sb(f"l1f_hb{i}", [128, 4096], BF16) for i in range(2)]
    xq2i = sb("l1f_xq2i", [128, 4096], I32)
    xq2 = sb("l1f_xq2", [128, 4096], BF16)
    x3b = [sb(f"l1f_x3b{i}", [128, E], F32) for i in range(2)]
    yb = [sb(f"l1f_yb{i}", [128, E], F32) for i in range(2)]

    lds2 = {}
    yst = {}
    mm2_tail = 0
    for ci in range(nch):
        @block.sync
        def _(sy, ci=ci, prev=lds2.get(ci - 2, 0)):
            if prev:
                sy.wait_ge(s_v.h, prev)
            r0 = ci * 128
            s_ld.dma(sy, hb[ci % 2][:], st["h_sp"][r0:r0 + 128, :])
            sy.wait_ge(s_ld.h, s_ld.n)
            s_ld.dma(sy, x3b[ci % 2][:], st["xres"][r0:r0 + 128, :])
        hb_ld = s_ld.n

        @block.vector
        def _(v, ci=ci, ld=hb_ld, br=br2_done, mt=mm2_tail):
            v.wait_ge(s_ld.h, ld)
            if ci == 0:
                v.wait_ge(s_v.h, br)
            if mt:
                v.wait_ge(s_t.h, mt)
            v.tensor_scalar(out=xq2i[:], in0=hb[ci % 2][:],
                            scalar1=scal[:, IS2f:IS2f + 1], scalar2=0.0,
                            op0=ALU.mult, op1=ALU.max)
            v.drain()
            v.tensor_copy(out=xq2[:], in_=xq2i[:])
            v.drain().then_inc(s_v.h, 1)
        s_v.n += 1
        lds2[ci] = s_v.n
        xq2_done = s_v.n

        @block.tensor
        def _(t2, d=xq2_done, ys=yst.get(ci - 1, 0)):
            t2.wait_ge(s_v.h, d)
            for b2 in range(2):
                for m in range(32):
                    t2.matmul(pb[b2][:], xq2[:, m * 128:(m + 1) * 128],
                              wT2[:, m * 1024 + b2 * 512: m * 1024 + b2 * 512 + 512],
                              start=(m == 0), stop=False)
                li = t2.matmul(pb[b2][:], C_ONESROW[:],
                               br2[0:1, b2 * 512:(b2 + 1) * 512],
                               start=False, stop=True)
            li.then_inc(s_t.h, 1)
        s_t.n += 1
        mm2_tail = s_t.n

        @block.vector
        def _(v, ci=ci, d=mm2_tail, ys=yst.get(ci - 2, 0)):
            v.wait_ge(s_t.h, d)
            if ys:
                v.wait_ge(s_st.h, ys)
            for b2 in range(2):
                v.scalar_tensor_tensor(out=yb[ci % 2][:, b2 * 512:(b2 + 1) * 512],
                                       in0=pb[b2][:],
                                       scalar=scal[:, S2f:S2f + 1],
                                       in1=x3b[ci % 2][:, b2 * 512:(b2 + 1) * 512],
                                       op0=ALU.mult, op1=ALU.add)
            v.drain().then_inc(s_v.h, 1)
        s_v.n += 1
        y_done = s_v.n

        @block.sync
        def _(sy, ci=ci, d=y_done):
            sy.wait_ge(s_v.h, d)
            s_st.dma(sy, st["y_out"][ci * 128:(ci + 1) * 128, :], yb[ci % 2][:])
        yst[ci] = s_st.n



def emit_finish(st):
    block = st["block"]
    st["phase_barrier"]()
    st["arena"].reset()
    s_ld, s_st = st["sems"]["ld"], st["sems"]["st"]
    sb = st["sb"]
    if DEV_PHASES >= 4:
        @block.sync
        def _(sy):
            sy.wait_ge(s_st.h, s_st.n)
        return
    tmp = [sb(f"fin_tmp{i}", [128, E], F32) for i in range(2)]

    @block.sync
    def _(sy):
        sy.wait_ge(s_st.h, s_st.n)
        for ci in range(_nchunks()):
            r0 = ci * 128
            s_ld.dma(sy, tmp[ci % 2][:], st["xres"][r0:r0 + 128, :])
            sy.wait_ge(s_ld.h, s_ld.n)
            s_st.dma(sy, st["y_out"][r0:r0 + 128, :], tmp[ci % 2][:])
        sy.wait_ge(s_st.h, s_st.n)


WMAP = {
    "l0_wq": "L0_attn_Wq", "l0_wk": "L0_attn_Wk", "l0_wv": "L0_attn_Wv",
    "l0_wo": "L0_attn_Wo", "l0_wf1": "L0_attn_Wf1", "l0_wf2": "L0_attn_Wf2",
    "l0_wg1": "L0_attn_Wg1", "l0_wg2": "L0_attn_Wg2",
    "l0f_wgate": "L0_ffn_w_gate", "l0f_wup": "L0_ffn_w_up", "l0f_wdown": "L0_ffn_w_down",
    "l1_wq": "L1_attn_wq", "l1_wk": "L1_attn_wk", "l1_wv": "L1_attn_wv",
    "l1_wgq": "L1_attn_wgq", "l1_wgk": "L1_attn_wgk", "l1_wgv": "L1_attn_wgv",
    "l1_rout": "L1_attn_r_out_w", "l1_wout": "L1_attn_w_out",
    "l1f_w1": "L1_ffn_w1", "l1f_w2": "L1_ffn_w2",
}


def make_in_maps(inputs):
    x = np.ascontiguousarray(inputs["L0_attn_x"], dtype=np.float32)
    cbv, _ = const_blob_bf16()
    base = {k: np.ascontiguousarray(inputs[v], dtype=np.float32) for k, v in WMAP.items()}
    base["l0_nw_fm"] = np.ascontiguousarray(
        np.asarray(inputs["L0_attn_norm_weight"], dtype=np.float32).reshape(8, 128).T)
    base["l1f_b1_fm"] = np.ascontiguousarray(
        np.asarray(inputs["L1_ffn_b1"], dtype=np.float32).reshape(32, 128).T)
    base["l1f_b2_fm"] = np.ascontiguousarray(
        np.asarray(inputs["L1_ffn_b2"], dtype=np.float32).reshape(8, 128).T)
    glw = np.asarray(inputs["L1_attn_gate_local_w"], dtype=np.float32)
    grw = np.asarray(inputs["L1_attn_gate_remote_w"], dtype=np.float32)
    glw_fm = np.zeros((128, 32), np.float32)
    for h in range(16):
        glw_fm[(h % 2) * 64:(h % 2) * 64 + 64, h] = glw[h]
        glw_fm[(h % 2) * 64:(h % 2) * 64 + 64, 16 + h] = grw[h]
    base["l1_glw_fm"] = np.ascontiguousarray(glw_fm)
    base["constb"] = cbv
    base["constf"] = const_blob_f32()
    in_maps = []
    for core in range(8):
        m = dict(base)
        m["x_in"] = np.ascontiguousarray(
            x[core * B_PER_CORE:(core + 1) * B_PER_CORE].reshape(TC, E))
        in_maps.append(m)
    return in_maps


def kernel(**inputs):
    nc = build_program()
    in_maps = make_in_maps(inputs)
    res = run_bass_kernel_spmd(nc, in_maps, core_ids=list(range(8)))
    out = np.concatenate([r["y_out"].reshape(B_PER_CORE, S, E) for r in res.results], axis=0)
    return out.astype(np.float32)


# version 03 (final)
# speedup vs baseline: 1.0563x; 1.0563x over previous
import math
import numpy as np
import ml_dtypes
import concourse.bass as bass
import concourse.mybir as mybir
from concourse.bass_utils import run_bass_kernel_spmd
from concourse.bass_isa_rust import ReduceOp

F32 = mybir.dt.float32
BF16 = mybir.dt.bfloat16
I32 = mybir.dt.int32
ALU = mybir.AluOpType
AF = mybir.ActivationFunctionType
AX = mybir.AxisListType

B_PER_CORE = 2
S = 2048
E = 1024
H = 16
HD = 64
TC = B_PER_CORE * S
NCHUNK = TC // 128
CPB = S // 128
LN_EPS = 1e-5
GLA_EPS = 1e-5
LOG8 = math.log(8.0)

DEV_PHASES = 4
DEV_NCHUNK = None
DEV_TAP = None
DEV_STAGE = 99
DEV_NO_SMM = False
DEV_SOLO = False
DEV_NO_OINTER = False
DEV_FAKE_S = False
ARENA_COLS = 46300
BS_OFF = ("l0f",)


def _nchunks():
    return DEV_NCHUNK if DEV_NCHUNK is not None else NCHUNK


class Arena:
    def __init__(self, nc, ctx, f32_cols):
        self.t = ctx.enter_context(nc.sbuf_tensor("arena", [128, f32_cols], F32))
        self.cols = f32_cols
        self.off = 0

    def alloc(self, name, shape, dt=F32):
        rows, cols = shape
        exact = cols if dt in (F32, I32) else (cols + 1) // 2
        need = exact + (exact % 2)
        if True:
            import os
            if os.environ.get("ARENA_TRACE"):
                print("ARENA", name, shape, dt, "off", self.off, "need", need)
        assert self.off + need <= self.cols, f"arena overflow at {name}: {self.off}+{need}"
        ap = self.t[0:rows, self.off:self.off + exact]
        self.off += need
        if dt == BF16:
            ap = ap.bitcast(BF16)
        elif dt == I32:
            ap = ap.bitcast(I32)
        if dt == BF16:
            ap = ap[:, 0:cols * 1]
        return ap

    def reset(self):
        self.off = 0

    def mark(self):
        return self.off

    def rollback(self, m):
        self.off = m


class ScalarBatcher:
    def __init__(self):
        self.block = None
        self.items = []
        self.passthrough = False

    def add(self, fn):
        if self.passthrough:
            @self.block.scalar
            def _(s2, fn=fn):
                fn(s2)
            return
        self.items.append(fn)

    def flush(self):
        if not self.items:
            return
        items, self.items = self.items, []

        @self.block.scalar
        def _(s2, items=items):
            for fn in items:
                fn(s2)


class Sem:
    def __init__(self, ctx, nc, name):
        self.h = ctx.enter_context(nc.semaphore(name))
        self.n = 0

    def dma(self, eng, out, in_):
        eng.wait_ge(self.h, self.n)
        r = eng.dma_start(out=out, in_=in_).then_inc(self.h, 16)
        self.n += 16
        return r


def host_constants():
    cb = {}
    cb["ident"] = np.eye(128, dtype=np.float32)
    tri = np.zeros((128, 128), np.float32)
    for blk in range(2):
        for tau in range(64):
            tri[blk * 64 + tau, blk * 64 + tau: blk * 64 + 64] = 1.0
    cb["tric"] = tri
    am = np.zeros((128, 128), np.float32)
    for p in range(128):
        for c in range(128):
            if (p % 64) <= (c % 64):
                am[p, c] = 1.0
    cb["amask"] = am
    ones = np.ones((128, 4), np.float32)
    ones[64:128, 1] = 0.0
    ones[0:64, 2] = 0.0
    cb["ones"] = ones
    wm = np.zeros((256, 512), np.float32)
    for i in range(256):
        wm[i, i + 1: i + 257] = 1.0
    cb["wmask0"] = wm[:128]
    cb["wmask1"] = wm[128:]
    m0 = np.tril(np.ones((256, 256), np.float32))
    cb["bmask0"] = m0[:128]
    cb["bmask1"] = m0[128:]
    lm = np.zeros((128, 16 * 64), np.float32)
    for qc in range(16):
        for p in range(128):
            lm[p, qc * 64: qc * 64 + (qc * 128 + p) // 32 + 1] = 1.0
    cb["lmask"] = lm
    return cb


CONST_ORDER = ["ident", "tric", "amask", "ones", "wmask0", "wmask1",
               "bmask0", "bmask1", "lmask"]

CONSTF_VALS = [-LOG8, 0.5, 32768.5, 256.5, 8.5, 128.5]
CF_ONES = len(CONSTF_VALS)
CF_IDENT = CF_ONES + 128
CF_W = CF_IDENT + 128


def const_blob_f32():
    m = np.zeros((128, CF_W), np.float32)
    m[:, :len(CONSTF_VALS)] = np.asarray(CONSTF_VALS, np.float32)[None, :]
    m[:, CF_ONES:CF_IDENT] = 1.0
    m[:, CF_IDENT:CF_W] = np.eye(128, dtype=np.float32)
    return m


def const_blob_bf16():
    cb = host_constants()
    cols, mats, off = {}, [], 0
    for k in CONST_ORDER:
        m = cb[k]
        cols[k] = (off, m.shape[1])
        mats.append(m.astype(ml_dtypes.bfloat16))
        off += m.shape[1]
    return np.concatenate(mats, axis=1), cols


def build_program():
    import contextlib
    nc = bass.Bass("TRN2", target_bir_lowering=False, debug=False)
    ctx = contextlib.ExitStack()

    x_in = nc.dram_tensor("x_in", [TC, E], F32, kind="ExternalInput").ap()
    y_out = nc.dram_tensor("y_out", [TC, E], F32, kind="ExternalOutput").ap()
    W = {}
    for nm, shp in [
        ("l0_wq", [E, E]), ("l0_wk", [E, E]), ("l0_wv", [E, E]), ("l0_wo", [E, E]),
        ("l0_wf1", [E, 64]), ("l0_wf2", [64, E]), ("l0_wg1", [E, 64]), ("l0_wg2", [64, E]),
        ("l0_nw_fm", [128, 8]),
        ("l0f_wgate", [3072, E]), ("l0f_wup", [3072, E]), ("l0f_wdown", [E, 3072]),
        ("l1_wq", [E, E]), ("l1_wk", [E, E]), ("l1_wv", [E, E]),
        ("l1_wgq", [E, 512]), ("l1_wgk", [E, 512]), ("l1_wgv", [E, 512]),
        ("l1_rout", [512, E]), ("l1_glw_fm", [128, 32]),
        ("l1_wout", [E, E]),
        ("l1f_w1", [4096, E]), ("l1f_b1_fm", [128, 32]), ("l1f_w2", [E, 4096]),
        ("l1f_b2_fm", [128, 8]),
    ]:
        W[nm] = nc.dram_tensor(nm, shp, F32, kind="ExternalInput").ap()
    cbv, ccols = const_blob_bf16()
    cb_h = nc.dram_tensor("constb", [128, cbv.shape[1]], BF16, kind="ExternalInput").ap()
    cf_h = nc.dram_tensor("constf", [128, CF_W], F32, kind="ExternalInput").ap()

    xres = nc.dram_tensor("xres", [TC, E], F32, kind="Internal").ap()
    ln1_sp = nc.dram_tensor("ln1_sp", [TC, E], BF16, kind="Internal").ap()
    up_sp = nc.dram_tensor("up_sp", [TC, 3072], BF16, kind="Internal").ap()
    gate_sp = nc.dram_tensor("gate_sp", [TC, 3072], BF16, kind="Internal").ap()
    t_sp = nc.dram_tensor("t_sp", [TC, 3072], BF16, kind="Internal").ap()
    zf_sp = nc.dram_tensor("zf_sp", [NCHUNK * 128, 3072], BF16, kind="Internal").ap()
    h_sp = nc.dram_tensor("h_sp", [NCHUNK * 128, 4096], BF16, kind="Internal").ap()
    cc_in = [nc.dram_tensor(f"cc_in{i}", [1, 4], F32, kind="Internal").ap() for i in range(8)]
    cc_stage = nc.dram_tensor("cc_stage", [128, 4], F32, kind="Internal").ap()
    bb_sp = nc.dram_tensor("bb_sp", [32, 128], F32, kind="Internal").ap()
    cc_out = [nc.dram_tensor(f"cc_out{i}", [1, 4], F32, kind="Internal").ap() for i in range(8)]

    sb = lambda name, shape, dt=F32: ctx.enter_context(nc.sbuf_tensor(name, shape, dt))
    ps = lambda name, shape, dt=F32: ctx.enter_context(nc.psum_tensor(name, shape, dt))

    consts = sb("consts", [128, cbv.shape[1]], BF16)
    C = {k: consts[:, off:off + w] for k, (off, w) in ccols.items()}
    constf = sb("constf_s", [128, CF_W], F32)
    for i, val in enumerate(CONSTF_VALS):
        nc.const_aps.register(F32, float(val), constf[:, i:i + 1])
    C_ONESROW = constf[0:1, CF_ONES:CF_IDENT]
    C_ONESCOL = constf[:, CF_ONES:CF_ONES + 4]
    C_IDF = constf[:, CF_IDENT:CF_W]

    arena = Arena(nc, ctx, ARENA_COLS)
    pb_bf = ctx.enter_context(nc.psum_tensor("pb_bf", [128, 1024], BF16))
    pb = [ctx.enter_context(nc.psum_tensor(f"pb{i}", [128, 512], F32)) for i in range(7)]

    block = ctx.enter_context(nc.Block())
    sems = {k: Sem(ctx, nc, "sem_" + k) for k in
            ("ld", "gd", "t", "v", "s", "g", "st", "cc")}

    @block.sync
    def _(sy):
        sems["ld"].dma(sy, consts[:], cb_h[:])
        sems["ld"].dma(sy, constf[:], cf_h[:])

    st = dict(nc=nc, block=block, sems=sems, C=C, W=W, sb=arena.alloc, ps=ps, ctx=ctx,
              arena=arena, sb_persist=sb,
              x_in=x_in, y_out=y_out, xres=xres, ln1_sp=ln1_sp, up_sp=up_sp,
              gate_sp=gate_sp, t_sp=t_sp, zf_sp=zf_sp, h_sp=h_sp, bb_sp=bb_sp,
              cc_in=cc_in, cc_out=cc_out, cc_stage=cc_stage, consts_ready=sems["ld"].n,
              C_ONESROW=C_ONESROW, C_ONESCOL=C_ONESCOL, C_IDF=C_IDF,
              pb_bf=pb_bf[:], pb=[p[:] for p in pb])

    st["bs"] = ScalarBatcher()
    st["bs"].block = block

    def phase_barrier():
        st["bs"].flush()
        marks = [(sems[k].h, sems[k].n) for k in ("ld", "gd", "t", "v", "s", "g", "st", "cc")]

        for engname in ("sync", "tensor", "vector", "scalar", "gpsimd"):
            @getattr(block, engname)
            def _(e):
                for h, n in marks:
                    e.wait_ge(h, n)
    st["phase_barrier"] = phase_barrier

    st["taps"] = {}

    def tap(name, ap, sem, val):
        if DEV_TAP is None:
            return
        if name in st["taps"]:
            return
        dt_ = ap.dtype
        t_h = nc.dram_tensor("tap_" + name, list(ap.shape), dt_, kind="ExternalOutput").ap()
        st["taps"][name] = t_h

        @block.gpsimd
        def _(g):
            g.wait_ge(sem.h, val)
            sems["gd"].dma(g, t_h[:], ap)
        done = sems["gd"].n

        @block.vector
        def _(v):
            v.wait_ge(sems["gd"].h, done)
            v.drain().then_inc(sems["v"].h, 1)
        sems["v"].n += 1

        @block.scalar
        def _(sc):
            sc.wait_ge(sems["gd"].h, done)
            sc.drain().then_inc(sems["s"].h, 1)
        sems["s"].n += 1
    st["tap"] = tap

    st["scal"] = sb("scal", [128, 16], F32)
    st["ln1_stats"] = sb("ln1_stats", [128, 2 * NCHUNK], F32)
    st["mxacc"] = sb("mxacc", [128, 12], F32)
    st["scrow"] = sb("scrow", [1, 16], F32)

    with nc.named_scope("l0a"):
        emit_l0a(st)
    if DEV_PHASES >= 2:
        with nc.named_scope("l0f"):
            emit_l0f(st)
    if DEV_PHASES >= 3:
        with nc.named_scope("l1a"):
            emit_l1a(st)
    if DEV_PHASES >= 4:
        with nc.named_scope("l1f"):
            emit_l1f(st)
    emit_finish(st)

    ctx.close()
    return nc


def emit_l0a(st):
    bsc = st["bs"]
    bsc.passthrough = "l0a" in BS_OFF
    nc, block, C, W, sb, ps = (st[k] for k in ("nc", "block", "C", "W", "sb", "ps"))
    s_ld, s_gd, s_t, s_v, s_s, s_g, s_st = (st["sems"][k] for k in
                                            ("ld", "gd", "t", "v", "s", "g", "st"))
    ln1_stats, mxacc = st["ln1_stats"], st["mxacc"]

    wq = sb("l0_wq_s", [128, 8 * E], BF16)
    wk = sb("l0_wk_s", [128, 8 * E], BF16)
    wv = sb("l0_wv_s", [128, 8 * E], BF16)
    wo = sb("l0_wo_s", [128, 8 * E], BF16)
    wf1 = sb("l0_wf1_s", [128, 8 * 64], BF16)
    wg1 = sb("l0_wg1_s", [128, 8 * 64], BF16)
    wf2 = sb("l0_wf2_s", [64, E], BF16)
    wg2 = sb("l0_wg2_s", [64, E], BF16)
    nwf = sb("l0_nwf", [128, 8], F32)

    @block.gpsimd
    def _(g):
        for wsb, hbm in ((wq, W["l0_wq"]), (wk, W["l0_wk"]), (wv, W["l0_wv"]),
                         (wo, W["l0_wo"])):
            s_gd.dma(g, wsb[:].rearrange("p (c e) -> p c e", c=8),
                     hbm.rearrange("(c p) e -> p c e", p=128))
        for wsb, hbm in ((wf1, W["l0_wf1"]), (wg1, W["l0_wg1"])):
            s_gd.dma(g, wsb[:].rearrange("p (c e) -> p c e", c=8),
                     hbm.rearrange("(c p) e -> p c e", p=128))
        for wsb, hbm in ((wf2, W["l0_wf2"]), (wg2, W["l0_wg2"])):
            s_gd.dma(g, wsb[:], hbm[:])
    w_loaded = s_gd.n

    @block.sync
    def _(sy):
        s_ld.dma(sy, nwf[:], W["l0_nw_fm"][:])
    nw_loaded = s_ld.n

    @block.vector
    def _(v):
        v.wait_ge(s_gd.h, w_loaded)
        v.wait_ge(s_ld.h, nw_loaded)
        for c in range(8):
            v.tensor_scalar(out=wo[:, c * E:(c + 1) * E], in0=wo[:, c * E:(c + 1) * E],
                            scalar1=nwf[:, c:c + 1], scalar2=None, op0=ALU.mult)
        v.memset(st["mxacc"][:], 0.0)
        v.drain().then_inc(s_v.h, 1)
    s_v.n += 1
    wo_ready = s_v.n

    p_tr = st["pb_bf"]
    p_qk = p_tr
    p_pr = st["pb"][0:4]
    p_nl = [p_pr[0], p_pr[1]]
    p_o = [p_pr[2], p_pr[3]]
    p_sml = st["pb"][4]
    p_sm = p_sml[:, 0:64]
    p_nc = p_sml[:, 64:80]
    p_at = p_sml[:, 128:256]
    p_S = st["pb"][5]
    p_S2 = st["pb"][6]

    xin = [sb(f"l0a_xin{i}", [128, E], F32) for i in range(2)]
    x1t = [sb(f"l0a_x1t{i}", [128, E], F32) for i in range(2)]
    ln_tm = sb("l0a_lntm", [128, E], BF16)
    ln_fm = sb("l0a_lnfm", [128, E], BF16)
    q_tm = sb("l0a_qtm", [128, E], BF16)
    k_tm = sb("l0a_ktm", [128, E], BF16)
    v_tm = sb("l0a_vtm", [128, E], BF16)
    g_tm = sb("l0a_gtm", [128, E], BF16)
    f1_tm = sb("l0a_f1tm", [128, 64], BF16)
    f1_fm = sb("l0a_f1fm", [64, 128], BF16)
    sp32 = sb("l0a_sp32", [128, E], F32)
    fv32 = sb("l0a_fv32", [128, E], F32)
    sq32 = sb("l0a_sq32", [128, E], F32)
    sp_hi = sb("l0a_sphi", [128, E], BF16)
    sp_lo = sb("l0a_splo", [128, E], BF16)
    e1t = sb("l0a_e1", [128, E], BF16)
    e2t = sb("l0a_e2", [128, E], BF16)
    qt_t = sb("l0a_qt", [128, E], BF16)
    kt_t = sb("l0a_kt", [128, E], BF16)
    qf_pad = [sb(f"l0a_qfp{i}", [128, 1024], BF16) for i in range(2)]
    kf_full = sb("l0a_kff", [128, 1024], BF16)
    kt_p = [sb(f"l0a_ktp{i}", [128, 1024], BF16) for i in range(2)]
    am_g = [sb(f"l0a_am{i}", [128, 8 * 128], BF16) for i in range(2)]
    aC = sb("l0a_aC", [128, 16], F32)
    S_all = sb("l0a_S", [128, 2 * 512], F32)
    S_bf = sb("l0a_Sbf", [128, 2 * 512], BF16)
    S_tmp = sb("l0a_Stmp", [128, 512], F32)
    og = sb("l0a_og", [128, E], F32)
    ms_t = sb("l0a_ms", [128, 16], F32)
    rstd_h = sb("l0a_rstdh", [128, 16], F32)
    ogn = sb("l0a_ogn", [128, E], BF16)
    og_fm = sb("l0a_ogfm", [128, E], BF16)
    lnst = sb("l0a_lnst", [128, 8], F32)
    ln1t = sb("l0a_ln1t", [128, E], BF16)

    @block.vector
    def _(v):
        v.memset(S_all[:], 0.0)
        v.memset(S_bf[:], 0.0)
        v.memset(am_g[0][:], 0.0)
        v.memset(am_g[1][:], 0.0)
        v.memset(qf_pad[0][:], 0.0)
        v.memset(qf_pad[1][:], 0.0)
        v.memset(kt_p[0][:], 0.0)
        v.memset(kt_p[1][:], 0.0)
        v.drain().then_inc(s_v.h, 1)
    s_v.n += 1
    sinit = s_v.n

    chunk_v_done = {}
    chunk_st_done = {}
    prev_supd = 0
    prev_qkevac = 0
    prev_amdone = 0

    def ln_block(get_in, out_bf16, store_stats_ci=None, s_wait_v=None):
        xi = get_in

        @block.vector
        def _(v):
            v.reduce_sum(out=lnst[:, 0:1], in_=xi[:], axis=AX.X)
            v.drain().then_inc(s_v.h, 1)
        s_v.n += 1

        @bsc.add
        def _(s, s_wait_v=s_wait_v):
            if s_wait_v is not None:
                s.wait_ge(s_v.h, s_wait_v)
            s.activation(out=sq32[:], in_=xi[:], func=AF.Square)
            s.drain().then_inc(s_s.h, 1)
        s_s.n += 1
        sqd = s_s.n

        @block.vector
        def _(v, sqd=sqd):
            v.wait_ge(s_s.h, sqd)
            v.reduce_sum(out=lnst[:, 1:2], in_=sq32[:], axis=AX.X)
            v.drain()
            v.tensor_scalar(out=lnst[:, 2:3], in0=lnst[:, 0:1], scalar1=1.0 / E,
                            scalar2=None, op0=ALU.mult)
            v.drain()
            v.tensor_scalar(out=lnst[:, 3:4], in0=lnst[:, 2:3], scalar1=lnst[:, 2:3],
                            scalar2=None, op0=ALU.mult)
            v.drain()
            v.scalar_tensor_tensor(out=lnst[:, 4:5], in0=lnst[:, 1:2], scalar=1.0 / E,
                                   in1=lnst[:, 3:4], op0=ALU.mult, op1=ALU.subtract)
            v.drain()
            v.tensor_scalar(out=lnst[:, 4:5], in0=lnst[:, 4:5], scalar1=LN_EPS,
                            scalar2=None, op0=ALU.add)
            v.drain().then_inc(s_v.h, 1)
        s_v.n += 1
        vard = s_v.n

        @bsc.add
        def _(s, vard=vard):
            s.wait_ge(s_v.h, vard)
            s.activation(out=lnst[:, 5:6], in_=lnst[:, 4:5], func=AF.Ln)
            s.drain()
            s.activation(out=lnst[:, 6:7], in_=lnst[:, 5:6], func=AF.Exp, scale=-0.5)
            s.drain().then_inc(s_s.h, 1)
        s_s.n += 1
        sqrtd = s_s.n

        @block.vector
        def _(v, sqrtd=sqrtd):
            v.wait_ge(s_s.h, sqrtd)
            if store_stats_ci is not None:
                ci2 = store_stats_ci
                v.tensor_copy(out=ln1_stats[:, 2 * ci2:2 * ci2 + 1], in_=lnst[:, 2:3])
                v.tensor_copy(out=ln1_stats[:, 2 * ci2 + 1:2 * ci2 + 2], in_=lnst[:, 6:7])
                v.drain()
            v.tensor_scalar(out=out_bf16[:], in0=xi[:], scalar1=lnst[:, 2:3],
                            scalar2=lnst[:, 6:7], op0=ALU.subtract, op1=ALU.mult)
            v.drain().then_inc(s_v.h, 1)
        s_v.n += 1
        return s_v.n

    for ci in range(_nchunks()):
        b = ci // CPB
        xi = xin[ci % 2]
        x1 = x1t[ci % 2]
        r0 = ci * 128

        @block.sync
        def _(sy, ci=ci, xi=xi, r0=r0):
            if ci >= 2:
                sy.wait_ge(s_v.h, chunk_v_done[ci - 2])
            s_ld.dma(sy, xi[:], st["x_in"][r0:r0 + 128, :])
        ld_x = s_ld.n

        @block.vector
        def _(v, ci=ci, ld_x=ld_x):
            v.wait_ge(s_ld.h, ld_x)
            if ci == 0:
                v.wait_ge(s_v.h, sinit)
                v.wait_ge(s_v.h, wo_ready)
                v.wait_ge(s_ld.h, st["consts_ready"])
            v.engine_nop().then_inc(s_v.h, 1)
        s_v.n += 1

        @bsc.add
        def _(s, ld_x=ld_x):
            s.wait_ge(s_ld.h, ld_x)
            s.nop().then_inc(s_s.h, 1)
        s_s.n += 1

        ln_done = ln_block(xi, ln_tm)

        if DEV_STAGE >= 2:
            @block.tensor
            def _(t, ln_done=ln_done, ci=ci):
                t.wait_ge(s_v.h, ln_done)
                for c in range(8):
                    li = t.transpose(p_tr[:, c * 128:(c + 1) * 128],
                                     ln_tm[:, c * 128:(c + 1) * 128], C["ident"])
                li.then_inc(s_t.h, 1)
            s_t.n += 1
            lnT = s_t.n

            @block.vector
            def _(v, lnT=lnT):
                v.wait_ge(s_t.h, lnT)
                v.tensor_copy(out=ln_fm[:], in_=p_tr[:])
                v.drain().then_inc(s_v.h, 1)
            s_v.n += 1
            lnfm_done = s_v.n

        if DEV_STAGE >= 3:
            def proj_mm(t, wsb, pl, ph, src=None):
                src = ln_fm if src is None else src
                for c in range(8):
                    t.matmul(pl[:], src[:, c * 128:(c + 1) * 128],
                             wsb[:, c * E: c * E + 512], start=(c == 0), stop=(c == 7))
                for c in range(8):
                    li = t.matmul(ph[:], src[:, c * 128:(c + 1) * 128],
                                  wsb[:, c * E + 512: (c + 1) * E], start=(c == 0), stop=(c == 7))
                return li

            @block.tensor
            def _(t, lnfm_done=lnfm_done):
                t.wait_ge(s_v.h, lnfm_done)
                proj_mm(t, wq, p_pr[0], p_pr[1]).then_inc(s_t.h, 1)
            s_t.n += 1
            qmm = s_t.n

            @bsc.add
            def _(s, qmm=qmm):
                s.wait_ge(s_t.h, qmm)
                s.activation(out=q_tm[:, 0:512], in_=p_pr[0][:], func=AF.Silu)
                s.activation(out=q_tm[:, 512:1024], in_=p_pr[1][:], func=AF.Silu)
                s.drain().then_inc(s_s.h, 1)
            s_s.n += 1
            q_done = s_s.n

            @block.tensor
            def _(t, q_done=q_done):
                t.wait_ge(s_s.h, q_done)
                proj_mm(t, wk, p_pr[0], p_pr[1]).then_inc(s_t.h, 1)
            s_t.n += 1
            kmm = s_t.n

            @bsc.add
            def _(s, kmm=kmm):
                s.wait_ge(s_t.h, kmm)
                s.activation(out=k_tm[:, 0:512], in_=p_pr[0][:], func=AF.Silu)
                s.activation(out=k_tm[:, 512:1024], in_=p_pr[1][:], func=AF.Silu)
                s.drain().then_inc(s_s.h, 1)
            s_s.n += 1
            k_done = s_s.n

            @block.tensor
            def _(t, k_done=k_done):
                t.wait_ge(s_s.h, k_done)
                proj_mm(t, wv, p_pr[2], p_pr[3]).then_inc(s_t.h, 1)
            s_t.n += 1
            vmm = s_t.n

            @block.vector
            def _(v, vmm=vmm):
                v.wait_ge(s_t.h, vmm)
                v.tensor_copy(out=v_tm[:, 0:512], in_=p_pr[2][:])
                v.tensor_copy(out=v_tm[:, 512:1024], in_=p_pr[3][:])
                v.drain().then_inc(s_v.h, 1)
            s_v.n += 1
            vevac = s_v.n

        if DEV_STAGE >= 4:
            @block.tensor
            def _(t, vevac=vevac):
                for c in range(8):
                    li = t.matmul(p_sm, ln_fm[:, c * 128:(c + 1) * 128],
                                  wf1[:, c * 64:(c + 1) * 64], start=(c == 0), stop=(c == 7))
                li.then_inc(s_t.h, 1)
            s_t.n += 1
            f1mm = s_t.n

            @block.vector
            def _(v, f1mm=f1mm):
                v.wait_ge(s_t.h, f1mm)
                v.tensor_copy(out=f1_tm[:], in_=p_sm)
                v.drain().then_inc(s_v.h, 1)
            s_v.n += 1
            f1e = s_v.n

            @block.tensor
            def _(t, f1e=f1e):
                t.wait_ge(s_v.h, f1e)
                t.transpose(p_tr[0:64, 0:128], f1_tm[:], C["ident"]).then_inc(s_t.h, 1)
            s_t.n += 1
            f1T = s_t.n

            @block.vector
            def _(v, f1T=f1T):
                v.wait_ge(s_t.h, f1T)
                v.tensor_copy(out=f1_fm[:], in_=p_tr[0:64, 0:128])
                v.drain().then_inc(s_v.h, 1)
            s_v.n += 1
            f1fm = s_v.n

            @block.tensor
            def _(t, f1fm=f1fm, vevac=vevac):
                t.wait_ge(s_v.h, f1fm)
                t.wait_ge(s_v.h, vevac)
                t.matmul(p_pr[0][:], f1_fm[:], wf2[:, 0:512], start=True, stop=True)
                t.matmul(p_pr[1][:], f1_fm[:], wf2[:, 512:1024], start=True,
                         stop=True).then_inc(s_t.h, 1)
            s_t.n += 1
            zmm = s_t.n

            @bsc.add
            def _(s, zmm=zmm):
                s.wait_ge(s_t.h, zmm)
                s.activation(out=fv32[:, 0:512], in_=p_pr[0][:], func=AF.Sigmoid)
                s.activation(out=fv32[:, 512:1024], in_=p_pr[1][:], func=AF.Sigmoid)
                s.drain()
                s.activation(out=sp32[:], in_=fv32[:], func=AF.Ln)
                s.drain().then_inc(s_s.h, 1)
            s_s.n += 1
            spd = s_s.n

            @block.vector
            def _(v, spd=spd):
                v.wait_ge(s_s.h, spd)
                v.tensor_copy(out=sp_hi[:], in_=sp32[:])
                v.drain()
                v.scalar_tensor_tensor(out=sp_lo[:], in0=sp_hi[:], scalar=-1.0,
                                       in1=sp32[:], op0=ALU.mult, op1=ALU.add)
                v.drain().then_inc(s_v.h, 1)
            s_v.n += 1
            sphl = s_v.n

            @block.tensor
            def _(t, sphl=sphl):
                t.wait_ge(s_v.h, sphl)
                for c in range(8):
                    li = t.matmul(p_sm, ln_fm[:, c * 128:(c + 1) * 128],
                                  wg1[:, c * 64:(c + 1) * 64], start=(c == 0), stop=(c == 7))
                li.then_inc(s_t.h, 1)
            s_t.n += 1
            g1mm = s_t.n

            @block.vector
            def _(v, g1mm=g1mm):
                v.wait_ge(s_t.h, g1mm)
                v.tensor_copy(out=f1_tm[:], in_=p_sm)
                v.drain().then_inc(s_v.h, 1)
            s_v.n += 1
            g1e = s_v.n

            @block.tensor
            def _(t, g1e=g1e):
                t.wait_ge(s_v.h, g1e)
                t.transpose(p_tr[0:64, 0:128], f1_tm[:], C["ident"]).then_inc(s_t.h, 1)
            s_t.n += 1
            g1T = s_t.n

            @block.vector
            def _(v, g1T=g1T):
                v.wait_ge(s_t.h, g1T)
                v.tensor_copy(out=f1_fm[:], in_=p_tr[0:64, 0:128])
                v.drain().then_inc(s_v.h, 1)
            s_v.n += 1
            g1fm = s_v.n

            @block.tensor
            def _(t, g1fm=g1fm):
                t.wait_ge(s_v.h, g1fm)
                t.matmul(p_pr[2][:], f1_fm[:], wg2[:, 0:512], start=True, stop=True)
                t.matmul(p_pr[3][:], f1_fm[:], wg2[:, 512:1024], start=True,
                         stop=True).then_inc(s_t.h, 1)
            s_t.n += 1
            g2mm = s_t.n

            @bsc.add
            def _(s, g2mm=g2mm):
                s.wait_ge(s_t.h, g2mm)
                s.activation(out=g_tm[:, 0:512], in_=p_pr[2][:], func=AF.Sigmoid)
                s.activation(out=g_tm[:, 512:1024], in_=p_pr[3][:], func=AF.Sigmoid)
                s.drain().then_inc(s_s.h, 1)
            s_s.n += 1
            g_done = s_s.n

        if DEV_STAGE >= 5:
            @block.tensor
            def _(t, g_done=g_done):
                t.wait_ge(s_s.h, g_done)
                for half in range(2):
                    t.matmul(p_nl[half][:], C["tric"], sp_hi[:, half * 512:(half + 1) * 512],
                             start=True, stop=False)
                    t.matmul(p_nl[half][:], C["tric"], sp_lo[:, half * 512:(half + 1) * 512],
                             start=False, stop=True)
                for gc in range(2):
                    for c in range(8):
                        cc = gc * 8 + c
                        t.matmul(p_nc[:, cc:cc + 1],
                                 sp_hi[:, c * 128:(c + 1) * 128],
                                 C["ones"][:, 1 + gc:2 + gc], start=True, stop=False)
                        li = t.matmul(p_nc[:, cc:cc + 1],
                                      sp_lo[:, c * 128:(c + 1) * 128],
                                      C["ones"][:, 1 + gc:2 + gc],
                                      start=False, stop=True)
                li.then_inc(s_t.h, 1)
            s_t.n += 1
            nld = s_t.n

            @bsc.add
            def _(s, nld=nld):
                s.wait_ge(s_t.h, nld)
                s.activation(out=e1t[:, 0:512], in_=p_nl[0][:], func=AF.Exp, scale=1.0, bias=-LOG8)
                s.activation(out=e1t[:, 512:1024], in_=p_nl[1][:], func=AF.Exp, scale=1.0, bias=-LOG8)
                s.activation(out=e2t[:, 0:512], in_=p_nl[0][:], func=AF.Exp, scale=-1.0)
                s.activation(out=e2t[:, 512:1024], in_=p_nl[1][:], func=AF.Exp, scale=-1.0)
                s.activation(out=aC[:], in_=p_nc, func=AF.Exp, scale=1.0)
                s.drain().then_inc(s_s.h, 1)
            s_s.n += 1
            e_done = s_s.n

            @block.gpsimd
            def _(g, e_done=e_done):
                g.wait_ge(s_s.h, e_done)
                g.tensor_tensor(out=qt_t[:], in0=q_tm[:], in1=e1t[:], op=ALU.mult)
                g.drain()
                g.tensor_tensor(out=kt_t[:], in0=k_tm[:], in1=e2t[:], op=ALU.mult)
                g.drain().then_inc(s_g.h, 1)
            s_g.n += 1
            qkd = s_g.n

            @block.vector
            def _(v, qkd=qkd):
                v.wait_ge(s_g.h, qkd)
                v.tensor_copy(out=kt_p[0][0:64, :], in_=kt_t[0:64, :])
                v.tensor_copy(out=kt_p[1][64:128, :], in_=kt_t[64:128, :])
                v.drain().then_inc(s_v.h, 1)
            s_v.n += 1
            ktp_done = s_v.n

        if DEV_STAGE >= 7:
            @block.tensor
            def _(t, qkd=qkd, pq=prev_qkevac):
                t.wait_ge(s_g.h, qkd)
                t.wait_ge(s_v.h, pq)
                for c in range(8):
                    li = t.transpose(p_qk[:, c * 128:(c + 1) * 128],
                                     qt_t[:, c * 128:(c + 1) * 128], C["ident"])
                li.then_inc(s_t.h, 1)
            s_t.n += 1
            qT = s_t.n

            @block.vector
            def _(v, qT=qT):
                v.wait_ge(s_t.h, qT)
                v.tensor_copy(out=qf_pad[0][0:64, :], in_=p_qk[0:64, :])
                v.tensor_copy(out=qf_pad[1][64:128, :], in_=p_qk[64:128, :])
                v.drain().then_inc(s_v.h, 1)
            s_v.n += 1
            qev = s_v.n

            @block.tensor
            def _(t, qev=qev):
                t.wait_ge(s_v.h, qev)
                for c in range(8):
                    li = t.transpose(p_tr[:, c * 128:(c + 1) * 128],
                                     kt_t[:, c * 128:(c + 1) * 128], C["ident"])
                li.then_inc(s_t.h, 1)
            s_t.n += 1
            kT = s_t.n

            @block.vector
            def _(v, kT=kT):
                v.wait_ge(s_t.h, kT)
                v.tensor_copy(out=kf_full[:], in_=p_tr[:])
                v.drain().then_inc(s_v.h, 1)
            s_v.n += 1
            prev_qkevac = s_v.n
            qkf = s_v.n

            amds = []
            for hp in (range(8) if (DEV_STAGE >= 72 or DEV_STAGE == 99) else []):
                @block.tensor
                def _(t, hp=hp, qkf=qkf, pa=prev_amdone):
                    t.wait_ge(s_v.h, qkf)
                    t.wait_ge(s_v.h, pa)
                    for hh in range(2):
                        for gc in range(2):
                            rr = gc * 64
                            co = hp * 128 + gc * 64
                            li = t.matmul(
                                p_at[rr:rr + 64, hh * 64:(hh + 1) * 64],
                                kf_full[:, co:co + 64],
                                qf_pad[hh][:, co:co + 64],
                                start=True, stop=True)
                    li.then_inc(s_t.h, 1)
                s_t.n += 1
                atm = s_t.n

                @block.vector
                def _(v, atm=atm, hp=hp):
                    v.wait_ge(s_t.h, atm)
                    v.tensor_tensor(out=am_g[0][0:64, hp * 128:(hp + 1) * 128],
                                    in0=p_at[0:64, :],
                                    in1=C["amask"][0:64, :], op=ALU.mult)
                    v.tensor_tensor(out=am_g[1][64:128, hp * 128:(hp + 1) * 128],
                                    in0=p_at[64:128, :],
                                    in1=C["amask"][64:128, :], op=ALU.mult)
                    v.drain().then_inc(s_v.h, 1)
                s_v.n += 1
                amd = s_v.n
                prev_amdone = amd
                amds.append(amd)

                if DEV_STAGE < 73 and DEV_STAGE != 99:
                    continue

                @block.tensor
                def _(t, hp=hp, b=b, amd=amd, psu=prev_supd):
                    t.wait_ge(s_v.h, amd)
                    if hp == 0:
                        t.wait_ge(s_v.h, psu)
                    for hh in range(2):
                        h = 2 * hp + hh
                        col = h * 64
                        po = p_o[col // 512]
                        pc = col % 512
                        t.matmul(po[0:64, pc:pc + 64],
                                 am_g[0][:, hp * 128 + hh * 64: hp * 128 + (hh + 1) * 64],
                                 v_tm[:, col:col + 64], start=True, stop=False)
                        li = t.matmul(po[0:64, pc:pc + 64],
                                      qf_pad[hh][:, hp * 128: hp * 128 + 64],
                                      S_bf[:, b * 512 + hp * 64: b * 512 + hp * 64 + 64],
                                      start=False, stop=True)
                    li.then_inc(s_t.h, 1)
                s_t.n += 1

            for gc in (range(2) if (DEV_STAGE >= 74 or DEV_STAGE == 99) else []):
                if gc == 1 and DEV_STAGE < 75 and DEV_STAGE != 99:
                    break
                rr = gc * 64
                if gc == 1:
                    if DEV_TAP == ci:
                        dbg_sbf = sb(f"dbg_sbf{ci}", [128, 512], BF16)

                        @block.vector
                        def _(v, psu=prev_supd, b=b):
                            v.wait_ge(s_v.h, psu)
                            v.tensor_copy(out=dbg_sbf[:], in_=S_bf[:, b * 512:(b + 1) * 512])
                            v.drain().then_inc(s_v.h, 1)
                        s_v.n += 1
                        st["tap"]("sbf_preB", dbg_sbf[:], s_v, s_v.n)
                    for hp in (range(8) if (DEV_STAGE >= 75 or DEV_STAGE == 99) else []):
                        @block.tensor
                        def _(t, hp=hp, b=b, psu=prev_supd, amd=amds[hp]):
                            t.wait_ge(s_v.h, psu)
                            t.wait_ge(s_v.h, amd)
                            for hh in range(2):
                                h = 2 * hp + hh
                                col = h * 64
                                po = p_o[col // 512]
                                pc = col % 512
                                if DEV_NO_OINTER:
                                    li = t.matmul(po[64:128, pc:pc + 64],
                                                  am_g[1][:, hp * 128 + hh * 64: hp * 128 + (hh + 1) * 64],
                                                  v_tm[:, col:col + 64], start=True, stop=True)
                                    continue
                                t.matmul(po[64:128, pc:pc + 64],
                                         am_g[1][:, hp * 128 + hh * 64: hp * 128 + (hh + 1) * 64],
                                         v_tm[:, col:col + 64], start=True, stop=False)
                                rhs_s = (qf_pad[0][:, 0:64] if DEV_FAKE_S else
                                         S_bf[:, b * 512 + hp * 64: b * 512 + hp * 64 + 64])
                                li = t.matmul(po[64:128, pc:pc + 64],
                                              qf_pad[hh][:, hp * 128 + 64: hp * 128 + 128],
                                              rhs_s,
                                              start=False, stop=True)
                            li.then_inc(s_t.h, 1)
                        s_t.n += 1

                if gc == 1 and DEV_STAGE < 76 and DEV_STAGE != 99:
                    break

                @block.tensor
                def _(t, gc=gc, psu=prev_supd, ktp=ktp_done):
                    if gc == 0:
                        t.wait_ge(s_v.h, psu)
                    t.wait_ge(s_v.h, ktp)
                    for hp in (range(8) if not DEV_NO_SMM else []):
                        for hh in range(2):
                            col = (2 * hp + hh) * 64
                            pdst = p_S if hh == 0 else p_S2
                            li = t.matmul(pdst[:, hp * 64:(hp + 1) * 64],
                                          kt_p[gc][:, (2 * hp) * 64:(2 * hp + 2) * 64],
                                          v_tm[:, col:col + 64],
                                          start=True, stop=True)
                    if DEV_NO_SMM:
                        li = t.nop()
                    li.then_inc(s_t.h, 1)
                s_t.n += 1
                omm = s_t.n

                @block.vector
                def _(v, gc=gc, b=b, omm=omm):
                    v.wait_ge(s_t.h, omm)
                    v.tensor_tensor(out=S_tmp[0:64, :], in0=S_all[0:64, b * 512:(b + 1) * 512],
                                    in1=p_S[0:64, :], op=ALU.add)
                    v.tensor_tensor(out=S_tmp[64:128, :], in0=S_all[64:128, b * 512:(b + 1) * 512],
                                    in1=p_S2[64:128, :], op=ALU.add)
                    v.drain()
                    for hp8 in range(8):
                        v.tensor_scalar(out=S_all[:, b * 512 + hp8 * 64: b * 512 + (hp8 + 1) * 64],
                                        in0=S_tmp[:, hp8 * 64:(hp8 + 1) * 64],
                                        scalar1=aC[:, gc * 8 + hp8: gc * 8 + hp8 + 1],
                                        scalar2=None, op0=ALU.mult)
                    v.drain()
                    v.tensor_copy(out=S_bf[:, b * 512:(b + 1) * 512],
                                  in_=S_all[:, b * 512:(b + 1) * 512])
                    v.drain().then_inc(s_v.h, 1)
                s_v.n += 1
                prev_supd = s_v.n

        if DEV_TAP == ci:
            st["tap"]("ln_tm", ln_tm[:], s_v, lnfm_done)
            st["tap"]("q_tm", q_tm[:], s_s, k_done)
            st["tap"]("k_tm", k_tm[:], s_s, k_done)
            st["tap"]("v_tm", v_tm[:], s_v, vevac)
            st["tap"]("fv32", fv32[:], s_s, spd)
            st["tap"]("sp32", sp32[:], s_s, spd)
            st["tap"]("g_tm", g_tm[:], s_s, g_done)
            st["tap"]("e1t", e1t[:], s_s, e_done)
            st["tap"]("e2t", e2t[:], s_s, e_done)
            st["tap"]("aC", aC[:], s_s, e_done)
            st["tap"]("qt_t", qt_t[:], s_g, qkd)
            st["tap"]("kt_t", kt_t[:], s_g, qkd)
            st["tap"]("S_all", S_all[:], s_v, prev_supd)
            st["tap"]("am_g0", am_g[0][:], s_v, prev_supd)
            st["tap"]("am_g1", am_g[1][:], s_v, prev_supd)
            st["tap"]("qf_p1", qf_pad[1][:], s_v, qkf)
            dbg_po = sb(f"dbg_po{ci}", [128, 1024], F32)

            @block.vector
            def _(v, allmm_d=s_t.n):
                v.wait_ge(s_t.h, allmm_d)
                v.tensor_copy(out=dbg_po[:, 0:512], in_=p_o[0][:])
                v.tensor_copy(out=dbg_po[:, 512:1024], in_=p_o[1][:])
                v.drain().then_inc(s_v.h, 1)
            s_v.n += 1
            st["tap"]("p_o_dump", dbg_po[:], s_v, s_v.n)
            st["tap"]("kt_p0", kt_p[0][:], s_g, qkd)
            st["tap"]("kt_p1", kt_p[1][:], s_g, qkd)
            st["tap"]("S_tmp", S_tmp[:], s_v, prev_supd)
            st["tap"]("qf_p0", qf_pad[0][:], s_v, qkf)
            st["tap"]("kf_f", kf_full[:], s_v, qkf)
            st["tap"]("S_bf", S_bf[:], s_v, prev_supd)

        if DEV_STAGE >= 99:
            allmm = s_t.n

            @block.vector
            def _(v, allmm=allmm):
                v.wait_ge(s_t.h, allmm)
                v.tensor_tensor(out=og[:, 0:512], in0=p_o[0][:], in1=g_tm[:, 0:512], op=ALU.mult)
                v.tensor_tensor(out=og[:, 512:1024], in0=p_o[1][:], in1=g_tm[:, 512:1024], op=ALU.mult)
                v.drain().then_inc(s_v.h, 1)
            s_v.n += 1
            ogd = s_v.n

            @bsc.add
            def _(s, ogd=ogd):
                s.wait_ge(s_v.h, ogd)
                s.activation(out=sq32[:], in_=og[:], func=AF.Square)
                s.drain().then_inc(s_s.h, 1)
            s_s.n += 1
            sqd2 = s_s.n

            @block.vector
            def _(v, sqd2=sqd2):
                v.wait_ge(s_s.h, sqd2)
                v.tensor_reduce(out=ms_t[:], in_=sq32[:].rearrange("p (h d) -> p h d", d=64),
                                axis=AX.X, op=ALU.add)
                v.drain()
                v.tensor_scalar(out=ms_t[:], in0=ms_t[:], scalar1=1.0 / 64, scalar2=GLA_EPS,
                                op0=ALU.mult, op1=ALU.add)
                v.drain().then_inc(s_v.h, 1)
            s_v.n += 1
            msd = s_v.n

            @bsc.add
            def _(s, msd=msd):
                s.wait_ge(s_v.h, msd)
                s.activation(out=rstd_h[:], in_=ms_t[:], func=AF.Ln)
                s.drain()
                s.activation(out=rstd_h[:], in_=rstd_h[:], func=AF.Exp, scale=-0.5)
                s.drain().then_inc(s_s.h, 1)
            s_s.n += 1
            msq = s_s.n

            @block.vector
            def _(v, msq=msq):
                v.wait_ge(s_s.h, msq)
                for h16 in range(16):
                    v.tensor_scalar(out=ogn[:, h16 * 64:(h16 + 1) * 64],
                                    in0=og[:, h16 * 64:(h16 + 1) * 64],
                                    scalar1=rstd_h[:, h16:h16 + 1],
                                    scalar2=None, op0=ALU.mult)
                v.drain().then_inc(s_v.h, 1)
            s_v.n += 1
            ognd = s_v.n

            @block.tensor
            def _(t, ognd=ognd):
                t.wait_ge(s_v.h, ognd)
                for c in range(8):
                    li = t.transpose(p_tr[:, c * 128:(c + 1) * 128],
                                     ogn[:, c * 128:(c + 1) * 128], C["ident"])
                li.then_inc(s_t.h, 1)
            s_t.n += 1
            ognT = s_t.n

            @block.vector
            def _(v, ognT=ognT):
                v.wait_ge(s_t.h, ognT)
                v.tensor_copy(out=og_fm[:], in_=p_tr[:])
                v.drain().then_inc(s_v.h, 1)
            s_v.n += 1
            ogfm = s_v.n

            @block.tensor
            def _(t, ogfm=ogfm):
                t.wait_ge(s_v.h, ogfm)
                proj_mm(t, wo, p_pr[0], p_pr[1], src=og_fm).then_inc(s_t.h, 1)
            s_t.n += 1
            womm = s_t.n

        if DEV_STAGE >= 99:
            @block.vector
            def _(v, ci=ci, xi=xi, x1=x1, womm=womm):
                v.wait_ge(s_t.h, womm)
                if ci >= 2:
                    v.wait_ge(s_st.h, chunk_st_done[ci - 2])
                v.tensor_tensor(out=x1[:, 0:512], in0=xi[:, 0:512], in1=p_pr[0][:], op=ALU.add)
                v.tensor_tensor(out=x1[:, 512:1024], in0=xi[:, 512:1024], in1=p_pr[1][:], op=ALU.add)
                v.drain().then_inc(s_v.h, 1)
            s_v.n += 1
            x1d = s_v.n
        else:
            @block.vector
            def _(v, ci=ci, xi=xi, x1=x1):
                v.wait_ge(s_t.h, s_t.n)
                v.wait_ge(s_s.h, s_s.n)
                v.wait_ge(s_g.h, s_g.n)
                if ci >= 2:
                    v.wait_ge(s_st.h, chunk_st_done[ci - 2])
                v.tensor_copy(out=x1[:], in_=xi[:])
                v.drain().then_inc(s_v.h, 1)
            s_v.n += 1
            x1d = s_v.n

        if DEV_TAP == ci:
            st["tap"]("og", og[:], s_v, ogd)
            st["tap"]("ogn", ogn[:], s_v, ognd)
            st["tap"]("x1", x1[:], s_v, x1d)

        @block.sync
        def _(sy, x1=x1, r0=r0, x1d=x1d):
            sy.wait_ge(s_v.h, x1d)
            s_st.dma(sy, st["xres"][r0:r0 + 128, :], x1[:])

        ln1_done = ln_block(x1, ln1t, store_stats_ci=ci, s_wait_v=x1d)

        @block.vector
        def _(v, ln1_done=ln1_done):
            v.reduce_max(out=lnst[:, 7:8], in_=ln1t[:], axis=AX.X, apply_absolute_value=True)
            v.drain()
            v.tensor_tensor(out=mxacc[:, 0:1], in0=mxacc[:, 0:1], in1=lnst[:, 7:8], op=ALU.max)
            v.drain().then_inc(s_v.h, 1)
        s_v.n += 1
        chunk_v_done[ci] = s_v.n

        @block.sync
        def _(sy, r0=r0):
            sy.wait_ge(s_v.h, chunk_v_done[ci])
            s_st.dma(sy, st["ln1_sp"][r0:r0 + 128, :], ln1t[:])
        chunk_st_done[ci] = s_st.n

        @block.vector
        def _(v, ci=ci):
            v.wait_ge(s_st.h, chunk_st_done[ci])
            v.engine_nop().then_inc(s_v.h, 1)
        s_v.n += 1
        chunk_v_done[ci] = s_v.n
        bsc.flush()



def emit_allreduce(st, idx, cols, compute_row, out_cols):
    nc, block = st["nc"], st["block"]
    s_cc, s_t, s_v = st["sems"]["cc"], st["sems"]["t"], st["sems"]["v"]
    mxacc, scal, scrow = st["mxacc"], st["scal"], st["scrow"]
    ps, C_IDF, C_ONESROW = st["ps"], st["C_IDF"], st["C_ONESROW"]
    c0 = cols[0]
    k = 4
    pbank = st["pb"][0]
    scal_base = st.setdefault("scal_next", 0)
    st["scal_next"] = scal_base + out_cols

    s_g = st["sems"]["g"]
    mxpr = st["mxpr"]

    @block.gpsimd
    def _(g, pre=s_v.n):
        g.wait_ge(s_v.h, pre)
        g.wait_ge(s_cc.h, s_cc.n)
        g.dma_start(out=st["cc_stage"][:], in_=mxacc[:, c0:c0 + k]).then_inc(s_cc.h, 16)
        s_cc.n += 16
        g.wait_ge(s_cc.h, s_cc.n)
        g.dma_start(out=mxpr[0:1, 0:512],
                    in_=st["cc_stage"].rearrange("p c -> (p c)").unsqueeze(0)
                    ).then_inc(s_cc.h, 16)
        s_cc.n += 16

    @block.vector
    def _(v, cv=s_cc.n):
        v.wait_ge(s_cc.h, cv)
        for c in range(k):
            v.reduce_max(out=mxpr[0:1, 512 + c:512 + c + 1],
                         in_=mxpr[0:1, c:512:4].unsqueeze(1) if False else
                         mxpr[0:1, 0:512].rearrange("a (p c) -> a p c", c=4)[:, :, c],
                         axis=AX.X)
        v.drain().then_inc(s_v.h, 1)
    s_v.n += 1

    @block.gpsimd
    def _(g, vv=s_v.n):
        g.wait_ge(s_v.h, vv)
        g.wait_ge(s_cc.h, s_cc.n)
        g.dma_start(out=st["cc_in"][idx][:],
                    in_=mxpr[0:1, 512:512 + k]).then_inc(s_cc.h, 16)
        s_cc.n += 16
        g.wait_ge(s_cc.h, s_cc.n)
        g.collective_compute(
            "AllReduce", ALU.max,
            replica_groups=[[0]] if DEV_SOLO else [[0, 1, 2, 3, 4, 5, 6, 7]],
            ins=[st["cc_in"][idx][:]], outs=[st["cc_out"][idx][:]],
        ).then_inc(s_g.h, 1)
        s_g.n += 1
        g.wait_ge(s_g.h, s_g.n)
        g.dma_start(out=scrow[0:1, 0:4], in_=st["cc_out"][idx][:]).then_inc(s_cc.h, 16)
        s_cc.n += 16

    @block.vector
    def _(v, cv=s_cc.n):
        v.wait_ge(s_cc.h, cv)
        compute_row(v, scrow[0:1, 0:4], scrow[0:1, 8:16])
        v.drain().then_inc(s_v.h, 1)
    s_v.n += 1

    s_s_ = st["sems"]["s"]

    @block.tensor
    def _(t, vv=s_v.n, ss=s_s_.n):
        t.wait_ge(s_v.h, vv)
        t.wait_ge(s_s_.h, ss)
        t.matmul(pbank[0:128, 0:out_cols], C_ONESROW[:],
                 scrow[0:1, 8:8 + out_cols], start=True, stop=True).then_inc(s_t.h, 1)
    s_t.n += 1

    @block.vector
    def _(v, tv=s_t.n):
        v.wait_ge(s_t.h, tv)
        v.tensor_copy(out=scal[:, scal_base:scal_base + out_cols],
                      in_=pbank[0:128, 0:out_cols])
        v.drain().then_inc(s_v.h, 1)
    s_v.n += 1
    return scal_base, s_v.n


def emit_wquant(st, w_hbm, n_rows, kdim, wT, blk=32, qmax=7.0):
    nc, block, sb, C = st["nc"], st["block"], st["sb"], st["C"]
    s_gd, s_t, s_v, s_s = (st["sems"][k] for k in ("gd", "t", "v", "s"))
    p_bf = st["pb_bf"]
    nblk = kdim // blk
    strip_rows = 384 if n_rows % 384 == 0 else 128
    rc_per = strip_rows // 128
    nstrip = n_rows // strip_rows
    if "wq_scr" not in st:
        st["wq_scr"] = dict(
            wstrip=sb("wq_strip", [128, 3072], F32),
            sblk=sb("wq_sblk", [128, 96], F32),
            sinv=sb("wq_sinv", [128, 96], F32),
            tscr=sb("wq_tscr", [128, 3072], F32),
            iscr=sb("wq_iscr", [128, 3072], I32),
        )
        st["wq_scr"]["qscr"] = st["wq_scr"]["iscr"][:, 0:1536].bitcast(BF16)
    sc_ = st["wq_scr"]
    wstrip = [sc_["wstrip"], sc_["wstrip"]]
    sblk = sc_["sblk"][:, 0:nblk]
    sinv = sc_["sinv"][:, 0:nblk]
    tscr = sc_["tscr"][:, 0:kdim]
    iscr = sc_["iscr"][:, 0:kdim]
    qscr = sc_["qscr"][:, 0:kdim]
    strip_done = {}

    for sidx in range(nstrip):
        buf = wstrip[sidx % 2]

        @block.gpsimd
        def _(g, sidx=sidx, buf=buf):
            if sidx >= 1:
                g.wait_ge(s_v.h, strip_done[(sidx - 1) * rc_per + rc_per - 1])
            else:
                g.wait_ge(s_v.h, st.get("wq_last_v", 0))
            s_gd.dma(g, buf[:].rearrange("p (c e) -> p c e", c=rc_per),
                     w_hbm[sidx * strip_rows:(sidx + 1) * strip_rows, :]
                     .rearrange("(c p) e -> p c e", p=128))
        ld = s_gd.n

        for rc in range(rc_per):
            f_row = sidx * rc_per + rc
            wt = buf[:, rc * kdim:(rc + 1) * kdim]

            @block.vector
            def _(v, wt=wt, ld=ld):
                v.wait_ge(s_gd.h, ld)
                v.tensor_reduce(out=sblk[:], in_=wt.rearrange("p (b i) -> p b i", i=blk),
                                axis=AX.X, op=ALU.max, apply_absolute_value=True)
                v.drain()
                v.tensor_scalar(out=sblk[:], in0=sblk[:], scalar1=1.0 / qmax,
                                scalar2=1e-12, op0=ALU.mult, op1=ALU.max)
                v.drain()
                v.reciprocal(out=sinv[:], in_=sblk[:])
                v.drain()
                sb_b = sinv[:].unsqueeze(2).broadcast_to([128, nblk, blk])
                v.tensor_tensor(out=tscr[:].rearrange("p (b i) -> p b i", i=blk),
                                in0=wt.rearrange("p (b i) -> p b i", i=blk),
                                in1=sb_b, op=ALU.mult)
                v.drain()
                v.tensor_copy(out=iscr[:], in_=tscr[:])
                v.drain()
                sb_b2 = sblk[:].unsqueeze(2).broadcast_to([128, nblk, blk])
                v.tensor_tensor(out=qscr[:].rearrange("p (b i) -> p b i", i=blk),
                                in0=iscr[:].rearrange("p (b i) -> p b i", i=blk),
                                in1=sb_b2, op=ALU.mult)
                v.drain().then_inc(s_v.h, 1)
            s_v.n += 1
            qd = s_v.n

            for t8 in range(kdim // 1024):
                @block.tensor
                def _(t, qd=qd, t8=t8):
                    t.wait_ge(s_v.h, qd)
                    if t8 > 0:
                        t.wait_ge(s_v.h, s_v.n)
                    for c in range(8):
                        cc = t8 * 8 + c
                        li = t.transpose(p_bf[:, c * 128:(c + 1) * 128],
                                         qscr[:, cc * 128:(cc + 1) * 128], C["ident"])
                    li.then_inc(s_t.h, 1)
                s_t.n += 1
                td = s_t.n

                @block.vector
                def _(v, td=td, f_row=f_row, t8=t8):
                    v.wait_ge(s_t.h, td)
                    v.tensor_copy(
                        out=wT[:].rearrange("p (c r) -> p c r", r=n_rows)[
                            :, t8 * 8:(t8 + 1) * 8, f_row * 128:(f_row + 1) * 128],
                        in_=p_bf[:].rearrange("p (c r) -> p c r", r=128))
                    v.drain().then_inc(s_v.h, 1)
                s_v.n += 1
            strip_done[f_row] = s_v.n
    st["wq_last_v"] = s_v.n
    return s_v.n


def emit_l0f(st):
    bsc = st["bs"]
    bsc.passthrough = "l0f" in BS_OFF
    nc, block, C, W, sb, ps = (st[k] for k in ("nc", "block", "C", "W", "sb", "ps"))
    s_ld, s_gd, s_t, s_v, s_s, s_g, s_st = (st["sems"][k] for k in
                                            ("ld", "gd", "t", "v", "s", "g", "st"))
    scal, mxacc = st["scal"], st["mxacc"]
    st["phase_barrier"]()
    st["arena"].reset()
    st["mxpr"] = st["mxpr"] if "mxpr" in st else st["sb_persist"]("mxpr", [1, 520], F32)
    pb = st["pb"]
    p_bf = st["pb_bf"]

    IN = 3072

    def row1(v, rrow, wrow):
        v.tensor_scalar(out=wrow[0:1, 0:1], in0=rrow[0:1, 0:1], scalar1=1.0 / 32767.0,
                        scalar2=1e-12, op0=ALU.mult, op1=ALU.max)
        v.drain()
        v.reciprocal(out=wrow[0:1, 1:2], in_=wrow[0:1, 0:1])
        v.drain()
    c_s1, ar1_done = emit_allreduce(st, 0, [0], row1, 2)
    S1, IS1 = c_s1, c_s1 + 1

    colm = sb("l0f_colm", [128, 1], F32)
    mark_a = st["arena"].mark()
    xqf = sb("l0f_xqf", [128, 8 * TC], BF16)
    wT = sb("l0f_wT", [128, 8 * IN], BF16)
    wup_done = emit_wquant(st, W["l0f_wup"], IN, E, wT)

    lnin = [sb(f"l0f_lnin{i}", [128, E], BF16) for i in range(2)]
    i32t = sb("l0f_i32", [128, E], I32)
    xq_tm = sb("l0f_xqtm", [128, E], BF16)
    upo = [sb(f"l0f_upo{i}", [128, IN], BF16) for i in range(2)]

    @block.vector
    def _(v):
        v.memset(mxacc[:, 1:3], 0.0)
        v.drain().then_inc(s_v.h, 1)
    s_v.n += 1

    def ffn_mm_pass(src_sp, dst_sp, wT_ap, mx_col, first_pass, wprep_done, scale_col):
        ch_done = {}
        st_done = {}
        ev_done = dict(st.get("ffn_ev_tail", {}))
        rd_done = dict(st.get("ffn_rd_tail", {}))
        for ci in range(_nchunks()):
            r0 = ci * 128
            lb = lnin[ci % 2]
            ub = upo[ci % 2]

            if first_pass:
                @block.sync
                def _(sy, ci=ci, r0=r0, lb=lb):
                    if ci >= 2:
                        sy.wait_ge(s_v.h, ch_done[ci - 2])
                    s_ld.dma(sy, lb[:], st["ln1_sp"][r0:r0 + 128, :])
                ld = s_ld.n

                @block.vector
                def _(v, ci=ci, lb=lb, ld=ld):
                    v.wait_ge(s_ld.h, ld)
                    if ci == 0:
                        v.wait_ge(s_v.h, ar1_done)
                    v.tensor_scalar(out=i32t[:], in0=lb[:], scalar1=scal[:, IS1:IS1 + 1],
                                    scalar2=None, op0=ALU.mult)
                    v.drain()
                    v.tensor_copy(out=xq_tm[:], in_=i32t[:])
                    v.drain().then_inc(s_v.h, 1)
                s_v.n += 1
                xqd = s_v.n

                @block.tensor
                def _(t, xqd=xqd, ci=ci, wprep_done=wprep_done):
                    t.wait_ge(s_v.h, xqd)
                    if ci == 0:
                        t.wait_ge(s_v.h, wprep_done)
                    for c in range(8):
                        li = t.transpose(p_bf[:, c * 128:(c + 1) * 128],
                                         xq_tm[:, c * 128:(c + 1) * 128], C["ident"])
                    li.then_inc(s_t.h, 1)
                s_t.n += 1
                xqT = s_t.n

                @block.vector
                def _(v, xqT=xqT, ci=ci):
                    v.wait_ge(s_t.h, xqT)
                    v.tensor_copy(
                        out=xqf[:].rearrange("p (c r) -> p c r", r=TC)[:, :, ci * 128:(ci + 1) * 128],
                        in_=p_bf[:].rearrange("p (c r) -> p c r", r=128))
                    v.drain().then_inc(s_v.h, 1)
                s_v.n += 1
                ch_done[ci] = s_v.n
            else:
                @block.vector
                def _(v, ci=ci):
                    if ci == 0:
                        v.wait_ge(s_v.h, wprep_done)
                    v.drain().then_inc(s_v.h, 1)
                s_v.n += 1
                ch_done[ci] = s_v.n

            for fs in range(6):
                pm = pb[fs % 2]
                prev_ev = ev_done.get((ci, fs - 2), ev_done.get((ci - 1, 4 + fs), 0))
                prev_rd = rd_done.get((ci, fs - 2), rd_done.get((ci - 1, 4 + fs), 0))

                @block.tensor
                def _(t, ci=ci, fs=fs, pm=pm, dep=ch_done[ci], prev_ev=prev_ev, prev_rd=prev_rd):
                    t.wait_ge(s_v.h, dep)
                    if prev_ev:
                        t.wait_ge(s_s.h, prev_ev)
                    if prev_rd:
                        t.wait_ge(s_v.h, prev_rd)
                    xc = xqf[:].rearrange("p (c r) -> p c r", r=TC)
                    for c in range(8):
                        li = t.matmul(pm[:], xc[:, c, ci * 128:(ci + 1) * 128],
                                      wT_ap[:].rearrange("p (c r) -> p c r", r=IN)[
                                          :, c, fs * 512:(fs + 1) * 512],
                                      start=(c == 0), stop=(c == 7))
                    li.then_inc(s_t.h, 1)
                s_t.n += 1
                mmd = s_t.n

                @bsc.add
                def _(sc, fs=fs, pm=pm, mmd=mmd, ub=ub, ci=ci):
                    sc.wait_ge(s_t.h, mmd)
                    if fs == 0:
                        sc.wait_ge(s_st.h, st_done.get(ci - 2, st.get("ffn_st_tail", 0)))
                    sc.activation(out=ub[:, fs * 512:(fs + 1) * 512], in_=pm[:],
                                  func=AF.Identity, scale=scal[:, scale_col:scale_col + 1])
                    sc.drain().then_inc(s_s.h, 1)
                s_s.n += 1
                ev_done[(ci, fs)] = s_s.n

                @block.vector
                def _(v, fs=fs, pm=pm, mmd=mmd, mx_col=mx_col):
                    v.wait_ge(s_t.h, mmd)
                    v.reduce_max(out=colm[:], in_=pm[:], axis=AX.X, apply_absolute_value=True)
                    v.drain()
                    v.tensor_tensor(out=mxacc[:, mx_col:mx_col + 1],
                                    in0=mxacc[:, mx_col:mx_col + 1], in1=colm[:], op=ALU.max)
                    v.drain().then_inc(s_v.h, 1)
                s_v.n += 1
                rd_done[(ci, fs)] = s_v.n

            @block.sync
            def _(sy, ci=ci, r0=r0, ub=ub, ev=s_s.n):
                sy.wait_ge(s_s.h, ev)
                if ci >= 2:
                    sy.wait_ge(s_st.h, st_done[ci - 2])
                s_st.dma(sy, dst_sp[r0:r0 + 128, :], ub[:])
            st_done[ci] = s_st.n
            bsc.flush()
        last = _nchunks() - 1
        st["ffn_ev_tail"] = {(-1, 4): ev_done[(last, 4)], (-1, 5): ev_done[(last, 5)]}
        st["ffn_rd_tail"] = {(-1, 4): rd_done[(last, 4)], (-1, 5): rd_done[(last, 5)]}
        st["ffn_st_tail"] = st_done[last]
        return s_v.n

    if DEV_TAP is not None:
        st["tap"]("scal_ar1", st["scal"][:, 0:8], st["sems"]["v"], ar1_done)
    up_done = ffn_mm_pass(None, st["up_sp"], wT, 1, True, wup_done, S1)
    if DEV_TAP is not None:
        st["tap"]("wT_up", wT[:], st["sems"]["v"], wup_done)
        st["tap"]("xqf", xqf[:], st["sems"]["v"], up_done)

        @block.gpsimd
        def _(g, dep=up_done):
            g.wait_ge(st["sems"]["v"].h, dep)
            st["sems"]["gd"].dma(g, st["taps"].setdefault(
                "up_c0", nc.dram_tensor("tap_up_c0", [128, 3072], BF16,
                                        kind="ExternalOutput").ap())[:],
                st["up_sp"][0:128, :])

    wgate_done = emit_wquant(st, W["l0f_wgate"], IN, E, wT)
    gate_done = ffn_mm_pass(None, st["gate_sp"], wT, 2, False, wgate_done, S1)

    def row2(v, rrow, wrow):
        v.tensor_scalar(out=wrow[0:1, 0:2], in0=rrow[0:1, 0:2],
                        scalar1=st["scal"][0:1, S1:S1 + 1], scalar2=1.0 / 32767.0,
                        op0=ALU.mult, op1=ALU.mult)
        v.drain()
        v.tensor_scalar(out=wrow[0:1, 0:2], in0=wrow[0:1, 0:2], scalar1=1e-12,
                        scalar2=None, op0=ALU.max)
        v.drain()
        v.reciprocal(out=wrow[0:1, 2:3], in_=wrow[0:1, 0:1])
        v.drain()
        v.reciprocal(out=wrow[0:1, 3:4], in_=wrow[0:1, 1:2])
        v.drain()
    c2, ar2_done = emit_allreduce(st, 1, [1, 2], row2, 4)
    S2, S3, IS2, IS3 = c2, c2 + 1, c2 + 2, c2 + 3

    if DEV_TAP is not None:
        st["tap"]("scal_ar2", st["scal"][:, 0:8], st["sems"]["v"], ar2_done)
        st["tap"]("mxacc_l0f", st["mxacc"][:], st["sems"]["v"], ar2_done)

    st["phase_barrier"]()
    st["arena"].rollback(mark_a)
    st.pop("wq_scr")
    tvals = [sb(f"l0f_tv{i}", [128, IN], BF16) for i in range(2)]
    gin = [sb(f"l0f_gin{i}", [128, IN], BF16) for i in range(2)]
    i32b = sb("l0f_i32b", [128, IN], I32)
    gint = sb("l0f_gint", [128, IN], F32)
    upq = sb("l0f_upq", [128, IN], F32)
    sig32 = upq
    sig_i = i32b

    @block.vector
    def _(v):
        v.memset(mxacc[:, 3:4], 0.0)
        v.drain().then_inc(s_v.h, 1)
    s_v.n += 1

    p3_done = {}
    p3_st = {}
    for ci in range(_nchunks()):
        r0 = ci * 128
        gb = gin[ci % 2]
        tb = tvals[ci % 2]

        @block.sync
        def _(sy, ci=ci, r0=r0, gb=gb):
            if ci >= 2:
                sy.wait_ge(s_v.h, p3_done[ci - 2])
            s_ld.dma(sy, gb[:], st["gate_sp"][r0:r0 + 128, :])
        ld = s_ld.n

        @bsc.add
        def _(sc, ci=ci, gb=gb, ld=ld, ar2_done=ar2_done):
            sc.wait_ge(s_ld.h, ld)
            if ci == 0:
                sc.wait_ge(s_v.h, ar2_done)
            sc.activation(out=i32b[:], in_=gb[:], func=AF.Identity,
                          scale=scal[:, IS3:IS3 + 1])
            sc.drain().then_inc(s_s.h, 1)
        s_s.n += 1
        gq = s_s.n

        @block.vector
        def _(v, gq=gq):
            v.wait_ge(s_s.h, gq)
            v.tensor_copy(out=gint[:], in_=i32b[:])
            v.drain().then_inc(s_v.h, 1)
        s_v.n += 1
        gi = s_v.n

        @bsc.add
        def _(sc, gi=gi):
            sc.wait_ge(s_v.h, gi)
            sc.activation(out=sig32[:], in_=gint[:], func=AF.Sigmoid,
                          scale=scal[:, S3:S3 + 1])
            sc.drain()
            sc.activation(out=sig_i[:], in_=sig32[:], func=AF.Identity,
                          scale=65536.0)
            sc.drain().then_inc(s_s.h, 1)
        s_s.n += 1
        sgd = s_s.n

        @block.vector
        def _(v, sgd=sgd, tb=tb, ci=ci):
            v.wait_ge(s_s.h, sgd)
            v.scalar_tensor_tensor(out=tb[:], in0=sig_i[:], scalar=65535.0,
                                   in1=gint[:], op0=ALU.min, op1=ALU.mult)
            v.drain()
            v.reduce_max(out=colm[:], in_=tb[:], axis=AX.X, apply_absolute_value=True)
            v.drain()
            v.tensor_tensor(out=mxacc[:, 3:4], in0=mxacc[:, 3:4], in1=colm[:], op=ALU.max)
            v.drain().then_inc(s_v.h, 1)
        s_v.n += 1
        p3_done[ci] = s_v.n

        @block.sync
        def _(sy, ci=ci, r0=r0, tb=tb, dep=p3_done[ci]):
            sy.wait_ge(s_v.h, dep)
            if ci >= 2:
                sy.wait_ge(s_st.h, p3_st[ci - 2])
            s_st.dma(sy, st["t_sp"][r0:r0 + 128, :], tb[:])
        p3_st[ci] = s_st.n
        bsc.flush()

    def row3(v, rrow, wrow):
        v.tensor_scalar(out=wrow[0:1, 0:1], in0=rrow[0:1, 0:1],
                        scalar1=st["scal"][0:1, S3:S3 + 1], scalar2=1.0 / 65536.0,
                        op0=ALU.mult, op1=ALU.mult)
        v.drain()
        v.tensor_scalar(out=wrow[0:1, 0:1], in0=wrow[0:1, 0:1], scalar1=1.0 / 32767.0,
                        scalar2=1e-12, op0=ALU.mult, op1=ALU.max)
        v.drain()
        v.reciprocal(out=wrow[0:1, 1:2], in_=wrow[0:1, 0:1])
        v.drain()
        v.tensor_scalar(out=wrow[0:1, 2:3], in0=st["scal"][0:1, S3:S3 + 1],
                        scalar1=1.0 / 65536.0, scalar2=wrow[0:1, 1:2],
                        op0=ALU.mult, op1=ALU.mult)
        v.drain()
    c3, ar3_done = emit_allreduce(st, 2, [3], row3, 3)
    S4, IS4, Q34 = c3, c3 + 1, c3 + 2

    upin = [sb(f"l0f_upin{i}", [128, IN], BF16) for i in range(2)]
    zt = sb("l0f_zt", [128, IN], BF16)
    zf = sb("l0f_zf", [128, IN], BF16)

    @block.vector
    def _(v):
        v.memset(mxacc[:, 4:5], 0.0)
        v.drain().then_inc(s_v.h, 1)
    s_v.n += 1

    p4_done = {}
    p4_st = {}
    for ci in range(_nchunks()):
        r0 = ci * 128
        tb = tvals[ci % 2]
        ub = upin[ci % 2]

        @block.sync
        def _(sy, ci=ci, r0=r0, tb=tb, ub=ub):
            if ci >= 2:
                sy.wait_ge(s_v.h, p4_done[ci - 2])
            s_ld.dma(sy, tb[:], st["t_sp"][r0:r0 + 128, :])
            s_ld.dma(sy, ub[:], st["up_sp"][r0:r0 + 128, :])
        ld = s_ld.n

        @bsc.add
        def _(sc, ci=ci, tb=tb, ld=ld, ar3_done=ar3_done):
            sc.wait_ge(s_ld.h, ld)
            if ci == 0:
                sc.wait_ge(s_v.h, ar3_done)
            sc.activation(out=i32b[:], in_=tb[:], func=AF.Identity,
                          scale=scal[:, Q34:Q34 + 1])
            sc.drain().then_inc(s_s.h, 1)
        s_s.n += 1
        qd1 = s_s.n

        @block.vector
        def _(v, qd1=qd1):
            v.wait_ge(s_s.h, qd1)
            v.tensor_copy(out=gint[:], in_=i32b[:])
            v.drain().then_inc(s_v.h, 1)
        s_v.n += 1
        gd1 = s_v.n

        @bsc.add
        def _(sc, ub=ub, gd1=gd1):
            sc.wait_ge(s_v.h, gd1)
            sc.activation(out=i32b[:], in_=ub[:], func=AF.Identity,
                          scale=scal[:, IS2:IS2 + 1])
            sc.drain().then_inc(s_s.h, 1)
        s_s.n += 1
        qd = s_s.n

        @block.vector
        def _(v, qd=qd, ci=ci):
            v.wait_ge(s_s.h, qd)
            v.tensor_copy(out=upq[:], in_=i32b[:])
            v.drain()
            v.tensor_tensor(out=zt[:], in0=gint[:], in1=upq[:], op=ALU.mult)
            v.drain()
            v.reduce_max(out=colm[:], in_=zt[:], axis=AX.X, apply_absolute_value=True)
            v.drain()
            v.tensor_tensor(out=mxacc[:, 4:5], in0=mxacc[:, 4:5], in1=colm[:], op=ALU.max)
            v.drain().then_inc(s_v.h, 1)
        s_v.n += 1
        zd = s_v.n

        for tb8 in range(3):
            @block.tensor
            def _(t, tb8=tb8, zd=zd, ci=ci, pe=s_s.n):
                t.wait_ge(s_v.h, zd)
                if tb8 > 0:
                    t.wait_ge(s_s.h, pe)
                for c in range(8):
                    cc = tb8 * 8 + c
                    li = t.transpose(p_bf[:, c * 128:(c + 1) * 128],
                                     zt[:, cc * 128:(cc + 1) * 128], C["ident"])
                li.then_inc(s_t.h, 1)
            s_t.n += 1
            td = s_t.n

            @bsc.add
            def _(sc, tb8=tb8, td=td, ci=ci):
                sc.wait_ge(s_t.h, td)
                if tb8 == 0 and ci >= 1:
                    sc.wait_ge(s_st.h, p4_st[ci - 1])
                sc.activation(out=zf[:, tb8 * 1024:(tb8 + 1) * 1024], in_=p_bf[:],
                              func=AF.Identity)
                sc.drain().then_inc(s_s.h, 1)
            s_s.n += 1
        zfd = s_s.n

        @block.vector
        def _(v, zfd=zfd, ci=ci):
            v.wait_ge(s_s.h, zfd)
            v.drain().then_inc(s_v.h, 1)
        s_v.n += 1
        p4_done[ci] = s_v.n

        @block.sync
        def _(sy, ci=ci, r0=r0, dep=zfd):
            sy.wait_ge(s_s.h, dep)
            if ci >= 2:
                sy.wait_ge(s_st.h, p4_st[ci - 2])
            s_st.dma(sy, st["zf_sp"][r0:r0 + 128, :], zf[:])
        p4_st[ci] = s_st.n
        bsc.flush()

    def row4(v, rrow, wrow):
        v.tensor_scalar(out=wrow[0:1, 3:4], in0=st["scal"][0:1, S4:S4 + 1],
                        scalar1=st["scal"][0:1, S2:S2 + 1], scalar2=None, op0=ALU.mult)
        v.drain()
        v.tensor_scalar(out=wrow[0:1, 0:1], in0=rrow[0:1, 0:1],
                        scalar1=wrow[0:1, 3:4], scalar2=1.0 / 32767.0,
                        op0=ALU.mult, op1=ALU.mult)
        v.drain()
        v.tensor_scalar(out=wrow[0:1, 0:1], in0=wrow[0:1, 0:1], scalar1=1e-12,
                        scalar2=None, op0=ALU.max)
        v.drain()
        v.reciprocal(out=wrow[0:1, 1:2], in_=wrow[0:1, 0:1])
        v.drain()
        v.tensor_scalar(out=wrow[0:1, 2:3], in0=wrow[0:1, 3:4],
                        scalar1=wrow[0:1, 1:2], scalar2=None, op0=ALU.mult)
        v.drain()
    c4, ar4_done = emit_allreduce(st, 3, [4], row4, 3)
    S5, IS5, Q45 = c4, c4 + 1, c4 + 2

    if DEV_TAP is not None:
        st["tap"]("scal_ar4", st["scal"][:, 0:16], st["sems"]["v"], ar4_done)

    wdT = sb("l0f_wdT", [128, 24 * E], BF16)
    wdown_done = emit_wquant(st, W["l0f_wdown"], E, IN, wdT)

    x1in = [gin[0][:, 0:2048].bitcast(F32), gin[1][:, 0:2048].bitcast(F32)]
    x2t = [upq[:, 0:1024], upq[:, 1024:2048]]
    obf = zt[:, 0:IN]

    p5_done = {}
    p5_st = {}
    for ci in range(_nchunks()):
        r0 = ci * 128
        zb = upin[ci % 2]
        xb = x1in[ci % 2]
        x2 = x2t[ci % 2]

        @block.sync
        def _(sy, ci=ci, r0=r0, zb=zb, xb=xb):
            if ci >= 2:
                sy.wait_ge(s_v.h, p5_done[ci - 2])
            s_ld.dma(sy, zb[:], st["zf_sp"][r0:r0 + 128, :])
            s_ld.dma(sy, xb[:], st["xres"][r0:r0 + 128, :])
        ld = s_ld.n

        @bsc.add
        def _(sc, ci=ci, zb=zb, ld=ld, ar4_done=ar4_done, wdown_done=wdown_done):
            sc.wait_ge(s_ld.h, ld)
            if ci == 0:
                sc.wait_ge(s_v.h, ar4_done)
                sc.wait_ge(s_v.h, wdown_done)
            sc.activation(out=i32b[:], in_=zb[:], func=AF.Identity,
                          scale=scal[:, Q45:Q45 + 1])
            sc.drain().then_inc(s_s.h, 1)
        s_s.n += 1
        od = s_s.n

        @block.vector
        def _(v, od=od):
            v.wait_ge(s_s.h, od)
            v.tensor_copy(out=obf[:], in_=i32b[:])
            v.drain().then_inc(s_v.h, 1)
        s_v.n += 1
        ob = s_v.n

        for half in range(2):
            pm = pb[half]

            @block.tensor
            def _(t, half=half, pm=pm, ob=ob, ci=ci):
                t.wait_ge(s_v.h, ob)
                if half == 0 and ci > 0:
                    t.wait_ge(s_v.h, p5_done[ci - 1])
                for kc in range(24):
                    li = t.matmul(pm[:], obf[:, kc * 128:(kc + 1) * 128],
                                  wdT[:].rearrange("p (c r) -> p c r", r=E)[
                                      :, kc, half * 512:(half + 1) * 512],
                                  start=(kc == 0), stop=(kc == 23))
                li.then_inc(s_t.h, 1)
            s_t.n += 1
        dmm = s_t.n

        @block.vector
        def _(v, dmm=dmm, xb=xb, x2=x2):
            v.wait_ge(s_t.h, dmm)
            for half in range(2):
                v.scalar_tensor_tensor(out=x2[:, half * 512:(half + 1) * 512],
                                       in0=pb[half][:], scalar=scal[:, S5:S5 + 1],
                                       in1=xb[:, half * 512:(half + 1) * 512],
                                       op0=ALU.mult, op1=ALU.add)
            v.drain().then_inc(s_v.h, 1)
        s_v.n += 1
        p5_done[ci] = s_v.n

        @block.sync
        def _(sy, ci=ci, r0=r0, x2=x2, dep=p5_done[ci]):
            sy.wait_ge(s_v.h, dep)
            if ci >= 2:
                sy.wait_ge(s_st.h, p5_st[ci - 2])
            s_st.dma(sy, st["xres"][r0:r0 + 128, :], x2[:])
        p5_st[ci] = s_st.n
        bsc.flush()


def emit_l1a(st):
    bsc = st["bs"]
    bsc.passthrough = "l1a" in BS_OFF
    nc, block, C, W, sb, ps = (st[k] for k in ("nc", "block", "C", "W", "sb", "ps"))
    s_ld, s_gd, s_t, s_v, s_s, s_g, s_st = (st["sems"][k] for k in
                                            ("ld", "gd", "t", "v", "s", "g", "st"))
    mxacc = st["mxacc"]
    st["phase_barrier"]()
    st["arena"].reset()
    pb = st["pb"]
    p_bf = st["pb_bf"]

    wq = sb("l1_wq_s", [128, 8 * E], BF16)
    wk = sb("l1_wk_s", [128, 8 * E], BF16)
    wv = sb("l1_wv_s", [128, 8 * E], BF16)
    wout = sb("l1_wout_s", [128, 8 * E], BF16)
    wgq = sb("l1_wgq_s", [128, 8 * 512], BF16)
    wgk = sb("l1_wgk_s", [128, 8 * 512], BF16)
    wgv = sb("l1_wgv_s", [128, 8 * 512], BF16)
    rout = sb("l1_rout_s", [128, 4 * E], BF16)
    glw_sb = sb("l1_glw_s", [128, 32], BF16)

    @block.gpsimd
    def _(g):
        for wsb, hbm in ((wq, W["l1_wq"]), (wk, W["l1_wk"]), (wv, W["l1_wv"]),
                         (wout, W["l1_wout"])):
            s_gd.dma(g, wsb[:].rearrange("p (c e) -> p c e", c=8),
                     hbm.rearrange("(c p) e -> p c e", p=128))
        for wsb, hbm in ((wgq, W["l1_wgq"]), (wgk, W["l1_wgk"]), (wgv, W["l1_wgv"])):
            s_gd.dma(g, wsb[:].rearrange("p (c e) -> p c e", c=8),
                     hbm.rearrange("(c p) e -> p c e", p=128))
        s_gd.dma(g, rout[:].rearrange("p (c e) -> p c e", c=4),
                 W["l1_rout"].rearrange("(c p) e -> p c e", p=128))
        s_gd.dma(g, glw_sb[:], W["l1_glw_fm"][:])
    w_loaded = s_gd.n

    x2in = [sb(f"l1a_x2in{i}", [128, E], F32) for i in range(2)]
    x3t = [sb(f"l1a_x3t{i}", [128, E], F32) for i in range(2)]
    lnst = sb("l1a_lnst", [128, 8], F32)
    sq32 = sb("l1a_sq32", [128, E], F32)
    ln_tm = sb("l1a_lntm", [128, E], BF16)
    ln_fm = sb("l1a_lnfm", [128, E], BF16)
    q_tm = sb("l1a_qtm", [128, E], BF16)
    k_tm = sb("l1a_ktm", [128, E], BF16)
    gq_tm = sb("l1a_gqtm", [128, 512], BF16)
    gkx_tm = sb("l1a_gkxtm", [128, 512], BF16)
    gvx_tm = sb("l1a_gvxtm", [128, 512], BF16)
    q_pad = [sb(f"l1a_qpad{i}", [128, 8 * 256], BF16) for i in range(2)]
    k_ring = sb("l1a_kring", [128, 8 * 512], BF16)
    v_ring = [sb(f"l1a_vring{i}", [128, E], BF16) for i in range(4)]
    gq_pad = [sb(f"l1a_gqpad{i}", [128, 4 * 128], BF16) for i in range(2)]
    gkx_fm = sb("l1a_gkxfm", [128, 4 * 128], BF16)
    gk_fm = sb("l1a_gkfm", [128, 4 * 64], BF16)
    gv_fm = sb("l1a_gvfm", [128, 4 * 64], BF16)
    gv_tm = sb("l1a_gvtm", [128, 4 * 128], BF16)
    plt_pad = sb("l1a_plt", [128, 8 * 128], BF16)
    pl_sb = sb("l1a_plsb", [128, 512], BF16)
    plm_sb = sb("l1a_plmsb", [128, 512], BF16)
    mrow = sb("l1a_mrow", [128, 16], F32)
    rsl = sb("l1a_rsl", [128, 8], F32)
    rsw = sb("l1a_rsw", [128, 16], F32)
    rem512 = sb("l1a_rem512", [128, 512], BF16)
    rem_fm = sb("l1a_remfm", [128, 512], BF16)
    rem2_tm = [sb(f"l1a_rem2{i}", [128, E], BF16) for i in range(2)]
    glr_t = [sb(f"l1a_glr{i}", [128, 32], F32) for i in range(2)]
    p_sb = sb("l1a_psb", [128, 512], BF16)
    pm_sb = sb("l1a_pmsb", [128, 512], BF16)
    pt_sb = sb("l1a_ptsb", [128, 512], BF16)
    locg = [sb(f"l1a_locg{i}", [128, E], BF16) for i in range(2)]
    out_tm = sb("l1a_outtm", [128, E], BF16)
    out_fm = sb("l1a_outfm", [128, E], BF16)
    ln3t = sb("l1a_ln3t", [128, E], BF16)

    @block.vector
    def _(v, wl=w_loaded):
        v.wait_ge(s_gd.h, wl)
        for tl in (gk_fm, gv_fm, gv_tm, plt_pad, q_pad[0], q_pad[1],
                   gq_pad[0], gq_pad[1]):
            v.memset(tl[:], 0.0)
        v.memset(mxacc[:, 5:12], -1e30)
        v.drain().then_inc(s_v.h, 1)
    s_v.n += 1
    init_done = s_v.n

    chunk_done = {}
    ln3_st = {}
    pbf_free = [0]
    pr_free = [0]

    def ln_block1(xi, out_bf16, s_wait_v=None, st_wait=0):
        @block.vector
        def _(v):
            v.reduce_sum(out=lnst[:, 0:1], in_=xi[:], axis=AX.X)
            v.drain().then_inc(s_v.h, 1)
        s_v.n += 1

        @bsc.add
        def _(s2, s_wait_v=s_wait_v):
            if s_wait_v is not None:
                s2.wait_ge(s_v.h, s_wait_v)
            s2.activation(out=sq32[:], in_=xi[:], func=AF.Square)
            s2.drain().then_inc(s_s.h, 1)
        s_s.n += 1
        sqd = s_s.n

        @block.vector
        def _(v, sqd=sqd):
            v.wait_ge(s_s.h, sqd)
            v.reduce_sum(out=lnst[:, 1:2], in_=sq32[:], axis=AX.X)
            v.drain()
            v.tensor_scalar(out=lnst[:, 2:3], in0=lnst[:, 0:1], scalar1=1.0 / E,
                            scalar2=None, op0=ALU.mult)
            v.drain()
            v.tensor_scalar(out=lnst[:, 3:4], in0=lnst[:, 2:3], scalar1=lnst[:, 2:3],
                            scalar2=None, op0=ALU.mult)
            v.drain()
            v.scalar_tensor_tensor(out=lnst[:, 4:5], in0=lnst[:, 1:2], scalar=1.0 / E,
                                   in1=lnst[:, 3:4], op0=ALU.mult, op1=ALU.subtract)
            v.drain()
            v.tensor_scalar(out=lnst[:, 4:5], in0=lnst[:, 4:5], scalar1=LN_EPS,
                            scalar2=None, op0=ALU.add)
            v.drain().then_inc(s_v.h, 1)
        s_v.n += 1
        vard = s_v.n

        @bsc.add
        def _(s2, vard=vard):
            s2.wait_ge(s_v.h, vard)
            s2.activation(out=lnst[:, 5:6], in_=lnst[:, 4:5], func=AF.Ln)
            s2.drain()
            s2.activation(out=lnst[:, 6:7], in_=lnst[:, 5:6], func=AF.Exp, scale=-0.5)
            s2.drain().then_inc(s_s.h, 1)
        s_s.n += 1
        sqrtd = s_s.n

        @block.vector
        def _(v, sqrtd=sqrtd, st_wait=st_wait):
            v.wait_ge(s_s.h, sqrtd)
            if st_wait:
                v.wait_ge(s_st.h, st_wait)
            v.tensor_scalar(out=out_bf16[:], in0=xi[:], scalar1=lnst[:, 2:3],
                            scalar2=lnst[:, 6:7], op0=ALU.subtract, op1=ALU.mult)
            v.drain().then_inc(s_v.h, 1)
        s_v.n += 1
        return s_v.n

    def transpose_n(src_bf16, n, dst_writes, dep_v, s_wait=None):
        @block.tensor
        def _(t, dep=dep_v, pf=pbf_free[0], sw=s_wait):
            t.wait_ge(s_v.h, dep)
            t.wait_ge(s_v.h, pf)
            if sw is not None:
                t.wait_ge(s_s.h, sw)
            for c in range(n):
                li = t.transpose(p_bf[:, c * 128:(c + 1) * 128],
                                 src_bf16[:, c * 128:(c + 1) * 128], C["ident"])
            li.then_inc(s_t.h, 1)
        s_t.n += 1
        td = s_t.n

        @block.vector
        def _(v, td=td):
            v.wait_ge(s_t.h, td)
            dst_writes(v)
            v.drain().then_inc(s_v.h, 1)
        s_v.n += 1
        pbf_free[0] = s_v.n
        return s_v.n

    for b in range(B_PER_CORE):
        for blk in range(8):
            if b * CPB + blk * 2 + 2 > _nchunks():
                continue
            chunk_vals = {}
            for a in range(2):
                qc = blk * 2 + a
                ci = b * CPB + qc
                r0 = ci * 128
                xi = x2in[a]

                @block.sync
                def _(sy, ci=ci, r0=r0, xi=xi):
                    if ci >= 2:
                        sy.wait_ge(s_v.h, chunk_done[ci - 2])
                    s_ld.dma(sy, xi[:], st["xres"][r0:r0 + 128, :])
                ldx = s_ld.n

                @block.vector
                def _(v, ci=ci, ldx=ldx):
                    v.wait_ge(s_ld.h, ldx)
                    if ci == 0:
                        v.wait_ge(s_v.h, init_done)
                    v.drain().then_inc(s_v.h, 1)
                s_v.n += 1

                @bsc.add
                def _(s2, ldx=ldx):
                    s2.wait_ge(s_ld.h, ldx)
                    s2.drain().then_inc(s_s.h, 1)
                s_s.n += 1

                lnd = ln_block1(xi, ln_tm)
                lnfm = transpose_n(ln_tm, 8,
                                   lambda v: v.tensor_copy(out=ln_fm[:], in_=p_bf[:]),
                                   lnd)

                def proj(wsb, width, dst, dep):
                    @block.tensor
                    def _(t, dep=dep, f0=pr_free[0], wsb=wsb, width=width):
                        t.wait_ge(s_v.h, dep)
                        t.wait_ge(s_v.h, f0)
                        half = width // 2
                        for c in range(8):
                            t.matmul(pb[0][:, 0:half], ln_fm[:, c * 128:(c + 1) * 128],
                                     wsb[:, c * width: c * width + half],
                                     start=(c == 0), stop=(c == 7))
                        for c in range(8):
                            li = t.matmul(pb[1][:, 0:half], ln_fm[:, c * 128:(c + 1) * 128],
                                          wsb[:, c * width + half: (c + 1) * width],
                                          start=(c == 0), stop=(c == 7))
                        li.then_inc(s_t.h, 1)
                    s_t.n += 1
                    mmd = s_t.n

                    @block.vector
                    def _(v, mmd=mmd, dst=dst, width=width):
                        v.wait_ge(s_t.h, mmd)
                        half = width // 2
                        v.tensor_copy(out=dst[:, 0:half], in_=pb[0][:, 0:half])
                        v.tensor_copy(out=dst[:, half:width], in_=pb[1][:, 0:half])
                        v.drain().then_inc(s_v.h, 1)
                    s_v.n += 1
                    pr_free[0] = s_v.n
                    return s_v.n

                qd = proj(wq, E, q_tm, lnfm)
                kd = proj(wk, E, k_tm, qd)
                vd = proj(wv, E, v_ring[qc % 4], kd)
                gqd = proj(wgq, 512, gq_tm, vd)
                gkd = proj(wgk, 512, gkx_tm, gqd)
                gvd = proj(wgv, 512, gvx_tm, gkd)

                def wr_qpad(v, a=a):
                    v.tensor_copy(
                        out=q_pad[0][0:64, :].rearrange("p (c r) -> p c r", r=256)[
                            :, :, a * 128:(a + 1) * 128],
                        in_=p_bf[0:64, :].rearrange("p (c r) -> p c r", r=128))
                    v.tensor_copy(
                        out=q_pad[1][64:128, :].rearrange("p (c r) -> p c r", r=256)[
                            :, :, a * 128:(a + 1) * 128],
                        in_=p_bf[64:128, :].rearrange("p (c r) -> p c r", r=128))
                transpose_n(q_tm, 8, wr_qpad, qd)

                def wr_kring(v, blk=blk, a=a):
                    o0 = (blk % 2) * 256 + a * 128
                    v.tensor_copy(
                        out=k_ring[:].rearrange("p (c r) -> p c r", r=512)[
                            :, :, o0:o0 + 128],
                        in_=p_bf[:].rearrange("p (c r) -> p c r", r=128))
                transpose_n(k_tm, 8, wr_kring, kd)

                def wr_gqpad(v):
                    v.tensor_copy(out=gq_pad[0][0:64, :], in_=p_bf[0:64, 0:512])
                    v.tensor_copy(out=gq_pad[1][64:128, :], in_=p_bf[64:128, 0:512])
                gqpd = transpose_n(gq_tm, 4, wr_gqpad, gqd)

                gkfd = transpose_n(gkx_tm, 4,
                                   lambda v: v.tensor_copy(out=gkx_fm[:], in_=p_bf[:, 0:512]),
                                   gkd)

                @block.vector
                def _(v, qc=qc, dep=gkfd):
                    v.wait_ge(s_v.h, dep)
                    v.tensor_reduce(out=mrow[:, 0:16],
                                    in_=gkx_fm[:].rearrange("p (l i) -> p l i", i=32),
                                    axis=AX.X, op=ALU.add)
                    v.drain()
                    v.tensor_scalar(
                        out=gk_fm[:].rearrange("p (c n) -> p c n", n=64)[
                            :, :, qc * 4:(qc + 1) * 4],
                        in0=mrow[:, 0:16].rearrange("p (c g) -> p c g", g=4),
                        scalar1=1.0 / 32, scalar2=None, op0=ALU.mult)
                    v.drain().then_inc(s_v.h, 1)
                s_v.n += 1
                gk_done = s_v.n

                gvfd = transpose_n(gvx_tm, 4,
                                   lambda v: v.tensor_copy(out=gkx_fm[:], in_=p_bf[:, 0:512]),
                                   gvd)

                @block.vector
                def _(v, qc=qc, dep=gvfd):
                    v.wait_ge(s_v.h, dep)
                    v.tensor_reduce(out=mrow[:, 0:16],
                                    in_=gkx_fm[:].rearrange("p (l i) -> p l i", i=32),
                                    axis=AX.X, op=ALU.add)
                    v.drain()
                    v.tensor_scalar(
                        out=gv_fm[:].rearrange("p (c n) -> p c n", n=64)[
                            :, :, qc * 4:(qc + 1) * 4],
                        in0=mrow[:, 0:16].rearrange("p (c g) -> p c g", g=4),
                        scalar1=1.0 / 32, scalar2=None, op0=ALU.mult)
                    v.drain().then_inc(s_v.h, 1)
                s_v.n += 1
                gv_done = s_v.n

                @block.tensor
                def _(t, dep=gv_done, pf=pbf_free[0]):
                    t.wait_ge(s_v.h, dep)
                    t.wait_ge(s_v.h, pf)
                    for c in range(4):
                        li = t.transpose(p_bf[0:64, c * 128:(c + 1) * 128],
                                         gv_fm[:, c * 64:(c + 1) * 64], C["ident"])
                    li.then_inc(s_t.h, 1)
                s_t.n += 1
                gvtd_t = s_t.n

                @block.vector
                def _(v, td=gvtd_t):
                    v.wait_ge(s_t.h, td)
                    v.tensor_copy(out=gv_tm[0:64, :], in_=p_bf[0:64, 0:512])
                    v.drain().then_inc(s_v.h, 1)
                s_v.n += 1
                pbf_free[0] = s_v.n
                gvt_done = s_v.n

                @block.tensor
                def _(t, dep=gk_done, dep2=gqpd, f=pr_free[0]):
                    t.wait_ge(s_v.h, dep)
                    t.wait_ge(s_v.h, dep2)
                    for lh in range(8):
                        li = t.matmul(pb[2][:, lh * 64:(lh + 1) * 64],
                                      gq_pad[lh % 2][:, (lh // 2) * 128:(lh // 2 + 1) * 128],
                                      gk_fm[:, (lh // 2) * 64:(lh // 2) * 64 + 64],
                                      start=True, stop=True)
                    li.then_inc(s_t.h, 1)
                s_t.n += 1
                lscd = s_t.n

                @block.vector
                def _(v, d=lscd):
                    v.wait_ge(s_t.h, d)
                    v.reduce_max(out=mrow[:, 0:1], in_=pb[2][:], axis=AX.X)
                    v.drain()
                    v.tensor_scalar(out=mrow[:, 1:2], in0=mrow[:, 0:1], scalar1=-0.125,
                                    scalar2=None, op0=ALU.mult)
                    v.drain().then_inc(s_v.h, 1)
                s_v.n += 1
                lmx = s_v.n

                @bsc.add
                def _(s2, d=lmx):
                    s2.wait_ge(s_v.h, d)
                    s2.activation(out=pl_sb[:], in_=pb[2][:], func=AF.Exp,
                                  scale=0.125, bias=mrow[:, 1:2])
                    s2.drain().then_inc(s_s.h, 1)
                s_s.n += 1
                lexp = s_s.n

                @block.vector
                def _(v, d=lexp, qc=qc):
                    v.wait_ge(s_s.h, d)
                    lm = C["lmask"][:, qc * 64:(qc + 1) * 64]
                    v.tensor_tensor(out=plm_sb[:].rearrange("p (l n) -> p l n", n=64),
                                    in0=pl_sb[:].rearrange("p (l n) -> p l n", n=64),
                                    in1=lm.unsqueeze(1).broadcast_to([128, 8, 64]),
                                    op=ALU.mult)
                    v.drain()
                    v.tensor_reduce(out=rsl[:], in_=plm_sb[:].rearrange(
                        "p (l n) -> p l n", n=64), axis=AX.X, op=ALU.add)
                    v.drain()
                    v.reciprocal(out=rsl[:], in_=rsl[:])
                    v.drain().then_inc(s_v.h, 1)
                s_v.n += 1
                lsum = s_v.n

                @block.tensor
                def _(t, d=lsum, pf=pbf_free[0]):
                    t.wait_ge(s_v.h, d)
                    t.wait_ge(s_v.h, pf)
                    for lh in range(8):
                        li = t.transpose(p_bf[0:64, lh * 128:(lh + 1) * 128],
                                         plm_sb[:, lh * 64:(lh + 1) * 64], C["ident"])
                    li.then_inc(s_t.h, 1)
                s_t.n += 1
                pltd = s_t.n

                @block.vector
                def _(v, d=pltd):
                    v.wait_ge(s_t.h, d)
                    v.tensor_copy(out=plt_pad[0:64, :], in_=p_bf[0:64, :])
                    v.drain().then_inc(s_v.h, 1)
                s_v.n += 1
                pbf_free[0] = s_v.n
                plte = s_v.n

                @block.tensor
                def _(t, d=plte, d2=gvt_done):
                    t.wait_ge(s_v.h, d)
                    t.wait_ge(s_v.h, d2)
                    for lh in range(8):
                        li = t.matmul(pb[3][:, lh * 64:(lh + 1) * 64],
                                      plt_pad[:, lh * 128:(lh + 1) * 128],
                                      gv_tm[:, (lh // 2) * 128 + (lh % 2) * 64:
                                            (lh // 2) * 128 + (lh % 2) * 64 + 64],
                                      start=True, stop=True)
                    li.then_inc(s_t.h, 1)
                s_t.n += 1
                remd = s_t.n

                @block.vector
                def _(v, d=remd):
                    v.wait_ge(s_t.h, d)
                    for lh in range(8):
                        v.tensor_scalar(out=rem512[:, lh * 64:(lh + 1) * 64],
                                        in0=pb[3][:, lh * 64:(lh + 1) * 64],
                                        scalar1=rsl[:, lh:lh + 1], scalar2=None,
                                        op0=ALU.mult)
                    v.drain().then_inc(s_v.h, 1)
                s_v.n += 1
                rem512d = s_v.n

                remfmd = transpose_n(rem512, 4,
                                     lambda v: v.tensor_copy(out=rem_fm[:],
                                                             in_=p_bf[:, 0:512]),
                                     rem512d)

                @block.tensor
                def _(t, d=remfmd, f=pr_free[0]):
                    t.wait_ge(s_v.h, d)
                    t.wait_ge(s_v.h, f)
                    for half in range(2):
                        for c in range(4):
                            li = t.matmul(pb[half][:],
                                          rem_fm[:, c * 128:(c + 1) * 128],
                                          rout[:, c * E + half * 512: c * E + (half + 1) * 512],
                                          start=(c == 0), stop=(c == 3))
                    li.then_inc(s_t.h, 1)
                s_t.n += 1
                rem2d = s_t.n

                @block.vector
                def _(v, d=rem2d, a=a):
                    v.wait_ge(s_t.h, d)
                    v.tensor_copy(out=rem2_tm[a][:, 0:512], in_=pb[0][:])
                    v.tensor_copy(out=rem2_tm[a][:, 512:1024], in_=pb[1][:])
                    v.drain().then_inc(s_v.h, 1)
                s_v.n += 1
                pr_free[0] = s_v.n

                @block.tensor
                def _(t, d=pr_free[0], dl=lexp, a=a):
                    t.wait_ge(s_s.h, dl)
                    t.wait_ge(s_v.h, lsum)
                    for h in range(16):
                        t.matmul(pb[2][:, h:h + 1],
                                 q_pad[h % 2][:, (h // 2) * 256 + a * 128:
                                              (h // 2) * 256 + (a + 1) * 128],
                                 glw_sb[:, h:h + 1], start=True, stop=True)
                    for h in range(16):
                        li = t.matmul(pb[2][:, 16 + h:17 + h],
                                      q_pad[h % 2][:, (h // 2) * 256 + a * 128:
                                                   (h // 2) * 256 + (a + 1) * 128],
                                      glw_sb[:, 16 + h:17 + h], start=True, stop=True)
                    li.then_inc(s_t.h, 1)
                s_t.n += 1
                gld = s_t.n

                @bsc.add
                def _(s2, d=gld, a=a):
                    s2.wait_ge(s_t.h, d)
                    s2.activation(out=glr_t[a][:], in_=pb[2][:, 0:32], func=AF.Sigmoid)
                    s2.drain().then_inc(s_s.h, 1)
                s_s.n += 1
                chunk_vals[a] = dict(glr=s_s.n)

            nkc = 2 if blk == 0 else 4
            kw = nkc * 128
            for a in range(2):
                loc_started = {0: False, 1: False}
                for h in range(16):
                    @block.tensor
                    def _(t, h=h, a=a, blk=blk, sexp=s_s.n, svx=s_v.n):
                        if h == 0:
                            t.wait_ge(s_s.h, sexp)
                        else:
                            t.wait_ge(s_s.h, wexp_d[h - 1])
                            t.wait_ge(s_v.h, wmax_d[h - 1])
                        if blk == 0:
                            li = t.matmul(pb[4][:, 0:256],
                                          q_pad[h % 2][:, (h // 2) * 256 + a * 128:
                                                       (h // 2) * 256 + (a + 1) * 128],
                                          k_ring[:].rearrange("p (c r) -> p c r", r=512)[
                                              :, h // 2, 0:256],
                                          start=True, stop=True)
                        else:
                            po = ((blk - 1) % 2) * 256
                            co = (blk % 2) * 256
                            t.matmul(pb[4][:, 0:256],
                                     q_pad[h % 2][:, (h // 2) * 256 + a * 128:
                                                  (h // 2) * 256 + (a + 1) * 128],
                                     k_ring[:].rearrange("p (c r) -> p c r", r=512)[
                                         :, h // 2, po:po + 256],
                                     start=True, stop=True)
                            li = t.matmul(pb[4][:, 256:512],
                                          q_pad[h % 2][:, (h // 2) * 256 + a * 128:
                                                       (h // 2) * 256 + (a + 1) * 128],
                                          k_ring[:].rearrange("p (c r) -> p c r", r=512)[
                                              :, h // 2, co:co + 256],
                                          start=True, stop=True)
                        li.then_inc(s_t.h, 1)
                    s_t.n += 1
                    wsc = s_t.n

                    @block.vector
                    def _(v, d=wsc, kw=kw):
                        v.wait_ge(s_t.h, d)
                        v.reduce_max(out=mrow[:, 2:3], in_=pb[4][:, 0:kw], axis=AX.X)
                        v.drain()
                        v.tensor_scalar(out=mrow[:, 3:4], in0=mrow[:, 2:3], scalar1=-0.125,
                                        scalar2=None, op0=ALU.mult)
                        v.drain().then_inc(s_v.h, 1)
                    s_v.n += 1
                    if h == 0:
                        wmax_d = {}
                        wexp_d = {}
                    wmax_d[h] = s_v.n

                    @bsc.add
                    def _(s2, d=wmax_d[h], kw=kw):
                        s2.wait_ge(s_v.h, d)
                        s2.activation(out=p_sb[:, 0:kw], in_=pb[4][:, 0:kw], func=AF.Exp,
                                      scale=0.125, bias=mrow[:, 3:4])
                        s2.drain().then_inc(s_s.h, 1)
                    s_s.n += 1
                    wexp_d[h] = s_s.n

                    @block.vector
                    def _(v, d=wexp_d[h], a=a, h=h, blk=blk, kw=kw):
                        v.wait_ge(s_s.h, d)
                        msk = (C["bmask0"] if a == 0 else C["bmask1"]) if blk == 0 else \
                              (C["wmask0"] if a == 0 else C["wmask1"])
                        v.tensor_tensor(out=pm_sb[:, 0:kw], in0=p_sb[:, 0:kw],
                                        in1=msk[:, 0:kw], op=ALU.mult)
                        v.drain()
                        v.reduce_sum(out=mrow[:, 4:5], in_=pm_sb[:, 0:kw], axis=AX.X)
                        v.drain()
                        v.reciprocal(out=rsw[:, h:h + 1], in_=mrow[:, 4:5])
                        v.drain().then_inc(s_v.h, 1)
                    s_v.n += 1
                    wmsk = s_v.n

                    @block.tensor
                    def _(t, d=wmsk, pf=pbf_free[0], nkc=nkc):
                        t.wait_ge(s_v.h, d)
                        t.wait_ge(s_v.h, pf)
                        for c in range(nkc):
                            li = t.transpose(p_bf[:, c * 128:(c + 1) * 128],
                                             pm_sb[:, c * 128:(c + 1) * 128], C["ident"])
                        li.then_inc(s_t.h, 1)
                    s_t.n += 1
                    ptd = s_t.n

                    @block.vector
                    def _(v, d=ptd, nkc=nkc):
                        v.wait_ge(s_t.h, d)
                        v.tensor_copy(out=pt_sb[:, 0:nkc * 128], in_=p_bf[:, 0:nkc * 128])
                        v.drain().then_inc(s_v.h, 1)
                    s_v.n += 1
                    pbf_free[0] = s_v.n
                    pte = s_v.n

                    @block.tensor
                    def _(t, d=pte, h=h, blk=blk, nkc=nkc):
                        t.wait_ge(s_v.h, d)
                        bank = 5 + h // 8
                        col = (h % 8) * 64
                        first = not loc_started[h // 8]
                        loc_started[h // 8] = True
                        for c in range(nkc):
                            if blk == 0:
                                vr = v_ring[(2 * blk + c) % 4]
                            else:
                                vr = v_ring[(2 * (blk - 1) + c) % 4]
                            li = t.matmul(pb[bank][:, col:col + 64],
                                          pt_sb[:, c * 128:(c + 1) * 128],
                                          vr[:, h * 64:(h + 1) * 64],
                                          start=(c == 0), stop=(c == nkc - 1))
                        li.then_inc(s_t.h, 1)
                    s_t.n += 1
                pv_done = s_t.n

                @block.vector
                def _(v, d=pv_done, a=a, gd=chunk_vals[a]["glr"]):
                    v.wait_ge(s_t.h, d)
                    v.wait_ge(s_s.h, gd)
                    for h in range(16):
                        bank = 5 + h // 8
                        col = (h % 8) * 64
                        v.tensor_scalar(out=locg[a][:, h * 64:(h + 1) * 64],
                                        in0=pb[bank][:, col:col + 64],
                                        scalar1=rsw[:, h:h + 1],
                                        scalar2=glr_t[a][:, h:h + 1],
                                        op0=ALU.mult, op1=ALU.mult)
                    v.drain().then_inc(s_v.h, 1)
                s_v.n += 1
                locg_d = s_v.n

                @block.vector
                def _(v, d=locg_d, a=a):
                    for h in range(16):
                        v.scalar_tensor_tensor(out=out_tm[:, h * 64:(h + 1) * 64],
                                               in0=rem2_tm[a][:, h * 64:(h + 1) * 64],
                                               scalar=glr_t[a][:, 16 + h:17 + h],
                                               in1=locg[a][:, h * 64:(h + 1) * 64],
                                               op0=ALU.mult, op1=ALU.add)
                    v.drain().then_inc(s_v.h, 1)
                s_v.n += 1
                outd = s_v.n

                outfmd = transpose_n(out_tm, 8,
                                     lambda v: v.tensor_copy(out=out_fm[:], in_=p_bf[:]),
                                     outd)

                @block.tensor
                def _(t, d=outfmd, f=pr_free[0]):
                    t.wait_ge(s_v.h, d)
                    t.wait_ge(s_v.h, f)
                    for c in range(8):
                        t.matmul(pb[0][:], out_fm[:, c * 128:(c + 1) * 128],
                                 wout[:, c * E: c * E + 512],
                                 start=(c == 0), stop=(c == 7))
                    for c in range(8):
                        li = t.matmul(pb[1][:], out_fm[:, c * 128:(c + 1) * 128],
                                      wout[:, c * E + 512: (c + 1) * E],
                                      start=(c == 0), stop=(c == 7))
                    li.then_inc(s_t.h, 1)
                s_t.n += 1
                woutd = s_t.n

                ci = b * CPB + blk * 2 + a
                r0 = ci * 128
                x3 = x3t[a]

                @block.vector
                def _(v, d=woutd, a=a, x3=x3, ci=ci):
                    v.wait_ge(s_t.h, d)
                    if ci >= 2:
                        v.wait_ge(s_st.h, ln3_st.get(ci - 2, 0))
                    v.tensor_tensor(out=x3[:, 0:512], in0=x2in[a][:, 0:512],
                                    in1=pb[0][:], op=ALU.add)
                    v.tensor_tensor(out=x3[:, 512:1024], in0=x2in[a][:, 512:1024],
                                    in1=pb[1][:], op=ALU.add)
                    v.drain().then_inc(s_v.h, 1)
                s_v.n += 1
                pr_free[0] = s_v.n
                x3d = s_v.n

                @block.sync
                def _(sy, x3=x3, r0=r0, d=x3d):
                    sy.wait_ge(s_v.h, d)
                    s_st.dma(sy, st["xres"][r0:r0 + 128, :], x3[:])

                ln3d = ln_block1(x3, ln3t, s_wait_v=x3d,
                                 st_wait=ln3_st.get(ci - 1, 0) if ci >= 1 else 0)

                @block.vector
                def _(v, d=ln3d, ci=ci):
                    v.reduce_max(out=lnst[:, 7:8], in_=ln3t[:], axis=AX.X)
                    v.drain()
                    v.tensor_tensor(out=mxacc[:, 5:6], in0=mxacc[:, 5:6],
                                    in1=lnst[:, 7:8], op=ALU.max)
                    v.drain().then_inc(s_v.h, 1)
                s_v.n += 1
                chunk_done[ci] = s_v.n

                @block.sync
                def _(sy, r0=r0, d=chunk_done[ci]):
                    sy.wait_ge(s_v.h, d)
                    s_st.dma(sy, st["ln1_sp"][r0:r0 + 128, :], ln3t[:])
                ln3_st[ci] = s_st.n
            bsc.flush()



def emit_l1f(st):
    nc, block, C, W, sb, ps = (st[k] for k in ("nc", "block", "C", "W", "sb", "ps"))
    s_ld, s_gd, s_t, s_v, s_s, s_g, s_st = (st["sems"][k] for k in
                                            ("ld", "gd", "t", "v", "s", "g", "st"))
    mxacc, scal = st["mxacc"], st["scal"]
    st["phase_barrier"]()
    st["arena"].reset()
    pb = st["pb"]
    p_bf = st["pb_bf"]
    nch = _nchunks()
    C_ONESROW = st["C_ONESROW"]

    def row5(v, rrow, wrow):
        v.tensor_scalar(out=wrow[0:1, 0:1], in0=rrow[0:1, 0:1], scalar1=1e-8,
                        scalar2=1.0 / 255.0, op0=ALU.max, op1=ALU.mult)
        v.drain()
        v.reciprocal(out=wrow[0:1, 1:2], in_=wrow[0:1, 0:1])
        v.drain()
    c5, ar5_done = emit_allreduce(st, 4, [5], row5, 2)
    S1f, IS1f = c5, c5 + 1

    mark = st["arena"].mark()
    wT1 = sb("l1f_wT1", [128, 8 * 4096], BF16)
    sw1 = sb("l1f_sw1", [128, 32], F32)
    isw1 = sb("l1f_isw1", [128, 32], F32)
    s1sw = sb("l1f_s1sw", [128, 32], F32)
    br1 = sb("l1f_br1", [1, 4096], F32)
    mark_wq = st["arena"].mark()
    wld = [sb(f"l1f_wld{i}", [128, 1024], F32) for i in range(2)]
    wi32 = sb("l1f_wi32", [128, 1024], I32)
    wbf = sb("l1f_wbf", [128, 1024], BF16)
    mcol = sb("l1f_mcol", [128, 1], F32)

    ld_done = {}
    ev_done = {}
    for t in range(32):
        @block.gpsimd
        def _(g, t=t, prev=ev_done.get(t - 2, 0)):
            g.wait_ge(s_v.h, prev)
            s_gd.dma(g, wld[t % 2][:], W["l1f_w1"][t * 128:(t + 1) * 128, :])
        ld_done[t] = s_gd.n

        @block.vector
        def _(v, t=t, ld=ld_done[t]):
            v.wait_ge(s_gd.h, ld)
            v.tensor_reduce(out=mcol[:], in_=wld[t % 2][:], axis=AX.X, op=ALU.max,
                            apply_absolute_value=True)
            v.drain()
            v.tensor_scalar(out=sw1[:, t:t + 1], in0=mcol[:], scalar1=1e-8,
                            scalar2=1.0 / 127.0, op0=ALU.max, op1=ALU.mult)
            v.drain()
            v.reciprocal(out=isw1[:, t:t + 1], in_=sw1[:, t:t + 1])
            v.drain()
            v.tensor_scalar(out=wi32[:], in0=wld[t % 2][:], scalar1=isw1[:, t:t + 1],
                            scalar2=None, op0=ALU.mult)
            v.drain()
            v.tensor_copy(out=wbf[:], in_=wi32[:])
            v.drain().then_inc(s_v.h, 1)
        s_v.n += 1
        qd = s_v.n

        @block.tensor
        def _(t2, qd=qd, pf=st.get("l1f_pbf", 0)):
            t2.wait_ge(s_v.h, qd)
            t2.wait_ge(s_v.h, pf)
            for c in range(8):
                li = t2.transpose(p_bf[:, c * 128:(c + 1) * 128],
                                  wbf[:, c * 128:(c + 1) * 128], C["ident"])
            li.then_inc(s_t.h, 1)
        s_t.n += 1
        td = s_t.n

        @block.vector
        def _(v, t=t, td=td):
            v.wait_ge(s_t.h, td)
            v.tensor_copy(
                out=wT1[:].rearrange("p (c f) -> p c f", f=4096)[:, :, t * 128:(t + 1) * 128],
                in_=p_bf[:].rearrange("p (c f) -> p c f", f=128))
            v.drain().then_inc(s_v.h, 1)
        s_v.n += 1
        st["l1f_pbf"] = s_v.n
        ev_done[t] = s_v.n
    w1_done = s_v.n

    b1f = sb("l1f_b1f", [128, 32], F32)
    bq1i = sb("l1f_bq1i", [128, 32], I32)
    bq1f = sb("l1f_bq1f", [128, 32], F32)

    @block.gpsimd
    def _(g):
        s_gd.dma(g, b1f[:], W["l1f_b1_fm"][:])
    b1_ld = s_gd.n

    bqb1 = sb("l1f_bqb1", [128, 32], BF16)
    brt1 = sb("l1f_brt1", [32, 128], BF16)

    @block.vector
    def _(v, ld=b1_ld, w1d=w1_done, ar=ar5_done):
        v.wait_ge(s_gd.h, ld)
        v.wait_ge(s_v.h, max(w1d, ar))
        v.tensor_tensor(out=bq1f[:], in0=b1f[:], in1=isw1[:], op=ALU.mult)
        v.drain()
        v.tensor_scalar(out=bq1i[:], in0=bq1f[:], scalar1=scal[:, IS1f:IS1f + 1],
                        scalar2=None, op0=ALU.mult)
        v.drain()
        v.tensor_copy(out=bqb1[:], in_=bq1i[:])
        v.drain()
        v.tensor_scalar(out=s1sw[:], in0=sw1[:], scalar1=scal[:, S1f:S1f + 1],
                        scalar2=None, op0=ALU.mult)
        v.drain().then_inc(s_v.h, 1)
    s_v.n += 1
    bq1_done = s_v.n

    @block.tensor
    def _(t2, d=bq1_done, pf=st["l1f_pbf"]):
        t2.wait_ge(s_v.h, d)
        t2.wait_ge(s_v.h, pf)
        t2.transpose(p_bf[0:32, 0:128], bqb1[:], C["ident"]).then_inc(s_t.h, 1)
    s_t.n += 1
    bq1_t = s_t.n

    @block.vector
    def _(v, d=bq1_t):
        v.wait_ge(s_t.h, d)
        v.tensor_copy(out=brt1[:], in_=p_bf[0:32, 0:128])
        v.drain().then_inc(s_v.h, 1)
    s_v.n += 1
    st["l1f_pbf"] = s_v.n

    @block.gpsimd
    def _(g, d=s_v.n):
        g.wait_ge(s_v.h, d)
        s_gd.dma(g, st["bb_sp"][:], brt1[:])

    @block.sync
    def _(sy, d=s_gd.n):
        sy.wait_ge(s_gd.h, d)
        s_ld.dma(sy, br1[0:1, :],
                 st["bb_sp"].rearrange("t p -> (t p)").unsqueeze(0))
    br1_ld = s_ld.n

    @block.vector
    def _(v, ld=br1_ld):
        v.wait_ge(s_ld.h, ld)
        v.drain().then_inc(s_v.h, 1)
    s_v.n += 1
    br1_done = s_v.n

    st["arena"].rollback(mark_wq)
    lb = [sb(f"l1f_lb{i}", [128, E], BF16) for i in range(2)]
    xqi = sb("l1f_xqi", [128, E], I32)
    xqt = sb("l1f_xqt", [128, E], BF16)
    xqf = sb("l1f_xqf", [128, E], BF16)
    hraw = sb("l1f_hraw", [128, 2048], BF16)
    hT = sb("l1f_hT", [128, 4096], BF16)
    hf = [sb(f"l1f_hf{i}", [128, 4096], BF16) for i in range(2)]
    mc2 = sb("l1f_mc2", [128, 1], F32)

    lds = {}
    hst = {}
    mm_done = 0
    ev_tail = 0
    for ci in range(nch):
        @block.sync
        def _(sy, ci=ci, prev=lds.get(ci - 2, 0)):
            if prev:
                sy.wait_ge(s_v.h, prev)
            r0 = ci * 128
            s_ld.dma(sy, lb[ci % 2][:], st["ln1_sp"][r0:r0 + 128, :])
        lb_ld = s_ld.n

        @block.vector
        def _(v, ci=ci, ld=lb_ld, br=br1_done):
            v.wait_ge(s_ld.h, ld)
            if ci == 0:
                v.wait_ge(s_v.h, br)
            v.tensor_scalar(out=xqi[:], in0=lb[ci % 2][:],
                            scalar1=scal[:, IS1f:IS1f + 1], scalar2=0.0,
                            op0=ALU.mult, op1=ALU.max)
            v.drain()
            v.tensor_copy(out=xqt[:], in_=xqi[:])
            v.drain().then_inc(s_v.h, 1)
        s_v.n += 1
        lds[ci] = s_v.n
        xq_done = s_v.n

        @block.tensor
        def _(t2, d=xq_done, pf=st["l1f_pbf"]):
            t2.wait_ge(s_v.h, d)
            t2.wait_ge(s_v.h, pf)
            for c in range(8):
                li = t2.transpose(p_bf[:, c * 128:(c + 1) * 128],
                                  xqt[:, c * 128:(c + 1) * 128], C["ident"])
            li.then_inc(s_t.h, 1)
        s_t.n += 1
        tq = s_t.n

        @block.vector
        def _(v, d=tq):
            v.wait_ge(s_t.h, d)
            v.tensor_copy(out=xqf[:], in_=p_bf[:])
            v.drain().then_inc(s_v.h, 1)
        s_v.n += 1
        st["l1f_pbf"] = s_v.n
        xqf_done = s_v.n

        for half in range(2):
            @block.tensor
            def _(t2, half=half, d=xqf_done, evt=ev_tail):
                t2.wait_ge(s_v.h, d)
                t2.wait_ge(s_v.h, evt)
                for b in range(4):
                    fb = (half * 4 + b) * 512
                    for c in range(8):
                        t2.matmul(pb[b][:], xqf[:, c * 128:(c + 1) * 128],
                                  wT1[:, c * 4096 + fb: c * 4096 + fb + 512],
                                  start=(c == 0), stop=False)
                for b in range(4):
                    fb = (half * 4 + b) * 512
                    li = t2.matmul(pb[b][:], C_ONESROW[:], br1[0:1, fb:fb + 512],
                                   start=False, stop=True)
                li.then_inc(s_t.h, 1)
            s_t.n += 1
            hm = s_t.n

            @block.vector
            def _(v, d=hm):
                v.wait_ge(s_t.h, d)
                for b in range(4):
                    v.tensor_copy(out=hraw[:, b * 512:(b + 1) * 512], in_=pb[b][:])
                v.drain().then_inc(s_v.h, 1)
            s_v.n += 1
            ev_tail = s_v.n
            hr_done = s_v.n

            for r in range(2):
                @block.tensor
                def _(t2, r=r, d=hr_done, pf=st["l1f_pbf"]):
                    t2.wait_ge(s_v.h, d)
                    t2.wait_ge(s_v.h, pf)
                    for c in range(8):
                        li = t2.transpose(p_bf[:, c * 128:(c + 1) * 128],
                                          hraw[:, r * 1024 + c * 128: r * 1024 + (c + 1) * 128],
                                          C["ident"])
                    li.then_inc(s_t.h, 1)
                s_t.n += 1
                tr = s_t.n

                @block.vector
                def _(v, half=half, r=r, d=tr):
                    v.wait_ge(s_t.h, d)
                    u0 = (half * 16 + r * 8) * 128
                    v.tensor_copy(out=hT[:, u0:u0 + 1024], in_=p_bf[:])
                    v.drain().then_inc(s_v.h, 1)
                s_v.n += 1
                st["l1f_pbf"] = s_v.n

        @block.vector
        def _(v, ci=ci, hs=hst.get(ci - 2, 0)):
            if hs:
                v.wait_ge(s_st.h, hs)
            v.tensor_tensor(out=hf[ci % 2][:].rearrange("p (m f) -> p m f", f=128),
                            in0=hT[:].rearrange("p (m f) -> p m f", f=128),
                            in1=s1sw[:].unsqueeze(2).broadcast_to([128, 32, 128]),
                            op=ALU.mult)
            v.drain()
            v.reduce_max(out=mc2[:], in_=hf[ci % 2][:], axis=AX.X)
            v.drain()
            v.tensor_tensor(out=mxacc[:, 6:7], in0=mxacc[:, 6:7], in1=mc2[:],
                            op=ALU.max)
            v.drain().then_inc(s_v.h, 1)
        s_v.n += 1
        hf_done = s_v.n

        @block.sync
        def _(sy, ci=ci, d=hf_done):
            sy.wait_ge(s_v.h, d)
            s_st.dma(sy, st["h_sp"][ci * 128:(ci + 1) * 128, :], hf[ci % 2][:])
        hst[ci] = s_st.n

    def row6(v, rrow, wrow):
        v.tensor_scalar(out=wrow[0:1, 0:1], in0=rrow[0:1, 0:1], scalar1=1e-8,
                        scalar2=1.0 / 255.0, op0=ALU.max, op1=ALU.mult)
        v.drain()
        v.reciprocal(out=wrow[0:1, 1:2], in_=wrow[0:1, 0:1])
        v.drain()
    c6, ar6_done = emit_allreduce(st, 5, [6], row6, 2)
    S2f, IS2f = c6, c6 + 1

    st["phase_barrier"]()
    st["arena"].rollback(mark)
    wT2 = sb("l1f_wT2", [128, 32 * 1024], BF16)
    sw2 = sb("l1f_sw2", [128, 8], F32)
    isw2 = sb("l1f_isw2", [128, 8], F32)
    br2 = sb("l1f_br2", [1, 1024], F32)
    mark_w2 = st["arena"].mark()
    w2ld = [sb(f"l1f_w2ld{i}", [128, 4096], F32) for i in range(2)]
    w2i = sb("l1f_w2i", [128, 4096], I32)
    w2bf = sb("l1f_w2bf", [128, 4096], BF16)
    mc3 = sb("l1f_mc3", [128, 1], F32)

    ev2 = {}
    for t in range(8):
        @block.gpsimd
        def _(g, t=t, prev=ev2.get(t - 2, 0)):
            g.wait_ge(s_v.h, prev)
            s_gd.dma(g, w2ld[t % 2][:], W["l1f_w2"][t * 128:(t + 1) * 128, :])
        w2_ld = s_gd.n

        @block.vector
        def _(v, t=t, ld=w2_ld):
            v.wait_ge(s_gd.h, ld)
            v.tensor_reduce(out=mc3[:], in_=w2ld[t % 2][:], axis=AX.X, op=ALU.max,
                            apply_absolute_value=True)
            v.drain()
            v.tensor_scalar(out=sw2[:, t:t + 1], in0=mc3[:], scalar1=1e-8,
                            scalar2=1.0 / 127.0, op0=ALU.max, op1=ALU.mult)
            v.drain()
            v.reciprocal(out=isw2[:, t:t + 1], in_=sw2[:, t:t + 1])
            v.drain()
            v.tensor_scalar(out=w2i[:], in0=w2ld[t % 2][:], scalar1=isw2[:, t:t + 1],
                            scalar2=None, op0=ALU.mult)
            v.drain()
            v.tensor_scalar(out=w2bf[:], in0=w2i[:], scalar1=sw2[:, t:t + 1],
                            scalar2=None, op0=ALU.mult)
            v.drain().then_inc(s_v.h, 1)
        s_v.n += 1
        q2 = s_v.n

        for r in range(4):
            @block.tensor
            def _(t2, r=r, d=q2, pf=st["l1f_pbf"]):
                t2.wait_ge(s_v.h, d)
                t2.wait_ge(s_v.h, pf)
                for c in range(8):
                    li = t2.transpose(p_bf[:, c * 128:(c + 1) * 128],
                                      w2bf[:, r * 1024 + c * 128: r * 1024 + (c + 1) * 128],
                                      C["ident"])
                li.then_inc(s_t.h, 1)
            s_t.n += 1
            t2d = s_t.n

            @block.vector
            def _(v, t=t, r=r, d=t2d):
                v.wait_ge(s_t.h, d)
                v.tensor_copy(
                    out=wT2[:].rearrange("p (c f) -> p c f", f=1024)[
                        :, r * 8:(r + 1) * 8, t * 128:(t + 1) * 128],
                    in_=p_bf[:].rearrange("p (c f) -> p c f", f=128))
                v.drain().then_inc(s_v.h, 1)
            s_v.n += 1
            st["l1f_pbf"] = s_v.n
        ev2[t] = s_v.n
    w2_done = s_v.n

    b2f = sb("l1f_b2f", [128, 8], F32)
    bq2i = sb("l1f_bq2i", [128, 8], I32)
    bq2f = sb("l1f_bq2f", [128, 8], F32)

    @block.gpsimd
    def _(g):
        s_gd.dma(g, b2f[:], W["l1f_b2_fm"][:])
    b2_ld = s_gd.n

    bqb2 = sb("l1f_bqb2", [128, 8], BF16)
    brt2 = sb("l1f_brt2", [8, 128], BF16)

    @block.vector
    def _(v, ld=b2_ld, w2d=w2_done, ar=ar6_done):
        v.wait_ge(s_gd.h, ld)
        v.wait_ge(s_v.h, max(w2d, ar))
        v.tensor_tensor(out=bq2f[:], in0=b2f[:], in1=isw2[:], op=ALU.mult)
        v.drain()
        v.tensor_scalar(out=bq2i[:], in0=bq2f[:], scalar1=scal[:, IS2f:IS2f + 1],
                        scalar2=None, op0=ALU.mult)
        v.drain()
        v.tensor_tensor(out=bq2f[:], in0=bq2i[:], in1=sw2[:], op=ALU.mult)
        v.drain()
        v.tensor_copy(out=bqb2[:], in_=bq2f[:])
        v.drain().then_inc(s_v.h, 1)
    s_v.n += 1
    bq2_done = s_v.n

    @block.tensor
    def _(t2, d=bq2_done, pf=st["l1f_pbf"]):
        t2.wait_ge(s_v.h, d)
        t2.wait_ge(s_v.h, pf)
        t2.transpose(p_bf[0:8, 0:128], bqb2[:], C["ident"]).then_inc(s_t.h, 1)
    s_t.n += 1
    bq2_t = s_t.n

    @block.vector
    def _(v, d=bq2_t):
        v.wait_ge(s_t.h, d)
        v.tensor_copy(out=brt2[:], in_=p_bf[0:8, 0:128])
        v.drain().then_inc(s_v.h, 1)
    s_v.n += 1
    st["l1f_pbf"] = s_v.n

    @block.gpsimd
    def _(g, d=s_v.n):
        g.wait_ge(s_v.h, d)
        s_gd.dma(g, st["bb_sp"][0:8, :], brt2[:])

    @block.sync
    def _(sy, d=s_gd.n):
        sy.wait_ge(s_gd.h, d)
        s_ld.dma(sy, br2[0:1, :],
                 st["bb_sp"][0:8, :].rearrange("t p -> (t p)").unsqueeze(0))
    br2_ld = s_ld.n

    @block.vector
    def _(v, ld=br2_ld):
        v.wait_ge(s_ld.h, ld)
        v.drain().then_inc(s_v.h, 1)
    s_v.n += 1
    br2_done = s_v.n

    st["arena"].rollback(mark_w2)
    hb = [sb(f"l1f_hb{i}", [128, 4096], BF16) for i in range(2)]
    xq2i = sb("l1f_xq2i", [128, 4096], I32)
    xq2 = sb("l1f_xq2", [128, 4096], BF16)
    x3b = [sb(f"l1f_x3b{i}", [128, E], F32) for i in range(2)]
    yb = [sb(f"l1f_yb{i}", [128, E], F32) for i in range(2)]

    lds2 = {}
    yst = {}
    mm2_tail = 0
    for ci in range(nch):
        @block.sync
        def _(sy, ci=ci, prev=lds2.get(ci - 2, 0)):
            if prev:
                sy.wait_ge(s_v.h, prev)
            r0 = ci * 128
            s_ld.dma(sy, hb[ci % 2][:], st["h_sp"][r0:r0 + 128, :])
            sy.wait_ge(s_ld.h, s_ld.n)
            s_ld.dma(sy, x3b[ci % 2][:], st["xres"][r0:r0 + 128, :])
        hb_ld = s_ld.n

        @block.vector
        def _(v, ci=ci, ld=hb_ld, br=br2_done, mt=mm2_tail):
            v.wait_ge(s_ld.h, ld)
            if ci == 0:
                v.wait_ge(s_v.h, br)
            if mt:
                v.wait_ge(s_t.h, mt)
            v.tensor_scalar(out=xq2i[:], in0=hb[ci % 2][:],
                            scalar1=scal[:, IS2f:IS2f + 1], scalar2=0.0,
                            op0=ALU.mult, op1=ALU.max)
            v.drain()
            v.tensor_copy(out=xq2[:], in_=xq2i[:])
            v.drain().then_inc(s_v.h, 1)
        s_v.n += 1
        lds2[ci] = s_v.n
        xq2_done = s_v.n

        @block.tensor
        def _(t2, d=xq2_done, ys=yst.get(ci - 1, 0)):
            t2.wait_ge(s_v.h, d)
            for b2 in range(2):
                for m in range(32):
                    t2.matmul(pb[b2][:], xq2[:, m * 128:(m + 1) * 128],
                              wT2[:, m * 1024 + b2 * 512: m * 1024 + b2 * 512 + 512],
                              start=(m == 0), stop=False)
                li = t2.matmul(pb[b2][:], C_ONESROW[:],
                               br2[0:1, b2 * 512:(b2 + 1) * 512],
                               start=False, stop=True)
            li.then_inc(s_t.h, 1)
        s_t.n += 1
        mm2_tail = s_t.n

        @block.vector
        def _(v, ci=ci, d=mm2_tail, ys=yst.get(ci - 2, 0)):
            v.wait_ge(s_t.h, d)
            if ys:
                v.wait_ge(s_st.h, ys)
            for b2 in range(2):
                v.scalar_tensor_tensor(out=yb[ci % 2][:, b2 * 512:(b2 + 1) * 512],
                                       in0=pb[b2][:],
                                       scalar=scal[:, S2f:S2f + 1],
                                       in1=x3b[ci % 2][:, b2 * 512:(b2 + 1) * 512],
                                       op0=ALU.mult, op1=ALU.add)
            v.drain().then_inc(s_v.h, 1)
        s_v.n += 1
        y_done = s_v.n

        @block.sync
        def _(sy, ci=ci, d=y_done):
            sy.wait_ge(s_v.h, d)
            s_st.dma(sy, st["y_out"][ci * 128:(ci + 1) * 128, :], yb[ci % 2][:])
        yst[ci] = s_st.n



def emit_finish(st):
    block = st["block"]
    st["phase_barrier"]()
    st["arena"].reset()
    s_ld, s_st = st["sems"]["ld"], st["sems"]["st"]
    sb = st["sb"]
    if DEV_PHASES >= 4:
        @block.sync
        def _(sy):
            sy.wait_ge(s_st.h, s_st.n)
        return
    tmp = [sb(f"fin_tmp{i}", [128, E], F32) for i in range(2)]

    @block.sync
    def _(sy):
        sy.wait_ge(s_st.h, s_st.n)
        for ci in range(_nchunks()):
            r0 = ci * 128
            s_ld.dma(sy, tmp[ci % 2][:], st["xres"][r0:r0 + 128, :])
            sy.wait_ge(s_ld.h, s_ld.n)
            s_st.dma(sy, st["y_out"][r0:r0 + 128, :], tmp[ci % 2][:])
        sy.wait_ge(s_st.h, s_st.n)


WMAP = {
    "l0_wq": "L0_attn_Wq", "l0_wk": "L0_attn_Wk", "l0_wv": "L0_attn_Wv",
    "l0_wo": "L0_attn_Wo", "l0_wf1": "L0_attn_Wf1", "l0_wf2": "L0_attn_Wf2",
    "l0_wg1": "L0_attn_Wg1", "l0_wg2": "L0_attn_Wg2",
    "l0f_wgate": "L0_ffn_w_gate", "l0f_wup": "L0_ffn_w_up", "l0f_wdown": "L0_ffn_w_down",
    "l1_wq": "L1_attn_wq", "l1_wk": "L1_attn_wk", "l1_wv": "L1_attn_wv",
    "l1_wgq": "L1_attn_wgq", "l1_wgk": "L1_attn_wgk", "l1_wgv": "L1_attn_wgv",
    "l1_rout": "L1_attn_r_out_w", "l1_wout": "L1_attn_w_out",
    "l1f_w1": "L1_ffn_w1", "l1f_w2": "L1_ffn_w2",
}


def make_in_maps(inputs):
    x = np.ascontiguousarray(inputs["L0_attn_x"], dtype=np.float32)
    cbv, _ = const_blob_bf16()
    base = {k: np.ascontiguousarray(inputs[v], dtype=np.float32) for k, v in WMAP.items()}
    base["l0_nw_fm"] = np.ascontiguousarray(
        np.asarray(inputs["L0_attn_norm_weight"], dtype=np.float32).reshape(8, 128).T)
    base["l1f_b1_fm"] = np.ascontiguousarray(
        np.asarray(inputs["L1_ffn_b1"], dtype=np.float32).reshape(32, 128).T)
    base["l1f_b2_fm"] = np.ascontiguousarray(
        np.asarray(inputs["L1_ffn_b2"], dtype=np.float32).reshape(8, 128).T)
    glw = np.asarray(inputs["L1_attn_gate_local_w"], dtype=np.float32)
    grw = np.asarray(inputs["L1_attn_gate_remote_w"], dtype=np.float32)
    glw_fm = np.zeros((128, 32), np.float32)
    for h in range(16):
        glw_fm[(h % 2) * 64:(h % 2) * 64 + 64, h] = glw[h]
        glw_fm[(h % 2) * 64:(h % 2) * 64 + 64, 16 + h] = grw[h]
    base["l1_glw_fm"] = np.ascontiguousarray(glw_fm)
    base["constb"] = cbv
    base["constf"] = const_blob_f32()
    in_maps = []
    for core in range(8):
        m = dict(base)
        m["x_in"] = np.ascontiguousarray(
            x[core * B_PER_CORE:(core + 1) * B_PER_CORE].reshape(TC, E))
        in_maps.append(m)
    return in_maps


def kernel(**inputs):
    nc = build_program()
    in_maps = make_in_maps(inputs)
    res = run_bass_kernel_spmd(nc, in_maps, core_ids=list(range(8)))
    out = np.concatenate([r["y_out"].reshape(B_PER_CORE, S, E) for r in res.results], axis=0)
    return out.astype(np.float32)


# version 04
# speedup vs baseline: 1.0604x; 1.0039x over previous
import math
import numpy as np
import ml_dtypes
import concourse.bass as bass
import concourse.mybir as mybir
from concourse.bass_utils import run_bass_kernel_spmd
from concourse.bass_isa_rust import ReduceOp

F32 = mybir.dt.float32
BF16 = mybir.dt.bfloat16
I32 = mybir.dt.int32
ALU = mybir.AluOpType
AF = mybir.ActivationFunctionType
AX = mybir.AxisListType

B_PER_CORE = 2
S = 2048
E = 1024
H = 16
HD = 64
TC = B_PER_CORE * S
NCHUNK = TC // 128
CPB = S // 128
LN_EPS = 1e-5
GLA_EPS = 1e-5
LOG8 = math.log(8.0)

DEV_PHASES = 4
DEV_NCHUNK = None
DEV_TAP = None
DEV_STAGE = 99
DEV_NO_SMM = False
DEV_SOLO = False
DEV_NO_OINTER = False
DEV_FAKE_S = False
ARENA_COLS = 46300
BS_OFF = ("l0f",)


def _nchunks():
    return DEV_NCHUNK if DEV_NCHUNK is not None else NCHUNK


class Arena:
    def __init__(self, nc, ctx, f32_cols):
        self.t = ctx.enter_context(nc.sbuf_tensor("arena", [128, f32_cols], F32))
        self.cols = f32_cols
        self.off = 0

    def alloc(self, name, shape, dt=F32):
        rows, cols = shape
        exact = cols if dt in (F32, I32) else (cols + 1) // 2
        need = exact + (exact % 2)
        if True:
            import os
            if os.environ.get("ARENA_TRACE"):
                print("ARENA", name, shape, dt, "off", self.off, "need", need)
        assert self.off + need <= self.cols, f"arena overflow at {name}: {self.off}+{need}"
        ap = self.t[0:rows, self.off:self.off + exact]
        self.off += need
        if dt == BF16:
            ap = ap.bitcast(BF16)
        elif dt == I32:
            ap = ap.bitcast(I32)
        if dt == BF16:
            ap = ap[:, 0:cols * 1]
        return ap

    def reset(self):
        self.off = 0

    def mark(self):
        return self.off

    def rollback(self, m):
        self.off = m


class ScalarBatcher:
    def __init__(self):
        self.block = None
        self.items = []
        self.passthrough = False

    def add(self, fn):
        if self.passthrough:
            @self.block.scalar
            def _(s2, fn=fn):
                fn(s2)
            return
        self.items.append(fn)

    def flush(self):
        if not self.items:
            return
        items, self.items = self.items, []

        @self.block.scalar
        def _(s2, items=items):
            for fn in items:
                fn(s2)


class Sem:
    def __init__(self, ctx, nc, name):
        self.h = ctx.enter_context(nc.semaphore(name))
        self.n = 0

    def dma(self, eng, out, in_):
        eng.wait_ge(self.h, self.n)
        r = eng.dma_start(out=out, in_=in_).then_inc(self.h, 16)
        self.n += 16
        return r


def host_constants():
    cb = {}
    cb["ident"] = np.eye(128, dtype=np.float32)
    tri = np.zeros((128, 128), np.float32)
    for blk in range(2):
        for tau in range(64):
            tri[blk * 64 + tau, blk * 64 + tau: blk * 64 + 64] = 1.0
    cb["tric"] = tri
    am = np.zeros((128, 128), np.float32)
    for p in range(128):
        for c in range(128):
            if (p % 64) <= (c % 64):
                am[p, c] = 1.0
    cb["amask"] = am
    ones = np.ones((128, 4), np.float32)
    ones[64:128, 1] = 0.0
    ones[0:64, 2] = 0.0
    cb["ones"] = ones
    wm = np.zeros((256, 512), np.float32)
    for i in range(256):
        wm[i, i + 1: i + 257] = 1.0
    cb["wmask0"] = wm[:128]
    cb["wmask1"] = wm[128:]
    m0 = np.tril(np.ones((256, 256), np.float32))
    cb["bmask0"] = m0[:128]
    cb["bmask1"] = m0[128:]
    lm = np.zeros((128, 16 * 64), np.float32)
    for qc in range(16):
        for p in range(128):
            lm[p, qc * 64: qc * 64 + (qc * 128 + p) // 32 + 1] = 1.0
    cb["lmask"] = lm
    return cb


CONST_ORDER = ["ident", "tric", "amask", "ones", "wmask0", "wmask1",
               "bmask0", "bmask1", "lmask"]

CONSTF_VALS = [-LOG8, 0.5, 32768.5, 256.5, 8.5, 128.5]
CF_ONES = len(CONSTF_VALS)
CF_IDENT = CF_ONES + 128
CF_W = CF_IDENT + 128


def const_blob_f32():
    m = np.zeros((128, CF_W), np.float32)
    m[:, :len(CONSTF_VALS)] = np.asarray(CONSTF_VALS, np.float32)[None, :]
    m[:, CF_ONES:CF_IDENT] = 1.0
    m[:, CF_IDENT:CF_W] = np.eye(128, dtype=np.float32)
    return m


def const_blob_bf16():
    cb = host_constants()
    cols, mats, off = {}, [], 0
    for k in CONST_ORDER:
        m = cb[k]
        cols[k] = (off, m.shape[1])
        mats.append(m.astype(ml_dtypes.bfloat16))
        off += m.shape[1]
    return np.concatenate(mats, axis=1), cols


def build_program():
    import contextlib
    nc = bass.Bass("TRN2", target_bir_lowering=False, debug=False)
    ctx = contextlib.ExitStack()

    x_in = nc.dram_tensor("x_in", [TC, E], F32, kind="ExternalInput").ap()
    y_out = nc.dram_tensor("y_out", [TC, E], F32, kind="ExternalOutput").ap()
    W = {}
    for nm, shp in [
        ("l0_wq", [E, E]), ("l0_wk", [E, E]), ("l0_wv", [E, E]), ("l0_wo", [E, E]),
        ("l0_wf1", [E, 64]), ("l0_wf2", [64, E]), ("l0_wg1", [E, 64]), ("l0_wg2", [64, E]),
        ("l0_nw_fm", [128, 8]),
        ("l0f_wgate", [3072, E]), ("l0f_wup", [3072, E]), ("l0f_wdown", [E, 3072]),
        ("l1_wq", [E, E]), ("l1_wk", [E, E]), ("l1_wv", [E, E]),
        ("l1_wgq", [E, 512]), ("l1_wgk", [E, 512]), ("l1_wgv", [E, 512]),
        ("l1_rout", [512, E]), ("l1_glw_fm", [128, 32]),
        ("l1_wout", [E, E]),
        ("l1f_w1", [4096, E]), ("l1f_b1_fm", [128, 32]), ("l1f_w2", [E, 4096]),
        ("l1f_b2_fm", [128, 8]),
    ]:
        W[nm] = nc.dram_tensor(nm, shp, F32, kind="ExternalInput").ap()
    cbv, ccols = const_blob_bf16()
    cb_h = nc.dram_tensor("constb", [128, cbv.shape[1]], BF16, kind="ExternalInput").ap()
    cf_h = nc.dram_tensor("constf", [128, CF_W], F32, kind="ExternalInput").ap()

    xres = nc.dram_tensor("xres", [TC, E], F32, kind="Internal").ap()
    ln1_sp = nc.dram_tensor("ln1_sp", [TC, E], BF16, kind="Internal").ap()
    up_sp = nc.dram_tensor("up_sp", [TC, 3072], BF16, kind="Internal").ap()
    gate_sp = nc.dram_tensor("gate_sp", [TC, 3072], BF16, kind="Internal").ap()
    t_sp = nc.dram_tensor("t_sp", [TC, 3072], BF16, kind="Internal").ap()
    zf_sp = nc.dram_tensor("zf_sp", [NCHUNK * 128, 3072], BF16, kind="Internal").ap()
    h_sp = nc.dram_tensor("h_sp", [NCHUNK * 128, 4096], BF16, kind="Internal").ap()
    cc_in = [nc.dram_tensor(f"cc_in{i}", [1, 4], F32, kind="Internal").ap() for i in range(8)]
    cc_stage = nc.dram_tensor("cc_stage", [128, 4], F32, kind="Internal").ap()
    bb_sp = nc.dram_tensor("bb_sp", [32, 128], F32, kind="Internal").ap()
    cc_out = [nc.dram_tensor(f"cc_out{i}", [1, 4], F32, kind="Internal").ap() for i in range(8)]

    sb = lambda name, shape, dt=F32: ctx.enter_context(nc.sbuf_tensor(name, shape, dt))
    ps = lambda name, shape, dt=F32: ctx.enter_context(nc.psum_tensor(name, shape, dt))

    consts = sb("consts", [128, cbv.shape[1]], BF16)
    C = {k: consts[:, off:off + w] for k, (off, w) in ccols.items()}
    constf = sb("constf_s", [128, CF_W], F32)
    for i, val in enumerate(CONSTF_VALS):
        nc.const_aps.register(F32, float(val), constf[:, i:i + 1])
    C_ONESROW = constf[0:1, CF_ONES:CF_IDENT]
    C_ONESCOL = constf[:, CF_ONES:CF_ONES + 4]
    C_IDF = constf[:, CF_IDENT:CF_W]

    arena = Arena(nc, ctx, ARENA_COLS)
    pb_bf = ctx.enter_context(nc.psum_tensor("pb_bf", [128, 1024], BF16))
    pb = [ctx.enter_context(nc.psum_tensor(f"pb{i}", [128, 512], F32)) for i in range(7)]

    block = ctx.enter_context(nc.Block())
    sems = {k: Sem(ctx, nc, "sem_" + k) for k in
            ("ld", "gd", "t", "v", "s", "g", "st", "cc")}

    @block.sync
    def _(sy):
        sems["ld"].dma(sy, consts[:], cb_h[:])
        sems["ld"].dma(sy, constf[:], cf_h[:])

    st = dict(nc=nc, block=block, sems=sems, C=C, W=W, sb=arena.alloc, ps=ps, ctx=ctx,
              arena=arena, sb_persist=sb,
              x_in=x_in, y_out=y_out, xres=xres, ln1_sp=ln1_sp, up_sp=up_sp,
              gate_sp=gate_sp, t_sp=t_sp, zf_sp=zf_sp, h_sp=h_sp, bb_sp=bb_sp,
              cc_in=cc_in, cc_out=cc_out, cc_stage=cc_stage, consts_ready=sems["ld"].n,
              C_ONESROW=C_ONESROW, C_ONESCOL=C_ONESCOL, C_IDF=C_IDF,
              pb_bf=pb_bf[:], pb=[p[:] for p in pb])

    st["bs"] = ScalarBatcher()
    st["bs"].block = block

    def phase_barrier():
        st["bs"].flush()
        marks = [(sems[k].h, sems[k].n) for k in ("ld", "gd", "t", "v", "s", "g", "st", "cc")]

        for engname in ("sync", "tensor", "vector", "scalar", "gpsimd"):
            @getattr(block, engname)
            def _(e):
                for h, n in marks:
                    e.wait_ge(h, n)
    st["phase_barrier"] = phase_barrier

    st["taps"] = {}

    def tap(name, ap, sem, val):
        if DEV_TAP is None:
            return
        if name in st["taps"]:
            return
        dt_ = ap.dtype
        t_h = nc.dram_tensor("tap_" + name, list(ap.shape), dt_, kind="ExternalOutput").ap()
        st["taps"][name] = t_h

        @block.gpsimd
        def _(g):
            g.wait_ge(sem.h, val)
            sems["gd"].dma(g, t_h[:], ap)
        done = sems["gd"].n

        @block.vector
        def _(v):
            v.wait_ge(sems["gd"].h, done)
            v.drain().then_inc(sems["v"].h, 1)
        sems["v"].n += 1

        @block.scalar
        def _(sc):
            sc.wait_ge(sems["gd"].h, done)
            sc.drain().then_inc(sems["s"].h, 1)
        sems["s"].n += 1
    st["tap"] = tap

    st["scal"] = sb("scal", [128, 16], F32)
    st["ln1_stats"] = sb("ln1_stats", [128, 2 * NCHUNK], F32)
    st["mxacc"] = sb("mxacc", [128, 12], F32)
    st["scrow"] = sb("scrow", [1, 16], F32)

    with nc.named_scope("l0a"):
        emit_l0a(st)
    if DEV_PHASES >= 2:
        with nc.named_scope("l0f"):
            emit_l0f(st)
    if DEV_PHASES >= 3:
        with nc.named_scope("l1a"):
            emit_l1a(st)
    if DEV_PHASES >= 4:
        with nc.named_scope("l1f"):
            emit_l1f(st)
    emit_finish(st)

    ctx.close()
    return nc


def emit_l0a(st):
    bsc = st["bs"]
    bsc.passthrough = "l0a" in BS_OFF
    nc, block, C, W, sb, ps = (st[k] for k in ("nc", "block", "C", "W", "sb", "ps"))
    s_ld, s_gd, s_t, s_v, s_s, s_g, s_st = (st["sems"][k] for k in
                                            ("ld", "gd", "t", "v", "s", "g", "st"))
    ln1_stats, mxacc = st["ln1_stats"], st["mxacc"]

    wq = sb("l0_wq_s", [128, 8 * E], BF16)
    wk = sb("l0_wk_s", [128, 8 * E], BF16)
    wv = sb("l0_wv_s", [128, 8 * E], BF16)
    wo = sb("l0_wo_s", [128, 8 * E], BF16)
    wf1 = sb("l0_wf1_s", [128, 8 * 64], BF16)
    wg1 = sb("l0_wg1_s", [128, 8 * 64], BF16)
    wf2 = sb("l0_wf2_s", [64, E], BF16)
    wg2 = sb("l0_wg2_s", [64, E], BF16)
    nwf = sb("l0_nwf", [128, 8], F32)

    @block.gpsimd
    def _(g):
        for wsb, hbm in ((wq, W["l0_wq"]), (wk, W["l0_wk"]), (wv, W["l0_wv"]),
                         (wo, W["l0_wo"])):
            s_gd.dma(g, wsb[:].rearrange("p (c e) -> p c e", c=8),
                     hbm.rearrange("(c p) e -> p c e", p=128))
        for wsb, hbm in ((wf1, W["l0_wf1"]), (wg1, W["l0_wg1"])):
            s_gd.dma(g, wsb[:].rearrange("p (c e) -> p c e", c=8),
                     hbm.rearrange("(c p) e -> p c e", p=128))
        for wsb, hbm in ((wf2, W["l0_wf2"]), (wg2, W["l0_wg2"])):
            s_gd.dma(g, wsb[:], hbm[:])
    w_loaded = s_gd.n

    @block.sync
    def _(sy):
        s_ld.dma(sy, nwf[:], W["l0_nw_fm"][:])
    nw_loaded = s_ld.n

    @block.vector
    def _(v):
        v.wait_ge(s_gd.h, w_loaded)
        v.wait_ge(s_ld.h, nw_loaded)
        for c in range(8):
            v.tensor_scalar(out=wo[:, c * E:(c + 1) * E], in0=wo[:, c * E:(c + 1) * E],
                            scalar1=nwf[:, c:c + 1], scalar2=None, op0=ALU.mult)
        v.memset(st["mxacc"][:], 0.0)
        v.drain().then_inc(s_v.h, 1)
    s_v.n += 1
    wo_ready = s_v.n

    p_tr = st["pb_bf"]
    p_qk = p_tr
    p_pr = st["pb"][0:4]
    p_nl = [p_pr[0], p_pr[1]]
    p_o = [p_pr[2], p_pr[3]]
    p_sml = st["pb"][4]
    p_sm = p_sml[:, 0:64]
    p_nc = p_sml[:, 64:80]
    p_at = p_sml[:, 128:256]
    p_S = st["pb"][5]
    p_S2 = st["pb"][6]

    xin = [sb(f"l0a_xin{i}", [128, E], F32) for i in range(2)]
    x1t = [sb(f"l0a_x1t{i}", [128, E], F32) for i in range(2)]
    ln_tm = sb("l0a_lntm", [128, E], BF16)
    ln_fm = sb("l0a_lnfm", [128, E], BF16)
    q_tm = sb("l0a_qtm", [128, E], BF16)
    k_tm = sb("l0a_ktm", [128, E], BF16)
    v_tm = sb("l0a_vtm", [128, E], BF16)
    g_tm = sb("l0a_gtm", [128, E], BF16)
    f1_tm = sb("l0a_f1tm", [128, 64], BF16)
    f1_fm = sb("l0a_f1fm", [64, 128], BF16)
    sp32 = sb("l0a_sp32", [128, E], F32)
    fv32 = sb("l0a_fv32", [128, E], F32)
    sq32 = sb("l0a_sq32", [128, E], F32)
    sp_hi = sb("l0a_sphi", [128, E], BF16)
    sp_lo = sb("l0a_splo", [128, E], BF16)
    e1t = sb("l0a_e1", [128, E], BF16)
    e2t = sb("l0a_e2", [128, E], BF16)
    qt_t = sb("l0a_qt", [128, E], BF16)
    kt_t = sb("l0a_kt", [128, E], BF16)
    qf_pad = [sb(f"l0a_qfp{i}", [128, 1024], BF16) for i in range(2)]
    kf_full = sb("l0a_kff", [128, 1024], BF16)
    kt_p = [sb(f"l0a_ktp{i}", [128, 1024], BF16) for i in range(2)]
    am_g = [sb(f"l0a_am{i}", [128, 8 * 128], BF16) for i in range(2)]
    aC = sb("l0a_aC", [128, 16], F32)
    S_all = sb("l0a_S", [128, 2 * 512], F32)
    S_bf = sb("l0a_Sbf", [128, 2 * 512], BF16)
    S_tmp = sb("l0a_Stmp", [128, 512], F32)
    og = sb("l0a_og", [128, E], F32)
    ms_t = sb("l0a_ms", [128, 16], F32)
    rstd_h = sb("l0a_rstdh", [128, 16], F32)
    ogn = sb("l0a_ogn", [128, E], BF16)
    og_fm = sb("l0a_ogfm", [128, E], BF16)
    lnst = sb("l0a_lnst", [128, 8], F32)
    ln1t = sb("l0a_ln1t", [128, E], BF16)

    @block.vector
    def _(v):
        v.memset(S_all[:], 0.0)
        v.memset(S_bf[:], 0.0)
        v.memset(am_g[0][:], 0.0)
        v.memset(am_g[1][:], 0.0)
        v.memset(qf_pad[0][:], 0.0)
        v.memset(qf_pad[1][:], 0.0)
        v.memset(kt_p[0][:], 0.0)
        v.memset(kt_p[1][:], 0.0)
        v.drain().then_inc(s_v.h, 1)
    s_v.n += 1
    sinit = s_v.n

    chunk_v_done = {}
    chunk_st_done = {}
    prev_supd = 0
    prev_qkevac = 0
    prev_amdone = 0

    def ln_block(get_in, out_bf16, store_stats_ci=None, s_wait_v=None):
        xi = get_in

        @block.vector
        def _(v):
            v.reduce_sum(out=lnst[:, 0:1], in_=xi[:], axis=AX.X)
            v.drain().then_inc(s_v.h, 1)
        s_v.n += 1

        @bsc.add
        def _(s, s_wait_v=s_wait_v):
            if s_wait_v is not None:
                s.wait_ge(s_v.h, s_wait_v)
            s.activation(out=sq32[:], in_=xi[:], func=AF.Square)
            s.drain().then_inc(s_s.h, 1)
        s_s.n += 1
        sqd = s_s.n

        @block.vector
        def _(v, sqd=sqd):
            v.wait_ge(s_s.h, sqd)
            v.reduce_sum(out=lnst[:, 1:2], in_=sq32[:], axis=AX.X)
            v.drain()
            v.tensor_scalar(out=lnst[:, 2:3], in0=lnst[:, 0:1], scalar1=1.0 / E,
                            scalar2=None, op0=ALU.mult)
            v.drain()
            v.tensor_scalar(out=lnst[:, 3:4], in0=lnst[:, 2:3], scalar1=lnst[:, 2:3],
                            scalar2=None, op0=ALU.mult)
            v.drain()
            v.scalar_tensor_tensor(out=lnst[:, 4:5], in0=lnst[:, 1:2], scalar=1.0 / E,
                                   in1=lnst[:, 3:4], op0=ALU.mult, op1=ALU.subtract)
            v.drain()
            v.tensor_scalar(out=lnst[:, 4:5], in0=lnst[:, 4:5], scalar1=LN_EPS,
                            scalar2=None, op0=ALU.add)
            v.drain().then_inc(s_v.h, 1)
        s_v.n += 1
        vard = s_v.n

        @bsc.add
        def _(s, vard=vard):
            s.wait_ge(s_v.h, vard)
            s.activation(out=lnst[:, 5:6], in_=lnst[:, 4:5], func=AF.Ln)
            s.drain()
            s.activation(out=lnst[:, 6:7], in_=lnst[:, 5:6], func=AF.Exp, scale=-0.5)
            s.drain().then_inc(s_s.h, 1)
        s_s.n += 1
        sqrtd = s_s.n

        @block.vector
        def _(v, sqrtd=sqrtd):
            v.wait_ge(s_s.h, sqrtd)
            if store_stats_ci is not None:
                ci2 = store_stats_ci
                v.tensor_copy(out=ln1_stats[:, 2 * ci2:2 * ci2 + 1], in_=lnst[:, 2:3])
                v.tensor_copy(out=ln1_stats[:, 2 * ci2 + 1:2 * ci2 + 2], in_=lnst[:, 6:7])
                v.drain()
            v.tensor_scalar(out=out_bf16[:], in0=xi[:], scalar1=lnst[:, 2:3],
                            scalar2=lnst[:, 6:7], op0=ALU.subtract, op1=ALU.mult)
            v.drain().then_inc(s_v.h, 1)
        s_v.n += 1
        return s_v.n

    for ci in range(_nchunks()):
        b = ci // CPB
        xi = xin[ci % 2]
        x1 = x1t[ci % 2]
        r0 = ci * 128

        @block.sync
        def _(sy, ci=ci, xi=xi, r0=r0):
            if ci >= 2:
                sy.wait_ge(s_v.h, chunk_v_done[ci - 2])
            s_ld.dma(sy, xi[:], st["x_in"][r0:r0 + 128, :])
        ld_x = s_ld.n

        @block.vector
        def _(v, ci=ci, ld_x=ld_x):
            v.wait_ge(s_ld.h, ld_x)
            if ci == 0:
                v.wait_ge(s_v.h, sinit)
                v.wait_ge(s_v.h, wo_ready)
                v.wait_ge(s_ld.h, st["consts_ready"])
            v.engine_nop().then_inc(s_v.h, 1)
        s_v.n += 1

        @bsc.add
        def _(s, ld_x=ld_x):
            s.wait_ge(s_ld.h, ld_x)
            s.nop().then_inc(s_s.h, 1)
        s_s.n += 1

        ln_done = ln_block(xi, ln_tm)

        if DEV_STAGE >= 2:
            @block.tensor
            def _(t, ln_done=ln_done, ci=ci):
                t.wait_ge(s_v.h, ln_done)
                for c in range(8):
                    li = t.transpose(p_tr[:, c * 128:(c + 1) * 128],
                                     ln_tm[:, c * 128:(c + 1) * 128], C["ident"])
                li.then_inc(s_t.h, 1)
            s_t.n += 1
            lnT = s_t.n

            @block.vector
            def _(v, lnT=lnT):
                v.wait_ge(s_t.h, lnT)
                v.tensor_copy(out=ln_fm[:], in_=p_tr[:])
                v.drain().then_inc(s_v.h, 1)
            s_v.n += 1
            lnfm_done = s_v.n

        if DEV_STAGE >= 3:
            def proj_mm(t, wsb, pl, ph, src=None):
                src = ln_fm if src is None else src
                for c in range(8):
                    t.matmul(pl[:], src[:, c * 128:(c + 1) * 128],
                             wsb[:, c * E: c * E + 512], start=(c == 0), stop=(c == 7))
                for c in range(8):
                    li = t.matmul(ph[:], src[:, c * 128:(c + 1) * 128],
                                  wsb[:, c * E + 512: (c + 1) * E], start=(c == 0), stop=(c == 7))
                return li

            @block.tensor
            def _(t, lnfm_done=lnfm_done):
                t.wait_ge(s_v.h, lnfm_done)
                proj_mm(t, wq, p_pr[0], p_pr[1]).then_inc(s_t.h, 1)
            s_t.n += 1
            qmm = s_t.n

            @bsc.add
            def _(s, qmm=qmm):
                s.wait_ge(s_t.h, qmm)
                s.activation(out=q_tm[:, 0:512], in_=p_pr[0][:], func=AF.Silu)
                s.activation(out=q_tm[:, 512:1024], in_=p_pr[1][:], func=AF.Silu)
                s.drain().then_inc(s_s.h, 1)
            s_s.n += 1
            q_done = s_s.n

            @block.tensor
            def _(t, q_done=q_done):
                t.wait_ge(s_s.h, q_done)
                proj_mm(t, wk, p_pr[0], p_pr[1]).then_inc(s_t.h, 1)
            s_t.n += 1
            kmm = s_t.n

            @bsc.add
            def _(s, kmm=kmm):
                s.wait_ge(s_t.h, kmm)
                s.activation(out=k_tm[:, 0:512], in_=p_pr[0][:], func=AF.Silu)
                s.activation(out=k_tm[:, 512:1024], in_=p_pr[1][:], func=AF.Silu)
                s.drain().then_inc(s_s.h, 1)
            s_s.n += 1
            k_done = s_s.n

            @block.tensor
            def _(t, k_done=k_done):
                t.wait_ge(s_s.h, k_done)
                proj_mm(t, wv, p_pr[2], p_pr[3]).then_inc(s_t.h, 1)
            s_t.n += 1
            vmm = s_t.n

            @block.vector
            def _(v, vmm=vmm):
                v.wait_ge(s_t.h, vmm)
                v.tensor_copy(out=v_tm[:, 0:512], in_=p_pr[2][:])
                v.tensor_copy(out=v_tm[:, 512:1024], in_=p_pr[3][:])
                v.drain().then_inc(s_v.h, 1)
            s_v.n += 1
            vevac = s_v.n

        if DEV_STAGE >= 4:
            @block.tensor
            def _(t, vevac=vevac):
                for c in range(8):
                    li = t.matmul(p_sm, ln_fm[:, c * 128:(c + 1) * 128],
                                  wf1[:, c * 64:(c + 1) * 64], start=(c == 0), stop=(c == 7))
                li.then_inc(s_t.h, 1)
            s_t.n += 1
            f1mm = s_t.n

            @block.vector
            def _(v, f1mm=f1mm):
                v.wait_ge(s_t.h, f1mm)
                v.tensor_copy(out=f1_tm[:], in_=p_sm)
                v.drain().then_inc(s_v.h, 1)
            s_v.n += 1
            f1e = s_v.n

            @block.tensor
            def _(t, f1e=f1e):
                t.wait_ge(s_v.h, f1e)
                t.transpose(p_tr[0:64, 0:128], f1_tm[:], C["ident"]).then_inc(s_t.h, 1)
            s_t.n += 1
            f1T = s_t.n

            @block.vector
            def _(v, f1T=f1T):
                v.wait_ge(s_t.h, f1T)
                v.tensor_copy(out=f1_fm[:], in_=p_tr[0:64, 0:128])
                v.drain().then_inc(s_v.h, 1)
            s_v.n += 1
            f1fm = s_v.n

            @block.tensor
            def _(t, f1fm=f1fm, vevac=vevac):
                t.wait_ge(s_v.h, f1fm)
                t.wait_ge(s_v.h, vevac)
                t.matmul(p_pr[0][:], f1_fm[:], wf2[:, 0:512], start=True, stop=True)
                t.matmul(p_pr[1][:], f1_fm[:], wf2[:, 512:1024], start=True,
                         stop=True).then_inc(s_t.h, 1)
            s_t.n += 1
            zmm = s_t.n

            @bsc.add
            def _(s, zmm=zmm):
                s.wait_ge(s_t.h, zmm)
                s.activation(out=fv32[:, 0:512], in_=p_pr[0][:], func=AF.Sigmoid)
                s.activation(out=fv32[:, 512:1024], in_=p_pr[1][:], func=AF.Sigmoid)
                s.drain()
                s.activation(out=sp32[:], in_=fv32[:], func=AF.Ln)
                s.drain().then_inc(s_s.h, 1)
            s_s.n += 1
            spd = s_s.n

            @block.vector
            def _(v, spd=spd):
                v.wait_ge(s_s.h, spd)
                v.tensor_copy(out=sp_hi[:], in_=sp32[:])
                v.drain()
                v.scalar_tensor_tensor(out=sp_lo[:], in0=sp_hi[:], scalar=-1.0,
                                       in1=sp32[:], op0=ALU.mult, op1=ALU.add)
                v.drain().then_inc(s_v.h, 1)
            s_v.n += 1
            sphl = s_v.n

            @block.tensor
            def _(t, sphl=sphl):
                t.wait_ge(s_v.h, sphl)
                for c in range(8):
                    li = t.matmul(p_sm, ln_fm[:, c * 128:(c + 1) * 128],
                                  wg1[:, c * 64:(c + 1) * 64], start=(c == 0), stop=(c == 7))
                li.then_inc(s_t.h, 1)
            s_t.n += 1
            g1mm = s_t.n

            @block.vector
            def _(v, g1mm=g1mm):
                v.wait_ge(s_t.h, g1mm)
                v.tensor_copy(out=f1_tm[:], in_=p_sm)
                v.drain().then_inc(s_v.h, 1)
            s_v.n += 1
            g1e = s_v.n

            @block.tensor
            def _(t, g1e=g1e):
                t.wait_ge(s_v.h, g1e)
                t.transpose(p_tr[0:64, 0:128], f1_tm[:], C["ident"]).then_inc(s_t.h, 1)
            s_t.n += 1
            g1T = s_t.n

            @block.vector
            def _(v, g1T=g1T):
                v.wait_ge(s_t.h, g1T)
                v.tensor_copy(out=f1_fm[:], in_=p_tr[0:64, 0:128])
                v.drain().then_inc(s_v.h, 1)
            s_v.n += 1
            g1fm = s_v.n

            @block.tensor
            def _(t, g1fm=g1fm):
                t.wait_ge(s_v.h, g1fm)
                t.matmul(p_pr[2][:], f1_fm[:], wg2[:, 0:512], start=True, stop=True)
                t.matmul(p_pr[3][:], f1_fm[:], wg2[:, 512:1024], start=True,
                         stop=True).then_inc(s_t.h, 1)
            s_t.n += 1
            g2mm = s_t.n

            @bsc.add
            def _(s, g2mm=g2mm):
                s.wait_ge(s_t.h, g2mm)
                s.activation(out=g_tm[:, 0:512], in_=p_pr[2][:], func=AF.Sigmoid)
                s.activation(out=g_tm[:, 512:1024], in_=p_pr[3][:], func=AF.Sigmoid)
                s.drain().then_inc(s_s.h, 1)
            s_s.n += 1
            g_done = s_s.n

        if DEV_STAGE >= 5:
            @block.tensor
            def _(t, g_done=g_done):
                t.wait_ge(s_s.h, g_done)
                for half in range(2):
                    t.matmul(p_nl[half][:], C["tric"], sp_hi[:, half * 512:(half + 1) * 512],
                             start=True, stop=False)
                    t.matmul(p_nl[half][:], C["tric"], sp_lo[:, half * 512:(half + 1) * 512],
                             start=False, stop=True)
                for gc in range(2):
                    for c in range(8):
                        cc = gc * 8 + c
                        t.matmul(p_nc[:, cc:cc + 1],
                                 sp_hi[:, c * 128:(c + 1) * 128],
                                 C["ones"][:, 1 + gc:2 + gc], start=True, stop=False)
                        li = t.matmul(p_nc[:, cc:cc + 1],
                                      sp_lo[:, c * 128:(c + 1) * 128],
                                      C["ones"][:, 1 + gc:2 + gc],
                                      start=False, stop=True)
                li.then_inc(s_t.h, 1)
            s_t.n += 1
            nld = s_t.n

            @bsc.add
            def _(s, nld=nld):
                s.wait_ge(s_t.h, nld)
                s.activation(out=e1t[:, 0:512], in_=p_nl[0][:], func=AF.Exp, scale=1.0, bias=-LOG8)
                s.activation(out=e1t[:, 512:1024], in_=p_nl[1][:], func=AF.Exp, scale=1.0, bias=-LOG8)
                s.activation(out=e2t[:, 0:512], in_=p_nl[0][:], func=AF.Exp, scale=-1.0)
                s.activation(out=e2t[:, 512:1024], in_=p_nl[1][:], func=AF.Exp, scale=-1.0)
                s.activation(out=aC[:], in_=p_nc, func=AF.Exp, scale=1.0)
                s.drain().then_inc(s_s.h, 1)
            s_s.n += 1
            e_done = s_s.n

            @block.gpsimd
            def _(g, e_done=e_done):
                g.wait_ge(s_s.h, e_done)
                g.tensor_tensor(out=qt_t[:], in0=q_tm[:], in1=e1t[:], op=ALU.mult)
                g.drain()
                g.tensor_tensor(out=kt_t[:], in0=k_tm[:], in1=e2t[:], op=ALU.mult)
                g.drain().then_inc(s_g.h, 1)
            s_g.n += 1
            qkd = s_g.n

            @block.vector
            def _(v, qkd=qkd):
                v.wait_ge(s_g.h, qkd)
                v.tensor_copy(out=kt_p[0][0:64, :], in_=kt_t[0:64, :])
                v.tensor_copy(out=kt_p[1][64:128, :], in_=kt_t[64:128, :])
                v.drain().then_inc(s_v.h, 1)
            s_v.n += 1
            ktp_done = s_v.n

        if DEV_STAGE >= 7:
            @block.tensor
            def _(t, qkd=qkd, pq=prev_qkevac):
                t.wait_ge(s_g.h, qkd)
                t.wait_ge(s_v.h, pq)
                for c in range(8):
                    li = t.transpose(p_qk[:, c * 128:(c + 1) * 128],
                                     qt_t[:, c * 128:(c + 1) * 128], C["ident"])
                li.then_inc(s_t.h, 1)
            s_t.n += 1
            qT = s_t.n

            @block.vector
            def _(v, qT=qT):
                v.wait_ge(s_t.h, qT)
                v.tensor_copy(out=qf_pad[0][0:64, :], in_=p_qk[0:64, :])
                v.tensor_copy(out=qf_pad[1][64:128, :], in_=p_qk[64:128, :])
                v.drain().then_inc(s_v.h, 1)
            s_v.n += 1
            qev = s_v.n

            @block.tensor
            def _(t, qev=qev):
                t.wait_ge(s_v.h, qev)
                for c in range(8):
                    li = t.transpose(p_tr[:, c * 128:(c + 1) * 128],
                                     kt_t[:, c * 128:(c + 1) * 128], C["ident"])
                li.then_inc(s_t.h, 1)
            s_t.n += 1
            kT = s_t.n

            @block.vector
            def _(v, kT=kT):
                v.wait_ge(s_t.h, kT)
                v.tensor_copy(out=kf_full[:], in_=p_tr[:])
                v.drain().then_inc(s_v.h, 1)
            s_v.n += 1
            prev_qkevac = s_v.n
            qkf = s_v.n

            amds = []
            for hp in (range(8) if (DEV_STAGE >= 72 or DEV_STAGE == 99) else []):
                @block.tensor
                def _(t, hp=hp, qkf=qkf, pa=prev_amdone):
                    t.wait_ge(s_v.h, qkf)
                    t.wait_ge(s_v.h, pa)
                    for hh in range(2):
                        for gc in range(2):
                            rr = gc * 64
                            co = hp * 128 + gc * 64
                            li = t.matmul(
                                p_at[rr:rr + 64, hh * 64:(hh + 1) * 64],
                                kf_full[:, co:co + 64],
                                qf_pad[hh][:, co:co + 64],
                                start=True, stop=True)
                    li.then_inc(s_t.h, 1)
                s_t.n += 1
                atm = s_t.n

                @block.vector
                def _(v, atm=atm, hp=hp):
                    v.wait_ge(s_t.h, atm)
                    v.tensor_tensor(out=am_g[0][0:64, hp * 128:(hp + 1) * 128],
                                    in0=p_at[0:64, :],
                                    in1=C["amask"][0:64, :], op=ALU.mult)
                    v.tensor_tensor(out=am_g[1][64:128, hp * 128:(hp + 1) * 128],
                                    in0=p_at[64:128, :],
                                    in1=C["amask"][64:128, :], op=ALU.mult)
                    v.drain().then_inc(s_v.h, 1)
                s_v.n += 1
                amd = s_v.n
                prev_amdone = amd
                amds.append(amd)

                if DEV_STAGE < 73 and DEV_STAGE != 99:
                    continue

                @block.tensor
                def _(t, hp=hp, b=b, amd=amd, psu=prev_supd):
                    t.wait_ge(s_v.h, amd)
                    if hp == 0:
                        t.wait_ge(s_v.h, psu)
                    for hh in range(2):
                        h = 2 * hp + hh
                        col = h * 64
                        po = p_o[col // 512]
                        pc = col % 512
                        t.matmul(po[0:64, pc:pc + 64],
                                 am_g[0][:, hp * 128 + hh * 64: hp * 128 + (hh + 1) * 64],
                                 v_tm[:, col:col + 64], start=True, stop=False)
                        li = t.matmul(po[0:64, pc:pc + 64],
                                      qf_pad[hh][:, hp * 128: hp * 128 + 64],
                                      S_bf[:, b * 512 + hp * 64: b * 512 + hp * 64 + 64],
                                      start=False, stop=True)
                    li.then_inc(s_t.h, 1)
                s_t.n += 1

            for gc in (range(2) if (DEV_STAGE >= 74 or DEV_STAGE == 99) else []):
                if gc == 1 and DEV_STAGE < 75 and DEV_STAGE != 99:
                    break
                rr = gc * 64
                if gc == 1:
                    if DEV_TAP == ci:
                        dbg_sbf = sb(f"dbg_sbf{ci}", [128, 512], BF16)

                        @block.vector
                        def _(v, psu=prev_supd, b=b):
                            v.wait_ge(s_v.h, psu)
                            v.tensor_copy(out=dbg_sbf[:], in_=S_bf[:, b * 512:(b + 1) * 512])
                            v.drain().then_inc(s_v.h, 1)
                        s_v.n += 1
                        st["tap"]("sbf_preB", dbg_sbf[:], s_v, s_v.n)
                    for hp in (range(8) if (DEV_STAGE >= 75 or DEV_STAGE == 99) else []):
                        @block.tensor
                        def _(t, hp=hp, b=b, psu=prev_supd, amd=amds[hp]):
                            t.wait_ge(s_v.h, psu)
                            t.wait_ge(s_v.h, amd)
                            for hh in range(2):
                                h = 2 * hp + hh
                                col = h * 64
                                po = p_o[col // 512]
                                pc = col % 512
                                if DEV_NO_OINTER:
                                    li = t.matmul(po[64:128, pc:pc + 64],
                                                  am_g[1][:, hp * 128 + hh * 64: hp * 128 + (hh + 1) * 64],
                                                  v_tm[:, col:col + 64], start=True, stop=True)
                                    continue
                                t.matmul(po[64:128, pc:pc + 64],
                                         am_g[1][:, hp * 128 + hh * 64: hp * 128 + (hh + 1) * 64],
                                         v_tm[:, col:col + 64], start=True, stop=False)
                                rhs_s = (qf_pad[0][:, 0:64] if DEV_FAKE_S else
                                         S_bf[:, b * 512 + hp * 64: b * 512 + hp * 64 + 64])
                                li = t.matmul(po[64:128, pc:pc + 64],
                                              qf_pad[hh][:, hp * 128 + 64: hp * 128 + 128],
                                              rhs_s,
                                              start=False, stop=True)
                            li.then_inc(s_t.h, 1)
                        s_t.n += 1

                if gc == 1 and DEV_STAGE < 76 and DEV_STAGE != 99:
                    break

                @block.tensor
                def _(t, gc=gc, psu=prev_supd, ktp=ktp_done):
                    if gc == 0:
                        t.wait_ge(s_v.h, psu)
                    t.wait_ge(s_v.h, ktp)
                    for hp in (range(8) if not DEV_NO_SMM else []):
                        for hh in range(2):
                            col = (2 * hp + hh) * 64
                            pdst = p_S if hh == 0 else p_S2
                            li = t.matmul(pdst[:, hp * 64:(hp + 1) * 64],
                                          kt_p[gc][:, (2 * hp) * 64:(2 * hp + 2) * 64],
                                          v_tm[:, col:col + 64],
                                          start=True, stop=True)
                    if DEV_NO_SMM:
                        li = t.nop()
                    li.then_inc(s_t.h, 1)
                s_t.n += 1
                omm = s_t.n

                @block.vector
                def _(v, gc=gc, b=b, omm=omm):
                    v.wait_ge(s_t.h, omm)
                    v.tensor_tensor(out=S_tmp[0:64, :], in0=S_all[0:64, b * 512:(b + 1) * 512],
                                    in1=p_S[0:64, :], op=ALU.add)
                    v.tensor_tensor(out=S_tmp[64:128, :], in0=S_all[64:128, b * 512:(b + 1) * 512],
                                    in1=p_S2[64:128, :], op=ALU.add)
                    v.drain()
                    for hp8 in range(8):
                        v.tensor_scalar(out=S_all[:, b * 512 + hp8 * 64: b * 512 + (hp8 + 1) * 64],
                                        in0=S_tmp[:, hp8 * 64:(hp8 + 1) * 64],
                                        scalar1=aC[:, gc * 8 + hp8: gc * 8 + hp8 + 1],
                                        scalar2=None, op0=ALU.mult)
                    v.drain()
                    v.tensor_copy(out=S_bf[:, b * 512:(b + 1) * 512],
                                  in_=S_all[:, b * 512:(b + 1) * 512])
                    v.drain().then_inc(s_v.h, 1)
                s_v.n += 1
                prev_supd = s_v.n

        if DEV_TAP == ci:
            st["tap"]("ln_tm", ln_tm[:], s_v, lnfm_done)
            st["tap"]("q_tm", q_tm[:], s_s, k_done)
            st["tap"]("k_tm", k_tm[:], s_s, k_done)
            st["tap"]("v_tm", v_tm[:], s_v, vevac)
            st["tap"]("fv32", fv32[:], s_s, spd)
            st["tap"]("sp32", sp32[:], s_s, spd)
            st["tap"]("g_tm", g_tm[:], s_s, g_done)
            st["tap"]("e1t", e1t[:], s_s, e_done)
            st["tap"]("e2t", e2t[:], s_s, e_done)
            st["tap"]("aC", aC[:], s_s, e_done)
            st["tap"]("qt_t", qt_t[:], s_g, qkd)
            st["tap"]("kt_t", kt_t[:], s_g, qkd)
            st["tap"]("S_all", S_all[:], s_v, prev_supd)
            st["tap"]("am_g0", am_g[0][:], s_v, prev_supd)
            st["tap"]("am_g1", am_g[1][:], s_v, prev_supd)
            st["tap"]("qf_p1", qf_pad[1][:], s_v, qkf)
            dbg_po = sb(f"dbg_po{ci}", [128, 1024], F32)

            @block.vector
            def _(v, allmm_d=s_t.n):
                v.wait_ge(s_t.h, allmm_d)
                v.tensor_copy(out=dbg_po[:, 0:512], in_=p_o[0][:])
                v.tensor_copy(out=dbg_po[:, 512:1024], in_=p_o[1][:])
                v.drain().then_inc(s_v.h, 1)
            s_v.n += 1
            st["tap"]("p_o_dump", dbg_po[:], s_v, s_v.n)
            st["tap"]("kt_p0", kt_p[0][:], s_g, qkd)
            st["tap"]("kt_p1", kt_p[1][:], s_g, qkd)
            st["tap"]("S_tmp", S_tmp[:], s_v, prev_supd)
            st["tap"]("qf_p0", qf_pad[0][:], s_v, qkf)
            st["tap"]("kf_f", kf_full[:], s_v, qkf)
            st["tap"]("S_bf", S_bf[:], s_v, prev_supd)

        if DEV_STAGE >= 99:
            allmm = s_t.n

            @block.vector
            def _(v, allmm=allmm):
                v.wait_ge(s_t.h, allmm)
                v.tensor_tensor(out=og[:, 0:512], in0=p_o[0][:], in1=g_tm[:, 0:512], op=ALU.mult)
                v.tensor_tensor(out=og[:, 512:1024], in0=p_o[1][:], in1=g_tm[:, 512:1024], op=ALU.mult)
                v.drain().then_inc(s_v.h, 1)
            s_v.n += 1
            ogd = s_v.n

            @bsc.add
            def _(s, ogd=ogd):
                s.wait_ge(s_v.h, ogd)
                s.activation(out=sq32[:], in_=og[:], func=AF.Square)
                s.drain().then_inc(s_s.h, 1)
            s_s.n += 1
            sqd2 = s_s.n

            @block.vector
            def _(v, sqd2=sqd2):
                v.wait_ge(s_s.h, sqd2)
                v.tensor_reduce(out=ms_t[:], in_=sq32[:].rearrange("p (h d) -> p h d", d=64),
                                axis=AX.X, op=ALU.add)
                v.drain()
                v.tensor_scalar(out=ms_t[:], in0=ms_t[:], scalar1=1.0 / 64, scalar2=GLA_EPS,
                                op0=ALU.mult, op1=ALU.add)
                v.drain().then_inc(s_v.h, 1)
            s_v.n += 1
            msd = s_v.n

            @bsc.add
            def _(s, msd=msd):
                s.wait_ge(s_v.h, msd)
                s.activation(out=rstd_h[:], in_=ms_t[:], func=AF.Ln)
                s.drain()
                s.activation(out=rstd_h[:], in_=rstd_h[:], func=AF.Exp, scale=-0.5)
                s.drain().then_inc(s_s.h, 1)
            s_s.n += 1
            msq = s_s.n

            @block.vector
            def _(v, msq=msq):
                v.wait_ge(s_s.h, msq)
                for h16 in range(16):
                    v.tensor_scalar(out=ogn[:, h16 * 64:(h16 + 1) * 64],
                                    in0=og[:, h16 * 64:(h16 + 1) * 64],
                                    scalar1=rstd_h[:, h16:h16 + 1],
                                    scalar2=None, op0=ALU.mult)
                v.drain().then_inc(s_v.h, 1)
            s_v.n += 1
            ognd = s_v.n

            @block.tensor
            def _(t, ognd=ognd):
                t.wait_ge(s_v.h, ognd)
                for c in range(8):
                    li = t.transpose(p_tr[:, c * 128:(c + 1) * 128],
                                     ogn[:, c * 128:(c + 1) * 128], C["ident"])
                li.then_inc(s_t.h, 1)
            s_t.n += 1
            ognT = s_t.n

            @block.vector
            def _(v, ognT=ognT):
                v.wait_ge(s_t.h, ognT)
                v.tensor_copy(out=og_fm[:], in_=p_tr[:])
                v.drain().then_inc(s_v.h, 1)
            s_v.n += 1
            ogfm = s_v.n

            @block.tensor
            def _(t, ogfm=ogfm):
                t.wait_ge(s_v.h, ogfm)
                proj_mm(t, wo, p_pr[0], p_pr[1], src=og_fm).then_inc(s_t.h, 1)
            s_t.n += 1
            womm = s_t.n

        if DEV_STAGE >= 99:
            @block.vector
            def _(v, ci=ci, xi=xi, x1=x1, womm=womm):
                v.wait_ge(s_t.h, womm)
                if ci >= 2:
                    v.wait_ge(s_st.h, chunk_st_done[ci - 2])
                v.tensor_tensor(out=x1[:, 0:512], in0=xi[:, 0:512], in1=p_pr[0][:], op=ALU.add)
                v.tensor_tensor(out=x1[:, 512:1024], in0=xi[:, 512:1024], in1=p_pr[1][:], op=ALU.add)
                v.drain().then_inc(s_v.h, 1)
            s_v.n += 1
            x1d = s_v.n
        else:
            @block.vector
            def _(v, ci=ci, xi=xi, x1=x1):
                v.wait_ge(s_t.h, s_t.n)
                v.wait_ge(s_s.h, s_s.n)
                v.wait_ge(s_g.h, s_g.n)
                if ci >= 2:
                    v.wait_ge(s_st.h, chunk_st_done[ci - 2])
                v.tensor_copy(out=x1[:], in_=xi[:])
                v.drain().then_inc(s_v.h, 1)
            s_v.n += 1
            x1d = s_v.n

        if DEV_TAP == ci:
            st["tap"]("og", og[:], s_v, ogd)
            st["tap"]("ogn", ogn[:], s_v, ognd)
            st["tap"]("x1", x1[:], s_v, x1d)

        @block.sync
        def _(sy, x1=x1, r0=r0, x1d=x1d):
            sy.wait_ge(s_v.h, x1d)
            s_st.dma(sy, st["xres"][r0:r0 + 128, :], x1[:])

        ln1_done = ln_block(x1, ln1t, store_stats_ci=ci, s_wait_v=x1d)

        @block.vector
        def _(v, ln1_done=ln1_done):
            v.reduce_max(out=lnst[:, 7:8], in_=ln1t[:], axis=AX.X, apply_absolute_value=True)
            v.drain()
            v.tensor_tensor(out=mxacc[:, 0:1], in0=mxacc[:, 0:1], in1=lnst[:, 7:8], op=ALU.max)
            v.drain().then_inc(s_v.h, 1)
        s_v.n += 1
        chunk_v_done[ci] = s_v.n

        @block.sync
        def _(sy, r0=r0):
            sy.wait_ge(s_v.h, chunk_v_done[ci])
            s_st.dma(sy, st["ln1_sp"][r0:r0 + 128, :], ln1t[:])
        chunk_st_done[ci] = s_st.n

        @block.vector
        def _(v, ci=ci):
            v.wait_ge(s_st.h, chunk_st_done[ci])
            v.engine_nop().then_inc(s_v.h, 1)
        s_v.n += 1
        chunk_v_done[ci] = s_v.n
        bsc.flush()



def emit_allreduce(st, idx, cols, compute_row, out_cols):
    nc, block = st["nc"], st["block"]
    s_cc, s_t, s_v = st["sems"]["cc"], st["sems"]["t"], st["sems"]["v"]
    mxacc, scal, scrow = st["mxacc"], st["scal"], st["scrow"]
    ps, C_IDF, C_ONESROW = st["ps"], st["C_IDF"], st["C_ONESROW"]
    c0 = cols[0]
    k = 4
    pbank = st["pb"][0]
    scal_base = st.setdefault("scal_next", 0)
    st["scal_next"] = scal_base + out_cols

    s_g = st["sems"]["g"]
    mxpr = st["mxpr"]

    @block.gpsimd
    def _(g, pre=s_v.n):
        g.wait_ge(s_v.h, pre)
        g.wait_ge(s_cc.h, s_cc.n)
        g.dma_start(out=st["cc_stage"][:], in_=mxacc[:, c0:c0 + k]).then_inc(s_cc.h, 16)
        s_cc.n += 16
        g.wait_ge(s_cc.h, s_cc.n)
        g.dma_start(out=mxpr[0:1, 0:512],
                    in_=st["cc_stage"].rearrange("p c -> (p c)").unsqueeze(0)
                    ).then_inc(s_cc.h, 16)
        s_cc.n += 16

    @block.vector
    def _(v, cv=s_cc.n):
        v.wait_ge(s_cc.h, cv)
        for c in range(k):
            v.reduce_max(out=mxpr[0:1, 512 + c:512 + c + 1],
                         in_=mxpr[0:1, c:512:4].unsqueeze(1) if False else
                         mxpr[0:1, 0:512].rearrange("a (p c) -> a p c", c=4)[:, :, c],
                         axis=AX.X)
        v.drain().then_inc(s_v.h, 1)
    s_v.n += 1

    @block.gpsimd
    def _(g, vv=s_v.n):
        g.wait_ge(s_v.h, vv)
        g.wait_ge(s_cc.h, s_cc.n)
        g.dma_start(out=st["cc_in"][idx][:],
                    in_=mxpr[0:1, 512:512 + k]).then_inc(s_cc.h, 16)
        s_cc.n += 16
        g.wait_ge(s_cc.h, s_cc.n)
        g.collective_compute(
            "AllReduce", ALU.max,
            replica_groups=[[0]] if DEV_SOLO else [[0, 1, 2, 3, 4, 5, 6, 7]],
            ins=[st["cc_in"][idx][:]], outs=[st["cc_out"][idx][:]],
        ).then_inc(s_g.h, 1)
        s_g.n += 1
        g.wait_ge(s_g.h, s_g.n)
        g.dma_start(out=scrow[0:1, 0:4], in_=st["cc_out"][idx][:]).then_inc(s_cc.h, 16)
        s_cc.n += 16

    @block.vector
    def _(v, cv=s_cc.n):
        v.wait_ge(s_cc.h, cv)
        compute_row(v, scrow[0:1, 0:4], scrow[0:1, 8:16])
        v.drain().then_inc(s_v.h, 1)
    s_v.n += 1

    s_s_ = st["sems"]["s"]

    @block.tensor
    def _(t, vv=s_v.n, ss=s_s_.n):
        t.wait_ge(s_v.h, vv)
        t.wait_ge(s_s_.h, ss)
        t.matmul(pbank[0:128, 0:out_cols], C_ONESROW[:],
                 scrow[0:1, 8:8 + out_cols], start=True, stop=True).then_inc(s_t.h, 1)
    s_t.n += 1

    @block.vector
    def _(v, tv=s_t.n):
        v.wait_ge(s_t.h, tv)
        v.tensor_copy(out=scal[:, scal_base:scal_base + out_cols],
                      in_=pbank[0:128, 0:out_cols])
        v.drain().then_inc(s_v.h, 1)
    s_v.n += 1
    return scal_base, s_v.n


def emit_wquant(st, w_hbm, n_rows, kdim, wT, blk=32, qmax=7.0):
    nc, block, sb, C = st["nc"], st["block"], st["sb"], st["C"]
    s_gd, s_t, s_v, s_s = (st["sems"][k] for k in ("gd", "t", "v", "s"))
    p_bf = st["pb_bf"]
    nblk = kdim // blk
    strip_rows = 384 if n_rows % 384 == 0 else 128
    rc_per = strip_rows // 128
    nstrip = n_rows // strip_rows
    if "wq_scr" not in st:
        st["wq_scr"] = dict(
            wstrip=sb("wq_strip", [128, 3072], F32),
            sblk=sb("wq_sblk", [128, 96], F32),
            sinv=sb("wq_sinv", [128, 96], F32),
            tscr=sb("wq_tscr", [128, 3072], F32),
            iscr=sb("wq_iscr", [128, 3072], I32),
        )
        st["wq_scr"]["qscr"] = st["wq_scr"]["iscr"][:, 0:1536].bitcast(BF16)
    sc_ = st["wq_scr"]
    wstrip = [sc_["wstrip"], sc_["wstrip"]]
    sblk = sc_["sblk"][:, 0:nblk]
    sinv = sc_["sinv"][:, 0:nblk]
    tscr = sc_["tscr"][:, 0:kdim]
    iscr = sc_["iscr"][:, 0:kdim]
    qscr = sc_["qscr"][:, 0:kdim]
    strip_done = {}

    for sidx in range(nstrip):
        buf = wstrip[sidx % 2]

        @block.gpsimd
        def _(g, sidx=sidx, buf=buf):
            if sidx >= 1:
                g.wait_ge(s_v.h, strip_done[(sidx - 1) * rc_per + rc_per - 1])
            else:
                g.wait_ge(s_v.h, st.get("wq_last_v", 0))
            s_gd.dma(g, buf[:].rearrange("p (c e) -> p c e", c=rc_per),
                     w_hbm[sidx * strip_rows:(sidx + 1) * strip_rows, :]
                     .rearrange("(c p) e -> p c e", p=128))
        ld = s_gd.n

        for rc in range(rc_per):
            f_row = sidx * rc_per + rc
            wt = buf[:, rc * kdim:(rc + 1) * kdim]

            @block.vector
            def _(v, wt=wt, ld=ld):
                v.wait_ge(s_gd.h, ld)
                v.tensor_reduce(out=sblk[:], in_=wt.rearrange("p (b i) -> p b i", i=blk),
                                axis=AX.X, op=ALU.max, apply_absolute_value=True)
                v.drain()
                v.tensor_scalar(out=sblk[:], in0=sblk[:], scalar1=1.0 / qmax,
                                scalar2=1e-12, op0=ALU.mult, op1=ALU.max)
                v.drain()
                v.reciprocal(out=sinv[:], in_=sblk[:])
                v.drain()
                sb_b = sinv[:].unsqueeze(2).broadcast_to([128, nblk, blk])
                v.tensor_tensor(out=tscr[:].rearrange("p (b i) -> p b i", i=blk),
                                in0=wt.rearrange("p (b i) -> p b i", i=blk),
                                in1=sb_b, op=ALU.mult)
                v.drain()
                v.tensor_copy(out=iscr[:], in_=tscr[:])
                v.drain()
                sb_b2 = sblk[:].unsqueeze(2).broadcast_to([128, nblk, blk])
                v.tensor_tensor(out=qscr[:].rearrange("p (b i) -> p b i", i=blk),
                                in0=iscr[:].rearrange("p (b i) -> p b i", i=blk),
                                in1=sb_b2, op=ALU.mult)
                v.drain().then_inc(s_v.h, 1)
            s_v.n += 1
            qd = s_v.n

            for t8 in range(kdim // 1024):
                @block.tensor
                def _(t, qd=qd, t8=t8):
                    t.wait_ge(s_v.h, qd)
                    if t8 > 0:
                        t.wait_ge(s_v.h, s_v.n)
                    for c in range(8):
                        cc = t8 * 8 + c
                        li = t.transpose(p_bf[:, c * 128:(c + 1) * 128],
                                         qscr[:, cc * 128:(cc + 1) * 128], C["ident"])
                    li.then_inc(s_t.h, 1)
                s_t.n += 1
                td = s_t.n

                @block.vector
                def _(v, td=td, f_row=f_row, t8=t8):
                    v.wait_ge(s_t.h, td)
                    v.tensor_copy(
                        out=wT[:].rearrange("p (c r) -> p c r", r=n_rows)[
                            :, t8 * 8:(t8 + 1) * 8, f_row * 128:(f_row + 1) * 128],
                        in_=p_bf[:].rearrange("p (c r) -> p c r", r=128))
                    v.drain().then_inc(s_v.h, 1)
                s_v.n += 1
            strip_done[f_row] = s_v.n
    st["wq_last_v"] = s_v.n
    return s_v.n


def emit_l0f(st):
    bsc = st["bs"]
    bsc.passthrough = "l0f" in BS_OFF
    nc, block, C, W, sb, ps = (st[k] for k in ("nc", "block", "C", "W", "sb", "ps"))
    s_ld, s_gd, s_t, s_v, s_s, s_g, s_st = (st["sems"][k] for k in
                                            ("ld", "gd", "t", "v", "s", "g", "st"))
    scal, mxacc = st["scal"], st["mxacc"]
    st["phase_barrier"]()
    st["arena"].reset()
    st["mxpr"] = st["mxpr"] if "mxpr" in st else st["sb_persist"]("mxpr", [1, 520], F32)
    pb = st["pb"]
    p_bf = st["pb_bf"]

    IN = 3072

    def row1(v, rrow, wrow):
        v.tensor_scalar(out=wrow[0:1, 0:1], in0=rrow[0:1, 0:1], scalar1=1.0 / 32767.0,
                        scalar2=1e-12, op0=ALU.mult, op1=ALU.max)
        v.drain()
        v.reciprocal(out=wrow[0:1, 1:2], in_=wrow[0:1, 0:1])
        v.drain()
    c_s1, ar1_done = emit_allreduce(st, 0, [0], row1, 2)
    S1, IS1 = c_s1, c_s1 + 1

    colm = sb("l0f_colm", [128, 1], F32)
    mark_a = st["arena"].mark()
    xqf = sb("l0f_xqf", [128, 8 * TC], BF16)
    wT = sb("l0f_wT", [128, 8 * IN], BF16)
    wup_done = emit_wquant(st, W["l0f_wup"], IN, E, wT)

    lnin = [sb(f"l0f_lnin{i}", [128, E], BF16) for i in range(2)]
    i32t = sb("l0f_i32", [128, E], I32)
    xq_tm = sb("l0f_xqtm", [128, E], BF16)
    upo = [sb(f"l0f_upo{i}", [128, IN], BF16) for i in range(2)]

    @block.vector
    def _(v):
        v.memset(mxacc[:, 1:3], 0.0)
        v.drain().then_inc(s_v.h, 1)
    s_v.n += 1

    def ffn_mm_pass(src_sp, dst_sp, wT_ap, mx_col, first_pass, wprep_done, scale_col):
        ch_done = {}
        st_done = {}
        ev_done = dict(st.get("ffn_ev_tail", {}))
        rd_done = dict(st.get("ffn_rd_tail", {}))
        for ci in range(_nchunks()):
            r0 = ci * 128
            lb = lnin[ci % 2]
            ub = upo[ci % 2]

            if first_pass:
                @block.sync
                def _(sy, ci=ci, r0=r0, lb=lb):
                    if ci >= 2:
                        sy.wait_ge(s_v.h, ch_done[ci - 2])
                    s_ld.dma(sy, lb[:], st["ln1_sp"][r0:r0 + 128, :])
                ld = s_ld.n

                @block.vector
                def _(v, ci=ci, lb=lb, ld=ld):
                    v.wait_ge(s_ld.h, ld)
                    if ci == 0:
                        v.wait_ge(s_v.h, ar1_done)
                    v.tensor_scalar(out=i32t[:], in0=lb[:], scalar1=scal[:, IS1:IS1 + 1],
                                    scalar2=None, op0=ALU.mult)
                    v.drain()
                    v.tensor_copy(out=xq_tm[:], in_=i32t[:])
                    v.drain().then_inc(s_v.h, 1)
                s_v.n += 1
                xqd = s_v.n

                @block.tensor
                def _(t, xqd=xqd, ci=ci, wprep_done=wprep_done):
                    t.wait_ge(s_v.h, xqd)
                    if ci == 0:
                        t.wait_ge(s_v.h, wprep_done)
                    for c in range(8):
                        li = t.transpose(p_bf[:, c * 128:(c + 1) * 128],
                                         xq_tm[:, c * 128:(c + 1) * 128], C["ident"])
                    li.then_inc(s_t.h, 1)
                s_t.n += 1
                xqT = s_t.n

                @block.vector
                def _(v, xqT=xqT, ci=ci):
                    v.wait_ge(s_t.h, xqT)
                    v.tensor_copy(
                        out=xqf[:].rearrange("p (c r) -> p c r", r=TC)[:, :, ci * 128:(ci + 1) * 128],
                        in_=p_bf[:].rearrange("p (c r) -> p c r", r=128))
                    v.drain().then_inc(s_v.h, 1)
                s_v.n += 1
                ch_done[ci] = s_v.n
            else:
                @block.vector
                def _(v, ci=ci):
                    if ci == 0:
                        v.wait_ge(s_v.h, wprep_done)
                    v.drain().then_inc(s_v.h, 1)
                s_v.n += 1
                ch_done[ci] = s_v.n

            for fs in range(6):
                pm = pb[fs % 2]
                prev_ev = ev_done.get((ci, fs - 2), ev_done.get((ci - 1, 4 + fs), 0))
                prev_rd = rd_done.get((ci, fs - 2), rd_done.get((ci - 1, 4 + fs), 0))

                @block.tensor
                def _(t, ci=ci, fs=fs, pm=pm, dep=ch_done[ci], prev_ev=prev_ev, prev_rd=prev_rd):
                    t.wait_ge(s_v.h, dep)
                    if prev_ev:
                        t.wait_ge(s_s.h, prev_ev)
                    if prev_rd:
                        t.wait_ge(s_v.h, prev_rd)
                    xc = xqf[:].rearrange("p (c r) -> p c r", r=TC)
                    for c in range(8):
                        li = t.matmul(pm[:], xc[:, c, ci * 128:(ci + 1) * 128],
                                      wT_ap[:].rearrange("p (c r) -> p c r", r=IN)[
                                          :, c, fs * 512:(fs + 1) * 512],
                                      start=(c == 0), stop=(c == 7))
                    li.then_inc(s_t.h, 1)
                s_t.n += 1
                mmd = s_t.n

                @bsc.add
                def _(sc, fs=fs, pm=pm, mmd=mmd, ub=ub, ci=ci):
                    sc.wait_ge(s_t.h, mmd)
                    if fs == 0:
                        sc.wait_ge(s_st.h, st_done.get(ci - 2, st.get("ffn_st_tail", 0)))
                    sc.activation(out=ub[:, fs * 512:(fs + 1) * 512], in_=pm[:],
                                  func=AF.Identity, scale=scal[:, scale_col:scale_col + 1])
                    sc.drain().then_inc(s_s.h, 1)
                s_s.n += 1
                ev_done[(ci, fs)] = s_s.n

                @block.vector
                def _(v, fs=fs, pm=pm, mmd=mmd, mx_col=mx_col):
                    v.wait_ge(s_t.h, mmd)
                    v.reduce_max(out=colm[:], in_=pm[:], axis=AX.X, apply_absolute_value=True)
                    v.drain()
                    v.tensor_tensor(out=mxacc[:, mx_col:mx_col + 1],
                                    in0=mxacc[:, mx_col:mx_col + 1], in1=colm[:], op=ALU.max)
                    v.drain().then_inc(s_v.h, 1)
                s_v.n += 1
                rd_done[(ci, fs)] = s_v.n

            @block.sync
            def _(sy, ci=ci, r0=r0, ub=ub, ev=s_s.n):
                sy.wait_ge(s_s.h, ev)
                if ci >= 2:
                    sy.wait_ge(s_st.h, st_done[ci - 2])
                s_st.dma(sy, dst_sp[r0:r0 + 128, :], ub[:])
            st_done[ci] = s_st.n
            bsc.flush()
        last = _nchunks() - 1
        st["ffn_ev_tail"] = {(-1, 4): ev_done[(last, 4)], (-1, 5): ev_done[(last, 5)]}
        st["ffn_rd_tail"] = {(-1, 4): rd_done[(last, 4)], (-1, 5): rd_done[(last, 5)]}
        st["ffn_st_tail"] = st_done[last]
        return s_v.n

    if DEV_TAP is not None:
        st["tap"]("scal_ar1", st["scal"][:, 0:8], st["sems"]["v"], ar1_done)
    up_done = ffn_mm_pass(None, st["up_sp"], wT, 1, True, wup_done, S1)
    if DEV_TAP is not None:
        st["tap"]("wT_up", wT[:], st["sems"]["v"], wup_done)
        st["tap"]("xqf", xqf[:], st["sems"]["v"], up_done)

        @block.gpsimd
        def _(g, dep=up_done):
            g.wait_ge(st["sems"]["v"].h, dep)
            st["sems"]["gd"].dma(g, st["taps"].setdefault(
                "up_c0", nc.dram_tensor("tap_up_c0", [128, 3072], BF16,
                                        kind="ExternalOutput").ap())[:],
                st["up_sp"][0:128, :])

    wgate_done = emit_wquant(st, W["l0f_wgate"], IN, E, wT)
    gate_done = ffn_mm_pass(None, st["gate_sp"], wT, 2, False, wgate_done, S1)

    def row2(v, rrow, wrow):
        v.tensor_scalar(out=wrow[0:1, 0:2], in0=rrow[0:1, 0:2],
                        scalar1=st["scal"][0:1, S1:S1 + 1], scalar2=1.0 / 32767.0,
                        op0=ALU.mult, op1=ALU.mult)
        v.drain()
        v.tensor_scalar(out=wrow[0:1, 0:2], in0=wrow[0:1, 0:2], scalar1=1e-12,
                        scalar2=None, op0=ALU.max)
        v.drain()
        v.reciprocal(out=wrow[0:1, 2:3], in_=wrow[0:1, 0:1])
        v.drain()
        v.reciprocal(out=wrow[0:1, 3:4], in_=wrow[0:1, 1:2])
        v.drain()
    c2, ar2_done = emit_allreduce(st, 1, [1, 2], row2, 4)
    S2, S3, IS2, IS3 = c2, c2 + 1, c2 + 2, c2 + 3

    if DEV_TAP is not None:
        st["tap"]("scal_ar2", st["scal"][:, 0:8], st["sems"]["v"], ar2_done)
        st["tap"]("mxacc_l0f", st["mxacc"][:], st["sems"]["v"], ar2_done)

    st["phase_barrier"]()
    st["arena"].rollback(mark_a)
    st.pop("wq_scr")
    tvals = [sb(f"l0f_tv{i}", [128, IN], BF16) for i in range(2)]
    gin = [sb(f"l0f_gin{i}", [128, IN], BF16) for i in range(2)]
    i32b = sb("l0f_i32b", [128, IN], I32)
    gint = sb("l0f_gint", [128, IN], F32)
    upq = sb("l0f_upq", [128, IN], F32)
    sig32 = upq
    sig_i = i32b

    @block.vector
    def _(v):
        v.memset(mxacc[:, 3:4], 0.0)
        v.drain().then_inc(s_v.h, 1)
    s_v.n += 1

    p3_done = {}
    p3_st = {}
    for ci in range(_nchunks()):
        r0 = ci * 128
        gb = gin[ci % 2]
        tb = tvals[ci % 2]

        @block.sync
        def _(sy, ci=ci, r0=r0, gb=gb):
            if ci >= 2:
                sy.wait_ge(s_v.h, p3_done[ci - 2])
            s_ld.dma(sy, gb[:], st["gate_sp"][r0:r0 + 128, :])
        ld = s_ld.n

        @bsc.add
        def _(sc, ci=ci, gb=gb, ld=ld, ar2_done=ar2_done):
            sc.wait_ge(s_ld.h, ld)
            if ci == 0:
                sc.wait_ge(s_v.h, ar2_done)
            sc.activation(out=i32b[:], in_=gb[:], func=AF.Identity,
                          scale=scal[:, IS3:IS3 + 1])
            sc.drain().then_inc(s_s.h, 1)
        s_s.n += 1
        gq = s_s.n

        @block.vector
        def _(v, gq=gq):
            v.wait_ge(s_s.h, gq)
            v.tensor_copy(out=gint[:], in_=i32b[:])
            v.drain().then_inc(s_v.h, 1)
        s_v.n += 1
        gi = s_v.n

        @bsc.add
        def _(sc, gi=gi):
            sc.wait_ge(s_v.h, gi)
            sc.activation(out=sig32[:], in_=gint[:], func=AF.Sigmoid,
                          scale=scal[:, S3:S3 + 1])
            sc.drain()
            sc.activation(out=sig_i[:], in_=sig32[:], func=AF.Identity,
                          scale=65536.0)
            sc.drain().then_inc(s_s.h, 1)
        s_s.n += 1
        sgd = s_s.n

        @block.vector
        def _(v, sgd=sgd, tb=tb, ci=ci):
            v.wait_ge(s_s.h, sgd)
            v.scalar_tensor_tensor(out=tb[:], in0=sig_i[:], scalar=65535.0,
                                   in1=gint[:], op0=ALU.min, op1=ALU.mult)
            v.drain()
            v.reduce_max(out=colm[:], in_=tb[:], axis=AX.X, apply_absolute_value=True)
            v.drain()
            v.tensor_tensor(out=mxacc[:, 3:4], in0=mxacc[:, 3:4], in1=colm[:], op=ALU.max)
            v.drain().then_inc(s_v.h, 1)
        s_v.n += 1
        p3_done[ci] = s_v.n

        @block.sync
        def _(sy, ci=ci, r0=r0, tb=tb, dep=p3_done[ci]):
            sy.wait_ge(s_v.h, dep)
            if ci >= 2:
                sy.wait_ge(s_st.h, p3_st[ci - 2])
            s_st.dma(sy, st["t_sp"][r0:r0 + 128, :], tb[:])
        p3_st[ci] = s_st.n
        bsc.flush()

    def row3(v, rrow, wrow):
        v.tensor_scalar(out=wrow[0:1, 0:1], in0=rrow[0:1, 0:1],
                        scalar1=st["scal"][0:1, S3:S3 + 1], scalar2=1.0 / 65536.0,
                        op0=ALU.mult, op1=ALU.mult)
        v.drain()
        v.tensor_scalar(out=wrow[0:1, 0:1], in0=wrow[0:1, 0:1], scalar1=1.0 / 32767.0,
                        scalar2=1e-12, op0=ALU.mult, op1=ALU.max)
        v.drain()
        v.reciprocal(out=wrow[0:1, 1:2], in_=wrow[0:1, 0:1])
        v.drain()
        v.tensor_scalar(out=wrow[0:1, 2:3], in0=st["scal"][0:1, S3:S3 + 1],
                        scalar1=1.0 / 65536.0, scalar2=wrow[0:1, 1:2],
                        op0=ALU.mult, op1=ALU.mult)
        v.drain()
    c3, ar3_done = emit_allreduce(st, 2, [3], row3, 3)
    S4, IS4, Q34 = c3, c3 + 1, c3 + 2

    upin = [sb(f"l0f_upin{i}", [128, IN], BF16) for i in range(2)]
    zt = sb("l0f_zt", [128, IN], BF16)
    zf = sb("l0f_zf", [128, IN], BF16)

    @block.vector
    def _(v):
        v.memset(mxacc[:, 4:5], 0.0)
        v.drain().then_inc(s_v.h, 1)
    s_v.n += 1

    p4_done = {}
    p4_st = {}
    for ci in range(_nchunks()):
        r0 = ci * 128
        tb = tvals[ci % 2]
        ub = upin[ci % 2]

        @block.sync
        def _(sy, ci=ci, r0=r0, tb=tb, ub=ub):
            if ci >= 2:
                sy.wait_ge(s_v.h, p4_done[ci - 2])
            s_ld.dma(sy, tb[:], st["t_sp"][r0:r0 + 128, :])
            s_ld.dma(sy, ub[:], st["up_sp"][r0:r0 + 128, :])
        ld = s_ld.n

        @bsc.add
        def _(sc, ci=ci, tb=tb, ld=ld, ar3_done=ar3_done):
            sc.wait_ge(s_ld.h, ld)
            if ci == 0:
                sc.wait_ge(s_v.h, ar3_done)
            sc.activation(out=i32b[:], in_=tb[:], func=AF.Identity,
                          scale=scal[:, Q34:Q34 + 1])
            sc.drain().then_inc(s_s.h, 1)
        s_s.n += 1
        qd1 = s_s.n

        @block.vector
        def _(v, qd1=qd1):
            v.wait_ge(s_s.h, qd1)
            v.tensor_copy(out=gint[:], in_=i32b[:])
            v.drain().then_inc(s_v.h, 1)
        s_v.n += 1
        gd1 = s_v.n

        @bsc.add
        def _(sc, ub=ub, gd1=gd1):
            sc.wait_ge(s_v.h, gd1)
            sc.activation(out=i32b[:], in_=ub[:], func=AF.Identity,
                          scale=scal[:, IS2:IS2 + 1])
            sc.drain().then_inc(s_s.h, 1)
        s_s.n += 1
        qd = s_s.n

        @block.vector
        def _(v, qd=qd, ci=ci):
            v.wait_ge(s_s.h, qd)
            v.tensor_copy(out=upq[:], in_=i32b[:])
            v.drain()
            v.tensor_tensor(out=zt[:], in0=gint[:], in1=upq[:], op=ALU.mult)
            v.drain()
            v.reduce_max(out=colm[:], in_=zt[:], axis=AX.X, apply_absolute_value=True)
            v.drain()
            v.tensor_tensor(out=mxacc[:, 4:5], in0=mxacc[:, 4:5], in1=colm[:], op=ALU.max)
            v.drain().then_inc(s_v.h, 1)
        s_v.n += 1
        zd = s_v.n

        for tb8 in range(3):
            @block.tensor
            def _(t, tb8=tb8, zd=zd, ci=ci, pe=s_s.n):
                t.wait_ge(s_v.h, zd)
                if tb8 > 0:
                    t.wait_ge(s_s.h, pe)
                for c in range(8):
                    cc = tb8 * 8 + c
                    li = t.transpose(p_bf[:, c * 128:(c + 1) * 128],
                                     zt[:, cc * 128:(cc + 1) * 128], C["ident"])
                li.then_inc(s_t.h, 1)
            s_t.n += 1
            td = s_t.n

            @bsc.add
            def _(sc, tb8=tb8, td=td, ci=ci):
                sc.wait_ge(s_t.h, td)
                if tb8 == 0 and ci >= 1:
                    sc.wait_ge(s_st.h, p4_st[ci - 1])
                sc.activation(out=zf[:, tb8 * 1024:(tb8 + 1) * 1024], in_=p_bf[:],
                              func=AF.Identity)
                sc.drain().then_inc(s_s.h, 1)
            s_s.n += 1
        zfd = s_s.n

        @block.vector
        def _(v, zfd=zfd, ci=ci):
            v.wait_ge(s_s.h, zfd)
            v.drain().then_inc(s_v.h, 1)
        s_v.n += 1
        p4_done[ci] = s_v.n

        @block.sync
        def _(sy, ci=ci, r0=r0, dep=zfd):
            sy.wait_ge(s_s.h, dep)
            if ci >= 2:
                sy.wait_ge(s_st.h, p4_st[ci - 2])
            s_st.dma(sy, st["zf_sp"][r0:r0 + 128, :], zf[:])
        p4_st[ci] = s_st.n
        bsc.flush()

    def row4(v, rrow, wrow):
        v.tensor_scalar(out=wrow[0:1, 3:4], in0=st["scal"][0:1, S4:S4 + 1],
                        scalar1=st["scal"][0:1, S2:S2 + 1], scalar2=None, op0=ALU.mult)
        v.drain()
        v.tensor_scalar(out=wrow[0:1, 0:1], in0=rrow[0:1, 0:1],
                        scalar1=wrow[0:1, 3:4], scalar2=1.0 / 32767.0,
                        op0=ALU.mult, op1=ALU.mult)
        v.drain()
        v.tensor_scalar(out=wrow[0:1, 0:1], in0=wrow[0:1, 0:1], scalar1=1e-12,
                        scalar2=None, op0=ALU.max)
        v.drain()
        v.reciprocal(out=wrow[0:1, 1:2], in_=wrow[0:1, 0:1])
        v.drain()
        v.tensor_scalar(out=wrow[0:1, 2:3], in0=wrow[0:1, 3:4],
                        scalar1=wrow[0:1, 1:2], scalar2=None, op0=ALU.mult)
        v.drain()
    c4, ar4_done = emit_allreduce(st, 3, [4], row4, 3)
    S5, IS5, Q45 = c4, c4 + 1, c4 + 2

    if DEV_TAP is not None:
        st["tap"]("scal_ar4", st["scal"][:, 0:16], st["sems"]["v"], ar4_done)

    wdT = sb("l0f_wdT", [128, 24 * E], BF16)
    wdown_done = emit_wquant(st, W["l0f_wdown"], E, IN, wdT)

    x1in = [gin[0][:, 0:2048].bitcast(F32), gin[1][:, 0:2048].bitcast(F32)]
    x2t = [upq[:, 0:1024], upq[:, 1024:2048]]
    obf = zt[:, 0:IN]

    p5_done = {}
    p5_st = {}
    for ci in range(_nchunks()):
        r0 = ci * 128
        zb = upin[ci % 2]
        xb = x1in[ci % 2]
        x2 = x2t[ci % 2]

        @block.sync
        def _(sy, ci=ci, r0=r0, zb=zb, xb=xb):
            if ci >= 2:
                sy.wait_ge(s_v.h, p5_done[ci - 2])
            s_ld.dma(sy, zb[:], st["zf_sp"][r0:r0 + 128, :])
            s_ld.dma(sy, xb[:], st["xres"][r0:r0 + 128, :])
        ld = s_ld.n

        @bsc.add
        def _(sc, ci=ci, zb=zb, ld=ld, ar4_done=ar4_done, wdown_done=wdown_done):
            sc.wait_ge(s_ld.h, ld)
            if ci == 0:
                sc.wait_ge(s_v.h, ar4_done)
                sc.wait_ge(s_v.h, wdown_done)
            sc.activation(out=i32b[:], in_=zb[:], func=AF.Identity,
                          scale=scal[:, Q45:Q45 + 1])
            sc.drain().then_inc(s_s.h, 1)
        s_s.n += 1
        od = s_s.n

        @block.vector
        def _(v, od=od):
            v.wait_ge(s_s.h, od)
            v.tensor_copy(out=obf[:], in_=i32b[:])
            v.drain().then_inc(s_v.h, 1)
        s_v.n += 1
        ob = s_v.n

        for half in range(2):
            pm = pb[half]

            @block.tensor
            def _(t, half=half, pm=pm, ob=ob, ci=ci):
                t.wait_ge(s_v.h, ob)
                if half == 0 and ci > 0:
                    t.wait_ge(s_v.h, p5_done[ci - 1])
                for kc in range(24):
                    li = t.matmul(pm[:], obf[:, kc * 128:(kc + 1) * 128],
                                  wdT[:].rearrange("p (c r) -> p c r", r=E)[
                                      :, kc, half * 512:(half + 1) * 512],
                                  start=(kc == 0), stop=(kc == 23))
                li.then_inc(s_t.h, 1)
            s_t.n += 1
        dmm = s_t.n

        @block.vector
        def _(v, dmm=dmm, xb=xb, x2=x2):
            v.wait_ge(s_t.h, dmm)
            for half in range(2):
                v.scalar_tensor_tensor(out=x2[:, half * 512:(half + 1) * 512],
                                       in0=pb[half][:], scalar=scal[:, S5:S5 + 1],
                                       in1=xb[:, half * 512:(half + 1) * 512],
                                       op0=ALU.mult, op1=ALU.add)
            v.drain().then_inc(s_v.h, 1)
        s_v.n += 1
        p5_done[ci] = s_v.n

        @block.sync
        def _(sy, ci=ci, r0=r0, x2=x2, dep=p5_done[ci]):
            sy.wait_ge(s_v.h, dep)
            if ci >= 2:
                sy.wait_ge(s_st.h, p5_st[ci - 2])
            s_st.dma(sy, st["xres"][r0:r0 + 128, :], x2[:])
        p5_st[ci] = s_st.n
        bsc.flush()


def emit_l1a(st):
    bsc = st["bs"]
    bsc.passthrough = "l1a" in BS_OFF
    nc, block, C, W, sb, ps = (st[k] for k in ("nc", "block", "C", "W", "sb", "ps"))
    s_ld, s_gd, s_t, s_v, s_s, s_g, s_st = (st["sems"][k] for k in
                                            ("ld", "gd", "t", "v", "s", "g", "st"))
    mxacc = st["mxacc"]
    st["phase_barrier"]()
    st["arena"].reset()
    pb = st["pb"]
    p_bf = st["pb_bf"]

    wq = sb("l1_wq_s", [128, 8 * E], BF16)
    wk = sb("l1_wk_s", [128, 8 * E], BF16)
    wv = sb("l1_wv_s", [128, 8 * E], BF16)
    wout = sb("l1_wout_s", [128, 8 * E], BF16)
    wgq = sb("l1_wgq_s", [128, 8 * 512], BF16)
    wgk = sb("l1_wgk_s", [128, 8 * 512], BF16)
    wgv = sb("l1_wgv_s", [128, 8 * 512], BF16)
    rout = sb("l1_rout_s", [128, 4 * E], BF16)
    glw_sb = sb("l1_glw_s", [128, 32], BF16)

    @block.gpsimd
    def _(g):
        for wsb, hbm in ((wq, W["l1_wq"]), (wk, W["l1_wk"]), (wv, W["l1_wv"]),
                         (wout, W["l1_wout"])):
            s_gd.dma(g, wsb[:].rearrange("p (c e) -> p c e", c=8),
                     hbm.rearrange("(c p) e -> p c e", p=128))
        for wsb, hbm in ((wgq, W["l1_wgq"]), (wgk, W["l1_wgk"]), (wgv, W["l1_wgv"])):
            s_gd.dma(g, wsb[:].rearrange("p (c e) -> p c e", c=8),
                     hbm.rearrange("(c p) e -> p c e", p=128))
        s_gd.dma(g, rout[:].rearrange("p (c e) -> p c e", c=4),
                 W["l1_rout"].rearrange("(c p) e -> p c e", p=128))
        s_gd.dma(g, glw_sb[:], W["l1_glw_fm"][:])
    w_loaded = s_gd.n

    x2in = [sb(f"l1a_x2in{i}", [128, E], F32) for i in range(2)]
    x3t = [sb(f"l1a_x3t{i}", [128, E], F32) for i in range(2)]
    lnst = sb("l1a_lnst", [128, 8], F32)
    sq32 = sb("l1a_sq32", [128, E], F32)
    ln_tm = sb("l1a_lntm", [128, E], BF16)
    ln_fm = sb("l1a_lnfm", [128, E], BF16)
    q_tm = sb("l1a_qtm", [128, E], BF16)
    k_tm = sb("l1a_ktm", [128, E], BF16)
    gq_tm = sb("l1a_gqtm", [128, 512], BF16)
    gkx_tm = sb("l1a_gkxtm", [128, 512], BF16)
    gvx_tm = sb("l1a_gvxtm", [128, 512], BF16)
    q_pad = [sb(f"l1a_qpad{i}", [128, 8 * 256], BF16) for i in range(2)]
    k_ring = sb("l1a_kring", [128, 8 * 512], BF16)
    v_ring = [sb(f"l1a_vring{i}", [128, E], BF16) for i in range(4)]
    gq_pad = [sb(f"l1a_gqpad{i}", [128, 4 * 128], BF16) for i in range(2)]
    gkx_fm = sb("l1a_gkxfm", [128, 4 * 128], BF16)
    gk_fm = sb("l1a_gkfm", [128, 4 * 64], BF16)
    gv_fm = sb("l1a_gvfm", [128, 4 * 64], BF16)
    gv_tm = sb("l1a_gvtm", [128, 4 * 128], BF16)
    plt_pad = sb("l1a_plt", [128, 8 * 128], BF16)
    pl_sb = sb("l1a_plsb", [128, 512], BF16)
    plm_sb = sb("l1a_plmsb", [128, 512], BF16)
    mrow = sb("l1a_mrow", [128, 16], F32)
    rsl = sb("l1a_rsl", [128, 8], F32)
    rsw = sb("l1a_rsw", [128, 16], F32)
    rem512 = sb("l1a_rem512", [128, 512], BF16)
    rem_fm = sb("l1a_remfm", [128, 512], BF16)
    rem2_tm = [sb(f"l1a_rem2{i}", [128, E], BF16) for i in range(2)]
    glr_t = [sb(f"l1a_glr{i}", [128, 32], F32) for i in range(2)]
    p_sb = sb("l1a_psb", [128, 512], BF16)
    pm_sb = sb("l1a_pmsb", [128, 512], BF16)
    pt_sb = sb("l1a_ptsb", [128, 512], BF16)
    locg = [sb(f"l1a_locg{i}", [128, E], BF16) for i in range(2)]
    out_tm = sb("l1a_outtm", [128, E], BF16)
    out_fm = sb("l1a_outfm", [128, E], BF16)
    ln3t = sb("l1a_ln3t", [128, E], BF16)

    @block.vector
    def _(v, wl=w_loaded):
        v.wait_ge(s_gd.h, wl)
        for tl in (gk_fm, gv_fm, gv_tm, plt_pad, q_pad[0], q_pad[1],
                   gq_pad[0], gq_pad[1]):
            v.memset(tl[:], 0.0)
        v.memset(mxacc[:, 5:12], -1e30)
        v.drain().then_inc(s_v.h, 1)
    s_v.n += 1
    init_done = s_v.n

    chunk_done = {}
    ln3_st = {}
    pbf_free = [0]
    pr_free = [0]

    def ln_block1(xi, out_bf16, s_wait_v=None, st_wait=0):
        @block.vector
        def _(v):
            v.reduce_sum(out=lnst[:, 0:1], in_=xi[:], axis=AX.X)
            v.drain().then_inc(s_v.h, 1)
        s_v.n += 1

        @bsc.add
        def _(s2, s_wait_v=s_wait_v):
            if s_wait_v is not None:
                s2.wait_ge(s_v.h, s_wait_v)
            s2.activation(out=sq32[:], in_=xi[:], func=AF.Square)
            s2.drain().then_inc(s_s.h, 1)
        s_s.n += 1
        sqd = s_s.n

        @block.vector
        def _(v, sqd=sqd):
            v.wait_ge(s_s.h, sqd)
            v.reduce_sum(out=lnst[:, 1:2], in_=sq32[:], axis=AX.X)
            v.drain()
            v.tensor_scalar(out=lnst[:, 2:3], in0=lnst[:, 0:1], scalar1=1.0 / E,
                            scalar2=None, op0=ALU.mult)
            v.drain()
            v.tensor_scalar(out=lnst[:, 3:4], in0=lnst[:, 2:3], scalar1=lnst[:, 2:3],
                            scalar2=None, op0=ALU.mult)
            v.drain()
            v.scalar_tensor_tensor(out=lnst[:, 4:5], in0=lnst[:, 1:2], scalar=1.0 / E,
                                   in1=lnst[:, 3:4], op0=ALU.mult, op1=ALU.subtract)
            v.drain()
            v.tensor_scalar(out=lnst[:, 4:5], in0=lnst[:, 4:5], scalar1=LN_EPS,
                            scalar2=None, op0=ALU.add)
            v.drain().then_inc(s_v.h, 1)
        s_v.n += 1
        vard = s_v.n

        @bsc.add
        def _(s2, vard=vard):
            s2.wait_ge(s_v.h, vard)
            s2.activation(out=lnst[:, 5:6], in_=lnst[:, 4:5], func=AF.Ln)
            s2.drain()
            s2.activation(out=lnst[:, 6:7], in_=lnst[:, 5:6], func=AF.Exp, scale=-0.5)
            s2.drain().then_inc(s_s.h, 1)
        s_s.n += 1
        sqrtd = s_s.n

        @block.vector
        def _(v, sqrtd=sqrtd, st_wait=st_wait):
            v.wait_ge(s_s.h, sqrtd)
            if st_wait:
                v.wait_ge(s_st.h, st_wait)
            v.tensor_scalar(out=out_bf16[:], in0=xi[:], scalar1=lnst[:, 2:3],
                            scalar2=lnst[:, 6:7], op0=ALU.subtract, op1=ALU.mult)
            v.drain().then_inc(s_v.h, 1)
        s_v.n += 1
        return s_v.n

    def transpose_n(src_bf16, n, dst_writes, dep_v, s_wait=None):
        @block.tensor
        def _(t, dep=dep_v, pf=pbf_free[0], sw=s_wait):
            t.wait_ge(s_v.h, dep)
            t.wait_ge(s_v.h, pf)
            if sw is not None:
                t.wait_ge(s_s.h, sw)
            for c in range(n):
                li = t.transpose(p_bf[:, c * 128:(c + 1) * 128],
                                 src_bf16[:, c * 128:(c + 1) * 128], C["ident"])
            li.then_inc(s_t.h, 1)
        s_t.n += 1
        td = s_t.n

        @block.vector
        def _(v, td=td):
            v.wait_ge(s_t.h, td)
            dst_writes(v)
            v.drain().then_inc(s_v.h, 1)
        s_v.n += 1
        pbf_free[0] = s_v.n
        return s_v.n

    for b in range(B_PER_CORE):
        for blk in range(8):
            if b * CPB + blk * 2 + 2 > _nchunks():
                continue
            chunk_vals = {}
            for a in range(2):
                qc = blk * 2 + a
                ci = b * CPB + qc
                r0 = ci * 128
                xi = x2in[a]

                @block.sync
                def _(sy, ci=ci, r0=r0, xi=xi):
                    if ci >= 2:
                        sy.wait_ge(s_v.h, chunk_done[ci - 2])
                    s_ld.dma(sy, xi[:], st["xres"][r0:r0 + 128, :])
                ldx = s_ld.n

                @block.vector
                def _(v, ci=ci, ldx=ldx):
                    v.wait_ge(s_ld.h, ldx)
                    if ci == 0:
                        v.wait_ge(s_v.h, init_done)
                    v.drain().then_inc(s_v.h, 1)
                s_v.n += 1

                @bsc.add
                def _(s2, ldx=ldx):
                    s2.wait_ge(s_ld.h, ldx)
                    s2.drain().then_inc(s_s.h, 1)
                s_s.n += 1

                lnd = ln_block1(xi, ln_tm)
                lnfm = transpose_n(ln_tm, 8,
                                   lambda v: v.tensor_copy(out=ln_fm[:], in_=p_bf[:]),
                                   lnd)

                def proj(wsb, width, dst, dep):
                    @block.tensor
                    def _(t, dep=dep, f0=pr_free[0], wsb=wsb, width=width):
                        t.wait_ge(s_v.h, dep)
                        t.wait_ge(s_v.h, f0)
                        half = width // 2
                        for c in range(8):
                            t.matmul(pb[0][:, 0:half], ln_fm[:, c * 128:(c + 1) * 128],
                                     wsb[:, c * width: c * width + half],
                                     start=(c == 0), stop=(c == 7))
                        for c in range(8):
                            li = t.matmul(pb[1][:, 0:half], ln_fm[:, c * 128:(c + 1) * 128],
                                          wsb[:, c * width + half: (c + 1) * width],
                                          start=(c == 0), stop=(c == 7))
                        li.then_inc(s_t.h, 1)
                    s_t.n += 1
                    mmd = s_t.n

                    @block.vector
                    def _(v, mmd=mmd, dst=dst, width=width):
                        v.wait_ge(s_t.h, mmd)
                        half = width // 2
                        v.tensor_copy(out=dst[:, 0:half], in_=pb[0][:, 0:half])
                        v.tensor_copy(out=dst[:, half:width], in_=pb[1][:, 0:half])
                        v.drain().then_inc(s_v.h, 1)
                    s_v.n += 1
                    pr_free[0] = s_v.n
                    return s_v.n

                qd = proj(wq, E, q_tm, lnfm)
                kd = proj(wk, E, k_tm, qd)
                vd = proj(wv, E, v_ring[qc % 4], kd)
                gqd = proj(wgq, 512, gq_tm, vd)
                gkd = proj(wgk, 512, gkx_tm, gqd)
                gvd = proj(wgv, 512, gvx_tm, gkd)

                def wr_qpad(v, a=a):
                    v.tensor_copy(
                        out=q_pad[0][0:64, :].rearrange("p (c r) -> p c r", r=256)[
                            :, :, a * 128:(a + 1) * 128],
                        in_=p_bf[0:64, :].rearrange("p (c r) -> p c r", r=128))
                    v.tensor_copy(
                        out=q_pad[1][64:128, :].rearrange("p (c r) -> p c r", r=256)[
                            :, :, a * 128:(a + 1) * 128],
                        in_=p_bf[64:128, :].rearrange("p (c r) -> p c r", r=128))
                transpose_n(q_tm, 8, wr_qpad, qd)

                def wr_kring(v, blk=blk, a=a):
                    o0 = (blk % 2) * 256 + a * 128
                    v.tensor_copy(
                        out=k_ring[:].rearrange("p (c r) -> p c r", r=512)[
                            :, :, o0:o0 + 128],
                        in_=p_bf[:].rearrange("p (c r) -> p c r", r=128))
                transpose_n(k_tm, 8, wr_kring, kd)

                def wr_gqpad(v):
                    v.tensor_copy(out=gq_pad[0][0:64, :], in_=p_bf[0:64, 0:512])
                    v.tensor_copy(out=gq_pad[1][64:128, :], in_=p_bf[64:128, 0:512])
                gqpd = transpose_n(gq_tm, 4, wr_gqpad, gqd)

                gkfd = transpose_n(gkx_tm, 4,
                                   lambda v: v.tensor_copy(out=gkx_fm[:], in_=p_bf[:, 0:512]),
                                   gkd)

                @block.vector
                def _(v, qc=qc, dep=gkfd):
                    v.wait_ge(s_v.h, dep)
                    v.tensor_reduce(out=mrow[:, 0:16],
                                    in_=gkx_fm[:].rearrange("p (l i) -> p l i", i=32),
                                    axis=AX.X, op=ALU.add)
                    v.drain()
                    v.tensor_scalar(
                        out=gk_fm[:].rearrange("p (c n) -> p c n", n=64)[
                            :, :, qc * 4:(qc + 1) * 4],
                        in0=mrow[:, 0:16].rearrange("p (c g) -> p c g", g=4),
                        scalar1=1.0 / 32, scalar2=None, op0=ALU.mult)
                    v.drain().then_inc(s_v.h, 1)
                s_v.n += 1
                gk_done = s_v.n

                gvfd = transpose_n(gvx_tm, 4,
                                   lambda v: v.tensor_copy(out=gkx_fm[:], in_=p_bf[:, 0:512]),
                                   gvd)

                @block.vector
                def _(v, qc=qc, dep=gvfd):
                    v.wait_ge(s_v.h, dep)
                    v.tensor_reduce(out=mrow[:, 0:16],
                                    in_=gkx_fm[:].rearrange("p (l i) -> p l i", i=32),
                                    axis=AX.X, op=ALU.add)
                    v.drain()
                    v.tensor_scalar(
                        out=gv_fm[:].rearrange("p (c n) -> p c n", n=64)[
                            :, :, qc * 4:(qc + 1) * 4],
                        in0=mrow[:, 0:16].rearrange("p (c g) -> p c g", g=4),
                        scalar1=1.0 / 32, scalar2=None, op0=ALU.mult)
                    v.drain().then_inc(s_v.h, 1)
                s_v.n += 1
                gv_done = s_v.n

                @block.tensor
                def _(t, dep=gv_done, pf=pbf_free[0]):
                    t.wait_ge(s_v.h, dep)
                    t.wait_ge(s_v.h, pf)
                    for c in range(4):
                        li = t.transpose(p_bf[0:64, c * 128:(c + 1) * 128],
                                         gv_fm[:, c * 64:(c + 1) * 64], C["ident"])
                    li.then_inc(s_t.h, 1)
                s_t.n += 1
                gvtd_t = s_t.n

                @block.vector
                def _(v, td=gvtd_t):
                    v.wait_ge(s_t.h, td)
                    v.tensor_copy(out=gv_tm[0:64, :], in_=p_bf[0:64, 0:512])
                    v.drain().then_inc(s_v.h, 1)
                s_v.n += 1
                pbf_free[0] = s_v.n
                gvt_done = s_v.n

                @block.tensor
                def _(t, dep=gk_done, dep2=gqpd, f=pr_free[0]):
                    t.wait_ge(s_v.h, dep)
                    t.wait_ge(s_v.h, dep2)
                    for lh in range(8):
                        li = t.matmul(pb[2][:, lh * 64:(lh + 1) * 64],
                                      gq_pad[lh % 2][:, (lh // 2) * 128:(lh // 2 + 1) * 128],
                                      gk_fm[:, (lh // 2) * 64:(lh // 2) * 64 + 64],
                                      start=True, stop=True)
                    li.then_inc(s_t.h, 1)
                s_t.n += 1
                lscd = s_t.n

                @block.vector
                def _(v, d=lscd):
                    v.wait_ge(s_t.h, d)
                    v.reduce_max(out=mrow[:, 0:1], in_=pb[2][:], axis=AX.X)
                    v.drain()
                    v.tensor_scalar(out=mrow[:, 1:2], in0=mrow[:, 0:1], scalar1=-0.125,
                                    scalar2=None, op0=ALU.mult)
                    v.drain().then_inc(s_v.h, 1)
                s_v.n += 1
                lmx = s_v.n

                @bsc.add
                def _(s2, d=lmx):
                    s2.wait_ge(s_v.h, d)
                    s2.activation(out=pl_sb[:], in_=pb[2][:], func=AF.Exp,
                                  scale=0.125, bias=mrow[:, 1:2])
                    s2.drain().then_inc(s_s.h, 1)
                s_s.n += 1
                lexp = s_s.n

                @block.vector
                def _(v, d=lexp, qc=qc):
                    v.wait_ge(s_s.h, d)
                    lm = C["lmask"][:, qc * 64:(qc + 1) * 64]
                    v.tensor_tensor(out=plm_sb[:].rearrange("p (l n) -> p l n", n=64),
                                    in0=pl_sb[:].rearrange("p (l n) -> p l n", n=64),
                                    in1=lm.unsqueeze(1).broadcast_to([128, 8, 64]),
                                    op=ALU.mult)
                    v.drain()
                    v.tensor_reduce(out=rsl[:], in_=plm_sb[:].rearrange(
                        "p (l n) -> p l n", n=64), axis=AX.X, op=ALU.add)
                    v.drain()
                    v.reciprocal(out=rsl[:], in_=rsl[:])
                    v.drain().then_inc(s_v.h, 1)
                s_v.n += 1
                lsum = s_v.n

                @block.tensor
                def _(t, d=lsum, pf=pbf_free[0]):
                    t.wait_ge(s_v.h, d)
                    t.wait_ge(s_v.h, pf)
                    for lh in range(8):
                        li = t.transpose(p_bf[0:64, lh * 128:(lh + 1) * 128],
                                         plm_sb[:, lh * 64:(lh + 1) * 64], C["ident"])
                    li.then_inc(s_t.h, 1)
                s_t.n += 1
                pltd = s_t.n

                @block.vector
                def _(v, d=pltd):
                    v.wait_ge(s_t.h, d)
                    v.tensor_copy(out=plt_pad[0:64, :], in_=p_bf[0:64, :])
                    v.drain().then_inc(s_v.h, 1)
                s_v.n += 1
                pbf_free[0] = s_v.n
                plte = s_v.n

                @block.tensor
                def _(t, d=plte, d2=gvt_done):
                    t.wait_ge(s_v.h, d)
                    t.wait_ge(s_v.h, d2)
                    for lh in range(8):
                        li = t.matmul(pb[3][:, lh * 64:(lh + 1) * 64],
                                      plt_pad[:, lh * 128:(lh + 1) * 128],
                                      gv_tm[:, (lh // 2) * 128 + (lh % 2) * 64:
                                            (lh // 2) * 128 + (lh % 2) * 64 + 64],
                                      start=True, stop=True)
                    li.then_inc(s_t.h, 1)
                s_t.n += 1
                remd = s_t.n

                @block.vector
                def _(v, d=remd):
                    v.wait_ge(s_t.h, d)
                    for lh in range(8):
                        v.tensor_scalar(out=rem512[:, lh * 64:(lh + 1) * 64],
                                        in0=pb[3][:, lh * 64:(lh + 1) * 64],
                                        scalar1=rsl[:, lh:lh + 1], scalar2=None,
                                        op0=ALU.mult)
                    v.drain().then_inc(s_v.h, 1)
                s_v.n += 1
                rem512d = s_v.n

                remfmd = transpose_n(rem512, 4,
                                     lambda v: v.tensor_copy(out=rem_fm[:],
                                                             in_=p_bf[:, 0:512]),
                                     rem512d)

                @block.tensor
                def _(t, d=remfmd, f=pr_free[0]):
                    t.wait_ge(s_v.h, d)
                    t.wait_ge(s_v.h, f)
                    for half in range(2):
                        for c in range(4):
                            li = t.matmul(pb[half][:],
                                          rem_fm[:, c * 128:(c + 1) * 128],
                                          rout[:, c * E + half * 512: c * E + (half + 1) * 512],
                                          start=(c == 0), stop=(c == 3))
                    li.then_inc(s_t.h, 1)
                s_t.n += 1
                rem2d = s_t.n

                @block.vector
                def _(v, d=rem2d, a=a):
                    v.wait_ge(s_t.h, d)
                    v.tensor_copy(out=rem2_tm[a][:, 0:512], in_=pb[0][:])
                    v.tensor_copy(out=rem2_tm[a][:, 512:1024], in_=pb[1][:])
                    v.drain().then_inc(s_v.h, 1)
                s_v.n += 1
                pr_free[0] = s_v.n

                @block.tensor
                def _(t, d=pr_free[0], dl=lexp, a=a):
                    t.wait_ge(s_s.h, dl)
                    t.wait_ge(s_v.h, lsum)
                    for h in range(16):
                        t.matmul(pb[2][:, h:h + 1],
                                 q_pad[h % 2][:, (h // 2) * 256 + a * 128:
                                              (h // 2) * 256 + (a + 1) * 128],
                                 glw_sb[:, h:h + 1], start=True, stop=True)
                    for h in range(16):
                        li = t.matmul(pb[2][:, 16 + h:17 + h],
                                      q_pad[h % 2][:, (h // 2) * 256 + a * 128:
                                                   (h // 2) * 256 + (a + 1) * 128],
                                      glw_sb[:, 16 + h:17 + h], start=True, stop=True)
                    li.then_inc(s_t.h, 1)
                s_t.n += 1
                gld = s_t.n

                @bsc.add
                def _(s2, d=gld, a=a):
                    s2.wait_ge(s_t.h, d)
                    s2.activation(out=glr_t[a][:], in_=pb[2][:, 0:32], func=AF.Sigmoid)
                    s2.drain().then_inc(s_s.h, 1)
                s_s.n += 1
                chunk_vals[a] = dict(glr=s_s.n)

            nkc = 2 if blk == 0 else 4
            kw = nkc * 128
            for a in range(2):
                loc_started = {0: False, 1: False}
                for h in range(16):
                    @block.tensor
                    def _(t, h=h, a=a, blk=blk, sexp=s_s.n, svx=s_v.n):
                        if h == 0:
                            t.wait_ge(s_s.h, sexp)
                        else:
                            t.wait_ge(s_s.h, wexp_d[h - 1])
                            t.wait_ge(s_v.h, wmax_d[h - 1])
                        if blk == 0:
                            li = t.matmul(pb[4][:, 0:256],
                                          q_pad[h % 2][:, (h // 2) * 256 + a * 128:
                                                       (h // 2) * 256 + (a + 1) * 128],
                                          k_ring[:].rearrange("p (c r) -> p c r", r=512)[
                                              :, h // 2, 0:256],
                                          start=True, stop=True)
                        else:
                            po = ((blk - 1) % 2) * 256
                            co = (blk % 2) * 256
                            t.matmul(pb[4][:, 0:256],
                                     q_pad[h % 2][:, (h // 2) * 256 + a * 128:
                                                  (h // 2) * 256 + (a + 1) * 128],
                                     k_ring[:].rearrange("p (c r) -> p c r", r=512)[
                                         :, h // 2, po:po + 256],
                                     start=True, stop=True)
                            li = t.matmul(pb[4][:, 256:512],
                                          q_pad[h % 2][:, (h // 2) * 256 + a * 128:
                                                       (h // 2) * 256 + (a + 1) * 128],
                                          k_ring[:].rearrange("p (c r) -> p c r", r=512)[
                                              :, h // 2, co:co + 256],
                                          start=True, stop=True)
                        li.then_inc(s_t.h, 1)
                    s_t.n += 1
                    wsc = s_t.n

                    @block.vector
                    def _(v, d=wsc, kw=kw):
                        v.wait_ge(s_t.h, d)
                        v.reduce_max(out=mrow[:, 2:3], in_=pb[4][:, 0:kw], axis=AX.X)
                        v.drain()
                        v.tensor_scalar(out=mrow[:, 3:4], in0=mrow[:, 2:3], scalar1=-0.125,
                                        scalar2=None, op0=ALU.mult).then_inc(s_v.h, 1)
                    s_v.n += 1
                    if h == 0:
                        wmax_d = {}
                        wexp_d = {}
                    wmax_d[h] = s_v.n

                    @bsc.add
                    def _(s2, d=wmax_d[h], kw=kw):
                        s2.wait_ge(s_v.h, d)
                        s2.activation(out=p_sb[:, 0:kw], in_=pb[4][:, 0:kw], func=AF.Exp,
                                      scale=0.125, bias=mrow[:, 3:4])
                        s2.drain().then_inc(s_s.h, 1)
                    s_s.n += 1
                    wexp_d[h] = s_s.n

                    @block.vector
                    def _(v, d=wexp_d[h], a=a, h=h, blk=blk, kw=kw):
                        v.wait_ge(s_s.h, d)
                        msk = (C["bmask0"] if a == 0 else C["bmask1"]) if blk == 0 else \
                              (C["wmask0"] if a == 0 else C["wmask1"])
                        v.tensor_tensor(out=pm_sb[:, 0:kw], in0=p_sb[:, 0:kw],
                                        in1=msk[:, 0:kw], op=ALU.mult)
                        v.drain()
                        v.reduce_sum(out=mrow[:, 4:5], in_=pm_sb[:, 0:kw], axis=AX.X)
                        v.drain()
                        v.reciprocal(out=rsw[:, h:h + 1], in_=mrow[:, 4:5])
                        v.drain().then_inc(s_v.h, 1)
                    s_v.n += 1
                    wmsk = s_v.n

                    @block.tensor
                    def _(t, d=wmsk, pf=pbf_free[0], nkc=nkc):
                        t.wait_ge(s_v.h, d)
                        t.wait_ge(s_v.h, pf)
                        for c in range(nkc):
                            li = t.transpose(p_bf[:, c * 128:(c + 1) * 128],
                                             pm_sb[:, c * 128:(c + 1) * 128], C["ident"])
                        li.then_inc(s_t.h, 1)
                    s_t.n += 1
                    ptd = s_t.n

                    @block.vector
                    def _(v, d=ptd, nkc=nkc):
                        v.wait_ge(s_t.h, d)
                        v.tensor_copy(out=pt_sb[:, 0:nkc * 128],
                                      in_=p_bf[:, 0:nkc * 128]).then_inc(s_v.h, 1)
                    s_v.n += 1
                    pbf_free[0] = s_v.n
                    pte = s_v.n

                    @block.tensor
                    def _(t, d=pte, h=h, blk=blk, nkc=nkc):
                        t.wait_ge(s_v.h, d)
                        bank = 5 + h // 8
                        col = (h % 8) * 64
                        first = not loc_started[h // 8]
                        loc_started[h // 8] = True
                        for c in range(nkc):
                            if blk == 0:
                                vr = v_ring[(2 * blk + c) % 4]
                            else:
                                vr = v_ring[(2 * (blk - 1) + c) % 4]
                            li = t.matmul(pb[bank][:, col:col + 64],
                                          pt_sb[:, c * 128:(c + 1) * 128],
                                          vr[:, h * 64:(h + 1) * 64],
                                          start=(c == 0), stop=(c == nkc - 1))
                        li.then_inc(s_t.h, 1)
                    s_t.n += 1
                pv_done = s_t.n

                @block.vector
                def _(v, d=pv_done, a=a, gd=chunk_vals[a]["glr"]):
                    v.wait_ge(s_t.h, d)
                    v.wait_ge(s_s.h, gd)
                    for h in range(16):
                        bank = 5 + h // 8
                        col = (h % 8) * 64
                        v.tensor_scalar(out=locg[a][:, h * 64:(h + 1) * 64],
                                        in0=pb[bank][:, col:col + 64],
                                        scalar1=rsw[:, h:h + 1],
                                        scalar2=glr_t[a][:, h:h + 1],
                                        op0=ALU.mult, op1=ALU.mult)
                    v.drain().then_inc(s_v.h, 1)
                s_v.n += 1
                locg_d = s_v.n

                @block.vector
                def _(v, d=locg_d, a=a):
                    for h in range(16):
                        v.scalar_tensor_tensor(out=out_tm[:, h * 64:(h + 1) * 64],
                                               in0=rem2_tm[a][:, h * 64:(h + 1) * 64],
                                               scalar=glr_t[a][:, 16 + h:17 + h],
                                               in1=locg[a][:, h * 64:(h + 1) * 64],
                                               op0=ALU.mult, op1=ALU.add)
                    v.drain().then_inc(s_v.h, 1)
                s_v.n += 1
                outd = s_v.n

                outfmd = transpose_n(out_tm, 8,
                                     lambda v: v.tensor_copy(out=out_fm[:], in_=p_bf[:]),
                                     outd)

                @block.tensor
                def _(t, d=outfmd, f=pr_free[0]):
                    t.wait_ge(s_v.h, d)
                    t.wait_ge(s_v.h, f)
                    for c in range(8):
                        t.matmul(pb[0][:], out_fm[:, c * 128:(c + 1) * 128],
                                 wout[:, c * E: c * E + 512],
                                 start=(c == 0), stop=(c == 7))
                    for c in range(8):
                        li = t.matmul(pb[1][:], out_fm[:, c * 128:(c + 1) * 128],
                                      wout[:, c * E + 512: (c + 1) * E],
                                      start=(c == 0), stop=(c == 7))
                    li.then_inc(s_t.h, 1)
                s_t.n += 1
                woutd = s_t.n

                ci = b * CPB + blk * 2 + a
                r0 = ci * 128
                x3 = x3t[a]

                @block.vector
                def _(v, d=woutd, a=a, x3=x3, ci=ci):
                    v.wait_ge(s_t.h, d)
                    if ci >= 2:
                        v.wait_ge(s_st.h, ln3_st.get(ci - 2, 0))
                    v.tensor_tensor(out=x3[:, 0:512], in0=x2in[a][:, 0:512],
                                    in1=pb[0][:], op=ALU.add)
                    v.tensor_tensor(out=x3[:, 512:1024], in0=x2in[a][:, 512:1024],
                                    in1=pb[1][:], op=ALU.add)
                    v.drain().then_inc(s_v.h, 1)
                s_v.n += 1
                pr_free[0] = s_v.n
                x3d = s_v.n

                @block.sync
                def _(sy, x3=x3, r0=r0, d=x3d):
                    sy.wait_ge(s_v.h, d)
                    s_st.dma(sy, st["xres"][r0:r0 + 128, :], x3[:])

                ln3d = ln_block1(x3, ln3t, s_wait_v=x3d,
                                 st_wait=ln3_st.get(ci - 1, 0) if ci >= 1 else 0)

                @block.vector
                def _(v, d=ln3d, ci=ci):
                    v.reduce_max(out=lnst[:, 7:8], in_=ln3t[:], axis=AX.X)
                    v.drain()
                    v.tensor_tensor(out=mxacc[:, 5:6], in0=mxacc[:, 5:6],
                                    in1=lnst[:, 7:8], op=ALU.max)
                    v.drain().then_inc(s_v.h, 1)
                s_v.n += 1
                chunk_done[ci] = s_v.n

                @block.sync
                def _(sy, r0=r0, d=chunk_done[ci]):
                    sy.wait_ge(s_v.h, d)
                    s_st.dma(sy, st["ln1_sp"][r0:r0 + 128, :], ln3t[:])
                ln3_st[ci] = s_st.n
            bsc.flush()



def emit_l1f(st):
    nc, block, C, W, sb, ps = (st[k] for k in ("nc", "block", "C", "W", "sb", "ps"))
    s_ld, s_gd, s_t, s_v, s_s, s_g, s_st = (st["sems"][k] for k in
                                            ("ld", "gd", "t", "v", "s", "g", "st"))
    mxacc, scal = st["mxacc"], st["scal"]
    st["phase_barrier"]()
    st["arena"].reset()
    pb = st["pb"]
    p_bf = st["pb_bf"]
    nch = _nchunks()
    C_ONESROW = st["C_ONESROW"]

    def row5(v, rrow, wrow):
        v.tensor_scalar(out=wrow[0:1, 0:1], in0=rrow[0:1, 0:1], scalar1=1e-8,
                        scalar2=1.0 / 255.0, op0=ALU.max, op1=ALU.mult)
        v.drain()
        v.reciprocal(out=wrow[0:1, 1:2], in_=wrow[0:1, 0:1])
        v.drain()
    c5, ar5_done = emit_allreduce(st, 4, [5], row5, 2)
    S1f, IS1f = c5, c5 + 1

    mark = st["arena"].mark()
    wT1 = sb("l1f_wT1", [128, 8 * 4096], BF16)
    sw1 = sb("l1f_sw1", [128, 32], F32)
    isw1 = sb("l1f_isw1", [128, 32], F32)
    s1sw = sb("l1f_s1sw", [128, 32], F32)
    br1 = sb("l1f_br1", [1, 4096], F32)
    mark_wq = st["arena"].mark()
    wld = [sb(f"l1f_wld{i}", [128, 1024], F32) for i in range(2)]
    wi32 = sb("l1f_wi32", [128, 1024], I32)
    wbf = sb("l1f_wbf", [128, 1024], BF16)
    mcol = sb("l1f_mcol", [128, 1], F32)

    ld_done = {}
    ev_done = {}
    for t in range(32):
        @block.gpsimd
        def _(g, t=t, prev=ev_done.get(t - 2, 0)):
            g.wait_ge(s_v.h, prev)
            s_gd.dma(g, wld[t % 2][:], W["l1f_w1"][t * 128:(t + 1) * 128, :])
        ld_done[t] = s_gd.n

        @block.vector
        def _(v, t=t, ld=ld_done[t]):
            v.wait_ge(s_gd.h, ld)
            v.tensor_reduce(out=mcol[:], in_=wld[t % 2][:], axis=AX.X, op=ALU.max,
                            apply_absolute_value=True)
            v.drain()
            v.tensor_scalar(out=sw1[:, t:t + 1], in0=mcol[:], scalar1=1e-8,
                            scalar2=1.0 / 127.0, op0=ALU.max, op1=ALU.mult)
            v.drain()
            v.reciprocal(out=isw1[:, t:t + 1], in_=sw1[:, t:t + 1])
            v.drain()
            v.tensor_scalar(out=wi32[:], in0=wld[t % 2][:], scalar1=isw1[:, t:t + 1],
                            scalar2=None, op0=ALU.mult)
            v.drain()
            v.tensor_copy(out=wbf[:], in_=wi32[:])
            v.drain().then_inc(s_v.h, 1)
        s_v.n += 1
        qd = s_v.n

        @block.tensor
        def _(t2, qd=qd, pf=st.get("l1f_pbf", 0)):
            t2.wait_ge(s_v.h, qd)
            t2.wait_ge(s_v.h, pf)
            for c in range(8):
                li = t2.transpose(p_bf[:, c * 128:(c + 1) * 128],
                                  wbf[:, c * 128:(c + 1) * 128], C["ident"])
            li.then_inc(s_t.h, 1)
        s_t.n += 1
        td = s_t.n

        @block.vector
        def _(v, t=t, td=td):
            v.wait_ge(s_t.h, td)
            v.tensor_copy(
                out=wT1[:].rearrange("p (c f) -> p c f", f=4096)[:, :, t * 128:(t + 1) * 128],
                in_=p_bf[:].rearrange("p (c f) -> p c f", f=128))
            v.drain().then_inc(s_v.h, 1)
        s_v.n += 1
        st["l1f_pbf"] = s_v.n
        ev_done[t] = s_v.n
    w1_done = s_v.n

    b1f = sb("l1f_b1f", [128, 32], F32)
    bq1i = sb("l1f_bq1i", [128, 32], I32)
    bq1f = sb("l1f_bq1f", [128, 32], F32)

    @block.gpsimd
    def _(g):
        s_gd.dma(g, b1f[:], W["l1f_b1_fm"][:])
    b1_ld = s_gd.n

    bqb1 = sb("l1f_bqb1", [128, 32], BF16)
    brt1 = sb("l1f_brt1", [32, 128], BF16)

    @block.vector
    def _(v, ld=b1_ld, w1d=w1_done, ar=ar5_done):
        v.wait_ge(s_gd.h, ld)
        v.wait_ge(s_v.h, max(w1d, ar))
        v.tensor_tensor(out=bq1f[:], in0=b1f[:], in1=isw1[:], op=ALU.mult)
        v.drain()
        v.tensor_scalar(out=bq1i[:], in0=bq1f[:], scalar1=scal[:, IS1f:IS1f + 1],
                        scalar2=None, op0=ALU.mult)
        v.drain()
        v.tensor_copy(out=bqb1[:], in_=bq1i[:])
        v.drain()
        v.tensor_scalar(out=s1sw[:], in0=sw1[:], scalar1=scal[:, S1f:S1f + 1],
                        scalar2=None, op0=ALU.mult)
        v.drain().then_inc(s_v.h, 1)
    s_v.n += 1
    bq1_done = s_v.n

    @block.tensor
    def _(t2, d=bq1_done, pf=st["l1f_pbf"]):
        t2.wait_ge(s_v.h, d)
        t2.wait_ge(s_v.h, pf)
        t2.transpose(p_bf[0:32, 0:128], bqb1[:], C["ident"]).then_inc(s_t.h, 1)
    s_t.n += 1
    bq1_t = s_t.n

    @block.vector
    def _(v, d=bq1_t):
        v.wait_ge(s_t.h, d)
        v.tensor_copy(out=brt1[:], in_=p_bf[0:32, 0:128])
        v.drain().then_inc(s_v.h, 1)
    s_v.n += 1
    st["l1f_pbf"] = s_v.n

    @block.gpsimd
    def _(g, d=s_v.n):
        g.wait_ge(s_v.h, d)
        s_gd.dma(g, st["bb_sp"][:], brt1[:])

    @block.sync
    def _(sy, d=s_gd.n):
        sy.wait_ge(s_gd.h, d)
        s_ld.dma(sy, br1[0:1, :],
                 st["bb_sp"].rearrange("t p -> (t p)").unsqueeze(0))
    br1_ld = s_ld.n

    @block.vector
    def _(v, ld=br1_ld):
        v.wait_ge(s_ld.h, ld)
        v.drain().then_inc(s_v.h, 1)
    s_v.n += 1
    br1_done = s_v.n

    st["arena"].rollback(mark_wq)
    lb = [sb(f"l1f_lb{i}", [128, E], BF16) for i in range(2)]
    xqi = sb("l1f_xqi", [128, E], I32)
    xqt = sb("l1f_xqt", [128, E], BF16)
    xqf = sb("l1f_xqf", [128, E], BF16)
    hraw = sb("l1f_hraw", [128, 2048], BF16)
    hT = sb("l1f_hT", [128, 4096], BF16)
    hf = [sb(f"l1f_hf{i}", [128, 4096], BF16) for i in range(2)]
    mc2 = sb("l1f_mc2", [128, 1], F32)

    lds = {}
    hst = {}
    mm_done = 0
    ev_tail = 0
    for ci in range(nch):
        @block.sync
        def _(sy, ci=ci, prev=lds.get(ci - 2, 0)):
            if prev:
                sy.wait_ge(s_v.h, prev)
            r0 = ci * 128
            s_ld.dma(sy, lb[ci % 2][:], st["ln1_sp"][r0:r0 + 128, :])
        lb_ld = s_ld.n

        @block.vector
        def _(v, ci=ci, ld=lb_ld, br=br1_done):
            v.wait_ge(s_ld.h, ld)
            if ci == 0:
                v.wait_ge(s_v.h, br)
            v.tensor_scalar(out=xqi[:], in0=lb[ci % 2][:],
                            scalar1=scal[:, IS1f:IS1f + 1], scalar2=0.0,
                            op0=ALU.mult, op1=ALU.max)
            v.drain()
            v.tensor_copy(out=xqt[:], in_=xqi[:])
            v.drain().then_inc(s_v.h, 1)
        s_v.n += 1
        lds[ci] = s_v.n
        xq_done = s_v.n

        @block.tensor
        def _(t2, d=xq_done, pf=st["l1f_pbf"]):
            t2.wait_ge(s_v.h, d)
            t2.wait_ge(s_v.h, pf)
            for c in range(8):
                li = t2.transpose(p_bf[:, c * 128:(c + 1) * 128],
                                  xqt[:, c * 128:(c + 1) * 128], C["ident"])
            li.then_inc(s_t.h, 1)
        s_t.n += 1
        tq = s_t.n

        @block.vector
        def _(v, d=tq):
            v.wait_ge(s_t.h, d)
            v.tensor_copy(out=xqf[:], in_=p_bf[:])
            v.drain().then_inc(s_v.h, 1)
        s_v.n += 1
        st["l1f_pbf"] = s_v.n
        xqf_done = s_v.n

        for half in range(2):
            @block.tensor
            def _(t2, half=half, d=xqf_done, evt=ev_tail):
                t2.wait_ge(s_v.h, d)
                t2.wait_ge(s_v.h, evt)
                for b in range(4):
                    fb = (half * 4 + b) * 512
                    for c in range(8):
                        t2.matmul(pb[b][:], xqf[:, c * 128:(c + 1) * 128],
                                  wT1[:, c * 4096 + fb: c * 4096 + fb + 512],
                                  start=(c == 0), stop=False)
                for b in range(4):
                    fb = (half * 4 + b) * 512
                    li = t2.matmul(pb[b][:], C_ONESROW[:], br1[0:1, fb:fb + 512],
                                   start=False, stop=True)
                li.then_inc(s_t.h, 1)
            s_t.n += 1
            hm = s_t.n

            @block.vector
            def _(v, d=hm):
                v.wait_ge(s_t.h, d)
                for b in range(4):
                    v.tensor_copy(out=hraw[:, b * 512:(b + 1) * 512], in_=pb[b][:])
                v.drain().then_inc(s_v.h, 1)
            s_v.n += 1
            ev_tail = s_v.n
            hr_done = s_v.n

            for r in range(2):
                @block.tensor
                def _(t2, r=r, d=hr_done, pf=st["l1f_pbf"]):
                    t2.wait_ge(s_v.h, d)
                    t2.wait_ge(s_v.h, pf)
                    for c in range(8):
                        li = t2.transpose(p_bf[:, c * 128:(c + 1) * 128],
                                          hraw[:, r * 1024 + c * 128: r * 1024 + (c + 1) * 128],
                                          C["ident"])
                    li.then_inc(s_t.h, 1)
                s_t.n += 1
                tr = s_t.n

                @block.vector
                def _(v, half=half, r=r, d=tr):
                    v.wait_ge(s_t.h, d)
                    u0 = (half * 16 + r * 8) * 128
                    v.tensor_copy(out=hT[:, u0:u0 + 1024], in_=p_bf[:])
                    v.drain().then_inc(s_v.h, 1)
                s_v.n += 1
                st["l1f_pbf"] = s_v.n

        @block.vector
        def _(v, ci=ci, hs=hst.get(ci - 2, 0)):
            if hs:
                v.wait_ge(s_st.h, hs)
            v.tensor_tensor(out=hf[ci % 2][:].rearrange("p (m f) -> p m f", f=128),
                            in0=hT[:].rearrange("p (m f) -> p m f", f=128),
                            in1=s1sw[:].unsqueeze(2).broadcast_to([128, 32, 128]),
                            op=ALU.mult)
            v.drain()
            v.reduce_max(out=mc2[:], in_=hf[ci % 2][:], axis=AX.X)
            v.drain()
            v.tensor_tensor(out=mxacc[:, 6:7], in0=mxacc[:, 6:7], in1=mc2[:],
                            op=ALU.max)
            v.drain().then_inc(s_v.h, 1)
        s_v.n += 1
        hf_done = s_v.n

        @block.sync
        def _(sy, ci=ci, d=hf_done):
            sy.wait_ge(s_v.h, d)
            s_st.dma(sy, st["h_sp"][ci * 128:(ci + 1) * 128, :], hf[ci % 2][:])
        hst[ci] = s_st.n

    def row6(v, rrow, wrow):
        v.tensor_scalar(out=wrow[0:1, 0:1], in0=rrow[0:1, 0:1], scalar1=1e-8,
                        scalar2=1.0 / 255.0, op0=ALU.max, op1=ALU.mult)
        v.drain()
        v.reciprocal(out=wrow[0:1, 1:2], in_=wrow[0:1, 0:1])
        v.drain()
    c6, ar6_done = emit_allreduce(st, 5, [6], row6, 2)
    S2f, IS2f = c6, c6 + 1

    st["phase_barrier"]()
    st["arena"].rollback(mark)
    wT2 = sb("l1f_wT2", [128, 32 * 1024], BF16)
    sw2 = sb("l1f_sw2", [128, 8], F32)
    isw2 = sb("l1f_isw2", [128, 8], F32)
    br2 = sb("l1f_br2", [1, 1024], F32)
    mark_w2 = st["arena"].mark()
    w2ld = [sb(f"l1f_w2ld{i}", [128, 4096], F32) for i in range(2)]
    w2i = sb("l1f_w2i", [128, 4096], I32)
    w2bf = sb("l1f_w2bf", [128, 4096], BF16)
    mc3 = sb("l1f_mc3", [128, 1], F32)

    ev2 = {}
    for t in range(8):
        @block.gpsimd
        def _(g, t=t, prev=ev2.get(t - 2, 0)):
            g.wait_ge(s_v.h, prev)
            s_gd.dma(g, w2ld[t % 2][:], W["l1f_w2"][t * 128:(t + 1) * 128, :])
        w2_ld = s_gd.n

        @block.vector
        def _(v, t=t, ld=w2_ld):
            v.wait_ge(s_gd.h, ld)
            v.tensor_reduce(out=mc3[:], in_=w2ld[t % 2][:], axis=AX.X, op=ALU.max,
                            apply_absolute_value=True)
            v.drain()
            v.tensor_scalar(out=sw2[:, t:t + 1], in0=mc3[:], scalar1=1e-8,
                            scalar2=1.0 / 127.0, op0=ALU.max, op1=ALU.mult)
            v.drain()
            v.reciprocal(out=isw2[:, t:t + 1], in_=sw2[:, t:t + 1])
            v.drain()
            v.tensor_scalar(out=w2i[:], in0=w2ld[t % 2][:], scalar1=isw2[:, t:t + 1],
                            scalar2=None, op0=ALU.mult)
            v.drain()
            v.tensor_scalar(out=w2bf[:], in0=w2i[:], scalar1=sw2[:, t:t + 1],
                            scalar2=None, op0=ALU.mult)
            v.drain().then_inc(s_v.h, 1)
        s_v.n += 1
        q2 = s_v.n

        for r in range(4):
            @block.tensor
            def _(t2, r=r, d=q2, pf=st["l1f_pbf"]):
                t2.wait_ge(s_v.h, d)
                t2.wait_ge(s_v.h, pf)
                for c in range(8):
                    li = t2.transpose(p_bf[:, c * 128:(c + 1) * 128],
                                      w2bf[:, r * 1024 + c * 128: r * 1024 + (c + 1) * 128],
                                      C["ident"])
                li.then_inc(s_t.h, 1)
            s_t.n += 1
            t2d = s_t.n

            @block.vector
            def _(v, t=t, r=r, d=t2d):
                v.wait_ge(s_t.h, d)
                v.tensor_copy(
                    out=wT2[:].rearrange("p (c f) -> p c f", f=1024)[
                        :, r * 8:(r + 1) * 8, t * 128:(t + 1) * 128],
                    in_=p_bf[:].rearrange("p (c f) -> p c f", f=128))
                v.drain().then_inc(s_v.h, 1)
            s_v.n += 1
            st["l1f_pbf"] = s_v.n
        ev2[t] = s_v.n
    w2_done = s_v.n

    b2f = sb("l1f_b2f", [128, 8], F32)
    bq2i = sb("l1f_bq2i", [128, 8], I32)
    bq2f = sb("l1f_bq2f", [128, 8], F32)

    @block.gpsimd
    def _(g):
        s_gd.dma(g, b2f[:], W["l1f_b2_fm"][:])
    b2_ld = s_gd.n

    bqb2 = sb("l1f_bqb2", [128, 8], BF16)
    brt2 = sb("l1f_brt2", [8, 128], BF16)

    @block.vector
    def _(v, ld=b2_ld, w2d=w2_done, ar=ar6_done):
        v.wait_ge(s_gd.h, ld)
        v.wait_ge(s_v.h, max(w2d, ar))
        v.tensor_tensor(out=bq2f[:], in0=b2f[:], in1=isw2[:], op=ALU.mult)
        v.drain()
        v.tensor_scalar(out=bq2i[:], in0=bq2f[:], scalar1=scal[:, IS2f:IS2f + 1],
                        scalar2=None, op0=ALU.mult)
        v.drain()
        v.tensor_tensor(out=bq2f[:], in0=bq2i[:], in1=sw2[:], op=ALU.mult)
        v.drain()
        v.tensor_copy(out=bqb2[:], in_=bq2f[:])
        v.drain().then_inc(s_v.h, 1)
    s_v.n += 1
    bq2_done = s_v.n

    @block.tensor
    def _(t2, d=bq2_done, pf=st["l1f_pbf"]):
        t2.wait_ge(s_v.h, d)
        t2.wait_ge(s_v.h, pf)
        t2.transpose(p_bf[0:8, 0:128], bqb2[:], C["ident"]).then_inc(s_t.h, 1)
    s_t.n += 1
    bq2_t = s_t.n

    @block.vector
    def _(v, d=bq2_t):
        v.wait_ge(s_t.h, d)
        v.tensor_copy(out=brt2[:], in_=p_bf[0:8, 0:128])
        v.drain().then_inc(s_v.h, 1)
    s_v.n += 1
    st["l1f_pbf"] = s_v.n

    @block.gpsimd
    def _(g, d=s_v.n):
        g.wait_ge(s_v.h, d)
        s_gd.dma(g, st["bb_sp"][0:8, :], brt2[:])

    @block.sync
    def _(sy, d=s_gd.n):
        sy.wait_ge(s_gd.h, d)
        s_ld.dma(sy, br2[0:1, :],
                 st["bb_sp"][0:8, :].rearrange("t p -> (t p)").unsqueeze(0))
    br2_ld = s_ld.n

    @block.vector
    def _(v, ld=br2_ld):
        v.wait_ge(s_ld.h, ld)
        v.drain().then_inc(s_v.h, 1)
    s_v.n += 1
    br2_done = s_v.n

    st["arena"].rollback(mark_w2)
    hb = [sb(f"l1f_hb{i}", [128, 4096], BF16) for i in range(2)]
    xq2i = sb("l1f_xq2i", [128, 4096], I32)
    xq2 = sb("l1f_xq2", [128, 4096], BF16)
    x3b = [sb(f"l1f_x3b{i}", [128, E], F32) for i in range(2)]
    yb = [sb(f"l1f_yb{i}", [128, E], F32) for i in range(2)]

    lds2 = {}
    yst = {}
    mm2_tail = 0
    for ci in range(nch):
        @block.sync
        def _(sy, ci=ci, prev=lds2.get(ci - 2, 0)):
            if prev:
                sy.wait_ge(s_v.h, prev)
            r0 = ci * 128
            s_ld.dma(sy, hb[ci % 2][:], st["h_sp"][r0:r0 + 128, :])
            sy.wait_ge(s_ld.h, s_ld.n)
            s_ld.dma(sy, x3b[ci % 2][:], st["xres"][r0:r0 + 128, :])
        hb_ld = s_ld.n

        @block.vector
        def _(v, ci=ci, ld=hb_ld, br=br2_done, mt=mm2_tail):
            v.wait_ge(s_ld.h, ld)
            if ci == 0:
                v.wait_ge(s_v.h, br)
            if mt:
                v.wait_ge(s_t.h, mt)
            v.tensor_scalar(out=xq2i[:], in0=hb[ci % 2][:],
                            scalar1=scal[:, IS2f:IS2f + 1], scalar2=0.0,
                            op0=ALU.mult, op1=ALU.max)
            v.drain()
            v.tensor_copy(out=xq2[:], in_=xq2i[:])
            v.drain().then_inc(s_v.h, 1)
        s_v.n += 1
        lds2[ci] = s_v.n
        xq2_done = s_v.n

        @block.tensor
        def _(t2, d=xq2_done, ys=yst.get(ci - 1, 0)):
            t2.wait_ge(s_v.h, d)
            for b2 in range(2):
                for m in range(32):
                    t2.matmul(pb[b2][:], xq2[:, m * 128:(m + 1) * 128],
                              wT2[:, m * 1024 + b2 * 512: m * 1024 + b2 * 512 + 512],
                              start=(m == 0), stop=False)
                li = t2.matmul(pb[b2][:], C_ONESROW[:],
                               br2[0:1, b2 * 512:(b2 + 1) * 512],
                               start=False, stop=True)
            li.then_inc(s_t.h, 1)
        s_t.n += 1
        mm2_tail = s_t.n

        @block.vector
        def _(v, ci=ci, d=mm2_tail, ys=yst.get(ci - 2, 0)):
            v.wait_ge(s_t.h, d)
            if ys:
                v.wait_ge(s_st.h, ys)
            for b2 in range(2):
                v.scalar_tensor_tensor(out=yb[ci % 2][:, b2 * 512:(b2 + 1) * 512],
                                       in0=pb[b2][:],
                                       scalar=scal[:, S2f:S2f + 1],
                                       in1=x3b[ci % 2][:, b2 * 512:(b2 + 1) * 512],
                                       op0=ALU.mult, op1=ALU.add)
            v.drain().then_inc(s_v.h, 1)
        s_v.n += 1
        y_done = s_v.n

        @block.sync
        def _(sy, ci=ci, d=y_done):
            sy.wait_ge(s_v.h, d)
            s_st.dma(sy, st["y_out"][ci * 128:(ci + 1) * 128, :], yb[ci % 2][:])
        yst[ci] = s_st.n



def emit_finish(st):
    block = st["block"]
    st["phase_barrier"]()
    st["arena"].reset()
    s_ld, s_st = st["sems"]["ld"], st["sems"]["st"]
    sb = st["sb"]
    if DEV_PHASES >= 4:
        @block.sync
        def _(sy):
            sy.wait_ge(s_st.h, s_st.n)
        return
    tmp = [sb(f"fin_tmp{i}", [128, E], F32) for i in range(2)]

    @block.sync
    def _(sy):
        sy.wait_ge(s_st.h, s_st.n)
        for ci in range(_nchunks()):
            r0 = ci * 128
            s_ld.dma(sy, tmp[ci % 2][:], st["xres"][r0:r0 + 128, :])
            sy.wait_ge(s_ld.h, s_ld.n)
            s_st.dma(sy, st["y_out"][r0:r0 + 128, :], tmp[ci % 2][:])
        sy.wait_ge(s_st.h, s_st.n)


WMAP = {
    "l0_wq": "L0_attn_Wq", "l0_wk": "L0_attn_Wk", "l0_wv": "L0_attn_Wv",
    "l0_wo": "L0_attn_Wo", "l0_wf1": "L0_attn_Wf1", "l0_wf2": "L0_attn_Wf2",
    "l0_wg1": "L0_attn_Wg1", "l0_wg2": "L0_attn_Wg2",
    "l0f_wgate": "L0_ffn_w_gate", "l0f_wup": "L0_ffn_w_up", "l0f_wdown": "L0_ffn_w_down",
    "l1_wq": "L1_attn_wq", "l1_wk": "L1_attn_wk", "l1_wv": "L1_attn_wv",
    "l1_wgq": "L1_attn_wgq", "l1_wgk": "L1_attn_wgk", "l1_wgv": "L1_attn_wgv",
    "l1_rout": "L1_attn_r_out_w", "l1_wout": "L1_attn_w_out",
    "l1f_w1": "L1_ffn_w1", "l1f_w2": "L1_ffn_w2",
}


def make_in_maps(inputs):
    x = np.ascontiguousarray(inputs["L0_attn_x"], dtype=np.float32)
    cbv, _ = const_blob_bf16()
    base = {k: np.ascontiguousarray(inputs[v], dtype=np.float32) for k, v in WMAP.items()}
    base["l0_nw_fm"] = np.ascontiguousarray(
        np.asarray(inputs["L0_attn_norm_weight"], dtype=np.float32).reshape(8, 128).T)
    base["l1f_b1_fm"] = np.ascontiguousarray(
        np.asarray(inputs["L1_ffn_b1"], dtype=np.float32).reshape(32, 128).T)
    base["l1f_b2_fm"] = np.ascontiguousarray(
        np.asarray(inputs["L1_ffn_b2"], dtype=np.float32).reshape(8, 128).T)
    glw = np.asarray(inputs["L1_attn_gate_local_w"], dtype=np.float32)
    grw = np.asarray(inputs["L1_attn_gate_remote_w"], dtype=np.float32)
    glw_fm = np.zeros((128, 32), np.float32)
    for h in range(16):
        glw_fm[(h % 2) * 64:(h % 2) * 64 + 64, h] = glw[h]
        glw_fm[(h % 2) * 64:(h % 2) * 64 + 64, 16 + h] = grw[h]
    base["l1_glw_fm"] = np.ascontiguousarray(glw_fm)
    base["constb"] = cbv
    base["constf"] = const_blob_f32()
    in_maps = []
    for core in range(8):
        m = dict(base)
        m["x_in"] = np.ascontiguousarray(
            x[core * B_PER_CORE:(core + 1) * B_PER_CORE].reshape(TC, E))
        in_maps.append(m)
    return in_maps


def kernel(**inputs):
    nc = build_program()
    in_maps = make_in_maps(inputs)
    res = run_bass_kernel_spmd(nc, in_maps, core_ids=list(range(8)))
    out = np.concatenate([r["y_out"].reshape(B_PER_CORE, S, E) for r in res.results], axis=0)
    return out.astype(np.float32)
